# version 07
# baseline (speedup 1.0000x reference)
import numpy as np
import ml_dtypes
from contextlib import ExitStack

import concourse.bass as bass
import concourse.mybir as mybir
from concourse.bass_utils import run_bass_kernel_spmd

F32 = mybir.dt.float32
BF16 = mybir.dt.bfloat16
I32 = mybir.dt.int32
AF = mybir.ActivationFunctionType
ALU = mybir.AluOpType
AX = mybir.AxisListType

T = 4096
D = 1024
DI = 2048
DS = 128
NH = 32
HP = 64
CS = 64
NCK = T // CS
NPAIR = NCK // 2
DPROJ = 4384
HID = 3072
EPS = 1e-5
BIG = 1e30
NT = 8
TT = 512

ENGINES = ("sync", "tensor", "vector", "scalar", "gpsimd")
DMA_ENGINES = ("sync", "scalar", "gpsimd")

DBG = []


class Rec:

    def __init__(self, nc, getsem, ctr, shared):
        self.nc = nc
        self.getsem = getsem
        self.ctr = ctr
        self.last_w, self.readers = shared
        self.ops = []

    def _rec(self, engine, fn, own, inc, reads, writes, forced=()):
        deps = {}
        for k in reads:
            h = self.last_w.get(k)
            if h is not None:
                deps[h] = True
        for k in writes:
            h = self.last_w.get(k)
            if h is not None:
                deps[h] = True
            for rh in self.readers.get(k, ()):
                deps[rh] = True
        for h in forced:
            if h is not None:
                deps[h] = True
        self.getsem(own)
        self.ctr[own] = self.ctr.get(own, 0) + inc
        handle = (own, self.ctr[own])
        emit_deps = [h for h in deps
                     if not (h[0] == own and h[1] >= handle[1])]
        self.ops.append((engine, fn, sorted(emit_deps), own, inc))
        for k in writes:
            self.last_w[k] = handle
            self.readers[k] = []
        for k in reads:
            self.readers.setdefault(k, []).append(handle)
        return handle

    def op(self, engine, fn, reads=(), writes=(), inc=1):
        return self._rec(engine, fn, "eng:" + engine, inc, reads, writes)

    def dma(self, engine, out, in_, stream=None, reads=(), writes=(), **kw):
        if stream is None:
            stream = (tuple(writes) + tuple(reads) + ("misc",))[0]
        xp = self.last_w.get("XPOSE")
        return self._rec(engine, lambda e: e.dma_start(out, in_, **kw),
                         "dma:" + stream, 16, reads, writes,
                         forced=[xp] if xp else ())

    def xpose(self, out, in_, reads=(), writes=()):
        forced = [(s, v) for s, v in self.ctr.items()
                  if s.startswith("dma:") and v > 0]
        return self._rec("sync", lambda e: e.dma_start_transpose(out, in_),
                         "dma:xpose", 16, reads, tuple(writes) + ("XPOSE",),
                         forced=forced)

    def build(self):
        getsem, ops = self.getsem, self.ops
        with self.nc.Block() as blk:
            for eng_name in ENGINES:
                mine = [(f, d, own, inc) for (e, f, d, own, inc) in ops
                        if e == eng_name]
                if not mine:
                    continue

                def emit(eng, mine=mine):
                    waited = {}
                    for fn, deps, own, inc in mine:
                        for (sname, pval) in deps:
                            if waited.get(sname, -1) < pval:
                                eng.wait_ge(getsem(sname), pval)
                                waited[sname] = pval
                        inst = fn(eng)
                        if inst is None:
                            inst = eng.nop()
                        if inc:
                            inst.then_inc(getsem(own), inc)
                getattr(blk, eng_name)(emit)


def _np_consts():
    i = np.arange(64)
    tril_is = (i[:, None] <= i[None, :]).astype(np.float32)
    dup = lambda a: np.concatenate([a, a], 0)
    bf = ml_dtypes.bfloat16
    return dict(
        TRILNEG=dup(-tril_is).astype(bf),
        TRILLT=dup(tril_is).astype(bf),
        MASKRX=dup(np.repeat(-BIG * (i[None, :] == i[:, None] - 1).astype(np.float32)[:, :, None], 32, axis=2).reshape(64, 2048)).astype(bf),
        ONES64=np.ones((128, 64), np.float32).astype(bf),
        TRILSL=dup((i[None, :] >= i[:, None]).astype(np.float32)),
        IDENT=np.eye(128, dtype=np.float32),
        ONESR=np.ones((1, 128), np.float32).astype(bf),
        IDENTB=np.eye(128, dtype=np.float32).astype(bf),
        CONSTS=np.array([[1e-5, 1.0, 0.0, 0.0]], np.float32),
    )


def bc(ap, shape, axis):
    return ap.unsqueeze(axis).broadcast_to(shape)


def build_nc(dbg=()):
    dbg = set(dbg)
    nc = bass.Bass("TRN2", target_bir_lowering=False, num_devices=8)

    u_seg = nc.dram_tensor("u_seg", [T, D], F32, kind="ExternalInput").ap()
    u_halo = nc.dram_tensor("u_halo", [16, D], F32, kind="ExternalInput").ap()
    W_in = nc.dram_tensor("W_in", [D, DPROJ], F32, kind="ExternalInput").ap()
    conv_w = nc.dram_tensor("conv_w", [2304, 4], F32, kind="ExternalInput").ap()
    conv_b = nc.dram_tensor("conv_b", [2304], F32, kind="ExternalInput").ap()
    dt_bias = nc.dram_tensor("dt_bias", [NH], F32, kind="ExternalInput").ap()
    A_log = nc.dram_tensor("A_log", [NH], F32, kind="ExternalInput").ap()
    norm_w = nc.dram_tensor("norm_w", [DI], F32, kind="ExternalInput").ap()
    norm_b = nc.dram_tensor("norm_b", [DI], F32, kind="ExternalInput").ap()
    W_out = nc.dram_tensor("W_out", [DI, D], F32, kind="ExternalInput").ap()
    wg = nc.dram_tensor("wg", [HID, D], I32, kind="ExternalInput").ap()
    wu = nc.dram_tensor("wu", [HID, D], I32, kind="ExternalInput").ap()
    wd = nc.dram_tensor("wd", [D, HID], I32, kind="ExternalInput").ap()
    TRILNEG = nc.dram_tensor("TRILNEG", [128, 64], BF16, kind="ExternalInput").ap()
    TRILLT = nc.dram_tensor("TRILLT", [128, 64], BF16, kind="ExternalInput").ap()
    MASKRX = nc.dram_tensor("MASKRX", [128, 2048], BF16, kind="ExternalInput").ap()
    ONES64 = nc.dram_tensor("ONES64", [128, 64], BF16, kind="ExternalInput").ap()
    TRILSL = nc.dram_tensor("TRILSL", [128, 64], F32, kind="ExternalInput").ap()
    IDENT = nc.dram_tensor("IDENT", [128, 128], F32, kind="ExternalInput").ap()
    ONESR = nc.dram_tensor("ONESR", [1, 128], BF16, kind="ExternalInput").ap()
    IDENTB = nc.dram_tensor("IDENTB", [128, 128], BF16, kind="ExternalInput").ap()
    CONSTS = nc.dram_tensor("CONSTS", [1, 4], F32, kind="ExternalInput").ap()
    SEL2 = nc.dram_tensor("SEL2", [8, 8], F32, kind="ExternalInput").ap()
    GMASK = nc.dram_tensor("GMASK", [8, 1], F32, kind="ExternalInput").ap()
    out_seg = nc.dram_tensor("out_seg", [T, D], F32, kind="ExternalOutput").ap()

    def scratch(name, shape, dtype):
        kind = "ExternalOutput" if name in dbg else "Internal"
        return nc.dram_tensor(name, shape, dtype, kind=kind).ap()

    SP_X0N = scratch("SP_X0N", [T, D], BF16)
    SP_X0NH = scratch("SP_X0NH", [16, D], BF16)
    SP_Z = scratch("SP_Z", [T, DI], BF16)
    SP_XC = scratch("SP_XC", [2176, T], BF16)
    SP_XS = scratch("SP_XS", [T, DI], BF16)
    SP_XD = scratch("SP_XD", [T, DI], BF16)
    SP_YN = scratch("SP_YN", [T, DI], BF16)
    SP_X1 = scratch("SP_X1", [T, D], F32)
    SP_X1N = scratch("SP_X1N", [T, D], BF16)
    SP_H = scratch("SP_H", [HID, T], BF16)
    SP_WGN = scratch("SP_WGN", [HID, D], BF16)
    SP_WUN = scratch("SP_WUN", [HID, D], BF16)
    SP_WDN = scratch("SP_WDN", [D, HID], BF16)
    SP_SB = scratch("SP_SB", [192, NH], F32)
    SP_NEGA = scratch("SP_NEGA", [NH, 1], F32)
    SP_W8 = scratch("SP_W8", [8, NH], F32)
    CBT = scratch("CBT", [128, 2049], F32)
    SP_DBGS = scratch("SP_DBGS", [128, 2049 + 64], F32)
    SP_DBGS2 = scratch("SP_DBGS2", [128, 2049 + 64], F32)
    GATH = nc.dram_tensor("GATH", [1024, 2049], F32, kind="Internal", addr_space="Shared").ap()

    es_g = ExitStack()
    es_mid = ExitStack()
    es_b = ExitStack()

    def mk(stack, name, shape, dtype=F32):
        return stack.enter_context(nc.sbuf_tensor(name, shape, dtype))

    trilneg = mk(es_g, "trilneg", [128, 64], BF16)
    trillt = mk(es_g, "trillt", [128, 64], BF16)
    maskrx = mk(es_g, "maskrx", [128, 2048], BF16)
    ones64 = mk(es_g, "ones64", [128, 64], BF16)
    trilsl = mk(es_g, "trilsl", [128, 64], F32)
    ident = mk(es_g, "ident", [128, 128], F32)
    onesr = mk(es_g, "onesr", [1, 128], BF16)
    identb = mk(es_g, "identb", [128, 128], BF16)
    cst = mk(es_g, "cst", [128, 4], F32)
    negA_col = mk(es_g, "negA_col", [NH, 1], F32)
    negA_rep = mk(es_g, "negA_rep", [128, NH], F32)
    dtb_rep = mk(es_g, "dtb_rep", [128, NH], F32)
    convw_sb = mk(es_g, "convw_sb", [128, 18, 4], F32)
    convb_sb = mk(es_g, "convb_sb", [128, 18], F32)

    B_FM = mk(es_mid, "B_FM", [128, T], BF16)
    C_FM = mk(es_mid, "C_FM", [128, T], BF16)
    dt_TM = mk(es_mid, "dt_TM", [128, 32, NH], F32)
    a_TM = mk(es_mid, "a_TM", [128, 32, NH], F32)
    a_TMb = mk(es_mid, "a_TMb", [128, 32, NH], BF16)
    atot_HM = mk(es_mid, "atot_HM", [NH, NCK], F32)
    cpi_HM = mk(es_mid, "cpi_HM", [NH, NCK], F32)
    rc_HM = mk(es_mid, "rc_HM", [NH, NCK], F32)
    b2_HM = mk(es_mid, "b2_HM", [NH, NCK], F32)
    dexp_HM = mk(es_mid, "dexp_HM", [NH, NCK], F32)
    aseg_col = mk(es_mid, "aseg_col", [NH, 1], F32)
    cs_TM = mk(es_mid, "cs_TM", [128, 32, NH], F32)
    dsT = mk(es_mid, "dsT", [128, 32, NH], F32)
    ds2T = mk(es_mid, "ds2T", [128, 32, NH], F32)
    sdT = mk(es_mid, "sdT", [128, 32, NH], F32)

    sem_map = {}
    sem_ctr = {}

    def getsem(name):
        if name not in sem_map:
            sem_map[name] = es_g.enter_context(
                nc.semaphore("s_" + name.replace(":", "_")))
        return sem_map[name]

    cc_sem = es_g.enter_context(nc.semaphore("sem_cc"))
    shared = ({}, {})

    def rec():
        return Rec(nc, getsem, sem_ctr, shared)

    def ln_apply(r, pre, stat_reads, stat_aps, st, mv, rs, npart=128):
        for i, ap in enumerate(stat_aps):
            r.op("vector", lambda e, i=i, ap=ap: e.bn_stats(st[:, 6 * i:6 * (i + 1)], ap),
                 reads=tuple(stat_reads) + (pre + "st",), writes=(pre + "st",))
        r.op("vector", lambda e: e.bn_aggr(mv[:, :], st[:, 0:6 * len(stat_aps)]),
             reads=(pre + "st", pre + "mv"), writes=(pre + "mv",))
        r.op("scalar", lambda e: e.activation(rs[:, 0:1], mv[:, 1:2], AF.Ln,
                                              bias=cst[0:npart, 0:1]),
             reads=(pre + "mv", pre + "rs", "cst"), writes=(pre + "rs",))
        r.op("scalar", lambda e: e.activation(rs[:, 0:1], rs[:, 0:1], AF.Exp, scale=-0.5),
             reads=(pre + "rs",), writes=(pre + "rs",))
        r.op("vector", lambda e: e.scalar_tensor_tensor(
            rs[:, 1:2], mv[:, 0:1], -1.0, rs[:, 0:1], op0=ALU.mult, op1=ALU.mult),
            reads=(pre + "mv", pre + "rs"), writes=(pre + "rs",))

    es_win = ExitStack()
    W_in_sb = es_win.enter_context(nc.sbuf_tensor("W_in_sb", [128, 8, DPROJ], BF16))
    es0 = ExitStack()
    w32 = [mk(es0, f"w32_{i}", [128, DPROJ], F32) for i in range(2)]
    w1 = mk(es0, "w1", [NH, 2], F32)

    r = rec()
    for dst, src in ((trilneg, TRILNEG), (trillt, TRILLT), (maskrx, MASKRX),
                     (ones64, ONES64), (trilsl, TRILSL), (ident, IDENT),
                     (onesr, ONESR), (identb, IDENTB)):
        r.dma("sync", dst[:, :], src, writes=(dst.name,))
    r.dma("sync", cst[:, :], bc(CONSTS.rearrange("one c -> (one c)"), [128, 4], 0), writes=("cst",))
    r.dma("sync", dtb_rep[:, :], bc(dt_bias, [128, NH], 0), writes=("dtb_rep",))
    r.dma("sync", convw_sb[:, :, :], conv_w.rearrange("(m p) k -> p m k", p=128),
          writes=("convw",))
    r.dma("sync", convb_sb[:, :], conv_b.rearrange("(m p) -> p m", p=128),
          writes=("convb",), allow_slow_non_contiguous=True)
    r.dma("sync", w1[:, 0:1], A_log.unsqueeze(1), writes=("w1",))
    r.op("scalar", lambda e: e.activation(w1[:, 1:2], w1[:, 0:1], AF.Exp),
         reads=("w1",), writes=("w1e",))
    r.op("scalar", lambda e: e.mul(negA_col[:, :], w1[:, 1:2], -1.0),
         reads=("w1e",), writes=("negA_col",))
    r.dma("sync", SP_NEGA, negA_col[:, :], reads=("negA_col",), writes=("SP_NEGA",))
    r.dma("sync", negA_rep[:, :],
          bc(SP_NEGA.rearrange("h one -> (h one)"), [128, NH], 0),
          reads=("SP_NEGA",), writes=("negA_rep",))
    for k in range(8):
        buf, key = w32[k % 2], f"w32_{k % 2}"
        r.dma("sync", buf[:, :], W_in[128 * k:128 * (k + 1), :], writes=(key,))
        r.op("scalar",
             lambda e, k=k, buf=buf: e.activation(W_in_sb[:, k, :], buf[:, :], AF.Identity),
             reads=(key,), writes=("W_in_sb",))
    esA = ExitStack()
    u_t = [mk(esA, f"u_t{i}", [128, 4, D], F32) for i in range(2)]
    x0_t = [mk(esA, f"x0_t{i}", [128, 4, D], BF16) for i in range(2)]
    st_t = mk(esA, "st_t", [128, 12], F32)
    mv_t = mk(esA, "mv_t", [128, 2], F32)
    rs_t = mk(esA, "rs_t", [128, 2], F32)

    r.dma("sync", u_t[0][0:16, 0, :], u_halo, writes=("u_t0",))
    ln_apply(r, "a", ("u_t0",), [u_t[0][0:16, 0, 0:512], u_t[0][0:16, 0, 512:1024]],
             st_t[0:16, :], mv_t[0:16, :], rs_t[0:16, :], npart=16)
    r.op("scalar", lambda e: e.activation(x0_t[0][0:16, 0, :], u_t[0][0:16, 0, :],
                                          AF.Identity, scale=rs_t[0:16, 0:1],
                                          bias=rs_t[0:16, 1:2]),
         reads=("u_t0", "ars", "x0_t0"), writes=("x0_t0",))
    r.dma("scalar", SP_X0NH, x0_t[0][0:16, 0, :], stream="x0_t0", reads=("x0_t0",), writes=("SP_X0NH",))
    for t in range(NT):
        ub, xb = u_t[t % 2], x0_t[t % 2]
        uk, xk = f"u_t{t % 2}", f"x0_t{t % 2}"
        r.dma("sync", ub[:, :, :],
              u_seg[TT * t:TT * (t + 1), :].rearrange("(g p) d -> p g d", p=128),
              reads=(uk,), writes=(uk,))
        for g in range(4):
            ln_apply(r, "a", (uk,), [ub[:, g, 0:512], ub[:, g, 512:1024]], st_t, mv_t, rs_t)
            r.op("scalar", lambda e, g=g, ub=ub, xb=xb: e.activation(
                xb[:, g, :], ub[:, g, :], AF.Identity,
                scale=rs_t[:, 0:1], bias=rs_t[:, 1:2]),
                reads=(uk, "ars", xk), writes=(xk,))
        r.dma("scalar", SP_X0N[TT * t:TT * (t + 1), :].rearrange("(g p) d -> p g d", p=128),
              xb[:, :, :], stream=xk, reads=(xk,), writes=(f"SP_X0N:{t}",))
    r.build()
    esA.close()
    es0.close()

    esB = ExitStack()
    x0F = [mk(esB, "x0F0", [128, 8, TT], BF16)] * 2
    x0tm = mk(esB, "x0tm", [128, 4, D], BF16)
    x0hF = mk(esB, "x0hF", [128, 8, 16], BF16)
    z_t = [mk(esB, f"z_t{i}", [128, 1024], BF16) for i in range(2)]
    xc = mk(esB, "xc", [128, 18, 515], BF16)
    tail_b = mk(esB, "tail_b", [128, 18, 3], BF16)
    cacc = mk(esB, "cacc", [128, 2, TT], F32)
    hm_buf = mk(esB, "hm_buf", [NH, T], F32)
    cs_HM = mk(esB, "cs_HM", [NH, T], F32)
    sp1 = mk(esB, "sp1", [128, 32, NH], F32)
    sp2 = mk(esB, "sp2", [128, 32, NH], F32)
    psZ = esB.enter_context(nc.psum_tensor("psZ", [128, 2, 1024], F32))
    psX = esB.enter_context(nc.psum_tensor("psX", [128, 2, 512], F32))
    psM = esB.enter_context(nc.psum_tensor("psM", [128, 128], F32))
    psTb = esB.enter_context(nc.psum_tensor("psTb", [128, 128], BF16))

    WINK = ("W_in_sb",)
    r = rec()
    r.xpose(x0hF[:, :, :], SP_X0NH, reads=("SP_X0NH",), writes=("x0hF",))
    for m in range(18):
        def f_hmm(e, m=m):
            inst = None
            for k in range(8):
                inst = e.matmul(psM[:, 0:16], W_in_sb[:, k, DI + 128 * m:DI + 128 * (m + 1)],
                                x0hF[:, k, :], start=(k == 0), stop=(k == 7))
            return inst
        r.op("tensor", f_hmm, reads=("x0hF", "psM") + WINK, writes=("psM",))
        r.op("scalar", lambda e, m=m: e.activation(tail_b[:, m, :], psM[:, 13:16], AF.Identity),
             reads=("psM", "tail"), writes=("tail",))

    for t in range(NT):
        x0b, x0k = x0F[0], "x0F0"
        r.dma("sync", x0tm[:, :, :],
              SP_X0N[TT * t:TT * (t + 1), :].rearrange("(g p) d -> p g d", p=128),
              stream="x0tm", reads=("x0tm",), writes=("x0tm",))
        for g4 in range(4):
            for kc in range(8):
                r.op("tensor", lambda e, g4=g4, kc=kc: e.transpose(
                    psTb[:, :], x0tm[:, g4, 128 * kc:128 * (kc + 1)], identb[:, :]),
                    reads=("x0tm", "identb", "psTb"), writes=("psTb",))
                r.op("scalar", lambda e, g4=g4, kc=kc: e.activation(
                    x0b[:, kc, 128 * g4:128 * (g4 + 1)], psTb[:, :], AF.Identity),
                    reads=("psTb", x0k), writes=(x0k, "psTb"))
        xcb, xck = xc, "xc"
        for g in range(4):
            for n2 in range(2):
                zb, zbk = z_t[n2], f"z_t{n2}"
                def f_zmm(e, g=g, n2=n2, x0b=x0b):
                    inst = None
                    for half in range(2):
                        for k in range(8):
                            inst = e.matmul(psZ[:, n2, 512 * half:512 * (half + 1)],
                                            x0b[:, k, 128 * g:128 * (g + 1)],
                                            W_in_sb[:, k, 1024 * n2 + 512 * half:1024 * n2 + 512 * (half + 1)],
                                            start=(k == 0), stop=(k == 7))
                    return inst
                r.op("tensor", f_zmm, reads=(x0k, f"psZ{n2}") + WINK, writes=(f"psZ{n2}",))
                r.op("scalar", lambda e, n2=n2, zb=zb: e.activation(
                    zb[:, :], psZ[:, n2, :], AF.Identity),
                    reads=(f"psZ{n2}", zbk), writes=(zbk, f"psZ{n2}"))
                r.dma("scalar", SP_Z[TT * t + 128 * g:TT * t + 128 * (g + 1),
                                     1024 * n2:1024 * (n2 + 1)], zb[:, :],
                      stream=zbk, reads=(zbk,), writes=(f"SP_Z:{t // 4}",))
        for g in range(4):
            def f_dtmm(e, g=g, x0b=x0b):
                inst = None
                for k in range(8):
                    inst = e.matmul(psM[:, 0:NH], x0b[:, k, 128 * g:128 * (g + 1)],
                                    W_in_sb[:, k, DPROJ - NH:DPROJ],
                                    start=(k == 0), stop=(k == 7))
                return inst
            r.op("tensor", f_dtmm, reads=(x0k, "psM") + WINK, writes=("psM",))
            r.op("vector", lambda e, t=t, g=g: e.tensor_copy(dt_TM[:, 4 * t + g, :],
                                                             psM[:, 0:NH]),
                 reads=("psM", "dt_TM"), writes=("dt_TM",))
        for m in range(18):
            pb = m % 2
            def f_xmm(e, m=m, pb=pb, x0b=x0b):
                inst = None
                for k in range(8):
                    inst = e.matmul(psX[:, pb, :],
                                    W_in_sb[:, k, DI + 128 * m:DI + 128 * (m + 1)],
                                    x0b[:, k, :], start=(k == 0), stop=(k == 7))
                return inst
            r.op("tensor", f_xmm, reads=(x0k, f"psX{pb}") + WINK, writes=(f"psX{pb}",))
            r.op("scalar", lambda e, m=m, pb=pb, xcb=xcb: e.activation(
                xcb[:, m, 3:515], psX[:, pb, :], AF.Identity),
                reads=(f"psX{pb}", xck), writes=(xck, f"psX{pb}"))
        r.op("vector", lambda e, xcb=xcb: e.tensor_copy(xcb[:, :, 0:3], tail_b[:, :, :]),
             reads=("tail", xck), writes=(xck,))
        for m in range(18):
            ca, cak = cacc[:, m % 2, :], f"cacc{m % 2}"
            r.op("vector", lambda e, m=m, ca=ca, xcb=xcb: e.tensor_scalar(
                ca, xcb[:, m, 0:512], convw_sb[:, m, 0:1], convb_sb[:, m:m + 1],
                op0=ALU.mult, op1=ALU.add),
                reads=(xck, "convw", "convb", cak), writes=(cak,))
            for i in range(1, 4):
                r.op("vector", lambda e, m=m, i=i, ca=ca, xcb=xcb: e.scalar_tensor_tensor(
                    ca, xcb[:, m, i:i + 512], convw_sb[:, m, i:i + 1], ca,
                    op0=ALU.mult, op1=ALU.add),
                    reads=(xck, "convw", cak), writes=(cak,))
            r.op("vector", lambda e, m=m, xcb=xcb: e.tensor_copy(tail_b[:, m, :],
                                                                 xcb[:, m, 512:515]),
                 reads=(xck, "tail"), writes=("tail",))
            r.op("scalar", lambda e, m=m, ca=ca, xcb=xcb: e.activation(
                xcb[:, m, 3:515], ca, AF.Silu),
                reads=(cak, xck), writes=(xck,))
        r.op("vector", lambda e, t=t, xcb=xcb: e.tensor_copy(B_FM[:, TT * t:TT * (t + 1)],
                                                             xcb[:, 16, 3:515]),
             reads=(xck,), writes=("B_FM",))
        r.op("vector", lambda e, t=t, xcb=xcb: e.tensor_copy(C_FM[:, TT * t:TT * (t + 1)],
                                                             xcb[:, 17, 3:515]),
             reads=(xck,), writes=("C_FM",))
        r.dma("scalar",
              SP_XC[0:2048, TT * t:TT * (t + 1)].rearrange("(m p) c -> p m c", p=128),
              xcb[:, 0:16, 3:515], stream=xck, reads=(xck,), writes=(f"SP_XC:{t}",))
        r.dma("scalar", SP_XC[2048:2176, TT * t:TT * (t + 1)], xcb[:, 16, 3:515],
              stream=xck, reads=(xck,), writes=(f"SP_XCB:{t}",))

    dtv = dt_TM[:, :, :].rearrange("p a b -> p (a b)")
    sp1v = sp1[:, :, :].rearrange("p a b -> p (a b)")
    sp2v = sp2[:, :, :].rearrange("p a b -> p (a b)")
    r.op("vector", lambda e: e.tensor_tensor(
        dt_TM[:, :, :], dt_TM[:, :, :], bc(dtb_rep[:, :], [128, 32, NH], 1),
        op=ALU.add), reads=("dt_TM", "dtb_rep"), writes=("dt_TM",))
    r.op("scalar", lambda e: e.activation(sp1v, dtv, AF.Abs),
         reads=("dt_TM", "sp1"), writes=("sp1",))
    r.op("scalar", lambda e: e.activation(sp1v, sp1v, AF.Exp, scale=-1.0),
         reads=("sp1",), writes=("sp1",))
    r.op("scalar", lambda e: e.activation(sp1v, sp1v, AF.Ln, bias=cst[:, 1:2]),
         reads=("sp1", "cst"), writes=("sp1",))
    r.op("scalar", lambda e: e.activation(sp2v, dtv, AF.Relu),
         reads=("dt_TM", "sp2"), writes=("sp2",))
    r.op("vector", lambda e: e.tensor_tensor(dtv, sp1v, sp2v, op=ALU.add),
         reads=("sp1", "sp2", "dt_TM"), writes=("dt_TM",))
    r.op("vector", lambda e: e.tensor_tensor(
        a_TM[:, :, :], dt_TM[:, :, :],
        bc(negA_rep[:, :], [128, 32, NH], 1), op=ALU.mult),
        reads=("dt_TM", "negA_rep"), writes=("a_TM",))
    r.op("vector", lambda e: e.tensor_copy(a_TMb[:, :, :], a_TM[:, :, :]),
         reads=("a_TM",), writes=("a_TMb",))
    for p in range(32):
        r.op("tensor", lambda e, p=p: e.transpose(psM[0:32, 0:128], dt_TM[:, p, :], ident[:, :]),
             reads=("dt_TM", "ident", "psM"), writes=("psM",))
        r.op("vector", lambda e, p=p: e.tensor_copy(hm_buf[:, 128 * p:128 * (p + 1)],
                                                    psM[0:32, 0:128]),
             reads=("psM", "hm_buf"), writes=("hm_buf",))
    r.op("vector", lambda e: e.tensor_scalar(hm_buf[:, :], hm_buf[:, :], negA_col[:, 0:1],
                                             None, op0=ALU.mult, op1=ALU.bypass),
         reads=("hm_buf", "negA_col"), writes=("hm_buf",))
    def f_scans(e):
        inst = None
        for c in range(NCK):
            inst = e.tensor_tensor_scan(cs_HM[:, CS * c:CS * (c + 1)], ones64[0:NH, :],
                                        hm_buf[:, CS * c:CS * (c + 1)], initial=0.0,
                                        op0=ALU.mult, op1=ALU.add)
        return inst
    r.op("vector", f_scans, reads=("hm_buf", "ones64", "cs_HM"), writes=("cs_HM",))
    r.op("vector", lambda e: e.tensor_copy(
        atot_HM[:, :], cs_HM[:, :].rearrange("h (c s) -> h c s", s=CS)[:, :, CS - 1]),
        reads=("cs_HM",), writes=("atot_HM",))
    r.op("vector", lambda e: e.tensor_tensor_scan(cpi_HM[:, :], ones64[0:NH, :], atot_HM[:, :],
                                                  initial=0.0, op0=ALU.mult, op1=ALU.add),
         reads=("atot_HM", "ones64"), writes=("cpi_HM",))
    r.op("vector", lambda e: e.reduce_sum(aseg_col[:, :], atot_HM[:, :], axis=AX.X),
         reads=("atot_HM",), writes=("aseg_col",))
    r.op("vector", lambda e: e.tensor_scalar(rc_HM[:, :], cpi_HM[:, :], -1.0, aseg_col[:, 0:1],
                                             op0=ALU.mult, op1=ALU.add),
         reads=("cpi_HM", "aseg_col"), writes=("rc_HM",))
    r.op("vector", lambda e: e.tensor_tensor(b2_HM[:, :], atot_HM[:, :], rc_HM[:, :], op=ALU.add),
         reads=("atot_HM", "rc_HM"), writes=("b2_HM",))
    r.op("scalar", lambda e: e.activation(dexp_HM[:, :], atot_HM[:, :], AF.Exp),
         reads=("atot_HM",), writes=("dexp_HM",))
    r.dma("sync", SP_SB[0:64, :].rearrange("c h -> h c"), atot_HM[:, :],
          reads=("atot_HM",), writes=("SP_SB0",), allow_slow_non_contiguous=True)
    r.dma("sync", SP_SB[64:128, :].rearrange("c h -> h c"), b2_HM[:, :],
          reads=("b2_HM",), writes=("SP_SB1",), allow_slow_non_contiguous=True)
    r.dma("sync", SP_SB[128:192, :].rearrange("c h -> h c"), dexp_HM[:, :],
          reads=("dexp_HM",), writes=("SP_SB2",), allow_slow_non_contiguous=True)
    for p in range(32):
        r.op("tensor", lambda e, p=p: e.transpose(psM[:, 0:32], cs_HM[:, 128 * p:128 * (p + 1)],
                                                  ident[0:32, 0:32]),
             reads=("cs_HM", "ident", "psM"), writes=("psM",))
        r.op("vector", lambda e, p=p: e.tensor_copy(cs_TM[:, p, :], psM[:, 0:32]),
             reads=("psM", "cs_TM"), writes=("cs_TM",))
    csv = cs_TM[:, :, :].rearrange("p a b -> p (a b)")
    for cpar in range(2):
        r.dma("sync", sp1[64 * cpar:64 * (cpar + 1), :, :],
              bass.AP(SP_SB.tensor, NH * cpar, [[0, 64], [2 * NH, 32], [1, NH]]),
              reads=("SP_SB0", "sp1"), writes=("sp1",))
        r.dma("sync", sp2[64 * cpar:64 * (cpar + 1), :, :],
              bass.AP(SP_SB.tensor, 64 * NH + NH * cpar, [[0, 64], [2 * NH, 32], [1, NH]]),
              reads=("SP_SB1", "sp2"), writes=("sp2",))
    r.op("vector", lambda e: e.tensor_tensor(sp1v, sp1v, csv, op=ALU.subtract),
         reads=("sp1", "cs_TM"), writes=("sp1",))
    r.op("vector", lambda e: e.tensor_tensor(sp2v, sp2v, csv, op=ALU.subtract),
         reads=("sp2", "cs_TM"), writes=("sp2",))
    def f_exps(e):
        e.activation(dsT[:, :, :].rearrange("p a b -> p (a b)"), sp1v, AF.Exp)
        e.activation(ds2T[:, :, :].rearrange("p a b -> p (a b)"), sp2v, AF.Exp)
        return e.activation(sdT[:, :, :].rearrange("p a b -> p (a b)"), csv, AF.Exp)
    r.op("scalar", f_exps, reads=("sp1", "sp2", "cs_TM"), writes=("dsT", "ds2T", "sdT"))
    r.build()
    esB.close()
    es_win.close()

    B_TM = mk(es_b, "B_TM", [128, 32, 128], BF16)
    D_rep = mk(es_b, "D_rep", [128, NCK, NH], F32)
    S_sb = mk(es_b, "S_sb", [128, DI], BF16)
    battn = mk(es_b, "battn", [1, D], BF16)
    W_out_sb = mk(es_b, "W_out_sb", [128, 16, D], BF16)

    esC = ExitStack()
    xcT = mk(esC, "xcT", [128, 8, 2176], BF16)
    xs_b = mk(esC, "xs_b", [128, DI], BF16)
    xd_b = mk(esC, "xd_b", [128, DI], BF16)
    xd2_b = mk(esC, "xd2_b", [128, DI], BF16)
    cb_f = mk(esC, "cb_f", [128, DI], F32)
    gth = mk(esC, "gth", [128, DI], F32)
    acc_f = mk(esC, "acc_f", [128, DI], F32)
    tmp_f = mk(esC, "tmp_f", [128, DI], F32)
    a8 = mk(esC, "a8", [8, NH + 8], F32)
    gm_col = mk(esC, "gm_col", [8, 1], F32)
    wrep = mk(esC, "wrep", [128, 8, NH], F32)
    smB = mk(esC, "smB", [8, NH], F32)
    smC = mk(esC, "smC", [8, NH], F32)
    wo32 = [mk(esC, "wo32_0", [128, D], F32)] * 2
    wo16 = mk(esC, "wo16", [128, D], BF16)
    nwb_col = mk(esC, "nwb_col", [128, 32], F32)
    nb_colb = mk(esC, "nb_colb", [128, 16], BF16)
    wi32b = [mk(esC, "wi32b0", [128, 2, D], I32)] * 2
    wibfb = [mk(esC, "wibfb0", [128, 2, D], BF16)] * 2
    psCB = esC.enter_context(nc.psum_tensor("psCB", [128, DI], F32))
    psE = esC.enter_context(nc.psum_tensor("psE", [8, NH], F32))
    psB2 = esC.enter_context(nc.psum_tensor("psB2", [1, D], F32))

    r = rec()
    r.dma("sync", nwb_col[:, 0:16], norm_w.rearrange("(k p) -> p k", p=128),
          writes=("nw_col",), allow_slow_non_contiguous=True)
    r.dma("sync", nwb_col[:, 16:32], norm_b.rearrange("(k p) -> p k", p=128),
          writes=("nb_col32",), allow_slow_non_contiguous=True)
    r.op("scalar", lambda e: e.activation(nb_colb[:, :], nwb_col[:, 16:32], AF.Identity),
         reads=("nb_col32",), writes=("nb_colb",))
    for k in range(16):
        buf, key = wo32[0], "wo32_0"
        r.dma("sync", buf[:, :], W_out[128 * k:128 * (k + 1), :], reads=(key,), writes=(key,))
        r.op("scalar", lambda e, buf=buf: e.activation(wo16[:, :], buf[:, :], AF.Identity),
             reads=(key, "wo16"), writes=("wo16",))
        def f_bmm(e, k=k):
            e.matmul(psB2[:, 0:512], nb_colb[:, k:k + 1], wo16[:, 0:512],
                     start=(k == 0), stop=(k == 15))
            return e.matmul(psB2[:, 512:1024], nb_colb[:, k:k + 1], wo16[:, 512:1024],
                            start=(k == 0), stop=(k == 15))
        r.op("tensor", f_bmm, reads=("wo16", "nb_colb", "psB2"), writes=("psB2",))
        r.op("scalar", lambda e, k=k, buf=buf: e.activation(
            W_out_sb[:, k, :], buf[:, :], AF.Identity, scale=nwb_col[:, k:k + 1]),
            reads=(key, "nw_col", "W_out_sb"), writes=("W_out_sb",))
    r.op("scalar", lambda e: e.activation(battn[:, :], psB2[:, :], AF.Identity),
         reads=("psB2",), writes=("battn",))
    rnd_i = [0]
    def bounce(wsrc, wdst, wname, r0, rows, wkey):
        i = rnd_i[0] % 2
        rnd_i[0] += 1
        m = rows // 128
        r.dma("scalar", wi32b[i][:, 0:m, :],
              wsrc[r0:r0 + rows, :].rearrange("(m p) c -> p m c", p=128),
              stream="wi32b0", reads=("wi32b0",), writes=("wi32b0",))
        r.op("scalar", lambda e, i=i, m=m: e.activation(
            wibfb[i][:, 0:m, :], wi32b[i][:, 0:m, :], AF.Identity),
            reads=("wi32b0", "wibfb0"), writes=("wibfb0",))
        r.dma("scalar", wdst[r0:r0 + rows, :].rearrange("(m p) c -> p m c", p=128),
              wibfb[i][:, 0:m, :], stream="wibfb0", reads=("wibfb0",),
              writes=(wkey,))
    for wsrc, wdst, wname in ((wg, SP_WGN, "SP_WGN"), (wu, SP_WUN, "SP_WUN")):
        for r0 in range(0, HID, 256):
            bounce(wsrc, wdst, wname, r0, 256, f"{wname}:{r0 // 1024 * 1024}")
    def bounce_wd(r0, co):
        i = rnd_i[0] % 2
        rnd_i[0] += 1
        fv32 = wi32b[i][:, :, :].rearrange("p a b -> p (a b)")
        fvbf = wibfb[i][:, :, :].rearrange("p a b -> p (a b)")
        r.dma("scalar", fv32[:, 0:1536], wd[r0:r0 + 128, co:co + 1536],
              stream="wi32b0", reads=("wi32b0",), writes=("wi32b0",))
        r.op("scalar", lambda e, i=i, fv32=fv32, fvbf=fvbf: e.activation(
            fvbf[:, 0:1536], fv32[:, 0:1536], AF.Identity),
            reads=("wi32b0", "wibfb0"), writes=("wibfb0",))
        r.dma("scalar", SP_WDN[r0:r0 + 128, co:co + 1536], fvbf[:, 0:1536],
              stream="wibfb0", reads=("wibfb0",),
              writes=(f"SP_WDN:{r0 // 256 * 256}",))
    for r0 in range(0, D, 128):
        for co in (0, 1536):
            bounce_wd(r0, co)
    r.dma("sync", D_rep[:, :, :],
          bass.AP(SP_SB.tensor, 128 * NH, [[0, 128], [NH, NCK], [1, NH]]),
          reads=("SP_SB2",), writes=("D_rep",))
    first = [True]
    for quarter in range(4):
        r.xpose(xcT[:, :, :], SP_XC[:, 1024 * quarter:1024 * (quarter + 1)],
                reads=(f"SP_XC:{2 * quarter}", f"SP_XC:{2 * quarter + 1}",
                       f"SP_XCB:{2 * quarter}", f"SP_XCB:{2 * quarter + 1}", "xcT"),
                writes=("xcT",))
        for m in range(8):
            pp = 8 * quarter + m
            r.op("vector", lambda e, m=m, pp=pp: e.tensor_tensor(
                xs_b[:, :].rearrange("p (h q) -> p h q", q=HP),
                xcT[:, m, 0:DI].rearrange("p (h q) -> p h q", q=HP),
                bc(dt_TM[:, pp, :], [128, NH, HP], 2), op=ALU.mult),
                reads=("xcT", "dt_TM", "xs_b"), writes=("xs_b",))
            r.op("vector", lambda e, pp=pp: e.tensor_tensor(
                xd_b[:, :].rearrange("p (h q) -> p h q", q=HP),
                xs_b[:, :].rearrange("p (h q) -> p h q", q=HP),
                bc(dsT[:, pp, :], [128, NH, HP], 2), op=ALU.mult),
                reads=("xs_b", "dsT", "xd_b"), writes=("xd_b",))
            r.op("gpsimd", lambda e, pp=pp: e.tensor_tensor(
                xd2_b[:, :].rearrange("p (h q) -> p h q", q=HP),
                xs_b[:, :].rearrange("p (h q) -> p h q", q=HP),
                bc(ds2T[:, pp, :], [128, NH, HP], 2), op=ALU.mult),
                reads=("xs_b", "ds2T", "xd2_b"), writes=("xd2_b",))
            r.op("vector", lambda e, m=m, pp=pp: e.tensor_copy(B_TM[:, pp, :],
                                                               xcT[:, m, 2048:2176]),
                 reads=("xcT", "B_TM"), writes=("B_TM",))
            r.dma("scalar", SP_XS[128 * pp:128 * (pp + 1), :], xs_b[:, :],
                  stream="xs_b", reads=("xs_b",), writes=(f"SP_XS:{pp // 4}",))
            r.dma("scalar", SP_XD[128 * pp:128 * (pp + 1), :], xd_b[:, :],
                  stream="xd_b", reads=("xd_b",), writes=(f"SP_XD:{pp // 4}",))
            def f_st2(e, m=m, first=first, pp=pp):
                inst = None
                for cc2 in range(2):
                    sl = slice(64 * cc2, 64 * (cc2 + 1))
                    for h in range(NH):
                        inst = e.matmul(psCB[:, HP * h:HP * (h + 1)], xcT[sl, m, 2048:2176],
                                        xd2_b[sl, HP * h:HP * (h + 1)], start=(first[0] and h % 8 == 0),
                                        stop=(pp == NPAIR - 1 and cc2 == 1 and h == NH - 1))
                        if h == NH - 1:
                            first[0] = False
                return inst
            r.op("tensor", f_st2, reads=("xcT", "xd2_b", "psCB"), writes=("psCB",))
    r.op("vector", lambda e: e.tensor_copy(cb_f[:, :], psCB[:, :]),
         reads=("psCB",), writes=("cb_f",))
    r.dma("sync", CBT[:, 0:DI], cb_f[:, :], reads=("cb_f",), writes=("CBT",))
    r.dma("sync", CBT[0:NH, DI:DI + 1], aseg_col[:, :], reads=("aseg_col",), writes=("CBT2",), allow_slow_non_contiguous=True)
    r.build()
    nc.all_core_barrier()
    r = rec()
    r.op("gpsimd", lambda e: e.collective_compute(
        "AllGather", ALU.bypass, replica_groups=[list(range(8))], ins=[CBT], outs=[GATH])
        .then_inc(cc_sem, 1) and None,
        reads=("CBT", "CBT2"), writes=(), inc=0)
    r.op("gpsimd", lambda e: e.wait_ge(cc_sem, 1) or e.nop(), reads=(), writes=("GATH",))
    r.build()
    nc.all_core_barrier()
    r = rec()
    r.dma("sync", a8[:, NH:NH + 8], SEL2, writes=("a8sel",))
    r.dma("sync", a8[0:8, 0:NH], GATH[:, DI].rearrange("(j p) -> j p", p=128)[:, 0:NH],
          reads=("GATH",), writes=("a8",), allow_slow_non_contiguous=True)
    r.dma("sync", gm_col[:, :], GMASK, writes=("gm_col",))
    r.op("tensor", lambda e: e.matmul(psE[:, :], a8[:, NH:NH + 8], a8[0:8, 0:NH],
                                      start=True, stop=True),
         reads=("a8sel", "a8", "psE"), writes=("psE",))
    r.op("scalar", lambda e: e.activation(smB[:, :], psE[:, :], AF.Exp),
         reads=("psE",), writes=("smB",))
    r.op("vector", lambda e: e.tensor_scalar(smC[:, :], smB[:, :], gm_col[:, 0:1], None,
                                             op0=ALU.mult, op1=ALU.bypass),
         reads=("smB", "gm_col"), writes=("smC",))
    r.dma("sync", SP_W8, smC[:, :], reads=("smC",), writes=("SP_W8",))
    r.dma("sync", wrep[:, :, :], bass.AP(SP_W8.tensor, 0, [[0, 128], [NH, 8], [1, NH]]),
          reads=("SP_W8",), writes=("wrep",))
    for j in range(8):
        r.dma("sync", gth[:, :], GATH[128 * j:128 * (j + 1), 0:DI],
              reads=("GATH", "gth"), writes=("gth",))
        if j == 0:
            r.op("vector", lambda e: e.tensor_tensor(
                acc_f[:, :].rearrange("p (h q) -> p h q", q=HP),
                gth[:, :].rearrange("p (h q) -> p h q", q=HP),
                bc(wrep[:, 0, :], [128, NH, HP], 2), op=ALU.mult),
                reads=("gth", "wrep", "acc_f"), writes=("acc_f",))
        else:
            r.op("vector", lambda e, j=j: e.tensor_tensor(
                tmp_f[:, :].rearrange("p (h q) -> p h q", q=HP),
                gth[:, :].rearrange("p (h q) -> p h q", q=HP),
                bc(wrep[:, j, :], [128, NH, HP], 2), op=ALU.mult),
                reads=("gth", "wrep", "tmp_f"), writes=("tmp_f",))
            r.op("vector", lambda e: e.tensor_tensor(acc_f[:, :], acc_f[:, :], tmp_f[:, :],
                                                     op=ALU.add),
                 reads=("tmp_f", "acc_f"), writes=("acc_f",))
    r.op("vector", lambda e: e.tensor_copy(S_sb[:, :], acc_f[:, :]),
         reads=("acc_f", "S_sb"), writes=("S_sb",))
    if "SP_DBGS2" in dbg:
        r.dma("sync", SP_DBGS2[:, 0:DI], cb_f[:, :], stream="cb_f", reads=("cb_f",), writes=("dbgs9",))
        r.dma("sync", SP_DBGS2[0:128, DI:DI + NH].rearrange("p c -> p c"), wrep[:, 0, :],
              reads=("wrep",), writes=("dbgs10",))
    if "SP_DBGS" in dbg:
        r.dma("sync", SP_DBGS[:, 0:DI], acc_f[:, :], stream="acc_f", reads=("acc_f",), writes=("dbgs",))
        r.dma("sync", SP_DBGS[0:8, DI:DI + NH], smC[:, :], reads=("smC",), writes=("dbgs2",))
        r.dma("sync", SP_DBGS[8:16, DI:DI + NH], a8[:, 0:NH], reads=("a8",), writes=("dbgs3",))
        r.dma("sync", SP_DBGS[16:24, DI:DI + NH], smB[:, :], reads=("smB",), writes=("dbgs4",))
    r.build()
    esC.close()

    esD = ExitStack()
    xs_p = [mk(esD, f"xs_p{i}", [128, DI], BF16) for i in range(2)]
    xd_p = [mk(esD, f"xd_p{i}", [128, DI], BF16) for i in range(2)]
    z_p = [mk(esD, f"z_p{i}", [128, DI], BF16) for i in range(2)]
    AM_sb = mk(esD, "AM_sb", [128, DI], BF16)
    M0_sb = mk(esD, "M0_sb", [128, DI], BF16)
    M_sb = mk(esD, "M_sb", [128, DI], BF16)
    Gt_sb = mk(esD, "Gt_sb", [128, 64], BF16)
    y_sb = mk(esD, "y_sb", [128, DI], F32)
    zs_sb = mk(esD, "zs_sb", [128, DI], BF16)
    yn_sb = mk(esD, "yn_sb", [128, DI], BF16)
    st2 = mk(esD, "st2", [128, 24], F32)
    mv2 = mk(esD, "mv2", [128, 2], F32)
    rs2 = mk(esD, "rs2", [128, 2], F32)
    ynF = mk(esD, "ynF", [128, 16, TT], BF16)
    at_f = mk(esD, "at_f", [128, D], F32)
    u2_t = [mk(esD, f"u2_t{i}", [128, D], F32) for i in range(2)]
    x1_t = mk(esD, "x1_t", [128, D], F32)
    x1n_t = mk(esD, "x1n_t", [128, D], BF16)
    psT = esD.enter_context(nc.psum_tensor("psT", [128, DI], F32))
    psO = esD.enter_context(nc.psum_tensor("psO", [128, DI], F32))

    r = rec()
    for pp in range(NPAIR):
        bi = pp % 2
        xsb, xdb, zb = xs_p[bi], xd_p[bi], z_p[bi]
        xsk, xdk, zk = f"xs_p{bi}", f"xd_p{bi}", f"z_p{bi}"
        r.dma("sync", xsb[:, :], SP_XS[128 * pp:128 * (pp + 1), :],
              reads=(f"SP_XS:{pp // 4}", xsk), writes=(xsk,))
        r.dma("sync", xdb[:, :], SP_XD[128 * pp:128 * (pp + 1), :],
              reads=(f"SP_XD:{pp // 4}", xdk), writes=(xdk,))
        r.dma("sync", zb[:, :], SP_Z[128 * pp:128 * (pp + 1), :],
              reads=(f"SP_Z:{pp // 4}", zk), writes=(zk,))
        for cc in range(2):
            c = 2 * pp + cc
            sl = slice(64 * cc, 64 * (cc + 1))
            amk = f"AM{cc}"
            r.op("gpsimd", lambda e, pp=pp, sl=sl: e.tensor_tensor(
                AM_sb[sl, :].rearrange("p (l h) -> p l h", h=NH),
                bc(a_TM[sl, pp, :], [64, 64, NH], 1),
                bc(trillt[sl, :], [64, 64, NH], 2), op=ALU.mult),
                reads=("a_TM", "trillt", amk), writes=(amk,))
            def f_tmm(e, pp=pp, sl=sl):
                inst = None
                for nb4 in range(4):
                    ls = slice(16 * nb4, 16 * (nb4 + 1))
                    ps = psT[sl, 512 * nb4:512 * (nb4 + 1)].rearrange("p (l h) -> p l h", h=NH)
                    e.matmul(ps, trilneg[sl, :],
                             bc(a_TMb[sl, pp, :], [64, 64, NH], 1)[:, ls, :],
                             start=True, stop=False)
                    e.matmul(ps, ones64[sl, :],
                             AM_sb[sl, :].rearrange("p (l h) -> p l h", h=NH)[:, ls, :],
                             start=False, stop=False)
                    inst = e.matmul(ps, trillt[sl, :],
                                    maskrx[sl, :].rearrange("p (l h) -> p l h", h=NH)[:, ls, :],
                                    start=False, stop=True)
                return inst
            r.op("tensor", f_tmm, reads=("a_TMb", "trilneg", "ones64", amk, "maskrx", f"psT{cc}"),
                 writes=(f"psT{cc}",))
            r.op("scalar", lambda e, sl=sl: e.activation(M0_sb[sl, :], psT[sl, :], AF.Exp),
                 reads=(f"psT{cc}", f"M0{cc}"), writes=(f"M0{cc}", f"psTr{cc}"))
            r.op("tensor", lambda e, c=c, sl=sl: e.matmul(
                psT[sl, 0:64], B_FM[:, 64 * c:64 * (c + 1)], C_FM[:, 64 * c:64 * (c + 1)],
                start=True, stop=True),
                reads=("B_FM", "C_FM", f"psTr{cc}"), writes=(f"psT{cc}",))
            r.op("vector", lambda e, sl=sl, cc=cc: e.tensor_tensor(
                Gt_sb[sl, :], psT[sl, 0:64], trilsl[sl, :], op=ALU.mult),
                reads=(f"psT{cc}", "trilsl", f"Gt{cc}"), writes=(f"Gt{cc}", f"psTr{cc}"))
            r.op("vector", lambda e, sl=sl, cc=cc: e.tensor_tensor(
                M_sb[sl, :].rearrange("p (l h) -> p l h", h=NH),
                M0_sb[sl, :].rearrange("p (l h) -> p l h", h=NH),
                bc(Gt_sb[sl, :], [64, 64, NH], 2), op=ALU.mult),
                reads=(f"M0{cc}", f"Gt{cc}", f"M{cc}"), writes=(f"M{cc}",))
            def f_yoff(e, c=c, sl=sl):
                inst = None
                for h in range(NH):
                    inst = e.matmul(psO[sl, HP * h:HP * (h + 1)], C_FM[:, 64 * c:64 * (c + 1)],
                                    S_sb[:, HP * h:HP * (h + 1)], start=True, stop=True)
                return inst
            r.op("tensor", f_yoff, reads=("S_sb", "C_FM", f"psO{cc}"), writes=(f"psO{cc}",))
            r.op("vector", lambda e, sl=sl, pp=pp: e.tensor_tensor(
                y_sb[sl, :].rearrange("p (h q) -> p h q", q=HP),
                psO[sl, :].rearrange("p (h q) -> p h q", q=HP),
                bc(sdT[sl, pp, :], [64, NH, HP], 2), op=ALU.mult),
                reads=(f"psO{cc}", "sdT", f"y{cc}", "psOF"), writes=(f"y{cc}", f"psO{cc}"))
            def f_st(e, pp=pp, sl=sl, bi=bi):
                inst = None
                for h in range(NH):
                    inst = e.matmul(psO[:, HP * h:HP * (h + 1)], B_TM[sl, pp, :],
                                    xd_p[bi][sl, HP * h:HP * (h + 1)], start=True, stop=True)
                return inst
            r.op("tensor", f_st, reads=("B_TM", xdk, "psOF"), writes=("psOF", "psO0", "psO1"))
            r.op("gpsimd", lambda e, c=c: e.tensor_tensor(
                S_sb[:, :].rearrange("p (h q) -> p h q", q=HP),
                S_sb[:, :].rearrange("p (h q) -> p h q", q=HP),
                bc(D_rep[:, c, :], [128, NH, HP], 2), op=ALU.mult),
                reads=("S_sb", "D_rep"), writes=("S_sb",))
            r.op("vector", lambda e: e.tensor_tensor(S_sb[:, :], S_sb[:, :], psO[:, :],
                                                     op=ALU.add),
                 reads=("S_sb", "psOF"), writes=("S_sb", "psOF"))
            def f_yd(e, sl=sl, bi=bi):
                inst = None
                for h in range(NH):
                    inst = e.matmul(psT[sl, HP * h:HP * (h + 1)],
                                    M_sb[sl, :].rearrange("p (l h) -> p l h", h=NH)[:, :, h],
                                    xs_p[bi][sl, HP * h:HP * (h + 1)], start=True, stop=True)
                return inst
            r.op("tensor", f_yd, reads=(f"M{cc}", xsk, f"psTr{cc}"), writes=(f"psT{cc}",))
            r.op("vector", lambda e, sl=sl, cc=cc: e.tensor_tensor(
                y_sb[sl, :], y_sb[sl, :], psT[sl, :], op=ALU.add),
                reads=(f"psT{cc}", f"y{cc}"), writes=(f"y{cc}", f"psTr{cc}"))
        r.op("scalar", lambda e, zb=zb: e.activation(zs_sb[:, :], zb[:, :], AF.Silu),
             reads=(zk, "zs"), writes=("zs",))
        r.op("gpsimd", lambda e: e.tensor_tensor(y_sb[:, :], y_sb[:, :], zs_sb[:, :],
                                                 op=ALU.mult),
             reads=("y0", "y1", "zs"), writes=("y0", "y1"))
        ln_apply(r, "ln", ("y0", "y1"), [y_sb[:, 512 * i:512 * (i + 1)] for i in range(4)],
                 st2, mv2, rs2)
        r.op("scalar", lambda e: e.activation(yn_sb[:, :], y_sb[:, :], AF.Identity,
                                              scale=rs2[:, 0:1], bias=rs2[:, 1:2]),
             reads=("y0", "y1", "lnrs", "yn"), writes=("yn",))
        r.dma("scalar", SP_YN[128 * pp:128 * (pp + 1), :], yn_sb[:, :],
              stream="yn", reads=("yn",), writes=(f"SP_YN:{pp // 4}",))
        if pp % 4 == 3:
            t = pp // 4
            r.xpose(ynF[:, :, :], SP_YN[TT * t:TT * (t + 1), :],
                    reads=(f"SP_YN:{t}", "ynF"), writes=("ynF",))
            for g in range(4):
                ub, ubk = u2_t[g % 2], f"u2_t{g % 2}"
                r.dma("sync", ub[:, :],
                      u_seg[TT * t + 128 * g:TT * t + 128 * (g + 1), :],
                      reads=(ubk,), writes=(ubk,))
                def f_op(e, g=g):
                    inst = None
                    for n2 in range(2):
                        for k in range(16):
                            e.matmul(psO[:, 512 * n2:512 * (n2 + 1)],
                                     ynF[:, k, 128 * g:128 * (g + 1)],
                                     W_out_sb[:, k, 512 * n2:512 * (n2 + 1)],
                                     start=(k == 0), stop=False)
                        inst = e.matmul(psO[:, 512 * n2:512 * (n2 + 1)], onesr[:, :],
                                        battn[:, 512 * n2:512 * (n2 + 1)],
                                        start=False, stop=True)
                    return inst
                r.op("tensor", f_op,
                     reads=("ynF", "battn", "onesr", "psOF", "W_out_sb"),
                     writes=("psOF", "psO0", "psO1"))
                ln_apply(r, "ln", ("psOF",), [psO[:, 0:512], psO[:, 512:1024]], st2, mv2, rs2)
                def f_atn(e):
                    e.activation(at_f[:, 0:512], psO[:, 0:512], AF.Identity,
                                 scale=rs2[:, 0:1], bias=rs2[:, 1:2])
                    return e.activation(at_f[:, 512:1024], psO[:, 512:1024], AF.Identity,
                                        scale=rs2[:, 0:1], bias=rs2[:, 1:2])
                r.op("scalar", f_atn, reads=("psOF", "lnrs", "at_f"), writes=("at_f", "psOF"))
                r.op("vector", lambda e, g=g, ub=ub: e.tensor_tensor(
                    x1_t[:, :], at_f[:, :], ub[:, :], op=ALU.add),
                    reads=("at_f", ubk, "x1_t"), writes=("x1_t",))
                ln_apply(r, "ln", ("x1_t",), [x1_t[:, 0:512], x1_t[:, 512:1024]],
                         st2, mv2, rs2)
                r.op("scalar", lambda e, g=g: e.activation(
                    x1n_t[:, :], x1_t[:, :], AF.Identity,
                    scale=rs2[:, 0:1], bias=rs2[:, 1:2]),
                    reads=("x1_t", "lnrs", "x1n_t"), writes=("x1n_t",))
                r.dma("scalar", SP_X1[TT * t + 128 * g:TT * t + 128 * (g + 1), :],
                      x1_t[:, :], stream="x1_t", reads=("x1_t",), writes=(f"SP_X1:{t}",))
                r.dma("scalar", SP_X1N[TT * t + 128 * g:TT * t + 128 * (g + 1), :],
                      x1n_t[:, :], stream="x1n_t", reads=("x1n_t",), writes=(f"SP_X1N:{t}",))
    r.build()
    esD.close()
    es_b.close()
    es_mid.close()

    esE = ExitStack()
    wgT = mk(esE, "wgT", [128, 8, HID], BF16)
    wuT = mk(esE, "wuT", [128, 8, HID], BF16)
    wdT = mk(esE, "wdT", [128, 24, D], BF16)
    x1F = [mk(esE, "x1F0", [128, 8, TT], BF16)] * 2
    x1ntm = mk(esE, "x1ntm", [128, 4, D], BF16)
    gs_sb = mk(esE, "gs_sb", [128, 512], BF16)
    h_sb = mk(esE, "h_sb", [128, 24, TT], BF16)
    x1_2 = mk(esE, "x1_2", [128, 2, D], F32)
    o_sb = mk(esE, "o_sb", [128, D], F32)
    stf = mk(esE, "stf", [128, 12], F32)
    mvf = mk(esE, "mvf", [128, 2], F32)
    rsf = mk(esE, "rsf", [128, 2], F32)
    psG2 = esE.enter_context(nc.psum_tensor("psG2", [128, 2, 512], F32))
    psU2 = esE.enter_context(nc.psum_tensor("psU2", [128, 2, 512], F32))
    psD2 = esE.enter_context(nc.psum_tensor("psD2", [128, 2, 512], F32))
    psTb2 = esE.enter_context(nc.psum_tensor("psTb2", [128, 128], BF16))

    r = rec()
    r.xpose(wgT[:, :, :], SP_WGN,
            reads=tuple(f"SP_WGN:{r0}" for r0 in range(0, HID, 1024)), writes=("wgT",))
    r.xpose(wuT[:, :, :], SP_WUN,
            reads=tuple(f"SP_WUN:{r0}" for r0 in range(0, HID, 1024)), writes=("wuT",))
    r.xpose(wdT[:, :, :], SP_WDN,
            reads=tuple(f"SP_WDN:{r0}" for r0 in range(0, D, 256)), writes=("wdT",))
    for t in range(NT):
        xb, xk = x1F[0], "x1F0"
        r.dma("sync", x1ntm[:, :, :],
              SP_X1N[TT * t:TT * (t + 1), :].rearrange("(g p) d -> p g d", p=128),
              stream="x1ntm", reads=("x1ntm",), writes=("x1ntm",))
        for g4 in range(4):
            for kc in range(8):
                r.op("tensor", lambda e, g4=g4, kc=kc: e.transpose(
                    psTb2[:, :], x1ntm[:, g4, 128 * kc:128 * (kc + 1)], identb[:, :]),
                    reads=("x1ntm", "identb", "psTb2"), writes=("psTb2",))
                r.op("scalar", lambda e, g4=g4, kc=kc: e.activation(
                    xb[:, kc, 128 * g4:128 * (g4 + 1)], psTb2[:, :], AF.Identity),
                    reads=("psTb2", xk), writes=(xk, "psTb2"))
        for m in range(24):
            pb = m % 2
            def f_gmm(e, m=m, pb=pb, xb=xb):
                inst = None
                for k in range(8):
                    inst = e.matmul(psG2[:, pb, :], wgT[:, k, 128 * m:128 * (m + 1)],
                                    xb[:, k, :], start=(k == 0), stop=(k == 7))
                return inst
            r.op("tensor", f_gmm, reads=(xk, "wgT", f"psG2{pb}"), writes=(f"psG2{pb}",))
            def f_umm(e, m=m, pb=pb, xb=xb):
                inst = None
                for k in range(8):
                    inst = e.matmul(psU2[:, pb, :], wuT[:, k, 128 * m:128 * (m + 1)],
                                    xb[:, k, :], start=(k == 0), stop=(k == 7))
                return inst
            r.op("tensor", f_umm, reads=(xk, "wuT", f"psU2{pb}"), writes=(f"psU2{pb}",))
            r.op("scalar", lambda e, pb=pb: e.activation(gs_sb[:, :], psG2[:, pb, :],
                                                         AF.Silu, scale=1.0 / 127.0),
                 reads=(f"psG2{pb}", "gs"), writes=("gs", f"psG2{pb}"))
            r.op("vector", lambda e, m=m, pb=pb: e.tensor_tensor(
                h_sb[:, m, :], gs_sb[:, :], psU2[:, pb, :], op=ALU.mult),
                reads=("gs", f"psU2{pb}", "h_sb"), writes=("h_sb", f"psU2{pb}"))
        for g in range(4):
            if g % 2 == 0:
                r.dma("sync", x1_2[:, :, :],
                      SP_X1[TT * t + 128 * g:TT * t + 128 * (g + 2), :]
                      .rearrange("(g2 p) d -> p g2 d", p=128),
                      stream="x1_2", reads=("x1_2",), writes=("x1_2",))
            def f_dmm(e, g=g):
                inst = None
                for n2 in range(2):
                    for k in range(24):
                        inst = e.matmul(psD2[:, n2, :], h_sb[:, k, 128 * g:128 * (g + 1)],
                                        wdT[:, k, 512 * n2:512 * (n2 + 1)],
                                        start=(k == 0), stop=(k == 23))
                return inst
            r.op("tensor", f_dmm, reads=("h_sb", "wdT", "psD2"), writes=("psD2",))
            r.op("vector", lambda e: e.bn_stats(stf[:, 0:6], psD2[:, 0, :]),
                 reads=("psD2", "stf"), writes=("stf",))
            r.op("vector", lambda e: e.bn_stats(stf[:, 6:12], psD2[:, 1, :]),
                 reads=("psD2", "stf"), writes=("stf",))
            r.op("vector", lambda e: e.bn_aggr(mvf[:, :], stf[:, 0:12]),
                 reads=("stf", "mvf"), writes=("mvf",))
            r.op("scalar", lambda e: e.activation(rsf[:, 0:1], mvf[:, 1:2], AF.Ln,
                                                  bias=cst[:, 0:1]),
                 reads=("mvf", "rsf", "cst"), writes=("rsf",))
            r.op("scalar", lambda e: e.activation(rsf[:, 0:1], rsf[:, 0:1], AF.Exp,
                                                  scale=-0.5),
                 reads=("rsf",), writes=("rsf",))
            r.op("vector", lambda e: e.scalar_tensor_tensor(
                rsf[:, 1:2], mvf[:, 0:1], -1.0, rsf[:, 0:1], op0=ALU.mult, op1=ALU.mult),
                reads=("mvf", "rsf"), writes=("rsf",))
            def f_lnf(e):
                e.activation(o_sb[:, 0:512], psD2[:, 0, :], AF.Identity,
                             scale=rsf[:, 0:1], bias=rsf[:, 1:2])
                return e.activation(o_sb[:, 512:1024], psD2[:, 1, :], AF.Identity,
                                    scale=rsf[:, 0:1], bias=rsf[:, 1:2])
            r.op("scalar", f_lnf, reads=("psD2", "rsf", "o_sb"), writes=("o_sb", "psD2"))
            r.op("vector", lambda e, g=g: e.tensor_tensor(
                o_sb[:, :], o_sb[:, :], x1_2[:, g % 2, :], op=ALU.add),
                reads=("x1_2", "o_sb"), writes=("o_sb",))
            r.dma("scalar",
                  out_seg[TT * t + 128 * g:TT * t + 128 * (g + 1), :], o_sb[:, :],
                  stream="o_sb", reads=("o_sb",), writes=(f"OUT:{t}",))
    r.op("gpsimd", lambda e: None, reads=tuple(f"OUT:{t}" for t in range(NT)), writes=())
    r.build()
    esE.close()
    es_g.close()
    return nc


def kernel(L0_attn_u, L0_attn_W_in, L0_attn_conv_w, L0_attn_conv_b,
           L0_attn_dt_bias, L0_attn_A_log, L0_attn_norm_w, L0_attn_norm_b,
           L0_attn_W_out, L0_ffn_w_gate, L0_ffn_w_up, L0_ffn_w_down):
    u = np.ascontiguousarray(np.asarray(L0_attn_u, np.float32))
    consts = _np_consts()
    nc = build_nc(dbg=DBG)
    in_maps = []
    for core in range(8):
        b, q = core // 4, core % 4
        seg = np.ascontiguousarray(u[b, T * q:T * (q + 1)])
        halo = np.zeros((16, D), np.float32)
        if q > 0:
            halo[13:16] = u[b, T * q - 3:T * q]
        sel2 = np.zeros((8, 8), np.float32)
        gmask = np.zeros((8, 1), np.float32)
        for j in range(8):
            if j // 4 == b and j % 4 < q:
                gmask[j, 0] = 1.0
                for i in range(8):
                    if i // 4 == b and j % 4 < i % 4 < q:
                        sel2[i, j] = 1.0
        m = dict(u_seg=seg, u_halo=halo,
                 W_in=np.ascontiguousarray(np.asarray(L0_attn_W_in, np.float32)),
                 conv_w=np.ascontiguousarray(np.asarray(L0_attn_conv_w, np.float32)),
                 conv_b=np.ascontiguousarray(np.asarray(L0_attn_conv_b, np.float32)),
                 dt_bias=np.ascontiguousarray(np.asarray(L0_attn_dt_bias, np.float32)),
                 A_log=np.ascontiguousarray(np.asarray(L0_attn_A_log, np.float32)),
                 norm_w=np.ascontiguousarray(np.asarray(L0_attn_norm_w, np.float32)),
                 norm_b=np.ascontiguousarray(np.asarray(L0_attn_norm_b, np.float32)),
                 W_out=np.ascontiguousarray(np.asarray(L0_attn_W_out, np.float32)),
                 wg=np.ascontiguousarray(np.asarray(L0_ffn_w_gate, np.int32)),
                 wu=np.ascontiguousarray(np.asarray(L0_ffn_w_up, np.int32)),
                 wd=np.ascontiguousarray(np.asarray(L0_ffn_w_down, np.int32)),
                 SEL2=sel2, GMASK=gmask, **consts)
        in_maps.append(m)
    res = run_bass_kernel_spmd(nc, in_maps, core_ids=list(range(8)))
    out = np.empty((2, 16384, D), np.float32)
    for core in range(8):
        b, q = core // 4, core % 4
        out[b, T * q:T * (q + 1)] = res.results[core]["out_seg"]
    return out


# version 08
# speedup vs baseline: 1.0259x; 1.0259x over previous
import numpy as np
import ml_dtypes
from contextlib import ExitStack

import concourse.bass as bass
import concourse.mybir as mybir
from concourse.bass_utils import run_bass_kernel_spmd

F32 = mybir.dt.float32
BF16 = mybir.dt.bfloat16
I32 = mybir.dt.int32
AF = mybir.ActivationFunctionType
ALU = mybir.AluOpType
AX = mybir.AxisListType

T = 4096
D = 1024
DI = 2048
DS = 128
NH = 32
HP = 64
CS = 64
NCK = T // CS
NPAIR = NCK // 2
DPROJ = 4384
HID = 3072
EPS = 1e-5
BIG = 1e30
NT = 8
TT = 512

ENGINES = ("sync", "tensor", "vector", "scalar", "gpsimd")
DMA_ENGINES = ("sync", "scalar", "gpsimd")

DBG = []


class Rec:

    def __init__(self, nc, getsem, ctr, shared):
        self.nc = nc
        self.getsem = getsem
        self.ctr = ctr
        self.last_w, self.readers = shared
        self.ops = []

    def _rec(self, engine, fn, own, inc, reads, writes, forced=()):
        deps = {}
        for k in reads:
            h = self.last_w.get(k)
            if h is not None:
                deps[h] = True
        for k in writes:
            h = self.last_w.get(k)
            if h is not None:
                deps[h] = True
            for rh in self.readers.get(k, ()):
                deps[rh] = True
        for h in forced:
            if h is not None:
                deps[h] = True
        self.getsem(own)
        self.ctr[own] = self.ctr.get(own, 0) + inc
        handle = (own, self.ctr[own])
        emit_deps = [h for h in deps
                     if not (h[0] == own and h[1] >= handle[1])]
        self.ops.append((engine, fn, sorted(emit_deps), own, inc))
        for k in writes:
            self.last_w[k] = handle
            self.readers[k] = []
        for k in reads:
            self.readers.setdefault(k, []).append(handle)
        return handle

    def op(self, engine, fn, reads=(), writes=(), inc=1):
        return self._rec(engine, fn, "eng:" + engine, inc, reads, writes)

    def dma(self, engine, out, in_, stream=None, reads=(), writes=(), **kw):
        if stream is None:
            stream = (tuple(writes) + tuple(reads) + ("misc",))[0]
        xp = self.last_w.get("XPOSE")
        return self._rec(engine, lambda e: e.dma_start(out, in_, **kw),
                         "dma:" + stream, 16, reads, writes,
                         forced=[xp] if xp else ())

    def xpose(self, out, in_, reads=(), writes=()):
        forced = [(s, v) for s, v in self.ctr.items()
                  if s.startswith("dma:") and v > 0]
        return self._rec("sync", lambda e: e.dma_start_transpose(out, in_),
                         "dma:xpose", 16, reads, tuple(writes) + ("XPOSE",),
                         forced=forced)

    def build(self):
        getsem, ops = self.getsem, self.ops
        with self.nc.Block() as blk:
            for eng_name in ENGINES:
                mine = [(f, d, own, inc) for (e, f, d, own, inc) in ops
                        if e == eng_name]
                if not mine:
                    continue

                def emit(eng, mine=mine):
                    waited = {}
                    for fn, deps, own, inc in mine:
                        for (sname, pval) in deps:
                            if waited.get(sname, -1) < pval:
                                eng.wait_ge(getsem(sname), pval)
                                waited[sname] = pval
                        inst = fn(eng)
                        if inst is None:
                            inst = eng.nop()
                        if inc:
                            inst.then_inc(getsem(own), inc)
                getattr(blk, eng_name)(emit)


def _np_consts():
    i = np.arange(64)
    tril_is = (i[:, None] <= i[None, :]).astype(np.float32)
    dup = lambda a: np.concatenate([a, a], 0)
    bf = ml_dtypes.bfloat16
    return dict(
        TRILNEG=dup(-tril_is).astype(bf),
        TRILLT=dup(tril_is).astype(bf),
        MASKRX=dup(np.repeat(-BIG * (i[None, :] == i[:, None] - 1).astype(np.float32)[:, :, None], 32, axis=2).reshape(64, 2048)).astype(bf),
        ONES64=np.ones((128, 64), np.float32).astype(bf),
        TRILSL=dup((i[None, :] >= i[:, None]).astype(np.float32)),
        IDENT=np.eye(128, dtype=np.float32),
        ONESR=np.ones((1, 128), np.float32).astype(bf),
        IDENTB=np.eye(128, dtype=np.float32).astype(bf),
        CONSTS=np.array([[1e-5, 1.0, 0.0, 0.0]], np.float32),
    )


def bc(ap, shape, axis):
    return ap.unsqueeze(axis).broadcast_to(shape)


def build_nc(dbg=()):
    dbg = set(dbg)
    nc = bass.Bass("TRN2", target_bir_lowering=False, num_devices=8)

    u_seg = nc.dram_tensor("u_seg", [T, D], F32, kind="ExternalInput").ap()
    u_halo = nc.dram_tensor("u_halo", [16, D], F32, kind="ExternalInput").ap()
    W_in = nc.dram_tensor("W_in", [D, DPROJ], F32, kind="ExternalInput").ap()
    conv_w = nc.dram_tensor("conv_w", [2304, 4], F32, kind="ExternalInput").ap()
    conv_b = nc.dram_tensor("conv_b", [2304], F32, kind="ExternalInput").ap()
    dt_bias = nc.dram_tensor("dt_bias", [NH], F32, kind="ExternalInput").ap()
    A_log = nc.dram_tensor("A_log", [NH], F32, kind="ExternalInput").ap()
    norm_w = nc.dram_tensor("norm_w", [DI], F32, kind="ExternalInput").ap()
    norm_b = nc.dram_tensor("norm_b", [DI], F32, kind="ExternalInput").ap()
    W_out = nc.dram_tensor("W_out", [DI, D], F32, kind="ExternalInput").ap()
    wg = nc.dram_tensor("wg", [HID, D], I32, kind="ExternalInput").ap()
    wu = nc.dram_tensor("wu", [HID, D], I32, kind="ExternalInput").ap()
    wd = nc.dram_tensor("wd", [D, HID], I32, kind="ExternalInput").ap()
    TRILNEG = nc.dram_tensor("TRILNEG", [128, 64], BF16, kind="ExternalInput").ap()
    TRILLT = nc.dram_tensor("TRILLT", [128, 64], BF16, kind="ExternalInput").ap()
    MASKRX = nc.dram_tensor("MASKRX", [128, 2048], BF16, kind="ExternalInput").ap()
    ONES64 = nc.dram_tensor("ONES64", [128, 64], BF16, kind="ExternalInput").ap()
    TRILSL = nc.dram_tensor("TRILSL", [128, 64], F32, kind="ExternalInput").ap()
    IDENT = nc.dram_tensor("IDENT", [128, 128], F32, kind="ExternalInput").ap()
    ONESR = nc.dram_tensor("ONESR", [1, 128], BF16, kind="ExternalInput").ap()
    IDENTB = nc.dram_tensor("IDENTB", [128, 128], BF16, kind="ExternalInput").ap()
    CONSTS = nc.dram_tensor("CONSTS", [1, 4], F32, kind="ExternalInput").ap()
    SEL2 = nc.dram_tensor("SEL2", [8, 8], F32, kind="ExternalInput").ap()
    GMASK = nc.dram_tensor("GMASK", [8, 1], F32, kind="ExternalInput").ap()
    out_seg = nc.dram_tensor("out_seg", [T, D], F32, kind="ExternalOutput").ap()

    def scratch(name, shape, dtype):
        kind = "ExternalOutput" if name in dbg else "Internal"
        return nc.dram_tensor(name, shape, dtype, kind=kind).ap()

    SP_X0N = scratch("SP_X0N", [T, D], BF16)
    SP_X0NH = scratch("SP_X0NH", [16, D], BF16)
    SP_Z = scratch("SP_Z", [T, DI], BF16)
    SP_XC = scratch("SP_XC", [2176, T], BF16)
    SP_XS = scratch("SP_XS", [T, DI], BF16)
    SP_XD = scratch("SP_XD", [T, DI], BF16)
    SP_YN = scratch("SP_YN", [T, DI], BF16)
    SP_X1 = scratch("SP_X1", [T, D], F32)
    SP_X1N = scratch("SP_X1N", [T, D], BF16)
    SP_H = scratch("SP_H", [HID, T], BF16)
    SP_WGN = scratch("SP_WGN", [HID, D], BF16)
    SP_WUN = scratch("SP_WUN", [HID, D], BF16)
    SP_WDN = scratch("SP_WDN", [D, HID], BF16)
    SP_SB = scratch("SP_SB", [192, NH], F32)
    SP_NEGA = scratch("SP_NEGA", [NH, 1], F32)
    SP_W8 = scratch("SP_W8", [8, NH], F32)
    CBT = scratch("CBT", [128, 2049], F32)
    SP_DBGS = scratch("SP_DBGS", [128, 2049 + 64], F32)
    SP_DBGS2 = scratch("SP_DBGS2", [128, 2049 + 64], F32)
    GATH = nc.dram_tensor("GATH", [1024, 2049], F32, kind="Internal", addr_space="Shared").ap()

    es_g = ExitStack()
    es_mid = ExitStack()
    es_b = ExitStack()

    def mk(stack, name, shape, dtype=F32):
        return stack.enter_context(nc.sbuf_tensor(name, shape, dtype))

    trilneg = mk(es_g, "trilneg", [128, 64], BF16)
    trillt = mk(es_g, "trillt", [128, 64], BF16)
    maskrx = mk(es_g, "maskrx", [128, 2048], BF16)
    ones64 = mk(es_g, "ones64", [128, 64], BF16)
    trilsl = mk(es_g, "trilsl", [128, 64], F32)
    ident = mk(es_g, "ident", [128, 128], F32)
    onesr = mk(es_g, "onesr", [1, 128], BF16)
    identb = mk(es_g, "identb", [128, 128], BF16)
    cst = mk(es_g, "cst", [128, 4], F32)
    negA_col = mk(es_g, "negA_col", [NH, 1], F32)
    negA_rep = mk(es_g, "negA_rep", [128, NH], F32)
    dtb_rep = mk(es_g, "dtb_rep", [128, NH], F32)
    convw_sb = mk(es_g, "convw_sb", [128, 18, 4], F32)
    convb_sb = mk(es_g, "convb_sb", [128, 18], F32)

    B_FM = mk(es_mid, "B_FM", [128, T], BF16)
    C_FM = mk(es_mid, "C_FM", [128, T], BF16)
    dt_TM = mk(es_mid, "dt_TM", [128, 32, NH], F32)
    a_TM = mk(es_mid, "a_TM", [128, 32, NH], F32)
    a_TMb = mk(es_mid, "a_TMb", [128, 32, NH], BF16)
    atot_HM = mk(es_mid, "atot_HM", [NH, NCK], F32)
    cpi_HM = mk(es_mid, "cpi_HM", [NH, NCK], F32)
    rc_HM = mk(es_mid, "rc_HM", [NH, NCK], F32)
    b2_HM = mk(es_mid, "b2_HM", [NH, NCK], F32)
    dexp_HM = mk(es_mid, "dexp_HM", [NH, NCK], F32)
    aseg_col = mk(es_mid, "aseg_col", [NH, 1], F32)
    cs_TM = mk(es_mid, "cs_TM", [128, 32, NH], F32)
    dsT = mk(es_mid, "dsT", [128, 32, NH], F32)
    ds2T = mk(es_mid, "ds2T", [128, 32, NH], F32)
    sdT = mk(es_mid, "sdT", [128, 32, NH], F32)

    sem_map = {}
    sem_ctr = {}

    def getsem(name):
        if name not in sem_map:
            sem_map[name] = es_g.enter_context(
                nc.semaphore("s_" + name.replace(":", "_")))
        return sem_map[name]

    cc_sem = es_g.enter_context(nc.semaphore("sem_cc"))
    shared = ({}, {})

    def rec():
        return Rec(nc, getsem, sem_ctr, shared)

    def ln_apply(r, pre, stat_reads, stat_aps, st, mv, rs, npart=128):
        for i, ap in enumerate(stat_aps):
            r.op("vector", lambda e, i=i, ap=ap: e.bn_stats(st[:, 6 * i:6 * (i + 1)], ap),
                 reads=tuple(stat_reads) + (pre + "st",), writes=(pre + "st",))
        r.op("vector", lambda e: e.bn_aggr(mv[:, :], st[:, 0:6 * len(stat_aps)]),
             reads=(pre + "st", pre + "mv"), writes=(pre + "mv",))
        r.op("scalar", lambda e: e.activation(rs[:, 0:1], mv[:, 1:2], AF.Ln,
                                              bias=cst[0:npart, 0:1]),
             reads=(pre + "mv", pre + "rs", "cst"), writes=(pre + "rs",))
        r.op("scalar", lambda e: e.activation(rs[:, 0:1], rs[:, 0:1], AF.Exp, scale=-0.5),
             reads=(pre + "rs",), writes=(pre + "rs",))
        r.op("vector", lambda e: e.scalar_tensor_tensor(
            rs[:, 1:2], mv[:, 0:1], -1.0, rs[:, 0:1], op0=ALU.mult, op1=ALU.mult),
            reads=(pre + "mv", pre + "rs"), writes=(pre + "rs",))

    es_win = ExitStack()
    W_in_sb = es_win.enter_context(nc.sbuf_tensor("W_in_sb", [128, 8, DPROJ], BF16))
    es0 = ExitStack()
    w32 = [mk(es0, f"w32_{i}", [128, DPROJ], F32) for i in range(2)]
    w1 = mk(es0, "w1", [NH, 2], F32)

    r = rec()
    for dst, src in ((trilneg, TRILNEG), (trillt, TRILLT), (maskrx, MASKRX),
                     (ones64, ONES64), (trilsl, TRILSL), (ident, IDENT),
                     (onesr, ONESR), (identb, IDENTB)):
        r.dma("sync", dst[:, :], src, writes=(dst.name,))
    r.dma("sync", cst[:, :], bc(CONSTS.rearrange("one c -> (one c)"), [128, 4], 0), writes=("cst",))
    r.dma("sync", dtb_rep[:, :], bc(dt_bias, [128, NH], 0), writes=("dtb_rep",))
    r.dma("sync", convw_sb[:, :, :], conv_w.rearrange("(m p) k -> p m k", p=128),
          writes=("convw",))
    r.dma("sync", convb_sb[:, :], conv_b.rearrange("(m p) -> p m", p=128),
          writes=("convb",), allow_slow_non_contiguous=True)
    r.dma("sync", w1[:, 0:1], A_log.unsqueeze(1), writes=("w1",))
    r.op("scalar", lambda e: e.activation(w1[:, 1:2], w1[:, 0:1], AF.Exp),
         reads=("w1",), writes=("w1e",))
    r.op("scalar", lambda e: e.mul(negA_col[:, :], w1[:, 1:2], -1.0),
         reads=("w1e",), writes=("negA_col",))
    r.dma("sync", SP_NEGA, negA_col[:, :], reads=("negA_col",), writes=("SP_NEGA",))
    r.dma("sync", negA_rep[:, :],
          bc(SP_NEGA.rearrange("h one -> (h one)"), [128, NH], 0),
          reads=("SP_NEGA",), writes=("negA_rep",))
    for k in range(8):
        buf, key = w32[k % 2], f"w32_{k % 2}"
        r.dma("sync", buf[:, :], W_in[128 * k:128 * (k + 1), :], writes=(key,))
        r.op("scalar",
             lambda e, k=k, buf=buf: e.activation(W_in_sb[:, k, :], buf[:, :], AF.Identity),
             reads=(key,), writes=("W_in_sb",))
    esA = ExitStack()
    u_t = [mk(esA, f"u_t{i}", [128, 4, D], F32) for i in range(2)]
    x0_t = [mk(esA, f"x0_t{i}", [128, 4, D], BF16) for i in range(2)]
    st_t = mk(esA, "st_t", [128, 12], F32)
    mv_t = mk(esA, "mv_t", [128, 2], F32)
    rs_t = mk(esA, "rs_t", [128, 2], F32)

    r.dma("sync", u_t[0][0:16, 0, :], u_halo, writes=("u_t0",))
    ln_apply(r, "a", ("u_t0",), [u_t[0][0:16, 0, 0:512], u_t[0][0:16, 0, 512:1024]],
             st_t[0:16, :], mv_t[0:16, :], rs_t[0:16, :], npart=16)
    r.op("scalar", lambda e: e.activation(x0_t[0][0:16, 0, :], u_t[0][0:16, 0, :],
                                          AF.Identity, scale=rs_t[0:16, 0:1],
                                          bias=rs_t[0:16, 1:2]),
         reads=("u_t0", "ars", "x0_t0"), writes=("x0_t0",))
    r.dma("scalar", SP_X0NH, x0_t[0][0:16, 0, :], stream="x0_t0", reads=("x0_t0",), writes=("SP_X0NH",))
    for t in range(NT):
        ub, xb = u_t[t % 2], x0_t[t % 2]
        uk, xk = f"u_t{t % 2}", f"x0_t{t % 2}"
        r.dma("sync", ub[:, :, :],
              u_seg[TT * t:TT * (t + 1), :].rearrange("(g p) d -> p g d", p=128),
              reads=(uk,), writes=(uk,))
        for g in range(4):
            ln_apply(r, "a", (uk,), [ub[:, g, 0:512], ub[:, g, 512:1024]], st_t, mv_t, rs_t)
            r.op("scalar", lambda e, g=g, ub=ub, xb=xb: e.activation(
                xb[:, g, :], ub[:, g, :], AF.Identity,
                scale=rs_t[:, 0:1], bias=rs_t[:, 1:2]),
                reads=(uk, "ars", xk), writes=(xk,))
        r.dma("scalar", SP_X0N[TT * t:TT * (t + 1), :].rearrange("(g p) d -> p g d", p=128),
              xb[:, :, :], stream=xk, reads=(xk,), writes=(f"SP_X0N:{t}",))
    r.build()
    esA.close()
    es0.close()

    esB = ExitStack()
    x0F = [mk(esB, f"x0F{i}", [128, 8, TT], BF16) for i in range(2)]
    x0hF = mk(esB, "x0hF", [128, 8, 16], BF16)
    z_t = [mk(esB, f"z_t{i}", [128, 1024], BF16) for i in range(2)]
    xc = mk(esB, "xc", [128, 18, 515], BF16)
    tail_b = mk(esB, "tail_b", [128, 18, 3], BF16)
    cacc = mk(esB, "cacc", [128, 2, TT], F32)
    hm_buf = mk(esB, "hm_buf", [NH, T], F32)
    cs_HM = mk(esB, "cs_HM", [NH, T], F32)
    sp1 = mk(esB, "sp1", [128, 32, NH], F32)
    sp2 = mk(esB, "sp2", [128, 32, NH], F32)
    psZ = esB.enter_context(nc.psum_tensor("psZ", [128, 2, 1024], F32))
    psX = esB.enter_context(nc.psum_tensor("psX", [128, 2, 512], F32))
    psM = esB.enter_context(nc.psum_tensor("psM", [128, 128], F32))

    WINK = ("W_in_sb",)
    r = rec()
    r.xpose(x0hF[:, :, :], SP_X0NH, reads=("SP_X0NH",), writes=("x0hF",))
    for m in range(18):
        def f_hmm(e, m=m):
            inst = None
            for k in range(8):
                inst = e.matmul(psM[:, 0:16], W_in_sb[:, k, DI + 128 * m:DI + 128 * (m + 1)],
                                x0hF[:, k, :], start=(k == 0), stop=(k == 7))
            return inst
        r.op("tensor", f_hmm, reads=("x0hF", "psM") + WINK, writes=("psM",))
        r.op("scalar", lambda e, m=m: e.activation(tail_b[:, m, :], psM[:, 13:16], AF.Identity),
             reads=("psM", "tail"), writes=("tail",))

    for t in range(NT):
        x0b, x0k = x0F[t % 2], f"x0F{t % 2}"
        r.xpose(x0b[:, :, :], SP_X0N[TT * t:TT * (t + 1), :],
                reads=(f"SP_X0N:{t}", x0k), writes=(x0k,))
        xcb, xck = xc, "xc"
        for g in range(4):
            for n2 in range(2):
                zb, zbk = z_t[n2], f"z_t{n2}"
                def f_zmm(e, g=g, n2=n2, x0b=x0b):
                    inst = None
                    for half in range(2):
                        for k in range(8):
                            inst = e.matmul(psZ[:, n2, 512 * half:512 * (half + 1)],
                                            x0b[:, k, 128 * g:128 * (g + 1)],
                                            W_in_sb[:, k, 1024 * n2 + 512 * half:1024 * n2 + 512 * (half + 1)],
                                            start=(k == 0), stop=(k == 7))
                    return inst
                r.op("tensor", f_zmm, reads=(x0k, f"psZ{n2}") + WINK, writes=(f"psZ{n2}",))
                r.op("scalar", lambda e, n2=n2, zb=zb: e.activation(
                    zb[:, :], psZ[:, n2, :], AF.Identity),
                    reads=(f"psZ{n2}", zbk), writes=(zbk, f"psZ{n2}"))
                r.dma("scalar", SP_Z[TT * t + 128 * g:TT * t + 128 * (g + 1),
                                     1024 * n2:1024 * (n2 + 1)], zb[:, :],
                      stream=zbk, reads=(zbk,), writes=(f"SP_Z:{t // 4}",))
        for g in range(4):
            def f_dtmm(e, g=g, x0b=x0b):
                inst = None
                for k in range(8):
                    inst = e.matmul(psM[:, 0:NH], x0b[:, k, 128 * g:128 * (g + 1)],
                                    W_in_sb[:, k, DPROJ - NH:DPROJ],
                                    start=(k == 0), stop=(k == 7))
                return inst
            r.op("tensor", f_dtmm, reads=(x0k, "psM") + WINK, writes=("psM",))
            r.op("vector", lambda e, t=t, g=g: e.tensor_copy(dt_TM[:, 4 * t + g, :],
                                                             psM[:, 0:NH]),
                 reads=("psM", "dt_TM"), writes=("dt_TM",))
        for m in range(18):
            pb = m % 2
            def f_xmm(e, m=m, pb=pb, x0b=x0b):
                inst = None
                for k in range(8):
                    inst = e.matmul(psX[:, pb, :],
                                    W_in_sb[:, k, DI + 128 * m:DI + 128 * (m + 1)],
                                    x0b[:, k, :], start=(k == 0), stop=(k == 7))
                return inst
            r.op("tensor", f_xmm, reads=(x0k, f"psX{pb}") + WINK, writes=(f"psX{pb}",))
            r.op("scalar", lambda e, m=m, pb=pb, xcb=xcb: e.activation(
                xcb[:, m, 3:515], psX[:, pb, :], AF.Identity),
                reads=(f"psX{pb}", xck), writes=(xck, f"psX{pb}"))
        r.op("vector", lambda e, xcb=xcb: e.tensor_copy(xcb[:, :, 0:3], tail_b[:, :, :]),
             reads=("tail", xck), writes=(xck,))
        for m in range(18):
            ca, cak = cacc[:, m % 2, :], f"cacc{m % 2}"
            r.op("vector", lambda e, m=m, ca=ca, xcb=xcb: e.tensor_scalar(
                ca, xcb[:, m, 0:512], convw_sb[:, m, 0:1], convb_sb[:, m:m + 1],
                op0=ALU.mult, op1=ALU.add),
                reads=(xck, "convw", "convb", cak), writes=(cak,))
            for i in range(1, 4):
                r.op("vector", lambda e, m=m, i=i, ca=ca, xcb=xcb: e.scalar_tensor_tensor(
                    ca, xcb[:, m, i:i + 512], convw_sb[:, m, i:i + 1], ca,
                    op0=ALU.mult, op1=ALU.add),
                    reads=(xck, "convw", cak), writes=(cak,))
            r.op("vector", lambda e, m=m, xcb=xcb: e.tensor_copy(tail_b[:, m, :],
                                                                 xcb[:, m, 512:515]),
                 reads=(xck, "tail"), writes=("tail",))
            r.op("scalar", lambda e, m=m, ca=ca, xcb=xcb: e.activation(
                xcb[:, m, 3:515], ca, AF.Silu),
                reads=(cak, xck), writes=(xck,))
        r.op("vector", lambda e, t=t, xcb=xcb: e.tensor_copy(B_FM[:, TT * t:TT * (t + 1)],
                                                             xcb[:, 16, 3:515]),
             reads=(xck,), writes=("B_FM",))
        r.op("vector", lambda e, t=t, xcb=xcb: e.tensor_copy(C_FM[:, TT * t:TT * (t + 1)],
                                                             xcb[:, 17, 3:515]),
             reads=(xck,), writes=("C_FM",))
        r.dma("scalar",
              SP_XC[0:2048, TT * t:TT * (t + 1)].rearrange("(m p) c -> p m c", p=128),
              xcb[:, 0:16, 3:515], stream=xck, reads=(xck,), writes=(f"SP_XC:{t}",))
        r.dma("scalar", SP_XC[2048:2176, TT * t:TT * (t + 1)], xcb[:, 16, 3:515],
              stream=xck, reads=(xck,), writes=(f"SP_XCB:{t}",))

    dtv = dt_TM[:, :, :].rearrange("p a b -> p (a b)")
    sp1v = sp1[:, :, :].rearrange("p a b -> p (a b)")
    sp2v = sp2[:, :, :].rearrange("p a b -> p (a b)")
    r.op("vector", lambda e: e.tensor_tensor(
        dt_TM[:, :, :], dt_TM[:, :, :], bc(dtb_rep[:, :], [128, 32, NH], 1),
        op=ALU.add), reads=("dt_TM", "dtb_rep"), writes=("dt_TM",))
    r.op("scalar", lambda e: e.activation(sp1v, dtv, AF.Abs),
         reads=("dt_TM", "sp1"), writes=("sp1",))
    r.op("scalar", lambda e: e.activation(sp1v, sp1v, AF.Exp, scale=-1.0),
         reads=("sp1",), writes=("sp1",))
    r.op("scalar", lambda e: e.activation(sp1v, sp1v, AF.Ln, bias=cst[:, 1:2]),
         reads=("sp1", "cst"), writes=("sp1",))
    r.op("scalar", lambda e: e.activation(sp2v, dtv, AF.Relu),
         reads=("dt_TM", "sp2"), writes=("sp2",))
    r.op("vector", lambda e: e.tensor_tensor(dtv, sp1v, sp2v, op=ALU.add),
         reads=("sp1", "sp2", "dt_TM"), writes=("dt_TM",))
    r.op("vector", lambda e: e.tensor_tensor(
        a_TM[:, :, :], dt_TM[:, :, :],
        bc(negA_rep[:, :], [128, 32, NH], 1), op=ALU.mult),
        reads=("dt_TM", "negA_rep"), writes=("a_TM",))
    r.op("vector", lambda e: e.tensor_copy(a_TMb[:, :, :], a_TM[:, :, :]),
         reads=("a_TM",), writes=("a_TMb",))
    for p in range(32):
        r.op("tensor", lambda e, p=p: e.transpose(psM[0:32, 0:128], dt_TM[:, p, :], ident[:, :]),
             reads=("dt_TM", "ident", "psM"), writes=("psM",))
        r.op("vector", lambda e, p=p: e.tensor_copy(hm_buf[:, 128 * p:128 * (p + 1)],
                                                    psM[0:32, 0:128]),
             reads=("psM", "hm_buf"), writes=("hm_buf",))
    r.op("vector", lambda e: e.tensor_scalar(hm_buf[:, :], hm_buf[:, :], negA_col[:, 0:1],
                                             None, op0=ALU.mult, op1=ALU.bypass),
         reads=("hm_buf", "negA_col"), writes=("hm_buf",))
    def f_scans(e):
        inst = None
        for c in range(NCK):
            inst = e.tensor_tensor_scan(cs_HM[:, CS * c:CS * (c + 1)], ones64[0:NH, :],
                                        hm_buf[:, CS * c:CS * (c + 1)], initial=0.0,
                                        op0=ALU.mult, op1=ALU.add)
        return inst
    r.op("vector", f_scans, reads=("hm_buf", "ones64", "cs_HM"), writes=("cs_HM",))
    r.op("vector", lambda e: e.tensor_copy(
        atot_HM[:, :], cs_HM[:, :].rearrange("h (c s) -> h c s", s=CS)[:, :, CS - 1]),
        reads=("cs_HM",), writes=("atot_HM",))
    r.op("vector", lambda e: e.tensor_tensor_scan(cpi_HM[:, :], ones64[0:NH, :], atot_HM[:, :],
                                                  initial=0.0, op0=ALU.mult, op1=ALU.add),
         reads=("atot_HM", "ones64"), writes=("cpi_HM",))
    r.op("vector", lambda e: e.reduce_sum(aseg_col[:, :], atot_HM[:, :], axis=AX.X),
         reads=("atot_HM",), writes=("aseg_col",))
    r.op("vector", lambda e: e.tensor_scalar(rc_HM[:, :], cpi_HM[:, :], -1.0, aseg_col[:, 0:1],
                                             op0=ALU.mult, op1=ALU.add),
         reads=("cpi_HM", "aseg_col"), writes=("rc_HM",))
    r.op("vector", lambda e: e.tensor_tensor(b2_HM[:, :], atot_HM[:, :], rc_HM[:, :], op=ALU.add),
         reads=("atot_HM", "rc_HM"), writes=("b2_HM",))
    r.op("scalar", lambda e: e.activation(dexp_HM[:, :], atot_HM[:, :], AF.Exp),
         reads=("atot_HM",), writes=("dexp_HM",))
    r.dma("sync", SP_SB[0:64, :].rearrange("c h -> h c"), atot_HM[:, :],
          reads=("atot_HM",), writes=("SP_SB0",), allow_slow_non_contiguous=True)
    r.dma("sync", SP_SB[64:128, :].rearrange("c h -> h c"), b2_HM[:, :],
          reads=("b2_HM",), writes=("SP_SB1",), allow_slow_non_contiguous=True)
    r.dma("sync", SP_SB[128:192, :].rearrange("c h -> h c"), dexp_HM[:, :],
          reads=("dexp_HM",), writes=("SP_SB2",), allow_slow_non_contiguous=True)
    for p in range(32):
        r.op("tensor", lambda e, p=p: e.transpose(psM[:, 0:32], cs_HM[:, 128 * p:128 * (p + 1)],
                                                  ident[0:32, 0:32]),
             reads=("cs_HM", "ident", "psM"), writes=("psM",))
        r.op("vector", lambda e, p=p: e.tensor_copy(cs_TM[:, p, :], psM[:, 0:32]),
             reads=("psM", "cs_TM"), writes=("cs_TM",))
    csv = cs_TM[:, :, :].rearrange("p a b -> p (a b)")
    for cpar in range(2):
        r.dma("sync", sp1[64 * cpar:64 * (cpar + 1), :, :],
              bass.AP(SP_SB.tensor, NH * cpar, [[0, 64], [2 * NH, 32], [1, NH]]),
              reads=("SP_SB0", "sp1"), writes=("sp1",))
        r.dma("sync", sp2[64 * cpar:64 * (cpar + 1), :, :],
              bass.AP(SP_SB.tensor, 64 * NH + NH * cpar, [[0, 64], [2 * NH, 32], [1, NH]]),
              reads=("SP_SB1", "sp2"), writes=("sp2",))
    r.op("vector", lambda e: e.tensor_tensor(sp1v, sp1v, csv, op=ALU.subtract),
         reads=("sp1", "cs_TM"), writes=("sp1",))
    r.op("vector", lambda e: e.tensor_tensor(sp2v, sp2v, csv, op=ALU.subtract),
         reads=("sp2", "cs_TM"), writes=("sp2",))
    def f_exps(e):
        e.activation(dsT[:, :, :].rearrange("p a b -> p (a b)"), sp1v, AF.Exp)
        e.activation(ds2T[:, :, :].rearrange("p a b -> p (a b)"), sp2v, AF.Exp)
        return e.activation(sdT[:, :, :].rearrange("p a b -> p (a b)"), csv, AF.Exp)
    r.op("scalar", f_exps, reads=("sp1", "sp2", "cs_TM"), writes=("dsT", "ds2T", "sdT"))
    r.build()
    esB.close()
    es_win.close()

    B_TM = mk(es_b, "B_TM", [128, 32, 128], BF16)
    D_rep = mk(es_b, "D_rep", [128, NCK, NH], F32)
    S_sb = mk(es_b, "S_sb", [128, DI], BF16)
    battn = mk(es_b, "battn", [1, D], BF16)
    W_out_sb = mk(es_b, "W_out_sb", [128, 16, D], BF16)

    esC = ExitStack()
    xcT = mk(esC, "xcT", [128, 8, 2176], BF16)
    xs_b = mk(esC, "xs_b", [128, DI], BF16)
    xd_b = mk(esC, "xd_b", [128, DI], BF16)
    xd2_b = mk(esC, "xd2_b", [128, DI], BF16)
    cb_f = mk(esC, "cb_f", [128, DI], F32)
    gth = mk(esC, "gth", [128, DI], F32)
    acc_f = mk(esC, "acc_f", [128, DI], F32)
    tmp_f = mk(esC, "tmp_f", [128, DI], F32)
    a8 = mk(esC, "a8", [8, NH + 8], F32)
    gm_col = mk(esC, "gm_col", [8, 1], F32)
    wrep = mk(esC, "wrep", [128, 8, NH], F32)
    smB = mk(esC, "smB", [8, NH], F32)
    smC = mk(esC, "smC", [8, NH], F32)
    wo32 = [mk(esC, "wo32_0", [128, D], F32)] * 2
    wo16 = mk(esC, "wo16", [128, D], BF16)
    nwb_col = mk(esC, "nwb_col", [128, 32], F32)
    nb_colb = mk(esC, "nb_colb", [128, 16], BF16)
    wi32b = [mk(esC, "wi32b0", [128, 2, D], I32)] * 2
    wibfb = [mk(esC, "wibfb0", [128, 2, D], BF16)] * 2
    psCB = esC.enter_context(nc.psum_tensor("psCB", [128, DI], F32))
    psE = esC.enter_context(nc.psum_tensor("psE", [8, NH], F32))
    psB2 = esC.enter_context(nc.psum_tensor("psB2", [1, D], F32))

    r = rec()
    r.dma("sync", nwb_col[:, 0:16], norm_w.rearrange("(k p) -> p k", p=128),
          writes=("nw_col",), allow_slow_non_contiguous=True)
    r.dma("sync", nwb_col[:, 16:32], norm_b.rearrange("(k p) -> p k", p=128),
          writes=("nb_col32",), allow_slow_non_contiguous=True)
    r.op("scalar", lambda e: e.activation(nb_colb[:, :], nwb_col[:, 16:32], AF.Identity),
         reads=("nb_col32",), writes=("nb_colb",))
    for k in range(16):
        buf, key = wo32[0], "wo32_0"
        r.dma("sync", buf[:, :], W_out[128 * k:128 * (k + 1), :], reads=(key,), writes=(key,))
        r.op("scalar", lambda e, buf=buf: e.activation(wo16[:, :], buf[:, :], AF.Identity),
             reads=(key, "wo16"), writes=("wo16",))
        def f_bmm(e, k=k):
            e.matmul(psB2[:, 0:512], nb_colb[:, k:k + 1], wo16[:, 0:512],
                     start=(k == 0), stop=(k == 15))
            return e.matmul(psB2[:, 512:1024], nb_colb[:, k:k + 1], wo16[:, 512:1024],
                            start=(k == 0), stop=(k == 15))
        r.op("tensor", f_bmm, reads=("wo16", "nb_colb", "psB2"), writes=("psB2",))
        r.op("scalar", lambda e, k=k, buf=buf: e.activation(
            W_out_sb[:, k, :], buf[:, :], AF.Identity, scale=nwb_col[:, k:k + 1]),
            reads=(key, "nw_col", "W_out_sb"), writes=("W_out_sb",))
    r.op("scalar", lambda e: e.activation(battn[:, :], psB2[:, :], AF.Identity),
         reads=("psB2",), writes=("battn",))
    rnd_i = [0]
    def bounce(wsrc, wdst, wname, r0, rows, wkey):
        i = rnd_i[0] % 2
        rnd_i[0] += 1
        m = rows // 128
        r.dma("scalar", wi32b[i][:, 0:m, :],
              wsrc[r0:r0 + rows, :].rearrange("(m p) c -> p m c", p=128),
              stream="wi32b0", reads=("wi32b0",), writes=("wi32b0",))
        r.op("scalar", lambda e, i=i, m=m: e.activation(
            wibfb[i][:, 0:m, :], wi32b[i][:, 0:m, :], AF.Identity),
            reads=("wi32b0", "wibfb0"), writes=("wibfb0",))
        r.dma("scalar", wdst[r0:r0 + rows, :].rearrange("(m p) c -> p m c", p=128),
              wibfb[i][:, 0:m, :], stream="wibfb0", reads=("wibfb0",),
              writes=(wkey,))
    for wsrc, wdst, wname in ((wg, SP_WGN, "SP_WGN"), (wu, SP_WUN, "SP_WUN")):
        for r0 in range(0, HID, 256):
            bounce(wsrc, wdst, wname, r0, 256, f"{wname}:{r0 // 1024 * 1024}")
    def bounce_wd(r0, co):
        i = rnd_i[0] % 2
        rnd_i[0] += 1
        fv32 = wi32b[i][:, :, :].rearrange("p a b -> p (a b)")
        fvbf = wibfb[i][:, :, :].rearrange("p a b -> p (a b)")
        r.dma("scalar", fv32[:, 0:1536], wd[r0:r0 + 128, co:co + 1536],
              stream="wi32b0", reads=("wi32b0",), writes=("wi32b0",))
        r.op("scalar", lambda e, i=i, fv32=fv32, fvbf=fvbf: e.activation(
            fvbf[:, 0:1536], fv32[:, 0:1536], AF.Identity),
            reads=("wi32b0", "wibfb0"), writes=("wibfb0",))
        r.dma("scalar", SP_WDN[r0:r0 + 128, co:co + 1536], fvbf[:, 0:1536],
              stream="wibfb0", reads=("wibfb0",),
              writes=(f"SP_WDN:{r0 // 256 * 256}",))
    for r0 in range(0, D, 128):
        for co in (0, 1536):
            bounce_wd(r0, co)
    r.dma("sync", D_rep[:, :, :],
          bass.AP(SP_SB.tensor, 128 * NH, [[0, 128], [NH, NCK], [1, NH]]),
          reads=("SP_SB2",), writes=("D_rep",))
    first = [True]
    for quarter in range(4):
        r.xpose(xcT[:, :, :], SP_XC[:, 1024 * quarter:1024 * (quarter + 1)],
                reads=(f"SP_XC:{2 * quarter}", f"SP_XC:{2 * quarter + 1}",
                       f"SP_XCB:{2 * quarter}", f"SP_XCB:{2 * quarter + 1}", "xcT"),
                writes=("xcT",))
        for m in range(8):
            pp = 8 * quarter + m
            r.op("vector", lambda e, m=m, pp=pp: e.tensor_tensor(
                xs_b[:, :].rearrange("p (h q) -> p h q", q=HP),
                xcT[:, m, 0:DI].rearrange("p (h q) -> p h q", q=HP),
                bc(dt_TM[:, pp, :], [128, NH, HP], 2), op=ALU.mult),
                reads=("xcT", "dt_TM", "xs_b"), writes=("xs_b",))
            r.op("vector", lambda e, pp=pp: e.tensor_tensor(
                xd_b[:, :].rearrange("p (h q) -> p h q", q=HP),
                xs_b[:, :].rearrange("p (h q) -> p h q", q=HP),
                bc(dsT[:, pp, :], [128, NH, HP], 2), op=ALU.mult),
                reads=("xs_b", "dsT", "xd_b"), writes=("xd_b",))
            r.op("gpsimd", lambda e, pp=pp: e.tensor_tensor(
                xd2_b[:, :].rearrange("p (h q) -> p h q", q=HP),
                xs_b[:, :].rearrange("p (h q) -> p h q", q=HP),
                bc(ds2T[:, pp, :], [128, NH, HP], 2), op=ALU.mult),
                reads=("xs_b", "ds2T", "xd2_b"), writes=("xd2_b",))
            r.op("vector", lambda e, m=m, pp=pp: e.tensor_copy(B_TM[:, pp, :],
                                                               xcT[:, m, 2048:2176]),
                 reads=("xcT", "B_TM"), writes=("B_TM",))
            r.dma("scalar", SP_XS[128 * pp:128 * (pp + 1), :], xs_b[:, :],
                  stream="xs_b", reads=("xs_b",), writes=(f"SP_XS:{pp // 4}",))
            r.dma("scalar", SP_XD[128 * pp:128 * (pp + 1), :], xd_b[:, :],
                  stream="xd_b", reads=("xd_b",), writes=(f"SP_XD:{pp // 4}",))
            def f_st2(e, m=m, first=first, pp=pp):
                inst = None
                for cc2 in range(2):
                    sl = slice(64 * cc2, 64 * (cc2 + 1))
                    for h in range(NH):
                        inst = e.matmul(psCB[:, HP * h:HP * (h + 1)], xcT[sl, m, 2048:2176],
                                        xd2_b[sl, HP * h:HP * (h + 1)], start=(first[0] and h % 8 == 0),
                                        stop=(pp == NPAIR - 1 and cc2 == 1 and h == NH - 1))
                        if h == NH - 1:
                            first[0] = False
                return inst
            r.op("tensor", f_st2, reads=("xcT", "xd2_b", "psCB"), writes=("psCB",))
    r.op("vector", lambda e: e.tensor_copy(cb_f[:, :], psCB[:, :]),
         reads=("psCB",), writes=("cb_f",))
    r.dma("sync", CBT[:, 0:DI], cb_f[:, :], reads=("cb_f",), writes=("CBT",))
    r.dma("sync", CBT[0:NH, DI:DI + 1], aseg_col[:, :], reads=("aseg_col",), writes=("CBT2",), allow_slow_non_contiguous=True)
    r.build()
    nc.all_core_barrier()
    r = rec()
    r.op("gpsimd", lambda e: e.collective_compute(
        "AllGather", ALU.bypass, replica_groups=[list(range(8))], ins=[CBT], outs=[GATH])
        .then_inc(cc_sem, 1) and None,
        reads=("CBT", "CBT2"), writes=(), inc=0)
    r.op("gpsimd", lambda e: e.wait_ge(cc_sem, 1) or e.nop(), reads=(), writes=("GATH",))
    r.build()
    nc.all_core_barrier()
    r = rec()
    r.dma("sync", a8[:, NH:NH + 8], SEL2, writes=("a8sel",))
    r.dma("sync", a8[0:8, 0:NH], GATH[:, DI].rearrange("(j p) -> j p", p=128)[:, 0:NH],
          reads=("GATH",), writes=("a8",), allow_slow_non_contiguous=True)
    r.dma("sync", gm_col[:, :], GMASK, writes=("gm_col",))
    r.op("tensor", lambda e: e.matmul(psE[:, :], a8[:, NH:NH + 8], a8[0:8, 0:NH],
                                      start=True, stop=True),
         reads=("a8sel", "a8", "psE"), writes=("psE",))
    r.op("scalar", lambda e: e.activation(smB[:, :], psE[:, :], AF.Exp),
         reads=("psE",), writes=("smB",))
    r.op("vector", lambda e: e.tensor_scalar(smC[:, :], smB[:, :], gm_col[:, 0:1], None,
                                             op0=ALU.mult, op1=ALU.bypass),
         reads=("smB", "gm_col"), writes=("smC",))
    r.dma("sync", SP_W8, smC[:, :], reads=("smC",), writes=("SP_W8",))
    r.dma("sync", wrep[:, :, :], bass.AP(SP_W8.tensor, 0, [[0, 128], [NH, 8], [1, NH]]),
          reads=("SP_W8",), writes=("wrep",))
    for j in range(8):
        r.dma("sync", gth[:, :], GATH[128 * j:128 * (j + 1), 0:DI],
              reads=("GATH", "gth"), writes=("gth",))
        if j == 0:
            r.op("vector", lambda e: e.tensor_tensor(
                acc_f[:, :].rearrange("p (h q) -> p h q", q=HP),
                gth[:, :].rearrange("p (h q) -> p h q", q=HP),
                bc(wrep[:, 0, :], [128, NH, HP], 2), op=ALU.mult),
                reads=("gth", "wrep", "acc_f"), writes=("acc_f",))
        else:
            r.op("vector", lambda e, j=j: e.tensor_tensor(
                tmp_f[:, :].rearrange("p (h q) -> p h q", q=HP),
                gth[:, :].rearrange("p (h q) -> p h q", q=HP),
                bc(wrep[:, j, :], [128, NH, HP], 2), op=ALU.mult),
                reads=("gth", "wrep", "tmp_f"), writes=("tmp_f",))
            r.op("vector", lambda e: e.tensor_tensor(acc_f[:, :], acc_f[:, :], tmp_f[:, :],
                                                     op=ALU.add),
                 reads=("tmp_f", "acc_f"), writes=("acc_f",))
    r.op("vector", lambda e: e.tensor_copy(S_sb[:, :], acc_f[:, :]),
         reads=("acc_f", "S_sb"), writes=("S_sb",))
    if "SP_DBGS2" in dbg:
        r.dma("sync", SP_DBGS2[:, 0:DI], cb_f[:, :], stream="cb_f", reads=("cb_f",), writes=("dbgs9",))
        r.dma("sync", SP_DBGS2[0:128, DI:DI + NH].rearrange("p c -> p c"), wrep[:, 0, :],
              reads=("wrep",), writes=("dbgs10",))
    if "SP_DBGS" in dbg:
        r.dma("sync", SP_DBGS[:, 0:DI], acc_f[:, :], stream="acc_f", reads=("acc_f",), writes=("dbgs",))
        r.dma("sync", SP_DBGS[0:8, DI:DI + NH], smC[:, :], reads=("smC",), writes=("dbgs2",))
        r.dma("sync", SP_DBGS[8:16, DI:DI + NH], a8[:, 0:NH], reads=("a8",), writes=("dbgs3",))
        r.dma("sync", SP_DBGS[16:24, DI:DI + NH], smB[:, :], reads=("smB",), writes=("dbgs4",))
    r.build()
    esC.close()

    esD = ExitStack()
    xs_p = [mk(esD, f"xs_p{i}", [128, DI], BF16) for i in range(2)]
    xd_p = [mk(esD, f"xd_p{i}", [128, DI], BF16) for i in range(2)]
    z_p = [mk(esD, f"z_p{i}", [128, DI], BF16) for i in range(2)]
    AM_sb = mk(esD, "AM_sb", [128, DI], BF16)
    M0_sb = mk(esD, "M0_sb", [128, DI], BF16)
    M_sb = mk(esD, "M_sb", [128, DI], BF16)
    Gt_sb = mk(esD, "Gt_sb", [128, 64], BF16)
    y_sb = mk(esD, "y_sb", [128, DI], F32)
    zs_sb = mk(esD, "zs_sb", [128, DI], BF16)
    yn_sb = mk(esD, "yn_sb", [128, DI], BF16)
    st2 = mk(esD, "st2", [128, 24], F32)
    mv2 = mk(esD, "mv2", [128, 2], F32)
    rs2 = mk(esD, "rs2", [128, 2], F32)
    ynF = mk(esD, "ynF", [128, 16, TT], BF16)
    at_f = mk(esD, "at_f", [128, D], F32)
    u2_t = [mk(esD, f"u2_t{i}", [128, D], F32) for i in range(2)]
    x1_t = mk(esD, "x1_t", [128, D], F32)
    x1n_t = mk(esD, "x1n_t", [128, D], BF16)
    psT = esD.enter_context(nc.psum_tensor("psT", [128, DI], F32))
    psO = esD.enter_context(nc.psum_tensor("psO", [128, DI], F32))

    r = rec()
    for pp in range(NPAIR):
        bi = pp % 2
        xsb, xdb, zb = xs_p[bi], xd_p[bi], z_p[bi]
        xsk, xdk, zk = f"xs_p{bi}", f"xd_p{bi}", f"z_p{bi}"
        r.dma("sync", xsb[:, :], SP_XS[128 * pp:128 * (pp + 1), :],
              reads=(f"SP_XS:{pp // 4}", xsk), writes=(xsk,))
        r.dma("sync", xdb[:, :], SP_XD[128 * pp:128 * (pp + 1), :],
              reads=(f"SP_XD:{pp // 4}", xdk), writes=(xdk,))
        r.dma("sync", zb[:, :], SP_Z[128 * pp:128 * (pp + 1), :],
              reads=(f"SP_Z:{pp // 4}", zk), writes=(zk,))
        for cc in range(2):
            c = 2 * pp + cc
            sl = slice(64 * cc, 64 * (cc + 1))
            amk = f"AM{cc}"
            r.op("gpsimd", lambda e, pp=pp, sl=sl: e.tensor_tensor(
                AM_sb[sl, :].rearrange("p (l h) -> p l h", h=NH),
                bc(a_TM[sl, pp, :], [64, 64, NH], 1),
                bc(trillt[sl, :], [64, 64, NH], 2), op=ALU.mult),
                reads=("a_TM", "trillt", amk), writes=(amk,))
            def f_tmm(e, pp=pp, sl=sl):
                inst = None
                for nb4 in range(4):
                    ls = slice(16 * nb4, 16 * (nb4 + 1))
                    ps = psT[sl, 512 * nb4:512 * (nb4 + 1)].rearrange("p (l h) -> p l h", h=NH)
                    e.matmul(ps, trilneg[sl, :],
                             bc(a_TMb[sl, pp, :], [64, 64, NH], 1)[:, ls, :],
                             start=True, stop=False)
                    e.matmul(ps, ones64[sl, :],
                             AM_sb[sl, :].rearrange("p (l h) -> p l h", h=NH)[:, ls, :],
                             start=False, stop=False)
                    inst = e.matmul(ps, trillt[sl, :],
                                    maskrx[sl, :].rearrange("p (l h) -> p l h", h=NH)[:, ls, :],
                                    start=False, stop=True)
                return inst
            r.op("tensor", f_tmm, reads=("a_TMb", "trilneg", "ones64", amk, "maskrx", f"psT{cc}"),
                 writes=(f"psT{cc}",))
            r.op("scalar", lambda e, sl=sl: e.activation(M0_sb[sl, :], psT[sl, :], AF.Exp),
                 reads=(f"psT{cc}", f"M0{cc}"), writes=(f"M0{cc}", f"psTr{cc}"))
            r.op("tensor", lambda e, c=c, sl=sl: e.matmul(
                psT[sl, 0:64], B_FM[:, 64 * c:64 * (c + 1)], C_FM[:, 64 * c:64 * (c + 1)],
                start=True, stop=True),
                reads=("B_FM", "C_FM", f"psTr{cc}"), writes=(f"psT{cc}",))
            r.op("vector", lambda e, sl=sl, cc=cc: e.tensor_tensor(
                Gt_sb[sl, :], psT[sl, 0:64], trilsl[sl, :], op=ALU.mult),
                reads=(f"psT{cc}", "trilsl", f"Gt{cc}"), writes=(f"Gt{cc}", f"psTr{cc}"))
            r.op("vector", lambda e, sl=sl, cc=cc: e.tensor_tensor(
                M_sb[sl, :].rearrange("p (l h) -> p l h", h=NH),
                M0_sb[sl, :].rearrange("p (l h) -> p l h", h=NH),
                bc(Gt_sb[sl, :], [64, 64, NH], 2), op=ALU.mult),
                reads=(f"M0{cc}", f"Gt{cc}", f"M{cc}"), writes=(f"M{cc}",))
            def f_yoff(e, c=c, sl=sl):
                inst = None
                for h in range(NH):
                    inst = e.matmul(psO[sl, HP * h:HP * (h + 1)], C_FM[:, 64 * c:64 * (c + 1)],
                                    S_sb[:, HP * h:HP * (h + 1)], start=True, stop=True)
                return inst
            r.op("tensor", f_yoff, reads=("S_sb", "C_FM", f"psO{cc}"), writes=(f"psO{cc}",))
            r.op("vector", lambda e, sl=sl, pp=pp: e.tensor_tensor(
                y_sb[sl, :].rearrange("p (h q) -> p h q", q=HP),
                psO[sl, :].rearrange("p (h q) -> p h q", q=HP),
                bc(sdT[sl, pp, :], [64, NH, HP], 2), op=ALU.mult),
                reads=(f"psO{cc}", "sdT", f"y{cc}", "psOF"), writes=(f"y{cc}", f"psO{cc}"))
            def f_st(e, pp=pp, sl=sl, bi=bi):
                inst = None
                for h in range(NH):
                    inst = e.matmul(psO[:, HP * h:HP * (h + 1)], B_TM[sl, pp, :],
                                    xd_p[bi][sl, HP * h:HP * (h + 1)], start=True, stop=True)
                return inst
            r.op("tensor", f_st, reads=("B_TM", xdk, "psOF"), writes=("psOF", "psO0", "psO1"))
            r.op("gpsimd", lambda e, c=c: e.tensor_tensor(
                S_sb[:, :].rearrange("p (h q) -> p h q", q=HP),
                S_sb[:, :].rearrange("p (h q) -> p h q", q=HP),
                bc(D_rep[:, c, :], [128, NH, HP], 2), op=ALU.mult),
                reads=("S_sb", "D_rep"), writes=("S_sb",))
            r.op("vector", lambda e: e.tensor_tensor(S_sb[:, :], S_sb[:, :], psO[:, :],
                                                     op=ALU.add),
                 reads=("S_sb", "psOF"), writes=("S_sb", "psOF"))
            def f_yd(e, sl=sl, bi=bi):
                inst = None
                for h in range(NH):
                    inst = e.matmul(psT[sl, HP * h:HP * (h + 1)],
                                    M_sb[sl, :].rearrange("p (l h) -> p l h", h=NH)[:, :, h],
                                    xs_p[bi][sl, HP * h:HP * (h + 1)], start=True, stop=True)
                return inst
            r.op("tensor", f_yd, reads=(f"M{cc}", xsk, f"psTr{cc}"), writes=(f"psT{cc}",))
            r.op("vector", lambda e, sl=sl, cc=cc: e.tensor_tensor(
                y_sb[sl, :], y_sb[sl, :], psT[sl, :], op=ALU.add),
                reads=(f"psT{cc}", f"y{cc}"), writes=(f"y{cc}", f"psTr{cc}"))
        r.op("scalar", lambda e, zb=zb: e.activation(zs_sb[:, :], zb[:, :], AF.Silu),
             reads=(zk, "zs"), writes=("zs",))
        r.op("gpsimd", lambda e: e.tensor_tensor(y_sb[:, :], y_sb[:, :], zs_sb[:, :],
                                                 op=ALU.mult),
             reads=("y0", "y1", "zs"), writes=("y0", "y1"))
        ln_apply(r, "ln", ("y0", "y1"), [y_sb[:, 512 * i:512 * (i + 1)] for i in range(4)],
                 st2, mv2, rs2)
        r.op("scalar", lambda e: e.activation(yn_sb[:, :], y_sb[:, :], AF.Identity,
                                              scale=rs2[:, 0:1], bias=rs2[:, 1:2]),
             reads=("y0", "y1", "lnrs", "yn"), writes=("yn",))
        r.dma("scalar", SP_YN[128 * pp:128 * (pp + 1), :], yn_sb[:, :],
              stream="yn", reads=("yn",), writes=(f"SP_YN:{pp // 4}",))
        if pp % 4 == 3:
            t = pp // 4
            r.xpose(ynF[:, :, :], SP_YN[TT * t:TT * (t + 1), :],
                    reads=(f"SP_YN:{t}", "ynF"), writes=("ynF",))
            for g in range(4):
                ub, ubk = u2_t[g % 2], f"u2_t{g % 2}"
                r.dma("sync", ub[:, :],
                      u_seg[TT * t + 128 * g:TT * t + 128 * (g + 1), :],
                      reads=(ubk,), writes=(ubk,))
                def f_op(e, g=g):
                    inst = None
                    for n2 in range(2):
                        for k in range(16):
                            e.matmul(psO[:, 512 * n2:512 * (n2 + 1)],
                                     ynF[:, k, 128 * g:128 * (g + 1)],
                                     W_out_sb[:, k, 512 * n2:512 * (n2 + 1)],
                                     start=(k == 0), stop=False)
                        inst = e.matmul(psO[:, 512 * n2:512 * (n2 + 1)], onesr[:, :],
                                        battn[:, 512 * n2:512 * (n2 + 1)],
                                        start=False, stop=True)
                    return inst
                r.op("tensor", f_op,
                     reads=("ynF", "battn", "onesr", "psOF", "W_out_sb"),
                     writes=("psOF", "psO0", "psO1"))
                ln_apply(r, "ln", ("psOF",), [psO[:, 0:512], psO[:, 512:1024]], st2, mv2, rs2)
                def f_atn(e):
                    e.activation(at_f[:, 0:512], psO[:, 0:512], AF.Identity,
                                 scale=rs2[:, 0:1], bias=rs2[:, 1:2])
                    return e.activation(at_f[:, 512:1024], psO[:, 512:1024], AF.Identity,
                                        scale=rs2[:, 0:1], bias=rs2[:, 1:2])
                r.op("scalar", f_atn, reads=("psOF", "lnrs", "at_f"), writes=("at_f", "psOF"))
                r.op("vector", lambda e, g=g, ub=ub: e.tensor_tensor(
                    x1_t[:, :], at_f[:, :], ub[:, :], op=ALU.add),
                    reads=("at_f", ubk, "x1_t"), writes=("x1_t",))
                ln_apply(r, "ln", ("x1_t",), [x1_t[:, 0:512], x1_t[:, 512:1024]],
                         st2, mv2, rs2)
                r.op("scalar", lambda e, g=g: e.activation(
                    x1n_t[:, :], x1_t[:, :], AF.Identity,
                    scale=rs2[:, 0:1], bias=rs2[:, 1:2]),
                    reads=("x1_t", "lnrs", "x1n_t"), writes=("x1n_t",))
                r.dma("scalar", SP_X1[TT * t + 128 * g:TT * t + 128 * (g + 1), :],
                      x1_t[:, :], stream="x1_t", reads=("x1_t",), writes=(f"SP_X1:{t}",))
                r.dma("scalar", SP_X1N[TT * t + 128 * g:TT * t + 128 * (g + 1), :],
                      x1n_t[:, :], stream="x1n_t", reads=("x1n_t",), writes=(f"SP_X1N:{t}",))
    r.build()
    esD.close()
    es_b.close()
    es_mid.close()

    esE = ExitStack()
    wgT = mk(esE, "wgT", [128, 8, HID], BF16)
    wuT = mk(esE, "wuT", [128, 8, HID], BF16)
    wdT = mk(esE, "wdT", [128, 24, D], BF16)
    x1F = [mk(esE, f"x1F{i}", [128, 8, TT], BF16) for i in range(2)]
    gs_sb = mk(esE, "gs_sb", [128, 512], BF16)
    h_sb = mk(esE, "h_sb", [128, 24, TT], BF16)
    x1_2 = mk(esE, "x1_2", [128, 2, D], F32)
    o_sb = mk(esE, "o_sb", [128, D], F32)
    stf = mk(esE, "stf", [128, 12], F32)
    mvf = mk(esE, "mvf", [128, 2], F32)
    rsf = mk(esE, "rsf", [128, 2], F32)
    psG2 = esE.enter_context(nc.psum_tensor("psG2", [128, 2, 512], F32))
    psU2 = esE.enter_context(nc.psum_tensor("psU2", [128, 2, 512], F32))
    psD2 = esE.enter_context(nc.psum_tensor("psD2", [128, 2, 512], F32))

    r = rec()
    r.xpose(wgT[:, :, :], SP_WGN,
            reads=tuple(f"SP_WGN:{r0}" for r0 in range(0, HID, 1024)), writes=("wgT",))
    r.xpose(wuT[:, :, :], SP_WUN,
            reads=tuple(f"SP_WUN:{r0}" for r0 in range(0, HID, 1024)), writes=("wuT",))
    r.xpose(wdT[:, :, :], SP_WDN,
            reads=tuple(f"SP_WDN:{r0}" for r0 in range(0, D, 256)), writes=("wdT",))
    for t in range(NT):
        xb, xk = x1F[t % 2], f"x1F{t % 2}"
        r.xpose(xb[:, :, :], SP_X1N[TT * t:TT * (t + 1), :],
                reads=(f"SP_X1N:{t}", xk), writes=(xk,))
        for m in range(24):
            pb = m % 2
            def f_gmm(e, m=m, pb=pb, xb=xb):
                inst = None
                for k in range(8):
                    inst = e.matmul(psG2[:, pb, :], wgT[:, k, 128 * m:128 * (m + 1)],
                                    xb[:, k, :], start=(k == 0), stop=(k == 7))
                return inst
            r.op("tensor", f_gmm, reads=(xk, "wgT", f"psG2{pb}"), writes=(f"psG2{pb}",))
            def f_umm(e, m=m, pb=pb, xb=xb):
                inst = None
                for k in range(8):
                    inst = e.matmul(psU2[:, pb, :], wuT[:, k, 128 * m:128 * (m + 1)],
                                    xb[:, k, :], start=(k == 0), stop=(k == 7))
                return inst
            r.op("tensor", f_umm, reads=(xk, "wuT", f"psU2{pb}"), writes=(f"psU2{pb}",))
            r.op("scalar", lambda e, pb=pb: e.activation(gs_sb[:, :], psG2[:, pb, :],
                                                         AF.Silu, scale=1.0 / 127.0),
                 reads=(f"psG2{pb}", "gs"), writes=("gs", f"psG2{pb}"))
            r.op("vector", lambda e, m=m, pb=pb: e.tensor_tensor(
                h_sb[:, m, :], gs_sb[:, :], psU2[:, pb, :], op=ALU.mult),
                reads=("gs", f"psU2{pb}", "h_sb"), writes=("h_sb", f"psU2{pb}"))
        for g in range(4):
            if g % 2 == 0:
                r.dma("sync", x1_2[:, :, :],
                      SP_X1[TT * t + 128 * g:TT * t + 128 * (g + 2), :]
                      .rearrange("(g2 p) d -> p g2 d", p=128),
                      stream="x1_2", reads=("x1_2",), writes=("x1_2",))
            def f_dmm(e, g=g):
                inst = None
                for n2 in range(2):
                    for k in range(24):
                        inst = e.matmul(psD2[:, n2, :], h_sb[:, k, 128 * g:128 * (g + 1)],
                                        wdT[:, k, 512 * n2:512 * (n2 + 1)],
                                        start=(k == 0), stop=(k == 23))
                return inst
            r.op("tensor", f_dmm, reads=("h_sb", "wdT", "psD2"), writes=("psD2",))
            r.op("vector", lambda e: e.bn_stats(stf[:, 0:6], psD2[:, 0, :]),
                 reads=("psD2", "stf"), writes=("stf",))
            r.op("vector", lambda e: e.bn_stats(stf[:, 6:12], psD2[:, 1, :]),
                 reads=("psD2", "stf"), writes=("stf",))
            r.op("vector", lambda e: e.bn_aggr(mvf[:, :], stf[:, 0:12]),
                 reads=("stf", "mvf"), writes=("mvf",))
            r.op("scalar", lambda e: e.activation(rsf[:, 0:1], mvf[:, 1:2], AF.Ln,
                                                  bias=cst[:, 0:1]),
                 reads=("mvf", "rsf", "cst"), writes=("rsf",))
            r.op("scalar", lambda e: e.activation(rsf[:, 0:1], rsf[:, 0:1], AF.Exp,
                                                  scale=-0.5),
                 reads=("rsf",), writes=("rsf",))
            r.op("vector", lambda e: e.scalar_tensor_tensor(
                rsf[:, 1:2], mvf[:, 0:1], -1.0, rsf[:, 0:1], op0=ALU.mult, op1=ALU.mult),
                reads=("mvf", "rsf"), writes=("rsf",))
            def f_lnf(e):
                e.activation(o_sb[:, 0:512], psD2[:, 0, :], AF.Identity,
                             scale=rsf[:, 0:1], bias=rsf[:, 1:2])
                return e.activation(o_sb[:, 512:1024], psD2[:, 1, :], AF.Identity,
                                    scale=rsf[:, 0:1], bias=rsf[:, 1:2])
            r.op("scalar", f_lnf, reads=("psD2", "rsf", "o_sb"), writes=("o_sb", "psD2"))
            r.op("vector", lambda e, g=g: e.tensor_tensor(
                o_sb[:, :], o_sb[:, :], x1_2[:, g % 2, :], op=ALU.add),
                reads=("x1_2", "o_sb"), writes=("o_sb",))
            r.dma("scalar",
                  out_seg[TT * t + 128 * g:TT * t + 128 * (g + 1), :], o_sb[:, :],
                  stream="o_sb", reads=("o_sb",), writes=(f"OUT:{t}",))
    r.op("gpsimd", lambda e: None, reads=tuple(f"OUT:{t}" for t in range(NT)), writes=())
    r.build()
    esE.close()
    es_g.close()
    return nc


def kernel(L0_attn_u, L0_attn_W_in, L0_attn_conv_w, L0_attn_conv_b,
           L0_attn_dt_bias, L0_attn_A_log, L0_attn_norm_w, L0_attn_norm_b,
           L0_attn_W_out, L0_ffn_w_gate, L0_ffn_w_up, L0_ffn_w_down):
    u = np.ascontiguousarray(np.asarray(L0_attn_u, np.float32))
    consts = _np_consts()
    nc = build_nc(dbg=DBG)
    in_maps = []
    for core in range(8):
        b, q = core // 4, core % 4
        seg = np.ascontiguousarray(u[b, T * q:T * (q + 1)])
        halo = np.zeros((16, D), np.float32)
        if q > 0:
            halo[13:16] = u[b, T * q - 3:T * q]
        sel2 = np.zeros((8, 8), np.float32)
        gmask = np.zeros((8, 1), np.float32)
        for j in range(8):
            if j // 4 == b and j % 4 < q:
                gmask[j, 0] = 1.0
                for i in range(8):
                    if i // 4 == b and j % 4 < i % 4 < q:
                        sel2[i, j] = 1.0
        m = dict(u_seg=seg, u_halo=halo,
                 W_in=np.ascontiguousarray(np.asarray(L0_attn_W_in, np.float32)),
                 conv_w=np.ascontiguousarray(np.asarray(L0_attn_conv_w, np.float32)),
                 conv_b=np.ascontiguousarray(np.asarray(L0_attn_conv_b, np.float32)),
                 dt_bias=np.ascontiguousarray(np.asarray(L0_attn_dt_bias, np.float32)),
                 A_log=np.ascontiguousarray(np.asarray(L0_attn_A_log, np.float32)),
                 norm_w=np.ascontiguousarray(np.asarray(L0_attn_norm_w, np.float32)),
                 norm_b=np.ascontiguousarray(np.asarray(L0_attn_norm_b, np.float32)),
                 W_out=np.ascontiguousarray(np.asarray(L0_attn_W_out, np.float32)),
                 wg=np.ascontiguousarray(np.asarray(L0_ffn_w_gate, np.int32)),
                 wu=np.ascontiguousarray(np.asarray(L0_ffn_w_up, np.int32)),
                 wd=np.ascontiguousarray(np.asarray(L0_ffn_w_down, np.int32)),
                 SEL2=sel2, GMASK=gmask, **consts)
        in_maps.append(m)
    res = run_bass_kernel_spmd(nc, in_maps, core_ids=list(range(8)))
    out = np.empty((2, 16384, D), np.float32)
    for core in range(8):
        b, q = core // 4, core % 4
        out[b, T * q:T * (q + 1)] = res.results[core]["out_seg"]
    return out


# version 10
# speedup vs baseline: 1.0356x; 1.0094x over previous
import numpy as np
import ml_dtypes
from contextlib import ExitStack

import concourse.bass as bass
import concourse.mybir as mybir
from concourse.bass_utils import run_bass_kernel_spmd

F32 = mybir.dt.float32
BF16 = mybir.dt.bfloat16
I32 = mybir.dt.int32
AF = mybir.ActivationFunctionType
ALU = mybir.AluOpType
AX = mybir.AxisListType

T = 4096
D = 1024
DI = 2048
DS = 128
NH = 32
HP = 64
CS = 64
NCK = T // CS
NPAIR = NCK // 2
DPROJ = 4384
HID = 3072
EPS = 1e-5
BIG = 1e30
NT = 8
TT = 512

ENGINES = ("sync", "tensor", "vector", "scalar", "gpsimd")
DMA_ENGINES = ("sync", "scalar", "gpsimd")

DBG = []


class Rec:

    def __init__(self, nc, getsem, ctr, shared):
        self.nc = nc
        self.getsem = getsem
        self.ctr = ctr
        self.last_w, self.readers = shared
        self.ops = []

    def _rec(self, engine, fn, own, inc, reads, writes, forced=()):
        deps = {}
        for k in reads:
            h = self.last_w.get(k)
            if h is not None:
                deps[h] = True
        for k in writes:
            h = self.last_w.get(k)
            if h is not None:
                deps[h] = True
            for rh in self.readers.get(k, ()):
                deps[rh] = True
        for h in forced:
            if h is not None:
                deps[h] = True
        self.getsem(own)
        self.ctr[own] = self.ctr.get(own, 0) + inc
        handle = (own, self.ctr[own])
        emit_deps = [h for h in deps
                     if not (h[0] == own and h[1] >= handle[1])]
        self.ops.append((engine, fn, sorted(emit_deps), own, inc))
        for k in writes:
            self.last_w[k] = handle
            self.readers[k] = []
        for k in reads:
            self.readers.setdefault(k, []).append(handle)
        return handle

    def op(self, engine, fn, reads=(), writes=(), inc=1):
        return self._rec(engine, fn, "eng:" + engine, inc, reads, writes)

    def dma(self, engine, out, in_, stream=None, reads=(), writes=(), **kw):
        if stream is None:
            stream = (tuple(writes) + tuple(reads) + ("misc",))[0]
        xp = self.last_w.get("XPOSE")
        return self._rec(engine, lambda e: e.dma_start(out, in_, **kw),
                         "dma:" + stream, 16, reads, writes,
                         forced=[xp] if xp else ())

    def xpose(self, out, in_, reads=(), writes=()):
        forced = [(s, v) for s, v in self.ctr.items()
                  if s.startswith("dma:") and v > 0]
        return self._rec("sync", lambda e: e.dma_start_transpose(out, in_),
                         "dma:xpose", 16, reads, tuple(writes) + ("XPOSE",),
                         forced=forced)

    def build(self):
        getsem, ops = self.getsem, self.ops
        with self.nc.Block() as blk:
            for eng_name in ENGINES:
                mine = [(f, d, own, inc) for (e, f, d, own, inc) in ops
                        if e == eng_name]
                if not mine:
                    continue

                def emit(eng, mine=mine):
                    waited = {}
                    for fn, deps, own, inc in mine:
                        for (sname, pval) in deps:
                            if waited.get(sname, -1) < pval:
                                eng.wait_ge(getsem(sname), pval)
                                waited[sname] = pval
                        inst = fn(eng)
                        if inst is None:
                            inst = eng.nop()
                        if inc:
                            inst.then_inc(getsem(own), inc)
                getattr(blk, eng_name)(emit)


def _np_consts():
    i = np.arange(64)
    tril_is = (i[:, None] <= i[None, :]).astype(np.float32)
    dup = lambda a: np.concatenate([a, a], 0)
    bf = ml_dtypes.bfloat16
    return dict(
        TRILNEG=dup(-tril_is).astype(bf),
        TRILLT=dup(tril_is).astype(bf),
        MASKRX=dup(np.repeat(-BIG * (i[None, :] == i[:, None] - 1).astype(np.float32)[:, :, None], 32, axis=2).reshape(64, 2048)).astype(bf),
        ONES64=np.ones((128, 64), np.float32).astype(bf),
        TRILSL=dup((i[None, :] >= i[:, None]).astype(np.float32)),
        IDENT=np.eye(128, dtype=np.float32),
        ONESR=np.ones((1, 128), np.float32).astype(bf),
        IDENTB=np.eye(128, dtype=np.float32).astype(bf),
        CONSTS=np.array([[1e-5, 1.0, 0.0, 0.0]], np.float32),
    )


def bc(ap, shape, axis):
    return ap.unsqueeze(axis).broadcast_to(shape)


def build_nc(dbg=()):
    dbg = set(dbg)
    nc = bass.Bass("TRN2", target_bir_lowering=False, num_devices=8)

    u_seg = nc.dram_tensor("u_seg", [T, D], F32, kind="ExternalInput").ap()
    u_halo = nc.dram_tensor("u_halo", [16, D], F32, kind="ExternalInput").ap()
    W_in = nc.dram_tensor("W_in", [D, DPROJ], F32, kind="ExternalInput").ap()
    conv_w = nc.dram_tensor("conv_w", [2304, 4], F32, kind="ExternalInput").ap()
    conv_b = nc.dram_tensor("conv_b", [2304], F32, kind="ExternalInput").ap()
    dt_bias = nc.dram_tensor("dt_bias", [NH], F32, kind="ExternalInput").ap()
    A_log = nc.dram_tensor("A_log", [NH], F32, kind="ExternalInput").ap()
    norm_w = nc.dram_tensor("norm_w", [DI], F32, kind="ExternalInput").ap()
    norm_b = nc.dram_tensor("norm_b", [DI], F32, kind="ExternalInput").ap()
    W_out = nc.dram_tensor("W_out", [DI, D], F32, kind="ExternalInput").ap()
    wg = nc.dram_tensor("wg", [HID, D], I32, kind="ExternalInput").ap()
    wu = nc.dram_tensor("wu", [HID, D], I32, kind="ExternalInput").ap()
    wd = nc.dram_tensor("wd", [D, HID], I32, kind="ExternalInput").ap()
    TRILNEG = nc.dram_tensor("TRILNEG", [128, 64], BF16, kind="ExternalInput").ap()
    TRILLT = nc.dram_tensor("TRILLT", [128, 64], BF16, kind="ExternalInput").ap()
    MASKRX = nc.dram_tensor("MASKRX", [128, 2048], BF16, kind="ExternalInput").ap()
    ONES64 = nc.dram_tensor("ONES64", [128, 64], BF16, kind="ExternalInput").ap()
    TRILSL = nc.dram_tensor("TRILSL", [128, 64], F32, kind="ExternalInput").ap()
    IDENT = nc.dram_tensor("IDENT", [128, 128], F32, kind="ExternalInput").ap()
    ONESR = nc.dram_tensor("ONESR", [1, 128], BF16, kind="ExternalInput").ap()
    IDENTB = nc.dram_tensor("IDENTB", [128, 128], BF16, kind="ExternalInput").ap()
    CONSTS = nc.dram_tensor("CONSTS", [1, 4], F32, kind="ExternalInput").ap()
    SEL2 = nc.dram_tensor("SEL2", [8, 8], F32, kind="ExternalInput").ap()
    GMASK = nc.dram_tensor("GMASK", [8, 1], F32, kind="ExternalInput").ap()
    out_seg = nc.dram_tensor("out_seg", [T, D], F32, kind="ExternalOutput").ap()

    def scratch(name, shape, dtype):
        kind = "ExternalOutput" if name in dbg else "Internal"
        return nc.dram_tensor(name, shape, dtype, kind=kind).ap()

    SP_X0N = scratch("SP_X0N", [T, D], BF16)
    SP_X0NH = scratch("SP_X0NH", [16, D], BF16)
    SP_Z = scratch("SP_Z", [T, DI], BF16)
    SP_XC = scratch("SP_XC", [2176, T], BF16)
    SP_XS = scratch("SP_XS", [T, DI], BF16)
    SP_XD = scratch("SP_XD", [T, DI], BF16)
    SP_YN = scratch("SP_YN", [T, DI], BF16)
    SP_X1 = scratch("SP_X1", [T, D], F32)
    SP_X1N = scratch("SP_X1N", [T, D], BF16)
    SP_H = scratch("SP_H", [HID, T], BF16)
    SP_WGN = scratch("SP_WGN", [HID, D], BF16)
    SP_WUN = scratch("SP_WUN", [HID, D], BF16)
    SP_WDN = scratch("SP_WDN", [D, HID], BF16)
    SP_SB = scratch("SP_SB", [192, NH], F32)
    SP_NEGA = scratch("SP_NEGA", [NH, 1], F32)
    SP_W8 = scratch("SP_W8", [8, NH], F32)
    CBT = scratch("CBT", [128, 2049], F32)
    SP_DBGS = scratch("SP_DBGS", [128, 2049 + 64], F32)
    SP_DBGS2 = scratch("SP_DBGS2", [128, 2049 + 64], F32)
    GATH = nc.dram_tensor("GATH", [1024, 2049], F32, kind="Internal", addr_space="Shared").ap()

    es_g = ExitStack()
    es_mid = ExitStack()
    es_b = ExitStack()

    def mk(stack, name, shape, dtype=F32):
        return stack.enter_context(nc.sbuf_tensor(name, shape, dtype))

    trilneg = mk(es_g, "trilneg", [128, 64], BF16)
    trillt = mk(es_g, "trillt", [128, 64], BF16)
    maskrx = mk(es_g, "maskrx", [128, 2048], BF16)
    ones64 = mk(es_g, "ones64", [128, 64], BF16)
    trilsl = mk(es_g, "trilsl", [128, 64], F32)
    ident = mk(es_g, "ident", [128, 128], F32)
    onesr = mk(es_g, "onesr", [1, 128], BF16)
    identb = mk(es_g, "identb", [128, 128], BF16)
    cst = mk(es_g, "cst", [128, 4], F32)
    negA_col = mk(es_g, "negA_col", [NH, 1], F32)
    negA_rep = mk(es_g, "negA_rep", [128, NH], F32)
    dtb_rep = mk(es_g, "dtb_rep", [128, NH], F32)
    convw_sb = mk(es_g, "convw_sb", [128, 18, 4], F32)
    convb_sb = mk(es_g, "convb_sb", [128, 18], F32)

    B_FM = mk(es_mid, "B_FM", [128, T], BF16)
    C_FM = mk(es_mid, "C_FM", [128, T], BF16)
    dt_TM = mk(es_mid, "dt_TM", [128, 32, NH], F32)
    a_TM = mk(es_mid, "a_TM", [128, 32, NH], F32)
    a_TMb = mk(es_mid, "a_TMb", [128, 32, NH], BF16)
    atot_HM = mk(es_mid, "atot_HM", [NH, NCK], F32)
    cpi_HM = mk(es_mid, "cpi_HM", [NH, NCK], F32)
    rc_HM = mk(es_mid, "rc_HM", [NH, NCK], F32)
    b2_HM = mk(es_mid, "b2_HM", [NH, NCK], F32)
    dexp_HM = mk(es_mid, "dexp_HM", [NH, NCK], F32)
    aseg_col = mk(es_mid, "aseg_col", [NH, 1], F32)
    cs_TM = mk(es_mid, "cs_TM", [128, 32, NH], F32)
    dsT = mk(es_mid, "dsT", [128, 32, NH], F32)
    ds2T = mk(es_mid, "ds2T", [128, 32, NH], F32)
    sdT = mk(es_mid, "sdT", [128, 32, NH], F32)

    sem_map = {}
    sem_ctr = {}

    def getsem(name):
        if name not in sem_map:
            sem_map[name] = es_g.enter_context(
                nc.semaphore("s_" + name.replace(":", "_")))
        return sem_map[name]

    cc_sem = es_g.enter_context(nc.semaphore("sem_cc"))
    shared = ({}, {})

    def rec():
        return Rec(nc, getsem, sem_ctr, shared)

    def ln_apply(r, pre, stat_reads, stat_aps, st, mv, rs, npart=128):
        for i, ap in enumerate(stat_aps):
            r.op("vector", lambda e, i=i, ap=ap: e.bn_stats(st[:, 6 * i:6 * (i + 1)], ap),
                 reads=tuple(stat_reads) + (pre + "st",), writes=(pre + "st",))
        r.op("vector", lambda e: e.bn_aggr(mv[:, :], st[:, 0:6 * len(stat_aps)]),
             reads=(pre + "st", pre + "mv"), writes=(pre + "mv",))
        r.op("scalar", lambda e: e.activation(rs[:, 0:1], mv[:, 1:2], AF.Ln,
                                              bias=cst[0:npart, 0:1]),
             reads=(pre + "mv", pre + "rs", "cst"), writes=(pre + "rs",))
        r.op("scalar", lambda e: e.activation(rs[:, 0:1], rs[:, 0:1], AF.Exp, scale=-0.5),
             reads=(pre + "rs",), writes=(pre + "rs",))
        r.op("vector", lambda e: e.scalar_tensor_tensor(
            rs[:, 1:2], mv[:, 0:1], -1.0, rs[:, 0:1], op0=ALU.mult, op1=ALU.mult),
            reads=(pre + "mv", pre + "rs"), writes=(pre + "rs",))

    es_win = ExitStack()
    W_in_sb = es_win.enter_context(nc.sbuf_tensor("W_in_sb", [128, 8, DPROJ], BF16))
    es0 = ExitStack()
    w32 = [mk(es0, f"w32_{i}", [128, DPROJ], F32) for i in range(2)]
    w1 = mk(es0, "w1", [NH, 2], F32)

    r = rec()
    for dst, src in ((trilneg, TRILNEG), (trillt, TRILLT), (maskrx, MASKRX),
                     (ones64, ONES64), (trilsl, TRILSL), (ident, IDENT),
                     (onesr, ONESR), (identb, IDENTB)):
        r.dma("sync", dst[:, :], src, writes=(dst.name,))
    r.dma("sync", cst[:, :], bc(CONSTS.rearrange("one c -> (one c)"), [128, 4], 0), writes=("cst",))
    r.dma("sync", dtb_rep[:, :], bc(dt_bias, [128, NH], 0), writes=("dtb_rep",))
    r.dma("sync", convw_sb[:, :, :], conv_w.rearrange("(m p) k -> p m k", p=128),
          writes=("convw",))
    r.dma("sync", convb_sb[:, :], conv_b.rearrange("(m p) -> p m", p=128),
          writes=("convb",), allow_slow_non_contiguous=True)
    r.dma("sync", w1[:, 0:1], A_log.unsqueeze(1), writes=("w1",))
    r.op("scalar", lambda e: e.activation(w1[:, 1:2], w1[:, 0:1], AF.Exp),
         reads=("w1",), writes=("w1e",))
    r.op("scalar", lambda e: e.mul(negA_col[:, :], w1[:, 1:2], -1.0),
         reads=("w1e",), writes=("negA_col",))
    r.dma("sync", SP_NEGA, negA_col[:, :], reads=("negA_col",), writes=("SP_NEGA",))
    r.dma("sync", negA_rep[:, :],
          bc(SP_NEGA.rearrange("h one -> (h one)"), [128, NH], 0),
          reads=("SP_NEGA",), writes=("negA_rep",))
    for k in range(8):
        buf, key = w32[k % 2], f"w32_{k % 2}"
        r.dma("sync", buf[:, :], W_in[128 * k:128 * (k + 1), :], writes=(key,))
        r.op("scalar",
             lambda e, k=k, buf=buf: e.activation(W_in_sb[:, k, :], buf[:, :], AF.Identity),
             reads=(key,), writes=("W_in_sb",))
    esA = ExitStack()
    u_t = [mk(esA, f"u_t{i}", [128, 4, D], F32) for i in range(2)]
    x0_t = [mk(esA, f"x0_t{i}", [128, 4, D], BF16) for i in range(2)]
    st_t = mk(esA, "st_t", [128, 12], F32)
    mv_t = mk(esA, "mv_t", [128, 2], F32)
    rs_t = mk(esA, "rs_t", [128, 2], F32)

    r.dma("sync", u_t[0][0:16, 0, :], u_halo, writes=("u_t0",))
    ln_apply(r, "a", ("u_t0",), [u_t[0][0:16, 0, 0:512], u_t[0][0:16, 0, 512:1024]],
             st_t[0:16, :], mv_t[0:16, :], rs_t[0:16, :], npart=16)
    r.op("scalar", lambda e: e.activation(x0_t[0][0:16, 0, :], u_t[0][0:16, 0, :],
                                          AF.Identity, scale=rs_t[0:16, 0:1],
                                          bias=rs_t[0:16, 1:2]),
         reads=("u_t0", "ars", "x0_t0"), writes=("x0_t0",))
    r.dma("scalar", SP_X0NH, x0_t[0][0:16, 0, :], stream="x0_t0", reads=("x0_t0",), writes=("SP_X0NH",))
    for t in range(NT):
        ub, xb = u_t[t % 2], x0_t[t % 2]
        uk, xk = f"u_t{t % 2}", f"x0_t{t % 2}"
        r.dma("sync", ub[:, :, :],
              u_seg[TT * t:TT * (t + 1), :].rearrange("(g p) d -> p g d", p=128),
              reads=(uk,), writes=(uk,))
        for g in range(4):
            ln_apply(r, "a", (uk,), [ub[:, g, 0:512], ub[:, g, 512:1024]], st_t, mv_t, rs_t)
            r.op("scalar", lambda e, g=g, ub=ub, xb=xb: e.activation(
                xb[:, g, :], ub[:, g, :], AF.Identity,
                scale=rs_t[:, 0:1], bias=rs_t[:, 1:2]),
                reads=(uk, "ars", xk), writes=(xk,))
        r.dma("scalar", SP_X0N[TT * t:TT * (t + 1), :].rearrange("(g p) d -> p g d", p=128),
              xb[:, :, :], stream=xk, reads=(xk,), writes=(f"SP_X0N:{t}",))
    r.build()
    esA.close()
    es0.close()

    esB = ExitStack()
    x0F = [mk(esB, f"x0F{i}", [128, 8, TT], BF16) for i in range(2)]
    x0hF = mk(esB, "x0hF", [128, 8, 16], BF16)
    z_t = [mk(esB, f"z_t{i}", [128, 1024], BF16) for i in range(2)]
    xc = mk(esB, "xc", [128, 18, 515], BF16)
    tail_b = mk(esB, "tail_b", [128, 18, 3], BF16)
    cacc = mk(esB, "cacc", [128, 2, TT], F32)
    hm_buf = mk(esB, "hm_buf", [NH, T], F32)
    cs_HM = mk(esB, "cs_HM", [NH, T], F32)
    sp1 = mk(esB, "sp1", [128, 32, NH], F32)
    sp2 = mk(esB, "sp2", [128, 32, NH], F32)
    psZ = esB.enter_context(nc.psum_tensor("psZ", [128, 2, 1024], F32))
    psX = esB.enter_context(nc.psum_tensor("psX", [128, 2, 512], F32))
    psM = esB.enter_context(nc.psum_tensor("psM", [128, 128], F32))

    WINK = ("W_in_sb",)
    r = rec()
    r.xpose(x0hF[:, :, :], SP_X0NH, reads=("SP_X0NH",), writes=("x0hF",))
    for m in range(18):
        def f_hmm(e, m=m):
            inst = None
            for k in range(8):
                inst = e.matmul(psM[:, 0:16], W_in_sb[:, k, DI + 128 * m:DI + 128 * (m + 1)],
                                x0hF[:, k, :], start=(k == 0), stop=(k == 7))
            return inst
        r.op("tensor", f_hmm, reads=("x0hF", "psM") + WINK, writes=("psM",))
        r.op("scalar", lambda e, m=m: e.activation(tail_b[:, m, :], psM[:, 13:16], AF.Identity),
             reads=("psM", "tail"), writes=("tail",))

    for t in range(NT):
        x0b, x0k = x0F[t % 2], f"x0F{t % 2}"
        r.xpose(x0b[:, :, :], SP_X0N[TT * t:TT * (t + 1), :],
                reads=(f"SP_X0N:{t}", x0k), writes=(x0k,))
        xcb, xck = xc, "xc"
        for g in range(4):
            for n2 in range(2):
                zb, zbk = z_t[n2], f"z_t{n2}"
                def f_zmm(e, g=g, n2=n2, x0b=x0b):
                    inst = None
                    for half in range(2):
                        for k in range(8):
                            inst = e.matmul(psZ[:, n2, 512 * half:512 * (half + 1)],
                                            x0b[:, k, 128 * g:128 * (g + 1)],
                                            W_in_sb[:, k, 1024 * n2 + 512 * half:1024 * n2 + 512 * (half + 1)],
                                            start=(k == 0), stop=(k == 7))
                    return inst
                r.op("tensor", f_zmm, reads=(x0k, f"psZ{n2}") + WINK, writes=(f"psZ{n2}",))
                r.op("scalar", lambda e, n2=n2, zb=zb: e.activation(
                    zb[:, :], psZ[:, n2, :], AF.Identity),
                    reads=(f"psZ{n2}", zbk), writes=(zbk, f"psZ{n2}"))
                r.dma("scalar", SP_Z[TT * t + 128 * g:TT * t + 128 * (g + 1),
                                     1024 * n2:1024 * (n2 + 1)], zb[:, :],
                      stream=zbk, reads=(zbk,), writes=(f"SP_Z:{t // 4}",))
        for g in range(4):
            def f_dtmm(e, g=g, x0b=x0b):
                inst = None
                for k in range(8):
                    inst = e.matmul(psM[:, 0:NH], x0b[:, k, 128 * g:128 * (g + 1)],
                                    W_in_sb[:, k, DPROJ - NH:DPROJ],
                                    start=(k == 0), stop=(k == 7))
                return inst
            r.op("tensor", f_dtmm, reads=(x0k, "psM") + WINK, writes=("psM",))
            r.op("vector", lambda e, t=t, g=g: e.tensor_copy(dt_TM[:, 4 * t + g, :],
                                                             psM[:, 0:NH]),
                 reads=("psM", "dt_TM"), writes=("dt_TM",))
        for m in range(18):
            pb = m % 2
            def f_xmm(e, m=m, pb=pb, x0b=x0b):
                inst = None
                for k in range(8):
                    inst = e.matmul(psX[:, pb, :],
                                    W_in_sb[:, k, DI + 128 * m:DI + 128 * (m + 1)],
                                    x0b[:, k, :], start=(k == 0), stop=(k == 7))
                return inst
            r.op("tensor", f_xmm, reads=(x0k, f"psX{pb}") + WINK, writes=(f"psX{pb}",))
            r.op("scalar", lambda e, m=m, pb=pb, xcb=xcb: e.activation(
                xcb[:, m, 3:515], psX[:, pb, :], AF.Identity),
                reads=(f"psX{pb}", f"{xck}:{m}"), writes=(f"{xck}:{m}", f"psX{pb}"))
        r.op("vector", lambda e, xcb=xcb: e.tensor_copy(xcb[:, :, 0:3], tail_b[:, :, :]),
             reads=("tail",) + tuple(f"{xck}:{m}" for m in range(18)),
             writes=tuple(f"{xck}:{m}t" for m in range(18)))
        for m in range(18):
            ca, cak = cacc[:, m % 2, :], f"cacc{m % 2}"
            xmk = (f"{xck}:{m}", f"{xck}:{m}t")
            r.op("vector", lambda e, m=m, ca=ca, xcb=xcb: e.tensor_scalar(
                ca, xcb[:, m, 0:512], convw_sb[:, m, 0:1], convb_sb[:, m:m + 1],
                op0=ALU.mult, op1=ALU.add),
                reads=(xmk, "convw", "convb", cak), writes=(cak,))
            for i in range(1, 4):
                r.op("vector", lambda e, m=m, i=i, ca=ca, xcb=xcb: e.scalar_tensor_tensor(
                    ca, xcb[:, m, i:i + 512], convw_sb[:, m, i:i + 1], ca,
                    op0=ALU.mult, op1=ALU.add),
                    reads=(xmk, "convw", cak), writes=(cak,))
            r.op("vector", lambda e, m=m, xcb=xcb: e.tensor_copy(tail_b[:, m, :],
                                                                 xcb[:, m, 512:515]),
                 reads=xmk + ("tail",), writes=("tail",))
            r.op("scalar", lambda e, m=m, ca=ca, xcb=xcb: e.activation(
                xcb[:, m, 3:515], ca, AF.Silu),
                reads=(cak,) + xmk, writes=(f"{xck}:{m}",))
        r.op("vector", lambda e, t=t, xcb=xcb: e.tensor_copy(B_FM[:, TT * t:TT * (t + 1)],
                                                             xcb[:, 16, 3:515]),
             reads=(f"{xck}:16",), writes=("B_FM",))
        r.op("vector", lambda e, t=t, xcb=xcb: e.tensor_copy(C_FM[:, TT * t:TT * (t + 1)],
                                                             xcb[:, 17, 3:515]),
             reads=(f"{xck}:17",), writes=("C_FM",))
        r.dma("scalar",
              SP_XC[0:2048, TT * t:TT * (t + 1)].rearrange("(m p) c -> p m c", p=128),
              xcb[:, 0:16, 3:515], stream=xck,
              reads=tuple(f"{xck}:{m}" for m in range(16)), writes=(f"SP_XC:{t}",))
        r.dma("scalar", SP_XC[2048:2176, TT * t:TT * (t + 1)], xcb[:, 16, 3:515],
              stream=xck, reads=(f"{xck}:16",), writes=(f"SP_XCB:{t}",))

    dtv = dt_TM[:, :, :].rearrange("p a b -> p (a b)")
    sp1v = sp1[:, :, :].rearrange("p a b -> p (a b)")
    sp2v = sp2[:, :, :].rearrange("p a b -> p (a b)")
    r.op("vector", lambda e: e.tensor_tensor(
        dt_TM[:, :, :], dt_TM[:, :, :], bc(dtb_rep[:, :], [128, 32, NH], 1),
        op=ALU.add), reads=("dt_TM", "dtb_rep"), writes=("dt_TM",))
    r.op("scalar", lambda e: e.activation(sp1v, dtv, AF.Abs),
         reads=("dt_TM", "sp1"), writes=("sp1",))
    r.op("scalar", lambda e: e.activation(sp1v, sp1v, AF.Exp, scale=-1.0),
         reads=("sp1",), writes=("sp1",))
    r.op("scalar", lambda e: e.activation(sp1v, sp1v, AF.Ln, bias=cst[:, 1:2]),
         reads=("sp1", "cst"), writes=("sp1",))
    r.op("scalar", lambda e: e.activation(sp2v, dtv, AF.Relu),
         reads=("dt_TM", "sp2"), writes=("sp2",))
    r.op("vector", lambda e: e.tensor_tensor(dtv, sp1v, sp2v, op=ALU.add),
         reads=("sp1", "sp2", "dt_TM"), writes=("dt_TM",))
    r.op("vector", lambda e: e.tensor_tensor(
        a_TM[:, :, :], dt_TM[:, :, :],
        bc(negA_rep[:, :], [128, 32, NH], 1), op=ALU.mult),
        reads=("dt_TM", "negA_rep"), writes=("a_TM",))
    r.op("vector", lambda e: e.tensor_copy(a_TMb[:, :, :], a_TM[:, :, :]),
         reads=("a_TM",), writes=("a_TMb",))
    for p in range(32):
        r.op("tensor", lambda e, p=p: e.transpose(psM[0:32, 0:128], dt_TM[:, p, :], ident[:, :]),
             reads=("dt_TM", "ident", "psM"), writes=("psM",))
        r.op("vector", lambda e, p=p: e.tensor_copy(hm_buf[:, 128 * p:128 * (p + 1)],
                                                    psM[0:32, 0:128]),
             reads=("psM", "hm_buf"), writes=("hm_buf",))
    r.op("vector", lambda e: e.tensor_scalar(hm_buf[:, :], hm_buf[:, :], negA_col[:, 0:1],
                                             None, op0=ALU.mult, op1=ALU.bypass),
         reads=("hm_buf", "negA_col"), writes=("hm_buf",))
    def f_scans(e):
        inst = None
        for c in range(NCK):
            inst = e.tensor_tensor_scan(cs_HM[:, CS * c:CS * (c + 1)], ones64[0:NH, :],
                                        hm_buf[:, CS * c:CS * (c + 1)], initial=0.0,
                                        op0=ALU.mult, op1=ALU.add)
        return inst
    r.op("vector", f_scans, reads=("hm_buf", "ones64", "cs_HM"), writes=("cs_HM",))
    r.op("vector", lambda e: e.tensor_copy(
        atot_HM[:, :], cs_HM[:, :].rearrange("h (c s) -> h c s", s=CS)[:, :, CS - 1]),
        reads=("cs_HM",), writes=("atot_HM",))
    r.op("vector", lambda e: e.tensor_tensor_scan(cpi_HM[:, :], ones64[0:NH, :], atot_HM[:, :],
                                                  initial=0.0, op0=ALU.mult, op1=ALU.add),
         reads=("atot_HM", "ones64"), writes=("cpi_HM",))
    r.op("vector", lambda e: e.reduce_sum(aseg_col[:, :], atot_HM[:, :], axis=AX.X),
         reads=("atot_HM",), writes=("aseg_col",))
    r.op("vector", lambda e: e.tensor_scalar(rc_HM[:, :], cpi_HM[:, :], -1.0, aseg_col[:, 0:1],
                                             op0=ALU.mult, op1=ALU.add),
         reads=("cpi_HM", "aseg_col"), writes=("rc_HM",))
    r.op("vector", lambda e: e.tensor_tensor(b2_HM[:, :], atot_HM[:, :], rc_HM[:, :], op=ALU.add),
         reads=("atot_HM", "rc_HM"), writes=("b2_HM",))
    r.op("scalar", lambda e: e.activation(dexp_HM[:, :], atot_HM[:, :], AF.Exp),
         reads=("atot_HM",), writes=("dexp_HM",))
    r.dma("sync", SP_SB[0:64, :].rearrange("c h -> h c"), atot_HM[:, :],
          reads=("atot_HM",), writes=("SP_SB0",), allow_slow_non_contiguous=True)
    r.dma("sync", SP_SB[64:128, :].rearrange("c h -> h c"), b2_HM[:, :],
          reads=("b2_HM",), writes=("SP_SB1",), allow_slow_non_contiguous=True)
    r.dma("sync", SP_SB[128:192, :].rearrange("c h -> h c"), dexp_HM[:, :],
          reads=("dexp_HM",), writes=("SP_SB2",), allow_slow_non_contiguous=True)
    for p in range(32):
        r.op("tensor", lambda e, p=p: e.transpose(psM[:, 0:32], cs_HM[:, 128 * p:128 * (p + 1)],
                                                  ident[0:32, 0:32]),
             reads=("cs_HM", "ident", "psM"), writes=("psM",))
        r.op("vector", lambda e, p=p: e.tensor_copy(cs_TM[:, p, :], psM[:, 0:32]),
             reads=("psM", "cs_TM"), writes=("cs_TM",))
    csv = cs_TM[:, :, :].rearrange("p a b -> p (a b)")
    for cpar in range(2):
        r.dma("sync", sp1[64 * cpar:64 * (cpar + 1), :, :],
              bass.AP(SP_SB.tensor, NH * cpar, [[0, 64], [2 * NH, 32], [1, NH]]),
              reads=("SP_SB0", "sp1"), writes=("sp1",))
        r.dma("sync", sp2[64 * cpar:64 * (cpar + 1), :, :],
              bass.AP(SP_SB.tensor, 64 * NH + NH * cpar, [[0, 64], [2 * NH, 32], [1, NH]]),
              reads=("SP_SB1", "sp2"), writes=("sp2",))
    r.op("vector", lambda e: e.tensor_tensor(sp1v, sp1v, csv, op=ALU.subtract),
         reads=("sp1", "cs_TM"), writes=("sp1",))
    r.op("vector", lambda e: e.tensor_tensor(sp2v, sp2v, csv, op=ALU.subtract),
         reads=("sp2", "cs_TM"), writes=("sp2",))
    def f_exps(e):
        e.activation(dsT[:, :, :].rearrange("p a b -> p (a b)"), sp1v, AF.Exp)
        e.activation(ds2T[:, :, :].rearrange("p a b -> p (a b)"), sp2v, AF.Exp)
        return e.activation(sdT[:, :, :].rearrange("p a b -> p (a b)"), csv, AF.Exp)
    r.op("scalar", f_exps, reads=("sp1", "sp2", "cs_TM"), writes=("dsT", "ds2T", "sdT"))
    r.build()
    esB.close()
    es_win.close()

    B_TM = mk(es_b, "B_TM", [128, 32, 128], BF16)
    D_rep = mk(es_b, "D_rep", [128, NCK, NH], F32)
    S_sb = mk(es_b, "S_sb", [128, DI], BF16)
    battn = mk(es_b, "battn", [1, D], BF16)
    W_out_sb = mk(es_b, "W_out_sb", [128, 16, D], BF16)

    esC = ExitStack()
    xcT = mk(esC, "xcT", [128, 8, 2176], BF16)
    xs_b = mk(esC, "xs_b", [128, DI], BF16)
    xd_b = mk(esC, "xd_b", [128, DI], BF16)
    xd2_b = mk(esC, "xd2_b", [128, DI], BF16)
    cb_f = mk(esC, "cb_f", [128, DI], F32)
    gth = mk(esC, "gth", [128, DI], F32)
    acc_f = mk(esC, "acc_f", [128, DI], F32)
    tmp_f = mk(esC, "tmp_f", [128, DI], F32)
    a8 = mk(esC, "a8", [8, NH + 8], F32)
    gm_col = mk(esC, "gm_col", [8, 1], F32)
    wrep = mk(esC, "wrep", [128, 8, NH], F32)
    smB = mk(esC, "smB", [8, NH], F32)
    smC = mk(esC, "smC", [8, NH], F32)
    wo32 = [mk(esC, "wo32_0", [128, D], F32)] * 2
    wo16 = mk(esC, "wo16", [128, D], BF16)
    nwb_col = mk(esC, "nwb_col", [128, 32], F32)
    nb_colb = mk(esC, "nb_colb", [128, 16], BF16)
    wi32b = [mk(esC, "wi32b0", [128, 2, D], I32)] * 2
    wibfb = [mk(esC, "wibfb0", [128, 2, D], BF16)] * 2
    psCB = esC.enter_context(nc.psum_tensor("psCB", [128, DI], F32))
    psE = esC.enter_context(nc.psum_tensor("psE", [8, NH], F32))
    psB2 = esC.enter_context(nc.psum_tensor("psB2", [1, D], F32))

    r = rec()
    r.dma("sync", nwb_col[:, 0:16], norm_w.rearrange("(k p) -> p k", p=128),
          writes=("nw_col",), allow_slow_non_contiguous=True)
    r.dma("sync", nwb_col[:, 16:32], norm_b.rearrange("(k p) -> p k", p=128),
          writes=("nb_col32",), allow_slow_non_contiguous=True)
    r.op("scalar", lambda e: e.activation(nb_colb[:, :], nwb_col[:, 16:32], AF.Identity),
         reads=("nb_col32",), writes=("nb_colb",))
    for k in range(16):
        buf, key = wo32[0], "wo32_0"
        r.dma("sync", buf[:, :], W_out[128 * k:128 * (k + 1), :], reads=(key,), writes=(key,))
        r.op("scalar", lambda e, buf=buf: e.activation(wo16[:, :], buf[:, :], AF.Identity),
             reads=(key, "wo16"), writes=("wo16",))
        def f_bmm(e, k=k):
            e.matmul(psB2[:, 0:512], nb_colb[:, k:k + 1], wo16[:, 0:512],
                     start=(k == 0), stop=(k == 15))
            return e.matmul(psB2[:, 512:1024], nb_colb[:, k:k + 1], wo16[:, 512:1024],
                            start=(k == 0), stop=(k == 15))
        r.op("tensor", f_bmm, reads=("wo16", "nb_colb", "psB2"), writes=("psB2",))
        r.op("scalar", lambda e, k=k, buf=buf: e.activation(
            W_out_sb[:, k, :], buf[:, :], AF.Identity, scale=nwb_col[:, k:k + 1]),
            reads=(key, "nw_col", "W_out_sb"), writes=("W_out_sb",))
    r.op("scalar", lambda e: e.activation(battn[:, :], psB2[:, :], AF.Identity),
         reads=("psB2",), writes=("battn",))
    rnd_i = [0]
    def bounce(wsrc, wdst, wname, r0, rows, wkey):
        i = rnd_i[0] % 2
        rnd_i[0] += 1
        m = rows // 128
        r.dma("scalar", wi32b[i][:, 0:m, :],
              wsrc[r0:r0 + rows, :].rearrange("(m p) c -> p m c", p=128),
              stream="wi32b0", reads=("wi32b0",), writes=("wi32b0",))
        r.op("scalar", lambda e, i=i, m=m: e.activation(
            wibfb[i][:, 0:m, :], wi32b[i][:, 0:m, :], AF.Identity),
            reads=("wi32b0", "wibfb0"), writes=("wibfb0",))
        r.dma("scalar", wdst[r0:r0 + rows, :].rearrange("(m p) c -> p m c", p=128),
              wibfb[i][:, 0:m, :], stream="wibfb0", reads=("wibfb0",),
              writes=(wkey,))
    for wsrc, wdst, wname in ((wg, SP_WGN, "SP_WGN"), (wu, SP_WUN, "SP_WUN")):
        for r0 in range(0, HID, 256):
            bounce(wsrc, wdst, wname, r0, 256, f"{wname}:{r0 // 1024 * 1024}")
    def bounce_wd(r0, co):
        i = rnd_i[0] % 2
        rnd_i[0] += 1
        fv32 = wi32b[i][:, :, :].rearrange("p a b -> p (a b)")
        fvbf = wibfb[i][:, :, :].rearrange("p a b -> p (a b)")
        r.dma("scalar", fv32[:, 0:1536], wd[r0:r0 + 128, co:co + 1536],
              stream="wi32b0", reads=("wi32b0",), writes=("wi32b0",))
        r.op("scalar", lambda e, i=i, fv32=fv32, fvbf=fvbf: e.activation(
            fvbf[:, 0:1536], fv32[:, 0:1536], AF.Identity),
            reads=("wi32b0", "wibfb0"), writes=("wibfb0",))
        r.dma("scalar", SP_WDN[r0:r0 + 128, co:co + 1536], fvbf[:, 0:1536],
              stream="wibfb0", reads=("wibfb0",),
              writes=(f"SP_WDN:{r0 // 256 * 256}",))
    for r0 in range(0, D, 128):
        for co in (0, 1536):
            bounce_wd(r0, co)
    r.dma("sync", D_rep[:, :, :],
          bass.AP(SP_SB.tensor, 128 * NH, [[0, 128], [NH, NCK], [1, NH]]),
          reads=("SP_SB2",), writes=("D_rep",))
    first = [True]
    for quarter in range(4):
        r.xpose(xcT[:, :, :], SP_XC[:, 1024 * quarter:1024 * (quarter + 1)],
                reads=(f"SP_XC:{2 * quarter}", f"SP_XC:{2 * quarter + 1}",
                       f"SP_XCB:{2 * quarter}", f"SP_XCB:{2 * quarter + 1}", "xcT"),
                writes=("xcT",))
        for m in range(8):
            pp = 8 * quarter + m
            r.op("vector", lambda e, m=m, pp=pp: e.tensor_tensor(
                xs_b[:, :].rearrange("p (h q) -> p h q", q=HP),
                xcT[:, m, 0:DI].rearrange("p (h q) -> p h q", q=HP),
                bc(dt_TM[:, pp, :], [128, NH, HP], 2), op=ALU.mult),
                reads=("xcT", "dt_TM", "xs_b"), writes=("xs_b",))
            r.op("gpsimd", lambda e, pp=pp: e.tensor_tensor(
                xd_b[:, :].rearrange("p (h q) -> p h q", q=HP),
                xs_b[:, :].rearrange("p (h q) -> p h q", q=HP),
                bc(dsT[:, pp, :], [128, NH, HP], 2), op=ALU.mult),
                reads=("xs_b", "dsT", "xd_b"), writes=("xd_b",))
            r.op("gpsimd", lambda e, pp=pp: e.tensor_tensor(
                xd2_b[:, :].rearrange("p (h q) -> p h q", q=HP),
                xs_b[:, :].rearrange("p (h q) -> p h q", q=HP),
                bc(ds2T[:, pp, :], [128, NH, HP], 2), op=ALU.mult),
                reads=("xs_b", "ds2T", "xd2_b"), writes=("xd2_b",))
            r.op("vector", lambda e, m=m, pp=pp: e.tensor_copy(B_TM[:, pp, :],
                                                               xcT[:, m, 2048:2176]),
                 reads=("xcT", "B_TM"), writes=("B_TM",))
            r.dma("scalar", SP_XS[128 * pp:128 * (pp + 1), :], xs_b[:, :],
                  stream="xs_b", reads=("xs_b",), writes=(f"SP_XS:{pp // 4}",))
            r.dma("scalar", SP_XD[128 * pp:128 * (pp + 1), :], xd_b[:, :],
                  stream="xd_b", reads=("xd_b",), writes=(f"SP_XD:{pp // 4}",))
            def f_st2(e, m=m, first=first, pp=pp):
                inst = None
                for cc2 in range(2):
                    sl = slice(64 * cc2, 64 * (cc2 + 1))
                    for h in range(NH):
                        inst = e.matmul(psCB[:, HP * h:HP * (h + 1)], xcT[sl, m, 2048:2176],
                                        xd2_b[sl, HP * h:HP * (h + 1)], start=(first[0] and h % 8 == 0),
                                        stop=(pp == NPAIR - 1 and cc2 == 1 and h == NH - 1))
                        if h == NH - 1:
                            first[0] = False
                return inst
            r.op("tensor", f_st2, reads=("xcT", "xd2_b", "psCB"), writes=("psCB",))
    r.op("vector", lambda e: e.tensor_copy(cb_f[:, :], psCB[:, :]),
         reads=("psCB",), writes=("cb_f",))
    r.dma("sync", CBT[:, 0:DI], cb_f[:, :], reads=("cb_f",), writes=("CBT",))
    r.dma("sync", CBT[0:NH, DI:DI + 1], aseg_col[:, :], reads=("aseg_col",), writes=("CBT2",), allow_slow_non_contiguous=True)
    r.build()
    nc.all_core_barrier()
    r = rec()
    r.op("gpsimd", lambda e: e.collective_compute(
        "AllGather", ALU.bypass, replica_groups=[list(range(8))], ins=[CBT], outs=[GATH])
        .then_inc(cc_sem, 1) and None,
        reads=("CBT", "CBT2"), writes=(), inc=0)
    r.op("gpsimd", lambda e: e.wait_ge(cc_sem, 1) or e.nop(), reads=(), writes=("GATH",))
    r.build()
    nc.all_core_barrier()
    r = rec()
    r.dma("sync", a8[:, NH:NH + 8], SEL2, writes=("a8sel",))
    r.dma("sync", a8[0:8, 0:NH], GATH[:, DI].rearrange("(j p) -> j p", p=128)[:, 0:NH],
          reads=("GATH",), writes=("a8",), allow_slow_non_contiguous=True)
    r.dma("sync", gm_col[:, :], GMASK, writes=("gm_col",))
    r.op("tensor", lambda e: e.matmul(psE[:, :], a8[:, NH:NH + 8], a8[0:8, 0:NH],
                                      start=True, stop=True),
         reads=("a8sel", "a8", "psE"), writes=("psE",))
    r.op("scalar", lambda e: e.activation(smB[:, :], psE[:, :], AF.Exp),
         reads=("psE",), writes=("smB",))
    r.op("vector", lambda e: e.tensor_scalar(smC[:, :], smB[:, :], gm_col[:, 0:1], None,
                                             op0=ALU.mult, op1=ALU.bypass),
         reads=("smB", "gm_col"), writes=("smC",))
    r.dma("sync", SP_W8, smC[:, :], reads=("smC",), writes=("SP_W8",))
    r.dma("sync", wrep[:, :, :], bass.AP(SP_W8.tensor, 0, [[0, 128], [NH, 8], [1, NH]]),
          reads=("SP_W8",), writes=("wrep",))
    for j in range(8):
        r.dma("sync", gth[:, :], GATH[128 * j:128 * (j + 1), 0:DI],
              reads=("GATH", "gth"), writes=("gth",))
        if j == 0:
            r.op("vector", lambda e: e.tensor_tensor(
                acc_f[:, :].rearrange("p (h q) -> p h q", q=HP),
                gth[:, :].rearrange("p (h q) -> p h q", q=HP),
                bc(wrep[:, 0, :], [128, NH, HP], 2), op=ALU.mult),
                reads=("gth", "wrep", "acc_f"), writes=("acc_f",))
        else:
            r.op("vector", lambda e, j=j: e.tensor_tensor(
                tmp_f[:, :].rearrange("p (h q) -> p h q", q=HP),
                gth[:, :].rearrange("p (h q) -> p h q", q=HP),
                bc(wrep[:, j, :], [128, NH, HP], 2), op=ALU.mult),
                reads=("gth", "wrep", "tmp_f"), writes=("tmp_f",))
            r.op("vector", lambda e: e.tensor_tensor(acc_f[:, :], acc_f[:, :], tmp_f[:, :],
                                                     op=ALU.add),
                 reads=("tmp_f", "acc_f"), writes=("acc_f",))
    r.op("vector", lambda e: e.tensor_copy(S_sb[:, :], acc_f[:, :]),
         reads=("acc_f", "S_sb"), writes=("S_sb",))
    if "SP_DBGS2" in dbg:
        r.dma("sync", SP_DBGS2[:, 0:DI], cb_f[:, :], stream="cb_f", reads=("cb_f",), writes=("dbgs9",))
        r.dma("sync", SP_DBGS2[0:128, DI:DI + NH].rearrange("p c -> p c"), wrep[:, 0, :],
              reads=("wrep",), writes=("dbgs10",))
    if "SP_DBGS" in dbg:
        r.dma("sync", SP_DBGS[:, 0:DI], acc_f[:, :], stream="acc_f", reads=("acc_f",), writes=("dbgs",))
        r.dma("sync", SP_DBGS[0:8, DI:DI + NH], smC[:, :], reads=("smC",), writes=("dbgs2",))
        r.dma("sync", SP_DBGS[8:16, DI:DI + NH], a8[:, 0:NH], reads=("a8",), writes=("dbgs3",))
        r.dma("sync", SP_DBGS[16:24, DI:DI + NH], smB[:, :], reads=("smB",), writes=("dbgs4",))
    r.build()
    esC.close()

    esD = ExitStack()
    xs_p = [mk(esD, f"xs_p{i}", [128, DI], BF16) for i in range(2)]
    xd_p = [mk(esD, f"xd_p{i}", [128, DI], BF16) for i in range(2)]
    z_p = [mk(esD, f"z_p{i}", [128, DI], BF16) for i in range(2)]
    AM_sb = mk(esD, "AM_sb", [128, DI], BF16)
    M0_sb = mk(esD, "M0_sb", [128, DI], BF16)
    M_sb = mk(esD, "M_sb", [128, DI], BF16)
    Gt_sb = mk(esD, "Gt_sb", [128, 64], BF16)
    y_sb = mk(esD, "y_sb", [128, DI], F32)
    zs_sb = mk(esD, "zs_sb", [128, DI], BF16)
    yn_sb = mk(esD, "yn_sb", [128, DI], BF16)
    st2 = mk(esD, "st2", [128, 24], F32)
    mv2 = mk(esD, "mv2", [128, 2], F32)
    rs2 = mk(esD, "rs2", [128, 2], F32)
    ynF = mk(esD, "ynF", [128, 16, TT], BF16)
    at_f = mk(esD, "at_f", [128, D], F32)
    u2_t = [mk(esD, f"u2_t{i}", [128, D], F32) for i in range(2)]
    x1_t = mk(esD, "x1_t", [128, D], F32)
    x1n_t = mk(esD, "x1n_t", [128, D], BF16)
    psT = esD.enter_context(nc.psum_tensor("psT", [128, DI], F32))
    psO = esD.enter_context(nc.psum_tensor("psO", [128, DI], F32))

    r = rec()
    for pp in range(NPAIR):
        bi = pp % 2
        xsb, xdb, zb = xs_p[bi], xd_p[bi], z_p[bi]
        xsk, xdk, zk = f"xs_p{bi}", f"xd_p{bi}", f"z_p{bi}"
        r.dma("sync", xsb[:, :], SP_XS[128 * pp:128 * (pp + 1), :],
              reads=(f"SP_XS:{pp // 4}", xsk), writes=(xsk,))
        r.dma("sync", xdb[:, :], SP_XD[128 * pp:128 * (pp + 1), :],
              reads=(f"SP_XD:{pp // 4}", xdk), writes=(xdk,))
        r.dma("sync", zb[:, :], SP_Z[128 * pp:128 * (pp + 1), :],
              reads=(f"SP_Z:{pp // 4}", zk), writes=(zk,))
        for cc in range(2):
            c = 2 * pp + cc
            sl = slice(64 * cc, 64 * (cc + 1))
            amk = f"AM{cc}"
            r.op("gpsimd", lambda e, pp=pp, sl=sl: e.tensor_tensor(
                AM_sb[sl, :].rearrange("p (l h) -> p l h", h=NH),
                bc(a_TM[sl, pp, :], [64, 64, NH], 1),
                bc(trillt[sl, :], [64, 64, NH], 2), op=ALU.mult),
                reads=("a_TM", "trillt", amk), writes=(amk,))
            def f_tmm(e, pp=pp, sl=sl):
                inst = None
                for nb4 in range(4):
                    ls = slice(16 * nb4, 16 * (nb4 + 1))
                    ps = psT[sl, 512 * nb4:512 * (nb4 + 1)].rearrange("p (l h) -> p l h", h=NH)
                    e.matmul(ps, trilneg[sl, :],
                             bc(a_TMb[sl, pp, :], [64, 64, NH], 1)[:, ls, :],
                             start=True, stop=False)
                    e.matmul(ps, ones64[sl, :],
                             AM_sb[sl, :].rearrange("p (l h) -> p l h", h=NH)[:, ls, :],
                             start=False, stop=False)
                    inst = e.matmul(ps, trillt[sl, :],
                                    maskrx[sl, :].rearrange("p (l h) -> p l h", h=NH)[:, ls, :],
                                    start=False, stop=True)
                return inst
            r.op("tensor", f_tmm, reads=("a_TMb", "trilneg", "ones64", amk, "maskrx", f"psT{cc}"),
                 writes=(f"psT{cc}",))
            r.op("scalar", lambda e, sl=sl: e.activation(M0_sb[sl, :], psT[sl, :], AF.Exp),
                 reads=(f"psT{cc}", f"M0{cc}"), writes=(f"M0{cc}", f"psTr{cc}"))
            r.op("tensor", lambda e, c=c, sl=sl: e.matmul(
                psT[sl, 0:64], B_FM[:, 64 * c:64 * (c + 1)], C_FM[:, 64 * c:64 * (c + 1)],
                start=True, stop=True),
                reads=("B_FM", "C_FM", f"psTr{cc}"), writes=(f"psT{cc}",))
            r.op("vector", lambda e, sl=sl, cc=cc: e.tensor_tensor(
                Gt_sb[sl, :], psT[sl, 0:64], trilsl[sl, :], op=ALU.mult),
                reads=(f"psT{cc}", "trilsl", f"Gt{cc}"), writes=(f"Gt{cc}", f"psTr{cc}"))
            r.op("gpsimd", lambda e, sl=sl, cc=cc: e.tensor_tensor(
                M_sb[sl, :].rearrange("p (l h) -> p l h", h=NH),
                M0_sb[sl, :].rearrange("p (l h) -> p l h", h=NH),
                bc(Gt_sb[sl, :], [64, 64, NH], 2), op=ALU.mult),
                reads=(f"M0{cc}", f"Gt{cc}", f"M{cc}"), writes=(f"M{cc}",))
            def f_yoff(e, c=c, sl=sl):
                inst = None
                for h in range(NH):
                    inst = e.matmul(psO[sl, HP * h:HP * (h + 1)], C_FM[:, 64 * c:64 * (c + 1)],
                                    S_sb[:, HP * h:HP * (h + 1)], start=True, stop=True)
                return inst
            r.op("tensor", f_yoff, reads=("S_sb", "C_FM", f"psO{cc}"), writes=(f"psO{cc}",))
            r.op("vector", lambda e, sl=sl, pp=pp: e.tensor_tensor(
                y_sb[sl, :].rearrange("p (h q) -> p h q", q=HP),
                psO[sl, :].rearrange("p (h q) -> p h q", q=HP),
                bc(sdT[sl, pp, :], [64, NH, HP], 2), op=ALU.mult),
                reads=(f"psO{cc}", "sdT", f"y{cc}", "psOF"), writes=(f"y{cc}", f"psO{cc}"))
            def f_st(e, pp=pp, sl=sl, bi=bi):
                inst = None
                for h in range(NH):
                    inst = e.matmul(psO[:, HP * h:HP * (h + 1)], B_TM[sl, pp, :],
                                    xd_p[bi][sl, HP * h:HP * (h + 1)], start=True, stop=True)
                return inst
            r.op("tensor", f_st, reads=("B_TM", xdk, "psOF"), writes=("psOF", "psO0", "psO1"))
            r.op("gpsimd", lambda e, c=c: e.tensor_tensor(
                S_sb[:, :].rearrange("p (h q) -> p h q", q=HP),
                S_sb[:, :].rearrange("p (h q) -> p h q", q=HP),
                bc(D_rep[:, c, :], [128, NH, HP], 2), op=ALU.mult),
                reads=("S_sb", "D_rep"), writes=("S_sb",))
            r.op("vector", lambda e: e.tensor_tensor(S_sb[:, :], S_sb[:, :], psO[:, :],
                                                     op=ALU.add),
                 reads=("S_sb", "psOF"), writes=("S_sb", "psOF"))
            def f_yd(e, sl=sl, bi=bi):
                inst = None
                for h in range(NH):
                    inst = e.matmul(psT[sl, HP * h:HP * (h + 1)],
                                    M_sb[sl, :].rearrange("p (l h) -> p l h", h=NH)[:, :, h],
                                    xs_p[bi][sl, HP * h:HP * (h + 1)], start=True, stop=True)
                return inst
            r.op("tensor", f_yd, reads=(f"M{cc}", xsk, f"psTr{cc}"), writes=(f"psT{cc}",))
            r.op("vector", lambda e, sl=sl, cc=cc: e.tensor_tensor(
                y_sb[sl, :], y_sb[sl, :], psT[sl, :], op=ALU.add),
                reads=(f"psT{cc}", f"y{cc}"), writes=(f"y{cc}", f"psTr{cc}"))
        r.op("scalar", lambda e, zb=zb: e.activation(zs_sb[:, :], zb[:, :], AF.Silu),
             reads=(zk, "zs"), writes=("zs",))
        r.op("gpsimd", lambda e: e.tensor_tensor(y_sb[:, :], y_sb[:, :], zs_sb[:, :],
                                                 op=ALU.mult),
             reads=("y0", "y1", "zs"), writes=("y0", "y1"))
        ln_apply(r, "ln", ("y0", "y1"), [y_sb[:, 512 * i:512 * (i + 1)] for i in range(4)],
                 st2, mv2, rs2)
        r.op("scalar", lambda e: e.activation(yn_sb[:, :], y_sb[:, :], AF.Identity,
                                              scale=rs2[:, 0:1], bias=rs2[:, 1:2]),
             reads=("y0", "y1", "lnrs", "yn"), writes=("yn",))
        r.dma("scalar", SP_YN[128 * pp:128 * (pp + 1), :], yn_sb[:, :],
              stream="yn", reads=("yn",), writes=(f"SP_YN:{pp // 4}",))
        if pp % 4 == 3:
            t = pp // 4
            r.xpose(ynF[:, :, :], SP_YN[TT * t:TT * (t + 1), :],
                    reads=(f"SP_YN:{t}", "ynF"), writes=("ynF",))
            for g in range(4):
                ub, ubk = u2_t[g % 2], f"u2_t{g % 2}"
                r.dma("sync", ub[:, :],
                      u_seg[TT * t + 128 * g:TT * t + 128 * (g + 1), :],
                      reads=(ubk,), writes=(ubk,))
                def f_op(e, g=g):
                    inst = None
                    for n2 in range(2):
                        for k in range(16):
                            e.matmul(psO[:, 512 * n2:512 * (n2 + 1)],
                                     ynF[:, k, 128 * g:128 * (g + 1)],
                                     W_out_sb[:, k, 512 * n2:512 * (n2 + 1)],
                                     start=(k == 0), stop=False)
                        inst = e.matmul(psO[:, 512 * n2:512 * (n2 + 1)], onesr[:, :],
                                        battn[:, 512 * n2:512 * (n2 + 1)],
                                        start=False, stop=True)
                    return inst
                r.op("tensor", f_op,
                     reads=("ynF", "battn", "onesr", "psOF", "W_out_sb"),
                     writes=("psOF", "psO0", "psO1"))
                ln_apply(r, "ln", ("psOF",), [psO[:, 0:512], psO[:, 512:1024]], st2, mv2, rs2)
                def f_atn(e):
                    e.activation(at_f[:, 0:512], psO[:, 0:512], AF.Identity,
                                 scale=rs2[:, 0:1], bias=rs2[:, 1:2])
                    return e.activation(at_f[:, 512:1024], psO[:, 512:1024], AF.Identity,
                                        scale=rs2[:, 0:1], bias=rs2[:, 1:2])
                r.op("scalar", f_atn, reads=("psOF", "lnrs", "at_f"), writes=("at_f", "psOF"))
                r.op("vector", lambda e, g=g, ub=ub: e.tensor_tensor(
                    x1_t[:, :], at_f[:, :], ub[:, :], op=ALU.add),
                    reads=("at_f", ubk, "x1_t"), writes=("x1_t",))
                ln_apply(r, "ln", ("x1_t",), [x1_t[:, 0:512], x1_t[:, 512:1024]],
                         st2, mv2, rs2)
                r.op("scalar", lambda e, g=g: e.activation(
                    x1n_t[:, :], x1_t[:, :], AF.Identity,
                    scale=rs2[:, 0:1], bias=rs2[:, 1:2]),
                    reads=("x1_t", "lnrs", "x1n_t"), writes=("x1n_t",))
                r.dma("scalar", SP_X1[TT * t + 128 * g:TT * t + 128 * (g + 1), :],
                      x1_t[:, :], stream="x1_t", reads=("x1_t",), writes=(f"SP_X1:{t}",))
                r.dma("scalar", SP_X1N[TT * t + 128 * g:TT * t + 128 * (g + 1), :],
                      x1n_t[:, :], stream="x1n_t", reads=("x1n_t",), writes=(f"SP_X1N:{t}",))
    r.build()
    esD.close()
    es_b.close()
    es_mid.close()

    esE = ExitStack()
    wgT = mk(esE, "wgT", [128, 8, HID], BF16)
    wuT = mk(esE, "wuT", [128, 8, HID], BF16)
    wdT = mk(esE, "wdT", [128, 24, D], BF16)
    x1F = [mk(esE, f"x1F{i}", [128, 8, TT], BF16) for i in range(2)]
    gs_sb = mk(esE, "gs_sb", [128, 512], BF16)
    h_sb = mk(esE, "h_sb", [128, 24, TT], BF16)
    x1_2 = mk(esE, "x1_2", [128, 2, D], F32)
    o_sb = mk(esE, "o_sb", [128, D], F32)
    stf = mk(esE, "stf", [128, 12], F32)
    mvf = mk(esE, "mvf", [128, 2], F32)
    rsf = mk(esE, "rsf", [128, 2], F32)
    psG2 = esE.enter_context(nc.psum_tensor("psG2", [128, 2, 512], F32))
    psU2 = esE.enter_context(nc.psum_tensor("psU2", [128, 2, 512], F32))
    psD2 = esE.enter_context(nc.psum_tensor("psD2", [128, 2, 512], F32))

    r = rec()
    r.xpose(wgT[:, :, :], SP_WGN,
            reads=tuple(f"SP_WGN:{r0}" for r0 in range(0, HID, 1024)), writes=("wgT",))
    r.xpose(wuT[:, :, :], SP_WUN,
            reads=tuple(f"SP_WUN:{r0}" for r0 in range(0, HID, 1024)), writes=("wuT",))
    r.xpose(wdT[:, :, :], SP_WDN,
            reads=tuple(f"SP_WDN:{r0}" for r0 in range(0, D, 256)), writes=("wdT",))
    for t in range(NT):
        xb, xk = x1F[t % 2], f"x1F{t % 2}"
        r.xpose(xb[:, :, :], SP_X1N[TT * t:TT * (t + 1), :],
                reads=(f"SP_X1N:{t}", xk), writes=(xk,))
        for m in range(24):
            pb = m % 2
            def f_gmm(e, m=m, pb=pb, xb=xb):
                inst = None
                for k in range(8):
                    inst = e.matmul(psG2[:, pb, :], wgT[:, k, 128 * m:128 * (m + 1)],
                                    xb[:, k, :], start=(k == 0), stop=(k == 7))
                return inst
            r.op("tensor", f_gmm, reads=(xk, "wgT", f"psG2{pb}"), writes=(f"psG2{pb}",))
            def f_umm(e, m=m, pb=pb, xb=xb):
                inst = None
                for k in range(8):
                    inst = e.matmul(psU2[:, pb, :], wuT[:, k, 128 * m:128 * (m + 1)],
                                    xb[:, k, :], start=(k == 0), stop=(k == 7))
                return inst
            r.op("tensor", f_umm, reads=(xk, "wuT", f"psU2{pb}"), writes=(f"psU2{pb}",))
            r.op("scalar", lambda e, pb=pb: e.activation(gs_sb[:, :], psG2[:, pb, :],
                                                         AF.Silu, scale=1.0 / 127.0),
                 reads=(f"psG2{pb}", "gs"), writes=("gs", f"psG2{pb}"))
            r.op("vector", lambda e, m=m, pb=pb: e.tensor_tensor(
                h_sb[:, m, :], gs_sb[:, :], psU2[:, pb, :], op=ALU.mult),
                reads=("gs", f"psU2{pb}", "h_sb"), writes=("h_sb", f"psU2{pb}"))
        for g in range(4):
            if g % 2 == 0:
                r.dma("sync", x1_2[:, :, :],
                      SP_X1[TT * t + 128 * g:TT * t + 128 * (g + 2), :]
                      .rearrange("(g2 p) d -> p g2 d", p=128),
                      stream="x1_2", reads=("x1_2",), writes=("x1_2",))
            def f_dmm(e, g=g):
                inst = None
                for n2 in range(2):
                    for k in range(24):
                        inst = e.matmul(psD2[:, n2, :], h_sb[:, k, 128 * g:128 * (g + 1)],
                                        wdT[:, k, 512 * n2:512 * (n2 + 1)],
                                        start=(k == 0), stop=(k == 23))
                return inst
            r.op("tensor", f_dmm, reads=("h_sb", "wdT", "psD2"), writes=("psD2",))
            r.op("vector", lambda e: e.bn_stats(stf[:, 0:6], psD2[:, 0, :]),
                 reads=("psD2", "stf"), writes=("stf",))
            r.op("vector", lambda e: e.bn_stats(stf[:, 6:12], psD2[:, 1, :]),
                 reads=("psD2", "stf"), writes=("stf",))
            r.op("vector", lambda e: e.bn_aggr(mvf[:, :], stf[:, 0:12]),
                 reads=("stf", "mvf"), writes=("mvf",))
            r.op("scalar", lambda e: e.activation(rsf[:, 0:1], mvf[:, 1:2], AF.Ln,
                                                  bias=cst[:, 0:1]),
                 reads=("mvf", "rsf", "cst"), writes=("rsf",))
            r.op("scalar", lambda e: e.activation(rsf[:, 0:1], rsf[:, 0:1], AF.Exp,
                                                  scale=-0.5),
                 reads=("rsf",), writes=("rsf",))
            r.op("vector", lambda e: e.scalar_tensor_tensor(
                rsf[:, 1:2], mvf[:, 0:1], -1.0, rsf[:, 0:1], op0=ALU.mult, op1=ALU.mult),
                reads=("mvf", "rsf"), writes=("rsf",))
            def f_lnf(e):
                e.activation(o_sb[:, 0:512], psD2[:, 0, :], AF.Identity,
                             scale=rsf[:, 0:1], bias=rsf[:, 1:2])
                return e.activation(o_sb[:, 512:1024], psD2[:, 1, :], AF.Identity,
                                    scale=rsf[:, 0:1], bias=rsf[:, 1:2])
            r.op("scalar", f_lnf, reads=("psD2", "rsf", "o_sb"), writes=("o_sb", "psD2"))
            r.op("vector", lambda e, g=g: e.tensor_tensor(
                o_sb[:, :], o_sb[:, :], x1_2[:, g % 2, :], op=ALU.add),
                reads=("x1_2", "o_sb"), writes=("o_sb",))
            r.dma("scalar",
                  out_seg[TT * t + 128 * g:TT * t + 128 * (g + 1), :], o_sb[:, :],
                  stream="o_sb", reads=("o_sb",), writes=(f"OUT:{t}",))
    r.op("gpsimd", lambda e: None, reads=tuple(f"OUT:{t}" for t in range(NT)), writes=())
    r.build()
    esE.close()
    es_g.close()
    return nc


def kernel(L0_attn_u, L0_attn_W_in, L0_attn_conv_w, L0_attn_conv_b,
           L0_attn_dt_bias, L0_attn_A_log, L0_attn_norm_w, L0_attn_norm_b,
           L0_attn_W_out, L0_ffn_w_gate, L0_ffn_w_up, L0_ffn_w_down):
    u = np.ascontiguousarray(np.asarray(L0_attn_u, np.float32))
    consts = _np_consts()
    nc = build_nc(dbg=DBG)
    in_maps = []
    for core in range(8):
        b, q = core // 4, core % 4
        seg = np.ascontiguousarray(u[b, T * q:T * (q + 1)])
        halo = np.zeros((16, D), np.float32)
        if q > 0:
            halo[13:16] = u[b, T * q - 3:T * q]
        sel2 = np.zeros((8, 8), np.float32)
        gmask = np.zeros((8, 1), np.float32)
        for j in range(8):
            if j // 4 == b and j % 4 < q:
                gmask[j, 0] = 1.0
                for i in range(8):
                    if i // 4 == b and j % 4 < i % 4 < q:
                        sel2[i, j] = 1.0
        m = dict(u_seg=seg, u_halo=halo,
                 W_in=np.ascontiguousarray(np.asarray(L0_attn_W_in, np.float32)),
                 conv_w=np.ascontiguousarray(np.asarray(L0_attn_conv_w, np.float32)),
                 conv_b=np.ascontiguousarray(np.asarray(L0_attn_conv_b, np.float32)),
                 dt_bias=np.ascontiguousarray(np.asarray(L0_attn_dt_bias, np.float32)),
                 A_log=np.ascontiguousarray(np.asarray(L0_attn_A_log, np.float32)),
                 norm_w=np.ascontiguousarray(np.asarray(L0_attn_norm_w, np.float32)),
                 norm_b=np.ascontiguousarray(np.asarray(L0_attn_norm_b, np.float32)),
                 W_out=np.ascontiguousarray(np.asarray(L0_attn_W_out, np.float32)),
                 wg=np.ascontiguousarray(np.asarray(L0_ffn_w_gate, np.int32)),
                 wu=np.ascontiguousarray(np.asarray(L0_ffn_w_up, np.int32)),
                 wd=np.ascontiguousarray(np.asarray(L0_ffn_w_down, np.int32)),
                 SEL2=sel2, GMASK=gmask, **consts)
        in_maps.append(m)
    res = run_bass_kernel_spmd(nc, in_maps, core_ids=list(range(8)))
    out = np.empty((2, 16384, D), np.float32)
    for core in range(8):
        b, q = core // 4, core % 4
        out[b, T * q:T * (q + 1)] = res.results[core]["out_seg"]
    return out


# version 11
# speedup vs baseline: 1.0623x; 1.0258x over previous
import numpy as np
import ml_dtypes
from contextlib import ExitStack

import concourse.bass as bass
import concourse.mybir as mybir
from concourse.bass_utils import run_bass_kernel_spmd

F32 = mybir.dt.float32
BF16 = mybir.dt.bfloat16
I32 = mybir.dt.int32
AF = mybir.ActivationFunctionType
ALU = mybir.AluOpType
AX = mybir.AxisListType

T = 4096
D = 1024
DI = 2048
DS = 128
NH = 32
HP = 64
CS = 64
NCK = T // CS
NPAIR = NCK // 2
DPROJ = 4384
HID = 3072
EPS = 1e-5
BIG = 1e30
NT = 8
TT = 512

ENGINES = ("sync", "tensor", "vector", "scalar", "gpsimd")
DMA_ENGINES = ("sync", "scalar", "gpsimd")

DBG = []


class Rec:

    def __init__(self, nc, getsem, ctr, shared):
        self.nc = nc
        self.getsem = getsem
        self.ctr = ctr
        self.last_w, self.readers = shared
        self.ops = []

    def _rec(self, engine, fn, own, inc, reads, writes, forced=()):
        deps = {}
        for k in reads:
            h = self.last_w.get(k)
            if h is not None:
                deps[h] = True
        for k in writes:
            h = self.last_w.get(k)
            if h is not None:
                deps[h] = True
            for rh in self.readers.get(k, ()):
                deps[rh] = True
        for h in forced:
            if h is not None:
                deps[h] = True
        self.getsem(own)
        self.ctr[own] = self.ctr.get(own, 0) + inc
        handle = (own, self.ctr[own])
        emit_deps = [h for h in deps
                     if not (h[0] == own and h[1] >= handle[1])]
        self.ops.append((engine, fn, sorted(emit_deps), own, inc))
        for k in writes:
            self.last_w[k] = handle
            self.readers[k] = []
        for k in reads:
            self.readers.setdefault(k, []).append(handle)
        return handle

    def op(self, engine, fn, reads=(), writes=(), inc=1):
        return self._rec(engine, fn, "eng:" + engine, inc, reads, writes)

    def dma(self, engine, out, in_, stream=None, reads=(), writes=(), **kw):
        if stream is None:
            stream = (tuple(writes) + tuple(reads) + ("misc",))[0]
        xp = self.last_w.get("XPOSE")
        return self._rec(engine, lambda e: e.dma_start(out, in_, **kw),
                         "dma:" + stream, 16, reads, writes,
                         forced=[xp] if xp else ())

    def xpose(self, out, in_, reads=(), writes=()):
        forced = [(s, v) for s, v in self.ctr.items()
                  if s.startswith("dma:") and v > 0]
        return self._rec("sync", lambda e: e.dma_start_transpose(out, in_),
                         "dma:xpose", 16, reads, tuple(writes) + ("XPOSE",),
                         forced=forced)

    def build(self):
        getsem, ops = self.getsem, self.ops
        with self.nc.Block() as blk:
            for eng_name in ENGINES:
                mine = [(f, d, own, inc) for (e, f, d, own, inc) in ops
                        if e == eng_name]
                if not mine:
                    continue

                def emit(eng, mine=mine):
                    waited = {}
                    for fn, deps, own, inc in mine:
                        for (sname, pval) in deps:
                            if waited.get(sname, -1) < pval:
                                eng.wait_ge(getsem(sname), pval)
                                waited[sname] = pval
                        inst = fn(eng)
                        if inst is None:
                            inst = eng.nop()
                        if inc:
                            inst.then_inc(getsem(own), inc)
                getattr(blk, eng_name)(emit)


def _np_consts():
    i = np.arange(64)
    tril_is = (i[:, None] <= i[None, :]).astype(np.float32)
    dup = lambda a: np.concatenate([a, a], 0)
    bf = ml_dtypes.bfloat16
    return dict(
        TRILNEG=dup(-tril_is).astype(bf),
        TRILLT=dup(tril_is).astype(bf),
        MASKRX=dup(np.repeat(-BIG * (i[None, :] == i[:, None] - 1).astype(np.float32)[:, :, None], 32, axis=2).reshape(64, 2048)).astype(bf),
        ONES64=np.ones((128, 64), np.float32).astype(bf),
        TRILSL=dup((i[None, :] >= i[:, None]).astype(np.float32)),
        IDENT=np.eye(128, dtype=np.float32),
        ONESR=np.ones((1, 128), np.float32).astype(bf),
        IDENTB=np.eye(128, dtype=np.float32).astype(bf),
        CONSTS=np.array([[1e-5, 1.0, 0.0, 0.0]], np.float32),
    )


def bc(ap, shape, axis):
    return ap.unsqueeze(axis).broadcast_to(shape)


def build_nc(dbg=()):
    dbg = set(dbg)
    nc = bass.Bass("TRN2", target_bir_lowering=False, num_devices=8)

    u_seg = nc.dram_tensor("u_seg", [T, D], F32, kind="ExternalInput").ap()
    u_halo = nc.dram_tensor("u_halo", [16, D], F32, kind="ExternalInput").ap()
    W_in = nc.dram_tensor("W_in", [D, DPROJ], F32, kind="ExternalInput").ap()
    conv_w = nc.dram_tensor("conv_w", [2304, 4], F32, kind="ExternalInput").ap()
    conv_b = nc.dram_tensor("conv_b", [2304], F32, kind="ExternalInput").ap()
    dt_bias = nc.dram_tensor("dt_bias", [NH], F32, kind="ExternalInput").ap()
    A_log = nc.dram_tensor("A_log", [NH], F32, kind="ExternalInput").ap()
    norm_w = nc.dram_tensor("norm_w", [DI], F32, kind="ExternalInput").ap()
    norm_b = nc.dram_tensor("norm_b", [DI], F32, kind="ExternalInput").ap()
    W_out = nc.dram_tensor("W_out", [DI, D], F32, kind="ExternalInput").ap()
    wg = nc.dram_tensor("wg", [HID, D], I32, kind="ExternalInput").ap()
    wu = nc.dram_tensor("wu", [HID, D], I32, kind="ExternalInput").ap()
    wd = nc.dram_tensor("wd", [D, HID], I32, kind="ExternalInput").ap()
    TRILNEG = nc.dram_tensor("TRILNEG", [128, 64], BF16, kind="ExternalInput").ap()
    TRILLT = nc.dram_tensor("TRILLT", [128, 64], BF16, kind="ExternalInput").ap()
    MASKRX = nc.dram_tensor("MASKRX", [128, 2048], BF16, kind="ExternalInput").ap()
    ONES64 = nc.dram_tensor("ONES64", [128, 64], BF16, kind="ExternalInput").ap()
    TRILSL = nc.dram_tensor("TRILSL", [128, 64], F32, kind="ExternalInput").ap()
    IDENT = nc.dram_tensor("IDENT", [128, 128], F32, kind="ExternalInput").ap()
    ONESR = nc.dram_tensor("ONESR", [1, 128], BF16, kind="ExternalInput").ap()
    IDENTB = nc.dram_tensor("IDENTB", [128, 128], BF16, kind="ExternalInput").ap()
    CONSTS = nc.dram_tensor("CONSTS", [1, 4], F32, kind="ExternalInput").ap()
    SEL2 = nc.dram_tensor("SEL2", [8, 8], F32, kind="ExternalInput").ap()
    GMASK = nc.dram_tensor("GMASK", [8, 1], F32, kind="ExternalInput").ap()
    out_seg = nc.dram_tensor("out_seg", [T, D], F32, kind="ExternalOutput").ap()

    def scratch(name, shape, dtype):
        kind = "ExternalOutput" if name in dbg else "Internal"
        return nc.dram_tensor(name, shape, dtype, kind=kind).ap()

    SP_X0N = scratch("SP_X0N", [T, D], BF16)
    SP_X0NH = scratch("SP_X0NH", [16, D], BF16)
    SP_Z = scratch("SP_Z", [T, DI], BF16)
    SP_XC = scratch("SP_XC", [2176, T], BF16)
    SP_XS = scratch("SP_XS", [T, DI], BF16)
    SP_XD = scratch("SP_XD", [T, DI], BF16)
    SP_YN = scratch("SP_YN", [T, DI], BF16)
    SP_X1 = scratch("SP_X1", [T, D], F32)
    SP_X1N = scratch("SP_X1N", [T, D], BF16)
    SP_H = scratch("SP_H", [HID, T], BF16)
    SP_WGN = scratch("SP_WGN", [HID, D], BF16)
    SP_WUN = scratch("SP_WUN", [HID, D], BF16)
    SP_WDN = scratch("SP_WDN", [D, HID], BF16)
    SP_SB = scratch("SP_SB", [192, NH], F32)
    SP_NEGA = scratch("SP_NEGA", [NH, 1], F32)
    SP_W8 = scratch("SP_W8", [8, NH], F32)
    CBT = scratch("CBT", [128, 2049], F32)
    SP_DBGS = scratch("SP_DBGS", [128, 2049 + 64], F32)
    SP_DBGS2 = scratch("SP_DBGS2", [128, 2049 + 64], F32)
    GATH = nc.dram_tensor("GATH", [1024, 2049], F32, kind="Internal", addr_space="Shared").ap()

    es_g = ExitStack()
    es_mid = ExitStack()
    es_b = ExitStack()

    def mk(stack, name, shape, dtype=F32):
        return stack.enter_context(nc.sbuf_tensor(name, shape, dtype))

    trilneg = mk(es_g, "trilneg", [128, 64], BF16)
    trillt = mk(es_g, "trillt", [128, 64], BF16)
    maskrx = mk(es_g, "maskrx", [128, 2048], BF16)
    ones64 = mk(es_g, "ones64", [128, 64], BF16)
    trilsl = mk(es_g, "trilsl", [128, 64], F32)
    ident = mk(es_g, "ident", [128, 128], F32)
    onesr = mk(es_g, "onesr", [1, 128], BF16)
    identb = mk(es_g, "identb", [128, 128], BF16)
    cst = mk(es_g, "cst", [128, 4], F32)
    negA_col = mk(es_g, "negA_col", [NH, 1], F32)
    negA_rep = mk(es_g, "negA_rep", [128, NH], F32)
    dtb_rep = mk(es_g, "dtb_rep", [128, NH], F32)
    convw_sb = mk(es_g, "convw_sb", [128, 18, 4], F32)
    convb_sb = mk(es_g, "convb_sb", [128, 18], F32)

    B_FM = mk(es_mid, "B_FM", [128, T], BF16)
    C_FM = mk(es_mid, "C_FM", [128, T], BF16)
    dt_TM = mk(es_mid, "dt_TM", [128, 32, NH], F32)
    a_TM = mk(es_mid, "a_TM", [128, 32, NH], F32)
    a_TMb = mk(es_mid, "a_TMb", [128, 32, NH], BF16)
    atot_HM = mk(es_mid, "atot_HM", [NH, NCK], F32)
    cpi_HM = mk(es_mid, "cpi_HM", [NH, NCK], F32)
    rc_HM = mk(es_mid, "rc_HM", [NH, NCK], F32)
    b2_HM = mk(es_mid, "b2_HM", [NH, NCK], F32)
    dexp_HM = mk(es_mid, "dexp_HM", [NH, NCK], F32)
    aseg_col = mk(es_mid, "aseg_col", [NH, 1], F32)
    cs_TM = mk(es_mid, "cs_TM", [128, 32, NH], F32)
    dsT = mk(es_mid, "dsT", [128, 32, NH], F32)
    ds2T = mk(es_mid, "ds2T", [128, 32, NH], F32)
    sdT = mk(es_mid, "sdT", [128, 32, NH], F32)

    sem_map = {}
    sem_ctr = {}

    def getsem(name):
        if name not in sem_map:
            sem_map[name] = es_g.enter_context(
                nc.semaphore("s_" + name.replace(":", "_")))
        return sem_map[name]

    cc_sem = es_g.enter_context(nc.semaphore("sem_cc"))
    shared = ({}, {})

    def rec():
        return Rec(nc, getsem, sem_ctr, shared)

    def ln_apply(r, pre, stat_reads, stat_aps, st, mv, rs, npart=128):
        for i, ap in enumerate(stat_aps):
            r.op("vector", lambda e, i=i, ap=ap: e.bn_stats(st[:, 6 * i:6 * (i + 1)], ap),
                 reads=tuple(stat_reads) + (pre + "st",), writes=(pre + "st",))
        r.op("vector", lambda e: e.bn_aggr(mv[:, :], st[:, 0:6 * len(stat_aps)]),
             reads=(pre + "st", pre + "mv"), writes=(pre + "mv",))
        r.op("scalar", lambda e: e.activation(rs[:, 0:1], mv[:, 1:2], AF.Ln,
                                              bias=cst[0:npart, 0:1]),
             reads=(pre + "mv", pre + "rs", "cst"), writes=(pre + "rs",))
        r.op("scalar", lambda e: e.activation(rs[:, 0:1], rs[:, 0:1], AF.Exp, scale=-0.5),
             reads=(pre + "rs",), writes=(pre + "rs",))
        r.op("vector", lambda e: e.scalar_tensor_tensor(
            rs[:, 1:2], mv[:, 0:1], -1.0, rs[:, 0:1], op0=ALU.mult, op1=ALU.mult),
            reads=(pre + "mv", pre + "rs"), writes=(pre + "rs",))

    es_win = ExitStack()
    W_in_sb = es_win.enter_context(nc.sbuf_tensor("W_in_sb", [128, 8, DPROJ], BF16))
    es0 = ExitStack()
    w32 = [mk(es0, f"w32_{i}", [128, DPROJ], F32) for i in range(2)]
    w1 = mk(es0, "w1", [NH, 2], F32)

    r = rec()
    for dst, src in ((trilneg, TRILNEG), (trillt, TRILLT), (maskrx, MASKRX),
                     (ones64, ONES64), (trilsl, TRILSL), (ident, IDENT),
                     (onesr, ONESR), (identb, IDENTB)):
        r.dma("sync", dst[:, :], src, writes=(dst.name,))
    r.dma("sync", cst[:, :], bc(CONSTS.rearrange("one c -> (one c)"), [128, 4], 0), writes=("cst",))
    r.dma("sync", dtb_rep[:, :], bc(dt_bias, [128, NH], 0), writes=("dtb_rep",))
    r.dma("sync", convw_sb[:, :, :], conv_w.rearrange("(m p) k -> p m k", p=128),
          writes=("convw",))
    r.dma("sync", convb_sb[:, :], conv_b.rearrange("(m p) -> p m", p=128),
          writes=("convb",), allow_slow_non_contiguous=True)
    r.dma("sync", w1[:, 0:1], A_log.unsqueeze(1), writes=("w1",))
    r.op("scalar", lambda e: e.activation(w1[:, 1:2], w1[:, 0:1], AF.Exp),
         reads=("w1",), writes=("w1e",))
    r.op("scalar", lambda e: e.mul(negA_col[:, :], w1[:, 1:2], -1.0),
         reads=("w1e",), writes=("negA_col",))
    r.dma("sync", SP_NEGA, negA_col[:, :], reads=("negA_col",), writes=("SP_NEGA",))
    r.dma("sync", negA_rep[:, :],
          bc(SP_NEGA.rearrange("h one -> (h one)"), [128, NH], 0),
          reads=("SP_NEGA",), writes=("negA_rep",))
    for k in range(8):
        buf, key = w32[k % 2], f"w32_{k % 2}"
        r.dma("sync", buf[:, :], W_in[128 * k:128 * (k + 1), :], writes=(key,))
        r.op("scalar",
             lambda e, k=k, buf=buf: e.activation(W_in_sb[:, k, :], buf[:, :], AF.Identity),
             reads=(key,), writes=("W_in_sb",))
    esA = ExitStack()
    u_t = [mk(esA, f"u_t{i}", [128, 4, D], F32) for i in range(2)]
    x0_t = [mk(esA, f"x0_t{i}", [128, 4, D], BF16) for i in range(2)]
    st_t = mk(esA, "st_t", [128, 12], F32)
    mv_t = mk(esA, "mv_t", [128, 2], F32)
    rs_t = mk(esA, "rs_t", [128, 2], F32)

    r.dma("sync", u_t[0][0:16, 0, :], u_halo, writes=("u_t0",))
    ln_apply(r, "a", ("u_t0",), [u_t[0][0:16, 0, 0:512], u_t[0][0:16, 0, 512:1024]],
             st_t[0:16, :], mv_t[0:16, :], rs_t[0:16, :], npart=16)
    r.op("scalar", lambda e: e.activation(x0_t[0][0:16, 0, :], u_t[0][0:16, 0, :],
                                          AF.Identity, scale=rs_t[0:16, 0:1],
                                          bias=rs_t[0:16, 1:2]),
         reads=("u_t0", "ars", "x0_t0"), writes=("x0_t0",))
    r.dma("scalar", SP_X0NH, x0_t[0][0:16, 0, :], stream="x0_t0", reads=("x0_t0",), writes=("SP_X0NH",))
    for t in range(NT):
        ub, xb = u_t[t % 2], x0_t[t % 2]
        uk, xk = f"u_t{t % 2}", f"x0_t{t % 2}"
        r.dma("sync", ub[:, :, :],
              u_seg[TT * t:TT * (t + 1), :].rearrange("(g p) d -> p g d", p=128),
              reads=(uk,), writes=(uk,))
        for g in range(4):
            ln_apply(r, "a", (uk,), [ub[:, g, 0:512], ub[:, g, 512:1024]], st_t, mv_t, rs_t)
            r.op("scalar", lambda e, g=g, ub=ub, xb=xb: e.activation(
                xb[:, g, :], ub[:, g, :], AF.Identity,
                scale=rs_t[:, 0:1], bias=rs_t[:, 1:2]),
                reads=(uk, "ars", xk), writes=(xk,))
        r.dma("scalar", SP_X0N[TT * t:TT * (t + 1), :].rearrange("(g p) d -> p g d", p=128),
              xb[:, :, :], stream=xk, reads=(xk,), writes=(f"SP_X0N:{t}",))
    r.build()
    esA.close()
    es0.close()

    esB = ExitStack()
    x0F = [mk(esB, f"x0F{i}", [128, 8, TT], BF16) for i in range(2)]
    x0hF = mk(esB, "x0hF", [128, 8, 16], BF16)
    z_t = [mk(esB, f"z_t{i}", [128, 1024], BF16) for i in range(2)]
    xc = mk(esB, "xc", [128, 18, 515], BF16)
    tail_b = mk(esB, "tail_b", [128, 18, 3], BF16)
    cacc = mk(esB, "cacc", [128, 2, TT], F32)
    hm_buf = mk(esB, "hm_buf", [NH, T], F32)
    cs_HM = mk(esB, "cs_HM", [NH, T], F32)
    sp1 = mk(esB, "sp1", [128, 32, NH], F32)
    sp2 = mk(esB, "sp2", [128, 32, NH], F32)
    psZ = esB.enter_context(nc.psum_tensor("psZ", [128, 2, 1024], F32))
    psX = esB.enter_context(nc.psum_tensor("psX", [128, 2, 512], F32))
    psM = esB.enter_context(nc.psum_tensor("psM", [128, 128], F32))

    WINK = ("W_in_sb",)
    r = rec()
    r.xpose(x0hF[:, :, :], SP_X0NH, reads=("SP_X0NH",), writes=("x0hF",))
    for m in range(18):
        def f_hmm(e, m=m):
            inst = None
            for k in range(8):
                inst = e.matmul(psM[:, 0:16], W_in_sb[:, k, DI + 128 * m:DI + 128 * (m + 1)],
                                x0hF[:, k, :], start=(k == 0), stop=(k == 7))
            return inst
        r.op("tensor", f_hmm, reads=("x0hF", "psM") + WINK, writes=("psM",))
        r.op("scalar", lambda e, m=m: e.activation(tail_b[:, m, :], psM[:, 13:16], AF.Identity),
             reads=("psM", "tail"), writes=("tail",))

    for t in range(NT):
        x0b, x0k = x0F[t % 2], f"x0F{t % 2}"
        r.xpose(x0b[:, :, :], SP_X0N[TT * t:TT * (t + 1), :],
                reads=(f"SP_X0N:{t}", x0k), writes=(x0k,))
        xcb, xck = xc, "xc"
        for g in range(4):
            for n2 in range(2):
                zb, zbk = z_t[n2], f"z_t{n2}"
                def f_zmm(e, g=g, n2=n2, x0b=x0b):
                    inst = None
                    for half in range(2):
                        for k in range(8):
                            inst = e.matmul(psZ[:, n2, 512 * half:512 * (half + 1)],
                                            x0b[:, k, 128 * g:128 * (g + 1)],
                                            W_in_sb[:, k, 1024 * n2 + 512 * half:1024 * n2 + 512 * (half + 1)],
                                            start=(k == 0), stop=(k == 7))
                    return inst
                r.op("tensor", f_zmm, reads=(x0k, f"psZ{n2}") + WINK, writes=(f"psZ{n2}",))
                r.op("scalar", lambda e, n2=n2, zb=zb: e.activation(
                    zb[:, :], psZ[:, n2, :], AF.Identity),
                    reads=(f"psZ{n2}", zbk), writes=(zbk, f"psZ{n2}"))
                r.dma("scalar", SP_Z[TT * t + 128 * g:TT * t + 128 * (g + 1),
                                     1024 * n2:1024 * (n2 + 1)], zb[:, :],
                      stream=zbk, reads=(zbk,), writes=(f"SP_Z:{t // 4}",))
        for g in range(4):
            def f_dtmm(e, g=g, x0b=x0b):
                inst = None
                for k in range(8):
                    inst = e.matmul(psM[:, 0:NH], x0b[:, k, 128 * g:128 * (g + 1)],
                                    W_in_sb[:, k, DPROJ - NH:DPROJ],
                                    start=(k == 0), stop=(k == 7))
                return inst
            r.op("tensor", f_dtmm, reads=(x0k, "psM") + WINK, writes=("psM",))
            r.op("vector", lambda e, t=t, g=g: e.tensor_copy(dt_TM[:, 4 * t + g, :],
                                                             psM[:, 0:NH]),
                 reads=("psM", "dt_TM"), writes=("dt_TM",))
        for m in range(18):
            pb = m % 2
            def f_xmm(e, m=m, pb=pb, x0b=x0b):
                inst = None
                for k in range(8):
                    inst = e.matmul(psX[:, pb, :],
                                    W_in_sb[:, k, DI + 128 * m:DI + 128 * (m + 1)],
                                    x0b[:, k, :], start=(k == 0), stop=(k == 7))
                return inst
            r.op("tensor", f_xmm, reads=(x0k, f"psX{pb}") + WINK, writes=(f"psX{pb}",))
            r.op("scalar", lambda e, m=m, pb=pb, xcb=xcb: e.activation(
                xcb[:, m, 3:515], psX[:, pb, :], AF.Identity),
                reads=(f"psX{pb}", f"{xck}:{m}"), writes=(f"{xck}:{m}", f"psX{pb}"))
        r.op("vector", lambda e, xcb=xcb: e.tensor_copy(xcb[:, :, 0:3], tail_b[:, :, :]),
             reads=("tail",) + tuple(f"{xck}:{m}" for m in range(18)),
             writes=tuple(f"{xck}:{m}t" for m in range(18)))
        for m in range(18):
            ca, cak = cacc[:, m % 2, :], f"cacc{m % 2}"
            xmk = (f"{xck}:{m}", f"{xck}:{m}t")
            r.op("vector", lambda e, m=m, ca=ca, xcb=xcb: e.tensor_scalar(
                ca, xcb[:, m, 0:512], convw_sb[:, m, 0:1], convb_sb[:, m:m + 1],
                op0=ALU.mult, op1=ALU.add),
                reads=(xmk, "convw", "convb", cak), writes=(cak,))
            for i in range(1, 4):
                r.op("vector", lambda e, m=m, i=i, ca=ca, xcb=xcb: e.scalar_tensor_tensor(
                    ca, xcb[:, m, i:i + 512], convw_sb[:, m, i:i + 1], ca,
                    op0=ALU.mult, op1=ALU.add),
                    reads=(xmk, "convw", cak), writes=(cak,))
            r.op("vector", lambda e, m=m, xcb=xcb: e.tensor_copy(tail_b[:, m, :],
                                                                 xcb[:, m, 512:515]),
                 reads=xmk + ("tail",), writes=("tail",))
            r.op("scalar", lambda e, m=m, ca=ca, xcb=xcb: e.activation(
                xcb[:, m, 3:515], ca, AF.Silu),
                reads=(cak,) + xmk, writes=(f"{xck}:{m}",))
        r.op("vector", lambda e, t=t, xcb=xcb: e.tensor_copy(B_FM[:, TT * t:TT * (t + 1)],
                                                             xcb[:, 16, 3:515]),
             reads=(f"{xck}:16",), writes=("B_FM",))
        r.op("vector", lambda e, t=t, xcb=xcb: e.tensor_copy(C_FM[:, TT * t:TT * (t + 1)],
                                                             xcb[:, 17, 3:515]),
             reads=(f"{xck}:17",), writes=("C_FM",))
        r.dma("scalar",
              SP_XC[0:2048, TT * t:TT * (t + 1)].rearrange("(m p) c -> p m c", p=128),
              xcb[:, 0:16, 3:515], stream=xck,
              reads=tuple(f"{xck}:{m}" for m in range(16)), writes=(f"SP_XC:{t}",))
        r.dma("scalar", SP_XC[2048:2176, TT * t:TT * (t + 1)], xcb[:, 16, 3:515],
              stream=xck, reads=(f"{xck}:16",), writes=(f"SP_XCB:{t}",))

    dtv = dt_TM[:, :, :].rearrange("p a b -> p (a b)")
    sp1v = sp1[:, :, :].rearrange("p a b -> p (a b)")
    sp2v = sp2[:, :, :].rearrange("p a b -> p (a b)")
    r.op("vector", lambda e: e.tensor_tensor(
        dt_TM[:, :, :], dt_TM[:, :, :], bc(dtb_rep[:, :], [128, 32, NH], 1),
        op=ALU.add), reads=("dt_TM", "dtb_rep"), writes=("dt_TM",))
    r.op("scalar", lambda e: e.activation(sp1v, dtv, AF.Abs),
         reads=("dt_TM", "sp1"), writes=("sp1",))
    r.op("scalar", lambda e: e.activation(sp1v, sp1v, AF.Exp, scale=-1.0),
         reads=("sp1",), writes=("sp1",))
    r.op("scalar", lambda e: e.activation(sp1v, sp1v, AF.Ln, bias=cst[:, 1:2]),
         reads=("sp1", "cst"), writes=("sp1",))
    r.op("scalar", lambda e: e.activation(sp2v, dtv, AF.Relu),
         reads=("dt_TM", "sp2"), writes=("sp2",))
    r.op("vector", lambda e: e.tensor_tensor(dtv, sp1v, sp2v, op=ALU.add),
         reads=("sp1", "sp2", "dt_TM"), writes=("dt_TM",))
    r.op("vector", lambda e: e.tensor_tensor(
        a_TM[:, :, :], dt_TM[:, :, :],
        bc(negA_rep[:, :], [128, 32, NH], 1), op=ALU.mult),
        reads=("dt_TM", "negA_rep"), writes=("a_TM",))
    r.op("vector", lambda e: e.tensor_copy(a_TMb[:, :, :], a_TM[:, :, :]),
         reads=("a_TM",), writes=("a_TMb",))
    for p in range(32):
        r.op("tensor", lambda e, p=p: e.transpose(psM[0:32, 0:128], dt_TM[:, p, :], ident[:, :]),
             reads=("dt_TM", "ident", "psM"), writes=("psM",))
        r.op("vector", lambda e, p=p: e.tensor_copy(hm_buf[:, 128 * p:128 * (p + 1)],
                                                    psM[0:32, 0:128]),
             reads=("psM", "hm_buf"), writes=("hm_buf",))
    r.op("vector", lambda e: e.tensor_scalar(hm_buf[:, :], hm_buf[:, :], negA_col[:, 0:1],
                                             None, op0=ALU.mult, op1=ALU.bypass),
         reads=("hm_buf", "negA_col"), writes=("hm_buf",))
    def f_scans(e):
        inst = None
        for c in range(NCK):
            inst = e.tensor_tensor_scan(cs_HM[:, CS * c:CS * (c + 1)], ones64[0:NH, :],
                                        hm_buf[:, CS * c:CS * (c + 1)], initial=0.0,
                                        op0=ALU.mult, op1=ALU.add)
        return inst
    r.op("vector", f_scans, reads=("hm_buf", "ones64", "cs_HM"), writes=("cs_HM",))
    r.op("vector", lambda e: e.tensor_copy(
        atot_HM[:, :], cs_HM[:, :].rearrange("h (c s) -> h c s", s=CS)[:, :, CS - 1]),
        reads=("cs_HM",), writes=("atot_HM",))
    r.op("vector", lambda e: e.tensor_tensor_scan(cpi_HM[:, :], ones64[0:NH, :], atot_HM[:, :],
                                                  initial=0.0, op0=ALU.mult, op1=ALU.add),
         reads=("atot_HM", "ones64"), writes=("cpi_HM",))
    r.op("vector", lambda e: e.reduce_sum(aseg_col[:, :], atot_HM[:, :], axis=AX.X),
         reads=("atot_HM",), writes=("aseg_col",))
    r.op("vector", lambda e: e.tensor_scalar(rc_HM[:, :], cpi_HM[:, :], -1.0, aseg_col[:, 0:1],
                                             op0=ALU.mult, op1=ALU.add),
         reads=("cpi_HM", "aseg_col"), writes=("rc_HM",))
    r.op("vector", lambda e: e.tensor_tensor(b2_HM[:, :], atot_HM[:, :], rc_HM[:, :], op=ALU.add),
         reads=("atot_HM", "rc_HM"), writes=("b2_HM",))
    r.op("scalar", lambda e: e.activation(dexp_HM[:, :], atot_HM[:, :], AF.Exp),
         reads=("atot_HM",), writes=("dexp_HM",))
    r.dma("sync", SP_SB[0:64, :].rearrange("c h -> h c"), atot_HM[:, :],
          reads=("atot_HM",), writes=("SP_SB0",), allow_slow_non_contiguous=True)
    r.dma("sync", SP_SB[64:128, :].rearrange("c h -> h c"), b2_HM[:, :],
          reads=("b2_HM",), writes=("SP_SB1",), allow_slow_non_contiguous=True)
    r.dma("sync", SP_SB[128:192, :].rearrange("c h -> h c"), dexp_HM[:, :],
          reads=("dexp_HM",), writes=("SP_SB2",), allow_slow_non_contiguous=True)
    for p in range(32):
        r.op("tensor", lambda e, p=p: e.transpose(psM[:, 0:32], cs_HM[:, 128 * p:128 * (p + 1)],
                                                  ident[0:32, 0:32]),
             reads=("cs_HM", "ident", "psM"), writes=("psM",))
        r.op("vector", lambda e, p=p: e.tensor_copy(cs_TM[:, p, :], psM[:, 0:32]),
             reads=("psM", "cs_TM"), writes=("cs_TM",))
    csv = cs_TM[:, :, :].rearrange("p a b -> p (a b)")
    for cpar in range(2):
        r.dma("sync", sp1[64 * cpar:64 * (cpar + 1), :, :],
              bass.AP(SP_SB.tensor, NH * cpar, [[0, 64], [2 * NH, 32], [1, NH]]),
              reads=("SP_SB0", "sp1"), writes=("sp1",))
        r.dma("sync", sp2[64 * cpar:64 * (cpar + 1), :, :],
              bass.AP(SP_SB.tensor, 64 * NH + NH * cpar, [[0, 64], [2 * NH, 32], [1, NH]]),
              reads=("SP_SB1", "sp2"), writes=("sp2",))
    r.op("vector", lambda e: e.tensor_tensor(sp1v, sp1v, csv, op=ALU.subtract),
         reads=("sp1", "cs_TM"), writes=("sp1",))
    r.op("vector", lambda e: e.tensor_tensor(sp2v, sp2v, csv, op=ALU.subtract),
         reads=("sp2", "cs_TM"), writes=("sp2",))
    def f_exps(e):
        e.activation(dsT[:, :, :].rearrange("p a b -> p (a b)"), sp1v, AF.Exp)
        e.activation(ds2T[:, :, :].rearrange("p a b -> p (a b)"), sp2v, AF.Exp)
        return e.activation(sdT[:, :, :].rearrange("p a b -> p (a b)"), csv, AF.Exp)
    r.op("scalar", f_exps, reads=("sp1", "sp2", "cs_TM"), writes=("dsT", "ds2T", "sdT"))
    r.build()
    esB.close()
    es_win.close()

    B_TM = mk(es_b, "B_TM", [128, 32, 128], BF16)
    D_rep = mk(es_b, "D_rep", [128, NCK, NH], F32)
    S_sb = mk(es_b, "S_sb", [128, DI], BF16)
    battn = mk(es_b, "battn", [1, D], BF16)
    W_out_sb = mk(es_b, "W_out_sb", [128, 16, D], BF16)

    esC = ExitStack()
    xcT = mk(esC, "xcT", [128, 8, 2176], BF16)
    xs_b = mk(esC, "xs_b", [128, DI], BF16)
    xd_b = mk(esC, "xd_b", [128, DI], BF16)
    xd2_b = mk(esC, "xd2_b", [128, DI], BF16)
    cb_f = mk(esC, "cb_f", [128, DI], F32)
    gth = mk(esC, "gth", [128, DI], F32)
    acc_f = mk(esC, "acc_f", [128, DI], F32)
    tmp_f = mk(esC, "tmp_f", [128, DI], F32)
    a8 = mk(esC, "a8", [8, NH + 8], F32)
    gm_col = mk(esC, "gm_col", [8, 1], F32)
    wrep = mk(esC, "wrep", [128, 8, NH], F32)
    smB = mk(esC, "smB", [8, NH], F32)
    smC = mk(esC, "smC", [8, NH], F32)
    wo32 = [mk(esC, "wo32_0", [128, D], F32)] * 2
    wo16 = mk(esC, "wo16", [128, D], BF16)
    nwb_col = mk(esC, "nwb_col", [128, 32], F32)
    nb_colb = mk(esC, "nb_colb", [128, 16], BF16)
    wi32b = [mk(esC, "wi32b0", [128, 2, D], I32)] * 2
    wibfb = [mk(esC, "wibfb0", [128, 2, D], BF16)] * 2
    psCB = esC.enter_context(nc.psum_tensor("psCB", [128, DI], F32))
    psE = esC.enter_context(nc.psum_tensor("psE", [8, NH], F32))
    psB2 = esC.enter_context(nc.psum_tensor("psB2", [1, D], F32))

    r = rec()
    r.dma("sync", nwb_col[:, 0:16], norm_w.rearrange("(k p) -> p k", p=128),
          writes=("nw_col",), allow_slow_non_contiguous=True)
    r.dma("sync", nwb_col[:, 16:32], norm_b.rearrange("(k p) -> p k", p=128),
          writes=("nb_col32",), allow_slow_non_contiguous=True)
    r.op("scalar", lambda e: e.activation(nb_colb[:, :], nwb_col[:, 16:32], AF.Identity),
         reads=("nb_col32",), writes=("nb_colb",))
    for k in range(16):
        buf, key = wo32[0], "wo32_0"
        r.dma("sync", buf[:, :], W_out[128 * k:128 * (k + 1), :], reads=(key,), writes=(key,))
        r.op("scalar", lambda e, buf=buf: e.activation(wo16[:, :], buf[:, :], AF.Identity),
             reads=(key, "wo16"), writes=("wo16",))
        def f_bmm(e, k=k):
            e.matmul(psB2[:, 0:512], nb_colb[:, k:k + 1], wo16[:, 0:512],
                     start=(k == 0), stop=(k == 15))
            return e.matmul(psB2[:, 512:1024], nb_colb[:, k:k + 1], wo16[:, 512:1024],
                            start=(k == 0), stop=(k == 15))
        r.op("tensor", f_bmm, reads=("wo16", "nb_colb", "psB2"), writes=("psB2",))
        r.op("scalar", lambda e, k=k, buf=buf: e.activation(
            W_out_sb[:, k, :], buf[:, :], AF.Identity, scale=nwb_col[:, k:k + 1]),
            reads=(key, "nw_col", "W_out_sb"), writes=("W_out_sb",))
    r.op("scalar", lambda e: e.activation(battn[:, :], psB2[:, :], AF.Identity),
         reads=("psB2",), writes=("battn",))
    rnd_i = [0]
    def bounce(wsrc, wdst, wname, r0, rows, wkey):
        i = rnd_i[0] % 2
        rnd_i[0] += 1
        m = rows // 128
        r.dma("scalar", wi32b[i][:, 0:m, :],
              wsrc[r0:r0 + rows, :].rearrange("(m p) c -> p m c", p=128),
              stream="wi32b0", reads=("wi32b0",), writes=("wi32b0",))
        r.op("scalar", lambda e, i=i, m=m: e.activation(
            wibfb[i][:, 0:m, :], wi32b[i][:, 0:m, :], AF.Identity),
            reads=("wi32b0", "wibfb0"), writes=("wibfb0",))
        r.dma("scalar", wdst[r0:r0 + rows, :].rearrange("(m p) c -> p m c", p=128),
              wibfb[i][:, 0:m, :], stream="wibfb0", reads=("wibfb0",),
              writes=(wkey,))
    for wsrc, wdst, wname in ((wg, SP_WGN, "SP_WGN"), (wu, SP_WUN, "SP_WUN")):
        for r0 in range(0, HID, 256):
            bounce(wsrc, wdst, wname, r0, 256, f"{wname}:{r0 // 1024 * 1024}")
    def bounce_wd(r0, co):
        i = rnd_i[0] % 2
        rnd_i[0] += 1
        fv32 = wi32b[i][:, :, :].rearrange("p a b -> p (a b)")
        fvbf = wibfb[i][:, :, :].rearrange("p a b -> p (a b)")
        r.dma("scalar", fv32[:, 0:1536], wd[r0:r0 + 128, co:co + 1536],
              stream="wi32b0", reads=("wi32b0",), writes=("wi32b0",))
        r.op("scalar", lambda e, i=i, fv32=fv32, fvbf=fvbf: e.activation(
            fvbf[:, 0:1536], fv32[:, 0:1536], AF.Identity),
            reads=("wi32b0", "wibfb0"), writes=("wibfb0",))
        r.dma("scalar", SP_WDN[r0:r0 + 128, co:co + 1536], fvbf[:, 0:1536],
              stream="wibfb0", reads=("wibfb0",),
              writes=(f"SP_WDN:{r0 // 256 * 256}",))
    for r0 in range(0, D, 128):
        for co in (0, 1536):
            bounce_wd(r0, co)
    r.dma("sync", D_rep[:, :, :],
          bass.AP(SP_SB.tensor, 128 * NH, [[0, 128], [NH, NCK], [1, NH]]),
          reads=("SP_SB2",), writes=("D_rep",))
    first = [True]
    for quarter in range(4):
        r.xpose(xcT[:, :, :], SP_XC[:, 1024 * quarter:1024 * (quarter + 1)],
                reads=(f"SP_XC:{2 * quarter}", f"SP_XC:{2 * quarter + 1}",
                       f"SP_XCB:{2 * quarter}", f"SP_XCB:{2 * quarter + 1}", "xcT"),
                writes=("xcT",))
        for m in range(8):
            pp = 8 * quarter + m
            r.op("vector", lambda e, m=m, pp=pp: e.tensor_tensor(
                xs_b[:, :].rearrange("p (h q) -> p h q", q=HP),
                xcT[:, m, 0:DI].rearrange("p (h q) -> p h q", q=HP),
                bc(dt_TM[:, pp, :], [128, NH, HP], 2), op=ALU.mult),
                reads=("xcT", "dt_TM", "xs_b"), writes=("xs_b",))
            r.op("gpsimd", lambda e, pp=pp: e.tensor_tensor(
                xd_b[:, :].rearrange("p (h q) -> p h q", q=HP),
                xs_b[:, :].rearrange("p (h q) -> p h q", q=HP),
                bc(dsT[:, pp, :], [128, NH, HP], 2), op=ALU.mult),
                reads=("xs_b", "dsT", "xd_b"), writes=("xd_b",))
            r.op("gpsimd", lambda e, pp=pp: e.tensor_tensor(
                xd2_b[:, :].rearrange("p (h q) -> p h q", q=HP),
                xs_b[:, :].rearrange("p (h q) -> p h q", q=HP),
                bc(ds2T[:, pp, :], [128, NH, HP], 2), op=ALU.mult),
                reads=("xs_b", "ds2T", "xd2_b"), writes=("xd2_b",))
            r.op("vector", lambda e, m=m, pp=pp: e.tensor_copy(B_TM[:, pp, :],
                                                               xcT[:, m, 2048:2176]),
                 reads=("xcT", "B_TM"), writes=("B_TM",))
            r.dma("scalar", SP_XS[128 * pp:128 * (pp + 1), :], xs_b[:, :],
                  stream="xs_b", reads=("xs_b",), writes=(f"SP_XS:{pp // 4}",))
            r.dma("scalar", SP_XD[128 * pp:128 * (pp + 1), :], xd_b[:, :],
                  stream="xd_b", reads=("xd_b",), writes=(f"SP_XD:{pp // 4}",))
            def f_st2(e, m=m, first=first, pp=pp):
                inst = None
                for cc2 in range(2):
                    sl = slice(64 * cc2, 64 * (cc2 + 1))
                    for h in range(NH):
                        inst = e.matmul(psCB[:, HP * h:HP * (h + 1)], xcT[sl, m, 2048:2176],
                                        xd2_b[sl, HP * h:HP * (h + 1)], start=(first[0] and h % 8 == 0),
                                        stop=(pp == NPAIR - 1 and cc2 == 1 and h == NH - 1))
                        if h == NH - 1:
                            first[0] = False
                return inst
            r.op("tensor", f_st2, reads=("xcT", "xd2_b", "psCB"), writes=("psCB",))
    r.op("vector", lambda e: e.tensor_copy(cb_f[:, :], psCB[:, :]),
         reads=("psCB",), writes=("cb_f",))
    r.dma("sync", CBT[:, 0:DI], cb_f[:, :], reads=("cb_f",), writes=("CBT",))
    r.dma("sync", CBT[0:NH, DI:DI + 1], aseg_col[:, :], reads=("aseg_col",), writes=("CBT2",), allow_slow_non_contiguous=True)
    r.build()
    nc.all_core_barrier()
    r = rec()
    r.op("gpsimd", lambda e: e.collective_compute(
        "AllGather", ALU.bypass, replica_groups=[list(range(8))], ins=[CBT], outs=[GATH])
        .then_inc(cc_sem, 1) and None,
        reads=("CBT", "CBT2"), writes=(), inc=0)
    r.op("gpsimd", lambda e: e.wait_ge(cc_sem, 1) or e.nop(), reads=(), writes=("GATH",))
    r.build()
    nc.all_core_barrier()
    r = rec()
    r.dma("sync", a8[:, NH:NH + 8], SEL2, writes=("a8sel",))
    r.dma("sync", a8[0:8, 0:NH], GATH[:, DI].rearrange("(j p) -> j p", p=128)[:, 0:NH],
          reads=("GATH",), writes=("a8",), allow_slow_non_contiguous=True)
    r.dma("sync", gm_col[:, :], GMASK, writes=("gm_col",))
    r.op("tensor", lambda e: e.matmul(psE[:, :], a8[:, NH:NH + 8], a8[0:8, 0:NH],
                                      start=True, stop=True),
         reads=("a8sel", "a8", "psE"), writes=("psE",))
    r.op("scalar", lambda e: e.activation(smB[:, :], psE[:, :], AF.Exp),
         reads=("psE",), writes=("smB",))
    r.op("vector", lambda e: e.tensor_scalar(smC[:, :], smB[:, :], gm_col[:, 0:1], None,
                                             op0=ALU.mult, op1=ALU.bypass),
         reads=("smB", "gm_col"), writes=("smC",))
    r.dma("sync", SP_W8, smC[:, :], reads=("smC",), writes=("SP_W8",))
    r.dma("sync", wrep[:, :, :], bass.AP(SP_W8.tensor, 0, [[0, 128], [NH, 8], [1, NH]]),
          reads=("SP_W8",), writes=("wrep",))
    for j in range(8):
        r.dma("sync", gth[:, :], GATH[128 * j:128 * (j + 1), 0:DI],
              reads=("GATH", "gth"), writes=("gth",))
        if j == 0:
            r.op("vector", lambda e: e.tensor_tensor(
                acc_f[:, :].rearrange("p (h q) -> p h q", q=HP),
                gth[:, :].rearrange("p (h q) -> p h q", q=HP),
                bc(wrep[:, 0, :], [128, NH, HP], 2), op=ALU.mult),
                reads=("gth", "wrep", "acc_f"), writes=("acc_f",))
        else:
            r.op("vector", lambda e, j=j: e.tensor_tensor(
                tmp_f[:, :].rearrange("p (h q) -> p h q", q=HP),
                gth[:, :].rearrange("p (h q) -> p h q", q=HP),
                bc(wrep[:, j, :], [128, NH, HP], 2), op=ALU.mult),
                reads=("gth", "wrep", "tmp_f"), writes=("tmp_f",))
            r.op("vector", lambda e: e.tensor_tensor(acc_f[:, :], acc_f[:, :], tmp_f[:, :],
                                                     op=ALU.add),
                 reads=("tmp_f", "acc_f"), writes=("acc_f",))
    r.op("vector", lambda e: e.tensor_copy(S_sb[:, :], acc_f[:, :]),
         reads=("acc_f", "S_sb"), writes=("S_sb",))
    if "SP_DBGS2" in dbg:
        r.dma("sync", SP_DBGS2[:, 0:DI], cb_f[:, :], stream="cb_f", reads=("cb_f",), writes=("dbgs9",))
        r.dma("sync", SP_DBGS2[0:128, DI:DI + NH].rearrange("p c -> p c"), wrep[:, 0, :],
              reads=("wrep",), writes=("dbgs10",))
    if "SP_DBGS" in dbg:
        r.dma("sync", SP_DBGS[:, 0:DI], acc_f[:, :], stream="acc_f", reads=("acc_f",), writes=("dbgs",))
        r.dma("sync", SP_DBGS[0:8, DI:DI + NH], smC[:, :], reads=("smC",), writes=("dbgs2",))
        r.dma("sync", SP_DBGS[8:16, DI:DI + NH], a8[:, 0:NH], reads=("a8",), writes=("dbgs3",))
        r.dma("sync", SP_DBGS[16:24, DI:DI + NH], smB[:, :], reads=("smB",), writes=("dbgs4",))
    r.build()
    esC.close()

    esD = ExitStack()
    xs_p = [mk(esD, f"xs_p{i}", [128, DI], BF16) for i in range(2)]
    xd_p = [mk(esD, f"xd_p{i}", [128, DI], BF16) for i in range(2)]
    z_p = [mk(esD, f"z_p{i}", [128, DI], BF16) for i in range(2)]
    AM_sb = mk(esD, "AM_sb", [128, DI], BF16)
    M0_sb = mk(esD, "M0_sb", [128, DI], BF16)
    M_sb = mk(esD, "M_sb", [128, DI], BF16)
    Gt_sb = mk(esD, "Gt_sb", [128, 64], BF16)
    y_sb = mk(esD, "y_sb", [128, DI], F32)
    zs_sb = mk(esD, "zs_sb", [128, DI], BF16)
    yn_sb = mk(esD, "yn_sb", [128, DI], BF16)
    st2 = mk(esD, "st2", [128, 24], F32)
    mv2 = mk(esD, "mv2", [128, 2], F32)
    rs2 = mk(esD, "rs2", [128, 2], F32)
    ynF = mk(esD, "ynF", [128, 16, TT], BF16)
    at_f = mk(esD, "at_f", [128, D], F32)
    u2_t = [mk(esD, f"u2_t{i}", [128, D], F32) for i in range(2)]
    x1_t = mk(esD, "x1_t", [128, D], F32)
    x1n_t = mk(esD, "x1n_t", [128, D], BF16)
    psT = esD.enter_context(nc.psum_tensor("psT", [128, DI], F32))
    psO = esD.enter_context(nc.psum_tensor("psO", [128, DI], F32))

    r = rec()
    for pp in range(NPAIR):
        bi = pp % 2
        xsb, xdb, zb = xs_p[bi], xd_p[bi], z_p[bi]
        xsk, xdk, zk = f"xs_p{bi}", f"xd_p{bi}", f"z_p{bi}"
        r.dma("sync", xsb[:, :], SP_XS[128 * pp:128 * (pp + 1), :],
              reads=(f"SP_XS:{pp // 4}", xsk), writes=(xsk,))
        r.dma("sync", xdb[:, :], SP_XD[128 * pp:128 * (pp + 1), :],
              reads=(f"SP_XD:{pp // 4}", xdk), writes=(xdk,))
        r.dma("sync", zb[:, :], SP_Z[128 * pp:128 * (pp + 1), :],
              reads=(f"SP_Z:{pp // 4}", zk), writes=(zk,))
        for cc in range(2):
            c = 2 * pp + cc
            sl = slice(64 * cc, 64 * (cc + 1))
            amk = f"AM{cc}"
            r.op("gpsimd", lambda e, pp=pp, sl=sl: e.tensor_tensor(
                AM_sb[sl, :].rearrange("p (l h) -> p l h", h=NH),
                bc(a_TM[sl, pp, :], [64, 64, NH], 1),
                bc(trillt[sl, :], [64, 64, NH], 2), op=ALU.mult),
                reads=("a_TM", "trillt", amk), writes=(amk,))
            def f_tmm(e, pp=pp, sl=sl):
                inst = None
                for nb4 in range(4):
                    ls = slice(16 * nb4, 16 * (nb4 + 1))
                    ps = psT[sl, 512 * nb4:512 * (nb4 + 1)].rearrange("p (l h) -> p l h", h=NH)
                    e.matmul(ps, trilneg[sl, :],
                             bc(a_TMb[sl, pp, :], [64, 64, NH], 1)[:, ls, :],
                             start=True, stop=False)
                    e.matmul(ps, ones64[sl, :],
                             AM_sb[sl, :].rearrange("p (l h) -> p l h", h=NH)[:, ls, :],
                             start=False, stop=False)
                    inst = e.matmul(ps, trillt[sl, :],
                                    maskrx[sl, :].rearrange("p (l h) -> p l h", h=NH)[:, ls, :],
                                    start=False, stop=True)
                return inst
            r.op("tensor", f_tmm, reads=("a_TMb", "trilneg", "ones64", amk, "maskrx", f"psT{cc}"),
                 writes=(f"psT{cc}",))
            r.op("scalar", lambda e, sl=sl: e.activation(M0_sb[sl, :], psT[sl, :], AF.Exp),
                 reads=(f"psT{cc}", f"M0{cc}"), writes=(f"M0{cc}", f"psTr{cc}"))
            r.op("tensor", lambda e, c=c, sl=sl: e.matmul(
                psT[sl, 0:64], B_FM[:, 64 * c:64 * (c + 1)], C_FM[:, 64 * c:64 * (c + 1)],
                start=True, stop=True),
                reads=("B_FM", "C_FM", f"psTr{cc}"), writes=(f"psT{cc}",))
            r.op("vector", lambda e, sl=sl, cc=cc: e.tensor_tensor(
                Gt_sb[sl, :], psT[sl, 0:64], trilsl[sl, :], op=ALU.mult),
                reads=(f"psT{cc}", "trilsl", f"Gt{cc}"), writes=(f"Gt{cc}", f"psTr{cc}"))
            r.op("gpsimd", lambda e, sl=sl, cc=cc: e.tensor_tensor(
                M_sb[sl, :].rearrange("p (l h) -> p l h", h=NH),
                M0_sb[sl, :].rearrange("p (l h) -> p l h", h=NH),
                bc(Gt_sb[sl, :], [64, 64, NH], 2), op=ALU.mult),
                reads=(f"M0{cc}", f"Gt{cc}", f"M{cc}"), writes=(f"M{cc}",))
            def f_yoff(e, c=c, sl=sl):
                inst = None
                for h in range(NH):
                    inst = e.matmul(psO[sl, HP * h:HP * (h + 1)], C_FM[:, 64 * c:64 * (c + 1)],
                                    S_sb[:, HP * h:HP * (h + 1)], start=True, stop=True)
                return inst
            r.op("tensor", f_yoff, reads=("S_sb", "C_FM", f"psO{cc}"), writes=(f"psO{cc}",))
            r.op("vector", lambda e, sl=sl, pp=pp: e.tensor_tensor(
                y_sb[sl, :].rearrange("p (h q) -> p h q", q=HP),
                psO[sl, :].rearrange("p (h q) -> p h q", q=HP),
                bc(sdT[sl, pp, :], [64, NH, HP], 2), op=ALU.mult),
                reads=(f"psO{cc}", "sdT", f"y{cc}", "psOF"), writes=(f"y{cc}", f"psO{cc}"))
            def f_st(e, pp=pp, sl=sl, bi=bi):
                inst = None
                for h in range(NH):
                    inst = e.matmul(psO[:, HP * h:HP * (h + 1)], B_TM[sl, pp, :],
                                    xd_p[bi][sl, HP * h:HP * (h + 1)], start=True, stop=True)
                return inst
            r.op("tensor", f_st, reads=("B_TM", xdk, "psOF"), writes=("psOF", "psO0", "psO1"))
            r.op("gpsimd", lambda e, c=c: e.tensor_tensor(
                S_sb[:, :].rearrange("p (h q) -> p h q", q=HP),
                S_sb[:, :].rearrange("p (h q) -> p h q", q=HP),
                bc(D_rep[:, c, :], [128, NH, HP], 2), op=ALU.mult),
                reads=("S_sb", "D_rep"), writes=("S_sb",))
            r.op("vector", lambda e: e.tensor_tensor(S_sb[:, :], S_sb[:, :], psO[:, :],
                                                     op=ALU.add),
                 reads=("S_sb", "psOF"), writes=("S_sb", "psOF"))
            def f_yd(e, sl=sl, bi=bi):
                inst = None
                for h in range(NH):
                    inst = e.matmul(psT[sl, HP * h:HP * (h + 1)],
                                    M_sb[sl, :].rearrange("p (l h) -> p l h", h=NH)[:, :, h],
                                    xs_p[bi][sl, HP * h:HP * (h + 1)], start=True, stop=True)
                return inst
            r.op("tensor", f_yd, reads=(f"M{cc}", xsk, f"psTr{cc}"), writes=(f"psT{cc}",))
            r.op("vector", lambda e, sl=sl, cc=cc: e.tensor_tensor(
                y_sb[sl, :], y_sb[sl, :], psT[sl, :], op=ALU.add),
                reads=(f"psT{cc}", f"y{cc}"), writes=(f"y{cc}", f"psTr{cc}"))
        r.op("scalar", lambda e, zb=zb: e.activation(zs_sb[:, :], zb[:, :], AF.Silu),
             reads=(zk, "zs"), writes=("zs",))
        r.op("gpsimd", lambda e: e.tensor_tensor(y_sb[:, :], y_sb[:, :], zs_sb[:, :],
                                                 op=ALU.mult),
             reads=("y0", "y1", "zs"), writes=("y0", "y1"))
        ln_apply(r, "ln", ("y0", "y1"), [y_sb[:, 512 * i:512 * (i + 1)] for i in range(4)],
                 st2, mv2, rs2)
        r.op("scalar", lambda e: e.activation(yn_sb[:, :], y_sb[:, :], AF.Identity,
                                              scale=rs2[:, 0:1], bias=rs2[:, 1:2]),
             reads=("y0", "y1", "lnrs", "yn"), writes=("yn",))
        r.dma("scalar", SP_YN[128 * pp:128 * (pp + 1), :], yn_sb[:, :],
              stream="yn", reads=("yn",), writes=(f"SP_YN:{pp // 4}",))
        if pp % 4 == 3:
            t = pp // 4
            r.xpose(ynF[:, :, :], SP_YN[TT * t:TT * (t + 1), :],
                    reads=(f"SP_YN:{t}", "ynF"), writes=("ynF",))
            for g in range(4):
                ub, ubk = u2_t[g % 2], f"u2_t{g % 2}"
                r.dma("sync", ub[:, :],
                      u_seg[TT * t + 128 * g:TT * t + 128 * (g + 1), :],
                      reads=(ubk,), writes=(ubk,))
                def f_op(e, g=g):
                    inst = None
                    for n2 in range(2):
                        for k in range(16):
                            e.matmul(psO[:, 512 * n2:512 * (n2 + 1)],
                                     ynF[:, k, 128 * g:128 * (g + 1)],
                                     W_out_sb[:, k, 512 * n2:512 * (n2 + 1)],
                                     start=(k == 0), stop=False)
                        inst = e.matmul(psO[:, 512 * n2:512 * (n2 + 1)], onesr[:, :],
                                        battn[:, 512 * n2:512 * (n2 + 1)],
                                        start=False, stop=True)
                    return inst
                r.op("tensor", f_op,
                     reads=("ynF", "battn", "onesr", "psOF", "W_out_sb"),
                     writes=("psOF", "psO0", "psO1"))
                ln_apply(r, "ln", ("psOF",), [psO[:, 0:512], psO[:, 512:1024]], st2, mv2, rs2)
                def f_atn(e):
                    e.activation(at_f[:, 0:512], psO[:, 0:512], AF.Identity,
                                 scale=rs2[:, 0:1], bias=rs2[:, 1:2])
                    return e.activation(at_f[:, 512:1024], psO[:, 512:1024], AF.Identity,
                                        scale=rs2[:, 0:1], bias=rs2[:, 1:2])
                r.op("scalar", f_atn, reads=("psOF", "lnrs", "at_f"), writes=("at_f", "psOF"))
                r.op("vector", lambda e, g=g, ub=ub: e.tensor_tensor(
                    x1_t[:, :], at_f[:, :], ub[:, :], op=ALU.add),
                    reads=("at_f", ubk, "x1_t"), writes=("x1_t",))
                ln_apply(r, "ln", ("x1_t",), [x1_t[:, 0:512], x1_t[:, 512:1024]],
                         st2, mv2, rs2)
                r.op("scalar", lambda e, g=g: e.activation(
                    x1n_t[:, :], x1_t[:, :], AF.Identity,
                    scale=rs2[:, 0:1], bias=rs2[:, 1:2]),
                    reads=("x1_t", "lnrs", "x1n_t"), writes=("x1n_t",))
                r.dma("scalar", SP_X1[TT * t + 128 * g:TT * t + 128 * (g + 1), :],
                      x1_t[:, :], stream="x1_t", reads=("x1_t",), writes=(f"SP_X1:{t}",))
                r.dma("scalar", SP_X1N[TT * t + 128 * g:TT * t + 128 * (g + 1), :],
                      x1n_t[:, :], stream="x1n_t", reads=("x1n_t",), writes=(f"SP_X1N:{t}",))
    r.build()
    esD.close()
    es_b.close()
    es_mid.close()

    esE = ExitStack()
    wgT = mk(esE, "wgT", [128, 8, HID], BF16)
    wuT = mk(esE, "wuT", [128, 8, HID], BF16)
    wdT = mk(esE, "wdT", [128, 24, D], BF16)
    x1F = [mk(esE, f"x1F{i}", [128, 8, TT], BF16) for i in range(2)]
    gs_sb = mk(esE, "gs_sb", [128, 512], BF16)
    h_sb = mk(esE, "h_sb", [128, 24, TT], BF16)
    x1_2 = mk(esE, "x1_2", [128, 2, D], F32)
    o_sb = mk(esE, "o_sb", [128, D], F32)
    stf = mk(esE, "stf", [128, 12], F32)
    mvf = mk(esE, "mvf", [128, 2], F32)
    rsf = mk(esE, "rsf", [128, 2], F32)
    psG2 = esE.enter_context(nc.psum_tensor("psG2", [128, 2, 512], F32))
    psU2 = esE.enter_context(nc.psum_tensor("psU2", [128, 2, 512], F32))
    psD2 = esE.enter_context(nc.psum_tensor("psD2", [128, 4, 512], F32))

    r = rec()
    r.xpose(wgT[:, :, :], SP_WGN,
            reads=tuple(f"SP_WGN:{r0}" for r0 in range(0, HID, 1024)), writes=("wgT",))
    r.xpose(wuT[:, :, :], SP_WUN,
            reads=tuple(f"SP_WUN:{r0}" for r0 in range(0, HID, 1024)), writes=("wuT",))
    r.xpose(wdT[:, :, :], SP_WDN,
            reads=tuple(f"SP_WDN:{r0}" for r0 in range(0, D, 256)), writes=("wdT",))
    for t in range(NT):
        xb, xk = x1F[t % 2], f"x1F{t % 2}"
        r.xpose(xb[:, :, :], SP_X1N[TT * t:TT * (t + 1), :],
                reads=(f"SP_X1N:{t}", xk), writes=(xk,))
        for m in range(24):
            pb = m % 2
            def f_gmm(e, m=m, pb=pb, xb=xb):
                inst = None
                for k in range(8):
                    inst = e.matmul(psG2[:, pb, :], wgT[:, k, 128 * m:128 * (m + 1)],
                                    xb[:, k, :], start=(k == 0), stop=(k == 7))
                return inst
            r.op("tensor", f_gmm, reads=(xk, "wgT", f"psG2{pb}"), writes=(f"psG2{pb}",))
            def f_umm(e, m=m, pb=pb, xb=xb):
                inst = None
                for k in range(8):
                    inst = e.matmul(psU2[:, pb, :], wuT[:, k, 128 * m:128 * (m + 1)],
                                    xb[:, k, :], start=(k == 0), stop=(k == 7))
                return inst
            r.op("tensor", f_umm, reads=(xk, "wuT", f"psU2{pb}"), writes=(f"psU2{pb}",))
            r.op("scalar", lambda e, pb=pb: e.activation(gs_sb[:, :], psG2[:, pb, :],
                                                         AF.Silu, scale=1.0 / 127.0),
                 reads=(f"psG2{pb}", "gs"), writes=("gs", f"psG2{pb}"))
            r.op("vector", lambda e, m=m, pb=pb: e.tensor_tensor(
                h_sb[:, m, :], gs_sb[:, :], psU2[:, pb, :], op=ALU.mult),
                reads=("gs", f"psU2{pb}", "h_sb"), writes=("h_sb", f"psU2{pb}"))
        for g in range(4):
            gp = g % 2
            if g % 2 == 0:
                r.dma("sync", x1_2[:, :, :],
                      SP_X1[TT * t + 128 * g:TT * t + 128 * (g + 2), :]
                      .rearrange("(g2 p) d -> p g2 d", p=128),
                      stream="x1_2", reads=("x1_2",), writes=("x1_2",))
            def f_dmm(e, g=g, gp=gp):
                inst = None
                for n2 in range(2):
                    for k in range(24):
                        inst = e.matmul(psD2[:, 2 * gp + n2, :],
                                        h_sb[:, k, 128 * g:128 * (g + 1)],
                                        wdT[:, k, 512 * n2:512 * (n2 + 1)],
                                        start=(k == 0), stop=(k == 23))
                return inst
            r.op("tensor", f_dmm, reads=("h_sb", "wdT", f"psD2{gp}"), writes=(f"psD2{gp}",))
            def f_stats(e, gp=gp):
                e.bn_stats(stf[:, 0:6], psD2[:, 2 * gp, :])
                return e.bn_stats(stf[:, 6:12], psD2[:, 2 * gp + 1, :])
            r.op("vector", lambda e, gp=gp: e.bn_stats(stf[:, 0:6], psD2[:, 2 * gp, :]),
                 reads=(f"psD2{gp}", "stf"), writes=("stf",))
            r.op("vector", lambda e, gp=gp: e.bn_stats(stf[:, 6:12], psD2[:, 2 * gp + 1, :]),
                 reads=(f"psD2{gp}", "stf"), writes=("stf",))
            r.op("vector", lambda e: e.bn_aggr(mvf[:, :], stf[:, 0:12]),
                 reads=("stf", "mvf"), writes=("mvf",))
            r.op("scalar", lambda e: e.activation(rsf[:, 0:1], mvf[:, 1:2], AF.Ln,
                                                  bias=cst[:, 0:1]),
                 reads=("mvf", "rsf", "cst"), writes=("rsf",))
            r.op("scalar", lambda e: e.activation(rsf[:, 0:1], rsf[:, 0:1], AF.Exp,
                                                  scale=-0.5),
                 reads=("rsf",), writes=("rsf",))
            r.op("vector", lambda e: e.scalar_tensor_tensor(
                rsf[:, 1:2], mvf[:, 0:1], -1.0, rsf[:, 0:1], op0=ALU.mult, op1=ALU.mult),
                reads=("mvf", "rsf"), writes=("rsf",))
            def f_lnf(e, gp=gp):
                e.activation(o_sb[:, 0:512], psD2[:, 2 * gp, :], AF.Identity,
                             scale=rsf[:, 0:1], bias=rsf[:, 1:2])
                return e.activation(o_sb[:, 512:1024], psD2[:, 2 * gp + 1, :], AF.Identity,
                                    scale=rsf[:, 0:1], bias=rsf[:, 1:2])
            r.op("scalar", f_lnf, reads=(f"psD2{gp}", "rsf", "o_sb"),
                 writes=("o_sb", f"psD2{gp}"))
            r.op("vector", lambda e, g=g: e.tensor_tensor(
                o_sb[:, :], o_sb[:, :], x1_2[:, g % 2, :], op=ALU.add),
                reads=("x1_2", "o_sb"), writes=("o_sb",))
            r.dma("scalar",
                  out_seg[TT * t + 128 * g:TT * t + 128 * (g + 1), :], o_sb[:, :],
                  stream="o_sb", reads=("o_sb",), writes=(f"OUT:{t}",))
    r.op("gpsimd", lambda e: None, reads=tuple(f"OUT:{t}" for t in range(NT)), writes=())
    r.build()
    esE.close()
    es_g.close()
    return nc


def kernel(L0_attn_u, L0_attn_W_in, L0_attn_conv_w, L0_attn_conv_b,
           L0_attn_dt_bias, L0_attn_A_log, L0_attn_norm_w, L0_attn_norm_b,
           L0_attn_W_out, L0_ffn_w_gate, L0_ffn_w_up, L0_ffn_w_down):
    u = np.ascontiguousarray(np.asarray(L0_attn_u, np.float32))
    consts = _np_consts()
    nc = build_nc(dbg=DBG)
    in_maps = []
    for core in range(8):
        b, q = core // 4, core % 4
        seg = np.ascontiguousarray(u[b, T * q:T * (q + 1)])
        halo = np.zeros((16, D), np.float32)
        if q > 0:
            halo[13:16] = u[b, T * q - 3:T * q]
        sel2 = np.zeros((8, 8), np.float32)
        gmask = np.zeros((8, 1), np.float32)
        for j in range(8):
            if j // 4 == b and j % 4 < q:
                gmask[j, 0] = 1.0
                for i in range(8):
                    if i // 4 == b and j % 4 < i % 4 < q:
                        sel2[i, j] = 1.0
        m = dict(u_seg=seg, u_halo=halo,
                 W_in=np.ascontiguousarray(np.asarray(L0_attn_W_in, np.float32)),
                 conv_w=np.ascontiguousarray(np.asarray(L0_attn_conv_w, np.float32)),
                 conv_b=np.ascontiguousarray(np.asarray(L0_attn_conv_b, np.float32)),
                 dt_bias=np.ascontiguousarray(np.asarray(L0_attn_dt_bias, np.float32)),
                 A_log=np.ascontiguousarray(np.asarray(L0_attn_A_log, np.float32)),
                 norm_w=np.ascontiguousarray(np.asarray(L0_attn_norm_w, np.float32)),
                 norm_b=np.ascontiguousarray(np.asarray(L0_attn_norm_b, np.float32)),
                 W_out=np.ascontiguousarray(np.asarray(L0_attn_W_out, np.float32)),
                 wg=np.ascontiguousarray(np.asarray(L0_ffn_w_gate, np.int32)),
                 wu=np.ascontiguousarray(np.asarray(L0_ffn_w_up, np.int32)),
                 wd=np.ascontiguousarray(np.asarray(L0_ffn_w_down, np.int32)),
                 SEL2=sel2, GMASK=gmask, **consts)
        in_maps.append(m)
    res = run_bass_kernel_spmd(nc, in_maps, core_ids=list(range(8)))
    out = np.empty((2, 16384, D), np.float32)
    for core in range(8):
        b, q = core // 4, core % 4
        out[b, T * q:T * (q + 1)] = res.results[core]["out_seg"]
    return out


# version 12
# speedup vs baseline: 1.0624x; 1.0001x over previous
import numpy as np
import ml_dtypes
from contextlib import ExitStack

import concourse.bass as bass
import concourse.mybir as mybir
from concourse.bass_utils import run_bass_kernel_spmd

F32 = mybir.dt.float32
BF16 = mybir.dt.bfloat16
I32 = mybir.dt.int32
AF = mybir.ActivationFunctionType
ALU = mybir.AluOpType
AX = mybir.AxisListType

T = 4096
D = 1024
DI = 2048
DS = 128
NH = 32
HP = 64
CS = 64
NCK = T // CS
NPAIR = NCK // 2
DPROJ = 4384
HID = 3072
EPS = 1e-5
BIG = 1e30
NT = 8
TT = 512

ENGINES = ("sync", "tensor", "vector", "scalar", "gpsimd")
DMA_ENGINES = ("sync", "scalar", "gpsimd")

DBG = []


class Rec:

    def __init__(self, nc, getsem, ctr, shared):
        self.nc = nc
        self.getsem = getsem
        self.ctr = ctr
        self.last_w, self.readers = shared
        self.ops = []

    def _rec(self, engine, fn, own, inc, reads, writes, forced=()):
        deps = {}
        for k in reads:
            h = self.last_w.get(k)
            if h is not None:
                deps[h] = True
        for k in writes:
            h = self.last_w.get(k)
            if h is not None:
                deps[h] = True
            for rh in self.readers.get(k, ()):
                deps[rh] = True
        for h in forced:
            if h is not None:
                deps[h] = True
        self.getsem(own)
        self.ctr[own] = self.ctr.get(own, 0) + inc
        handle = (own, self.ctr[own])
        emit_deps = [h for h in deps
                     if not (h[0] == own and h[1] >= handle[1])]
        self.ops.append((engine, fn, sorted(emit_deps), own, inc))
        for k in writes:
            self.last_w[k] = handle
            self.readers[k] = []
        for k in reads:
            self.readers.setdefault(k, []).append(handle)
        return handle

    def op(self, engine, fn, reads=(), writes=(), inc=1):
        return self._rec(engine, fn, "eng:" + engine, inc, reads, writes)

    def dma(self, engine, out, in_, stream=None, reads=(), writes=(), **kw):
        if stream is None:
            stream = (tuple(writes) + tuple(reads) + ("misc",))[0]
        xp = self.last_w.get("XPOSE")
        return self._rec(engine, lambda e: e.dma_start(out, in_, **kw),
                         "dma:" + stream, 16, reads, writes,
                         forced=[xp] if xp else ())

    def xpose(self, out, in_, reads=(), writes=()):
        forced = [(s, v) for s, v in self.ctr.items()
                  if s.startswith("dma:") and v > 0]
        return self._rec("sync", lambda e: e.dma_start_transpose(out, in_),
                         "dma:xpose", 16, reads, tuple(writes) + ("XPOSE",),
                         forced=forced)

    def build(self):
        getsem, ops = self.getsem, self.ops
        with self.nc.Block() as blk:
            for eng_name in ENGINES:
                mine = [(f, d, own, inc) for (e, f, d, own, inc) in ops
                        if e == eng_name]
                if not mine:
                    continue

                def emit(eng, mine=mine):
                    waited = {}
                    for fn, deps, own, inc in mine:
                        for (sname, pval) in deps:
                            if waited.get(sname, -1) < pval:
                                eng.wait_ge(getsem(sname), pval)
                                waited[sname] = pval
                        inst = fn(eng)
                        if inst is None:
                            inst = eng.nop()
                        if inc:
                            inst.then_inc(getsem(own), inc)
                getattr(blk, eng_name)(emit)


def _np_consts():
    i = np.arange(64)
    tril_is = (i[:, None] <= i[None, :]).astype(np.float32)
    dup = lambda a: np.concatenate([a, a], 0)
    bf = ml_dtypes.bfloat16
    return dict(
        TRILNEG=dup(-tril_is).astype(bf),
        TRILLT=dup(tril_is).astype(bf),
        MASKRX=dup(np.repeat(-BIG * (i[None, :] == i[:, None] - 1).astype(np.float32)[:, :, None], 32, axis=2).reshape(64, 2048)).astype(bf),
        ONES64=np.ones((128, 64), np.float32).astype(bf),
        TRILSL=dup((i[None, :] >= i[:, None]).astype(np.float32)),
        IDENT=np.eye(128, dtype=np.float32),
        ONESR=np.ones((1, 128), np.float32).astype(bf),
        IDENTB=np.eye(128, dtype=np.float32).astype(bf),
        CONSTS=np.array([[1e-5, 1.0, 0.0, 0.0]], np.float32),
    )


def bc(ap, shape, axis):
    return ap.unsqueeze(axis).broadcast_to(shape)


def build_nc(dbg=()):
    dbg = set(dbg)
    nc = bass.Bass("TRN2", target_bir_lowering=False, num_devices=8)

    u_seg = nc.dram_tensor("u_seg", [T, D], F32, kind="ExternalInput").ap()
    u_halo = nc.dram_tensor("u_halo", [16, D], F32, kind="ExternalInput").ap()
    W_in = nc.dram_tensor("W_in", [D, DPROJ], F32, kind="ExternalInput").ap()
    conv_w = nc.dram_tensor("conv_w", [2304, 4], F32, kind="ExternalInput").ap()
    conv_b = nc.dram_tensor("conv_b", [2304], F32, kind="ExternalInput").ap()
    dt_bias = nc.dram_tensor("dt_bias", [NH], F32, kind="ExternalInput").ap()
    A_log = nc.dram_tensor("A_log", [NH], F32, kind="ExternalInput").ap()
    norm_w = nc.dram_tensor("norm_w", [DI], F32, kind="ExternalInput").ap()
    norm_b = nc.dram_tensor("norm_b", [DI], F32, kind="ExternalInput").ap()
    W_out = nc.dram_tensor("W_out", [DI, D], F32, kind="ExternalInput").ap()
    wg = nc.dram_tensor("wg", [HID, D], I32, kind="ExternalInput").ap()
    wu = nc.dram_tensor("wu", [HID, D], I32, kind="ExternalInput").ap()
    wd = nc.dram_tensor("wd", [D, HID], I32, kind="ExternalInput").ap()
    TRILNEG = nc.dram_tensor("TRILNEG", [128, 64], BF16, kind="ExternalInput").ap()
    TRILLT = nc.dram_tensor("TRILLT", [128, 64], BF16, kind="ExternalInput").ap()
    MASKRX = nc.dram_tensor("MASKRX", [128, 2048], BF16, kind="ExternalInput").ap()
    ONES64 = nc.dram_tensor("ONES64", [128, 64], BF16, kind="ExternalInput").ap()
    TRILSL = nc.dram_tensor("TRILSL", [128, 64], F32, kind="ExternalInput").ap()
    IDENT = nc.dram_tensor("IDENT", [128, 128], F32, kind="ExternalInput").ap()
    ONESR = nc.dram_tensor("ONESR", [1, 128], BF16, kind="ExternalInput").ap()
    IDENTB = nc.dram_tensor("IDENTB", [128, 128], BF16, kind="ExternalInput").ap()
    CONSTS = nc.dram_tensor("CONSTS", [1, 4], F32, kind="ExternalInput").ap()
    SEL2 = nc.dram_tensor("SEL2", [8, 8], F32, kind="ExternalInput").ap()
    GMASK = nc.dram_tensor("GMASK", [8, 1], F32, kind="ExternalInput").ap()
    out_seg = nc.dram_tensor("out_seg", [T, D], F32, kind="ExternalOutput").ap()

    def scratch(name, shape, dtype):
        kind = "ExternalOutput" if name in dbg else "Internal"
        return nc.dram_tensor(name, shape, dtype, kind=kind).ap()

    SP_X0N = scratch("SP_X0N", [T, D], BF16)
    SP_X0NH = scratch("SP_X0NH", [16, D], BF16)
    SP_Z = scratch("SP_Z", [T, DI], BF16)
    SP_XC = scratch("SP_XC", [2176, T], BF16)
    SP_XS = scratch("SP_XS", [T, DI], BF16)
    SP_XD = scratch("SP_XD", [T, DI], BF16)
    SP_YN = scratch("SP_YN", [T, DI], BF16)
    SP_X1 = scratch("SP_X1", [T, D], F32)
    SP_X1N = scratch("SP_X1N", [T, D], BF16)
    SP_H = scratch("SP_H", [HID, T], BF16)
    SP_WGN = scratch("SP_WGN", [HID, D], BF16)
    SP_WUN = scratch("SP_WUN", [HID, D], BF16)
    SP_WDN = scratch("SP_WDN", [D, HID], BF16)
    SP_SB = scratch("SP_SB", [192, NH], F32)
    SP_NEGA = scratch("SP_NEGA", [NH, 1], F32)
    SP_W8 = scratch("SP_W8", [8, NH], F32)
    CBT = scratch("CBT", [128, 2049], F32)
    SP_DBGS = scratch("SP_DBGS", [128, 2049 + 64], F32)
    SP_DBGS2 = scratch("SP_DBGS2", [128, 2049 + 64], F32)
    GATH = nc.dram_tensor("GATH", [1024, 2049], F32, kind="Internal", addr_space="Shared").ap()

    es_g = ExitStack()
    es_mid = ExitStack()
    es_b = ExitStack()

    def mk(stack, name, shape, dtype=F32):
        return stack.enter_context(nc.sbuf_tensor(name, shape, dtype))

    trilneg = mk(es_g, "trilneg", [128, 64], BF16)
    trillt = mk(es_g, "trillt", [128, 64], BF16)
    maskrx = mk(es_g, "maskrx", [128, 2048], BF16)
    ones64 = mk(es_g, "ones64", [128, 64], BF16)
    trilsl = mk(es_g, "trilsl", [128, 64], F32)
    ident = mk(es_g, "ident", [128, 128], F32)
    onesr = mk(es_g, "onesr", [1, 128], BF16)
    identb = mk(es_g, "identb", [128, 128], BF16)
    cst = mk(es_g, "cst", [128, 4], F32)
    negA_col = mk(es_g, "negA_col", [NH, 1], F32)
    negA_rep = mk(es_g, "negA_rep", [128, NH], F32)
    dtb_rep = mk(es_g, "dtb_rep", [128, NH], F32)
    convw_sb = mk(es_g, "convw_sb", [128, 18, 4], F32)
    convb_sb = mk(es_g, "convb_sb", [128, 18], F32)

    B_FM = mk(es_mid, "B_FM", [128, T], BF16)
    C_FM = mk(es_mid, "C_FM", [128, T], BF16)
    dt_TM = mk(es_mid, "dt_TM", [128, 32, NH], F32)
    a_TM = mk(es_mid, "a_TM", [128, 32, NH], F32)
    a_TMb = mk(es_mid, "a_TMb", [128, 32, NH], BF16)
    atot_HM = mk(es_mid, "atot_HM", [NH, NCK], F32)
    cpi_HM = mk(es_mid, "cpi_HM", [NH, NCK], F32)
    rc_HM = mk(es_mid, "rc_HM", [NH, NCK], F32)
    b2_HM = mk(es_mid, "b2_HM", [NH, NCK], F32)
    dexp_HM = mk(es_mid, "dexp_HM", [NH, NCK], F32)
    aseg_col = mk(es_mid, "aseg_col", [NH, 1], F32)
    cs_TM = mk(es_mid, "cs_TM", [128, 32, NH], F32)
    dsT = mk(es_mid, "dsT", [128, 32, NH], F32)
    ds2T = mk(es_mid, "ds2T", [128, 32, NH], F32)
    sdT = mk(es_mid, "sdT", [128, 32, NH], F32)

    sem_map = {}
    sem_ctr = {}

    def getsem(name):
        if name not in sem_map:
            sem_map[name] = es_g.enter_context(
                nc.semaphore("s_" + name.replace(":", "_")))
        return sem_map[name]

    cc_sem = es_g.enter_context(nc.semaphore("sem_cc"))
    shared = ({}, {})

    def rec():
        return Rec(nc, getsem, sem_ctr, shared)

    def ln_apply(r, pre, stat_reads, stat_aps, st, mv, rs, npart=128):
        for i, ap in enumerate(stat_aps):
            r.op("vector", lambda e, i=i, ap=ap: e.bn_stats(st[:, 6 * i:6 * (i + 1)], ap),
                 reads=tuple(stat_reads) + (pre + "st",), writes=(pre + "st",))
        r.op("vector", lambda e: e.bn_aggr(mv[:, :], st[:, 0:6 * len(stat_aps)]),
             reads=(pre + "st", pre + "mv"), writes=(pre + "mv",))
        r.op("scalar", lambda e: e.activation(rs[:, 0:1], mv[:, 1:2], AF.Ln,
                                              bias=cst[0:npart, 0:1]),
             reads=(pre + "mv", pre + "rs", "cst"), writes=(pre + "rs",))
        r.op("scalar", lambda e: e.activation(rs[:, 0:1], rs[:, 0:1], AF.Exp, scale=-0.5),
             reads=(pre + "rs",), writes=(pre + "rs",))
        r.op("vector", lambda e: e.scalar_tensor_tensor(
            rs[:, 1:2], mv[:, 0:1], -1.0, rs[:, 0:1], op0=ALU.mult, op1=ALU.mult),
            reads=(pre + "mv", pre + "rs"), writes=(pre + "rs",))

    es_win = ExitStack()
    W_in_sb = es_win.enter_context(nc.sbuf_tensor("W_in_sb", [128, 8, DPROJ], BF16))
    es0 = ExitStack()
    w32 = [mk(es0, f"w32_{i}", [128, DPROJ], F32) for i in range(2)]
    w1 = mk(es0, "w1", [NH, 2], F32)

    r = rec()
    for dst, src in ((trilneg, TRILNEG), (trillt, TRILLT), (maskrx, MASKRX),
                     (ones64, ONES64), (trilsl, TRILSL), (ident, IDENT),
                     (onesr, ONESR), (identb, IDENTB)):
        r.dma("sync", dst[:, :], src, writes=(dst.name,))
    r.dma("sync", cst[:, :], bc(CONSTS.rearrange("one c -> (one c)"), [128, 4], 0), writes=("cst",))
    r.dma("sync", dtb_rep[:, :], bc(dt_bias, [128, NH], 0), writes=("dtb_rep",))
    r.dma("sync", convw_sb[:, :, :], conv_w.rearrange("(m p) k -> p m k", p=128),
          writes=("convw",))
    r.dma("sync", convb_sb[:, :], conv_b.rearrange("(m p) -> p m", p=128),
          writes=("convb",), allow_slow_non_contiguous=True)
    r.dma("sync", w1[:, 0:1], A_log.unsqueeze(1), writes=("w1",))
    r.op("scalar", lambda e: e.activation(w1[:, 1:2], w1[:, 0:1], AF.Exp),
         reads=("w1",), writes=("w1e",))
    r.op("scalar", lambda e: e.mul(negA_col[:, :], w1[:, 1:2], -1.0),
         reads=("w1e",), writes=("negA_col",))
    r.dma("sync", SP_NEGA, negA_col[:, :], reads=("negA_col",), writes=("SP_NEGA",))
    r.dma("sync", negA_rep[:, :],
          bc(SP_NEGA.rearrange("h one -> (h one)"), [128, NH], 0),
          reads=("SP_NEGA",), writes=("negA_rep",))
    for k in range(8):
        buf, key = w32[k % 2], f"w32_{k % 2}"
        r.dma("sync", buf[:, :], W_in[128 * k:128 * (k + 1), :], writes=(key,))
        r.op("scalar",
             lambda e, k=k, buf=buf: e.activation(W_in_sb[:, k, :], buf[:, :], AF.Identity),
             reads=(key,), writes=("W_in_sb",))
    esA = ExitStack()
    u_t = [mk(esA, f"u_t{i}", [128, 4, D], F32) for i in range(2)]
    x0_t = [mk(esA, f"x0_t{i}", [128, 4, D], BF16) for i in range(2)]
    st_t = mk(esA, "st_t", [128, 12], F32)
    mv_t = mk(esA, "mv_t", [128, 2], F32)
    rs_t = mk(esA, "rs_t", [128, 2], F32)

    r.dma("sync", u_t[0][0:16, 0, :], u_halo, writes=("u_t0",))
    ln_apply(r, "a", ("u_t0",), [u_t[0][0:16, 0, 0:512], u_t[0][0:16, 0, 512:1024]],
             st_t[0:16, :], mv_t[0:16, :], rs_t[0:16, :], npart=16)
    r.op("scalar", lambda e: e.activation(x0_t[0][0:16, 0, :], u_t[0][0:16, 0, :],
                                          AF.Identity, scale=rs_t[0:16, 0:1],
                                          bias=rs_t[0:16, 1:2]),
         reads=("u_t0", "ars", "x0_t0"), writes=("x0_t0",))
    r.dma("scalar", SP_X0NH, x0_t[0][0:16, 0, :], stream="x0_t0", reads=("x0_t0",), writes=("SP_X0NH",))
    for t in range(NT):
        ub, xb = u_t[t % 2], x0_t[t % 2]
        uk, xk = f"u_t{t % 2}", f"x0_t{t % 2}"
        r.dma("sync", ub[:, :, :],
              u_seg[TT * t:TT * (t + 1), :].rearrange("(g p) d -> p g d", p=128),
              reads=(uk,), writes=(uk,))
        for g in range(4):
            ln_apply(r, "a", (uk,), [ub[:, g, 0:512], ub[:, g, 512:1024]], st_t, mv_t, rs_t)
            r.op("scalar", lambda e, g=g, ub=ub, xb=xb: e.activation(
                xb[:, g, :], ub[:, g, :], AF.Identity,
                scale=rs_t[:, 0:1], bias=rs_t[:, 1:2]),
                reads=(uk, "ars", xk), writes=(xk,))
        r.dma("scalar", SP_X0N[TT * t:TT * (t + 1), :].rearrange("(g p) d -> p g d", p=128),
              xb[:, :, :], stream=xk, reads=(xk,), writes=(f"SP_X0N:{t}",))
    r.build()
    esA.close()
    es0.close()

    esB = ExitStack()
    x0F = [mk(esB, f"x0F{i}", [128, 8, TT], BF16) for i in range(2)]
    x0hF = mk(esB, "x0hF", [128, 8, 16], BF16)
    z_t = [mk(esB, f"z_t{i}", [128, 1024], BF16) for i in range(2)]
    xc = mk(esB, "xc", [128, 18, 515], BF16)
    tail_b = mk(esB, "tail_b", [128, 18, 3], BF16)
    cacc = mk(esB, "cacc", [128, 2, TT], F32)
    hm_buf = mk(esB, "hm_buf", [NH, T], F32)
    cs_HM = mk(esB, "cs_HM", [NH, T], F32)
    sp1 = mk(esB, "sp1", [128, 32, NH], F32)
    sp2 = mk(esB, "sp2", [128, 32, NH], F32)
    psZ = esB.enter_context(nc.psum_tensor("psZ", [128, 2, 1024], F32))
    psX = esB.enter_context(nc.psum_tensor("psX", [128, 2, 512], F32))
    psM = esB.enter_context(nc.psum_tensor("psM", [128, 128], F32))

    WINK = ("W_in_sb",)
    r = rec()
    r.xpose(x0hF[:, :, :], SP_X0NH, reads=("SP_X0NH",), writes=("x0hF",))
    for m in range(18):
        def f_hmm(e, m=m):
            inst = None
            for k in range(8):
                inst = e.matmul(psM[:, 0:16], W_in_sb[:, k, DI + 128 * m:DI + 128 * (m + 1)],
                                x0hF[:, k, :], start=(k == 0), stop=(k == 7))
            return inst
        r.op("tensor", f_hmm, reads=("x0hF", "psM") + WINK, writes=("psM",))
        r.op("scalar", lambda e, m=m: e.activation(tail_b[:, m, :], psM[:, 13:16], AF.Identity),
             reads=("psM", "tail"), writes=("tail",))

    for t in range(NT):
        x0b, x0k = x0F[t % 2], f"x0F{t % 2}"
        r.xpose(x0b[:, :, :], SP_X0N[TT * t:TT * (t + 1), :],
                reads=(f"SP_X0N:{t}", x0k), writes=(x0k,))
        xcb, xck = xc, "xc"
        for g in range(4):
            for n2 in range(2):
                zb, zbk = z_t[n2], f"z_t{n2}"
                def f_zmm(e, g=g, n2=n2, x0b=x0b):
                    inst = None
                    for half in range(2):
                        for k in range(8):
                            inst = e.matmul(psZ[:, n2, 512 * half:512 * (half + 1)],
                                            x0b[:, k, 128 * g:128 * (g + 1)],
                                            W_in_sb[:, k, 1024 * n2 + 512 * half:1024 * n2 + 512 * (half + 1)],
                                            start=(k == 0), stop=(k == 7))
                    return inst
                r.op("tensor", f_zmm, reads=(x0k, f"psZ{n2}") + WINK, writes=(f"psZ{n2}",))
                r.op("scalar", lambda e, n2=n2, zb=zb: e.activation(
                    zb[:, :], psZ[:, n2, :], AF.Identity),
                    reads=(f"psZ{n2}", zbk), writes=(zbk, f"psZ{n2}"))
                r.dma("scalar", SP_Z[TT * t + 128 * g:TT * t + 128 * (g + 1),
                                     1024 * n2:1024 * (n2 + 1)], zb[:, :],
                      stream=zbk, reads=(zbk,), writes=(f"SP_Z:{t // 4}",))
        for g in range(4):
            def f_dtmm(e, g=g, x0b=x0b):
                inst = None
                for k in range(8):
                    inst = e.matmul(psM[:, 0:NH], x0b[:, k, 128 * g:128 * (g + 1)],
                                    W_in_sb[:, k, DPROJ - NH:DPROJ],
                                    start=(k == 0), stop=(k == 7))
                return inst
            r.op("tensor", f_dtmm, reads=(x0k, "psM") + WINK, writes=("psM",))
            r.op("vector", lambda e, t=t, g=g: e.tensor_copy(dt_TM[:, 4 * t + g, :],
                                                             psM[:, 0:NH]),
                 reads=("psM", "dt_TM"), writes=("dt_TM",))
        for m in range(18):
            pb = m % 2
            def f_xmm(e, m=m, pb=pb, x0b=x0b):
                inst = None
                for k in range(8):
                    inst = e.matmul(psX[:, pb, :],
                                    W_in_sb[:, k, DI + 128 * m:DI + 128 * (m + 1)],
                                    x0b[:, k, :], start=(k == 0), stop=(k == 7))
                return inst
            r.op("tensor", f_xmm, reads=(x0k, f"psX{pb}") + WINK, writes=(f"psX{pb}",))
            r.op("scalar", lambda e, m=m, pb=pb, xcb=xcb: e.activation(
                xcb[:, m, 3:515], psX[:, pb, :], AF.Identity),
                reads=(f"psX{pb}", f"{xck}:{m}"), writes=(f"{xck}:{m}", f"psX{pb}"))
        r.op("vector", lambda e, xcb=xcb: e.tensor_copy(xcb[:, :, 0:3], tail_b[:, :, :]),
             reads=("tail",) + tuple(f"{xck}:{m}" for m in range(18)),
             writes=tuple(f"{xck}:{m}t" for m in range(18)))
        for m in range(18):
            ca, cak = cacc[:, m % 2, :], f"cacc{m % 2}"
            xmk = (f"{xck}:{m}", f"{xck}:{m}t")
            r.op("vector", lambda e, m=m, ca=ca, xcb=xcb: e.tensor_scalar(
                ca, xcb[:, m, 0:512], convw_sb[:, m, 0:1], convb_sb[:, m:m + 1],
                op0=ALU.mult, op1=ALU.add),
                reads=xmk + ("convw", "convb", cak), writes=(cak,))
            for i in range(1, 4):
                r.op("vector", lambda e, m=m, i=i, ca=ca, xcb=xcb: e.scalar_tensor_tensor(
                    ca, xcb[:, m, i:i + 512], convw_sb[:, m, i:i + 1], ca,
                    op0=ALU.mult, op1=ALU.add),
                    reads=xmk + ("convw", cak), writes=(cak,))
            r.op("vector", lambda e, m=m, xcb=xcb: e.tensor_copy(tail_b[:, m, :],
                                                                 xcb[:, m, 512:515]),
                 reads=xmk + ("tail",), writes=("tail",))
            r.op("scalar", lambda e, m=m, ca=ca, xcb=xcb: e.activation(
                xcb[:, m, 3:515], ca, AF.Silu),
                reads=(cak,) + xmk, writes=(f"{xck}:{m}",))
        r.op("vector", lambda e, t=t, xcb=xcb: e.tensor_copy(B_FM[:, TT * t:TT * (t + 1)],
                                                             xcb[:, 16, 3:515]),
             reads=(f"{xck}:16",), writes=("B_FM",))
        r.op("vector", lambda e, t=t, xcb=xcb: e.tensor_copy(C_FM[:, TT * t:TT * (t + 1)],
                                                             xcb[:, 17, 3:515]),
             reads=(f"{xck}:17",), writes=("C_FM",))
        r.dma("scalar",
              SP_XC[0:2048, TT * t:TT * (t + 1)].rearrange("(m p) c -> p m c", p=128),
              xcb[:, 0:16, 3:515], stream=xck,
              reads=tuple(f"{xck}:{m}" for m in range(16)), writes=(f"SP_XC:{t}",))
        r.dma("scalar", SP_XC[2048:2176, TT * t:TT * (t + 1)], xcb[:, 16, 3:515],
              stream=xck, reads=(f"{xck}:16",), writes=(f"SP_XCB:{t}",))

    dtv = dt_TM[:, :, :].rearrange("p a b -> p (a b)")
    sp1v = sp1[:, :, :].rearrange("p a b -> p (a b)")
    sp2v = sp2[:, :, :].rearrange("p a b -> p (a b)")
    r.op("vector", lambda e: e.tensor_tensor(
        dt_TM[:, :, :], dt_TM[:, :, :], bc(dtb_rep[:, :], [128, 32, NH], 1),
        op=ALU.add), reads=("dt_TM", "dtb_rep"), writes=("dt_TM",))
    r.op("scalar", lambda e: e.activation(sp1v, dtv, AF.Abs),
         reads=("dt_TM", "sp1"), writes=("sp1",))
    r.op("scalar", lambda e: e.activation(sp1v, sp1v, AF.Exp, scale=-1.0),
         reads=("sp1",), writes=("sp1",))
    r.op("scalar", lambda e: e.activation(sp1v, sp1v, AF.Ln, bias=cst[:, 1:2]),
         reads=("sp1", "cst"), writes=("sp1",))
    r.op("scalar", lambda e: e.activation(sp2v, dtv, AF.Relu),
         reads=("dt_TM", "sp2"), writes=("sp2",))
    r.op("vector", lambda e: e.tensor_tensor(dtv, sp1v, sp2v, op=ALU.add),
         reads=("sp1", "sp2", "dt_TM"), writes=("dt_TM",))
    r.op("vector", lambda e: e.tensor_tensor(
        a_TM[:, :, :], dt_TM[:, :, :],
        bc(negA_rep[:, :], [128, 32, NH], 1), op=ALU.mult),
        reads=("dt_TM", "negA_rep"), writes=("a_TM",))
    r.op("vector", lambda e: e.tensor_copy(a_TMb[:, :, :], a_TM[:, :, :]),
         reads=("a_TM",), writes=("a_TMb",))
    for p in range(32):
        r.op("tensor", lambda e, p=p: e.transpose(psM[0:32, 0:128], dt_TM[:, p, :], ident[:, :]),
             reads=("dt_TM", "ident", "psM"), writes=("psM",))
        r.op("vector", lambda e, p=p: e.tensor_copy(hm_buf[:, 128 * p:128 * (p + 1)],
                                                    psM[0:32, 0:128]),
             reads=("psM", "hm_buf"), writes=("hm_buf",))
    r.op("vector", lambda e: e.tensor_scalar(hm_buf[:, :], hm_buf[:, :], negA_col[:, 0:1],
                                             None, op0=ALU.mult, op1=ALU.bypass),
         reads=("hm_buf", "negA_col"), writes=("hm_buf",))
    def f_scans(e):
        inst = None
        for c in range(NCK):
            inst = e.tensor_tensor_scan(cs_HM[:, CS * c:CS * (c + 1)], ones64[0:NH, :],
                                        hm_buf[:, CS * c:CS * (c + 1)], initial=0.0,
                                        op0=ALU.mult, op1=ALU.add)
        return inst
    r.op("vector", f_scans, reads=("hm_buf", "ones64", "cs_HM"), writes=("cs_HM",))
    r.op("vector", lambda e: e.tensor_copy(
        atot_HM[:, :], cs_HM[:, :].rearrange("h (c s) -> h c s", s=CS)[:, :, CS - 1]),
        reads=("cs_HM",), writes=("atot_HM",))
    r.op("vector", lambda e: e.tensor_tensor_scan(cpi_HM[:, :], ones64[0:NH, :], atot_HM[:, :],
                                                  initial=0.0, op0=ALU.mult, op1=ALU.add),
         reads=("atot_HM", "ones64"), writes=("cpi_HM",))
    r.op("vector", lambda e: e.reduce_sum(aseg_col[:, :], atot_HM[:, :], axis=AX.X),
         reads=("atot_HM",), writes=("aseg_col",))
    r.op("vector", lambda e: e.tensor_scalar(rc_HM[:, :], cpi_HM[:, :], -1.0, aseg_col[:, 0:1],
                                             op0=ALU.mult, op1=ALU.add),
         reads=("cpi_HM", "aseg_col"), writes=("rc_HM",))
    r.op("vector", lambda e: e.tensor_tensor(b2_HM[:, :], atot_HM[:, :], rc_HM[:, :], op=ALU.add),
         reads=("atot_HM", "rc_HM"), writes=("b2_HM",))
    r.op("scalar", lambda e: e.activation(dexp_HM[:, :], atot_HM[:, :], AF.Exp),
         reads=("atot_HM",), writes=("dexp_HM",))
    r.dma("sync", SP_SB[0:64, :].rearrange("c h -> h c"), atot_HM[:, :],
          reads=("atot_HM",), writes=("SP_SB0",), allow_slow_non_contiguous=True)
    r.dma("sync", SP_SB[64:128, :].rearrange("c h -> h c"), b2_HM[:, :],
          reads=("b2_HM",), writes=("SP_SB1",), allow_slow_non_contiguous=True)
    r.dma("sync", SP_SB[128:192, :].rearrange("c h -> h c"), dexp_HM[:, :],
          reads=("dexp_HM",), writes=("SP_SB2",), allow_slow_non_contiguous=True)
    for p in range(32):
        r.op("tensor", lambda e, p=p: e.transpose(psM[:, 0:32], cs_HM[:, 128 * p:128 * (p + 1)],
                                                  ident[0:32, 0:32]),
             reads=("cs_HM", "ident", "psM"), writes=("psM",))
        r.op("vector", lambda e, p=p: e.tensor_copy(cs_TM[:, p, :], psM[:, 0:32]),
             reads=("psM", "cs_TM"), writes=("cs_TM",))
    csv = cs_TM[:, :, :].rearrange("p a b -> p (a b)")
    for cpar in range(2):
        r.dma("sync", sp1[64 * cpar:64 * (cpar + 1), :, :],
              bass.AP(SP_SB.tensor, NH * cpar, [[0, 64], [2 * NH, 32], [1, NH]]),
              reads=("SP_SB0", "sp1"), writes=("sp1",))
        r.dma("sync", sp2[64 * cpar:64 * (cpar + 1), :, :],
              bass.AP(SP_SB.tensor, 64 * NH + NH * cpar, [[0, 64], [2 * NH, 32], [1, NH]]),
              reads=("SP_SB1", "sp2"), writes=("sp2",))
    r.op("vector", lambda e: e.tensor_tensor(sp1v, sp1v, csv, op=ALU.subtract),
         reads=("sp1", "cs_TM"), writes=("sp1",))
    r.op("vector", lambda e: e.tensor_tensor(sp2v, sp2v, csv, op=ALU.subtract),
         reads=("sp2", "cs_TM"), writes=("sp2",))
    def f_exps(e):
        e.activation(dsT[:, :, :].rearrange("p a b -> p (a b)"), sp1v, AF.Exp)
        e.activation(ds2T[:, :, :].rearrange("p a b -> p (a b)"), sp2v, AF.Exp)
        return e.activation(sdT[:, :, :].rearrange("p a b -> p (a b)"), csv, AF.Exp)
    r.op("scalar", f_exps, reads=("sp1", "sp2", "cs_TM"), writes=("dsT", "ds2T", "sdT"))
    r.build()
    esB.close()
    es_win.close()

    B_TM = mk(es_b, "B_TM", [128, 32, 128], BF16)
    D_rep = mk(es_b, "D_rep", [128, NCK, NH], F32)
    S_sb = mk(es_b, "S_sb", [128, DI], BF16)
    battn = mk(es_b, "battn", [1, D], BF16)
    W_out_sb = mk(es_b, "W_out_sb", [128, 16, D], BF16)

    esC = ExitStack()
    xcT = mk(esC, "xcT", [128, 8, 2176], BF16)
    xs_b = mk(esC, "xs_b", [128, DI], BF16)
    xd_b = mk(esC, "xd_b", [128, DI], BF16)
    xd2_b = mk(esC, "xd2_b", [128, DI], BF16)
    cb_f = mk(esC, "cb_f", [128, DI], F32)
    gth = mk(esC, "gth", [128, DI], F32)
    acc_f = mk(esC, "acc_f", [128, DI], F32)
    tmp_f = mk(esC, "tmp_f", [128, DI], F32)
    a8 = mk(esC, "a8", [8, NH + 8], F32)
    gm_col = mk(esC, "gm_col", [8, 1], F32)
    wrep = mk(esC, "wrep", [128, 8, NH], F32)
    smB = mk(esC, "smB", [8, NH], F32)
    smC = mk(esC, "smC", [8, NH], F32)
    wo32 = [mk(esC, "wo32_0", [128, D], F32)] * 2
    wo16 = mk(esC, "wo16", [128, D], BF16)
    nwb_col = mk(esC, "nwb_col", [128, 32], F32)
    nb_colb = mk(esC, "nb_colb", [128, 16], BF16)
    wi32b = [mk(esC, "wi32b0", [128, 2, D], I32)] * 2
    wibfb = [mk(esC, "wibfb0", [128, 2, D], BF16)] * 2
    psCB = esC.enter_context(nc.psum_tensor("psCB", [128, DI], F32))
    psE = esC.enter_context(nc.psum_tensor("psE", [8, NH], F32))
    psB2 = esC.enter_context(nc.psum_tensor("psB2", [1, D], F32))

    r = rec()
    r.dma("sync", nwb_col[:, 0:16], norm_w.rearrange("(k p) -> p k", p=128),
          writes=("nw_col",), allow_slow_non_contiguous=True)
    r.dma("sync", nwb_col[:, 16:32], norm_b.rearrange("(k p) -> p k", p=128),
          writes=("nb_col32",), allow_slow_non_contiguous=True)
    r.op("scalar", lambda e: e.activation(nb_colb[:, :], nwb_col[:, 16:32], AF.Identity),
         reads=("nb_col32",), writes=("nb_colb",))
    for k in range(16):
        buf, key = wo32[0], "wo32_0"
        r.dma("sync", buf[:, :], W_out[128 * k:128 * (k + 1), :], reads=(key,), writes=(key,))
        r.op("scalar", lambda e, buf=buf: e.activation(wo16[:, :], buf[:, :], AF.Identity),
             reads=(key, "wo16"), writes=("wo16",))
        def f_bmm(e, k=k):
            e.matmul(psB2[:, 0:512], nb_colb[:, k:k + 1], wo16[:, 0:512],
                     start=(k == 0), stop=(k == 15))
            return e.matmul(psB2[:, 512:1024], nb_colb[:, k:k + 1], wo16[:, 512:1024],
                            start=(k == 0), stop=(k == 15))
        r.op("tensor", f_bmm, reads=("wo16", "nb_colb", "psB2"), writes=("psB2",))
        r.op("scalar", lambda e, k=k, buf=buf: e.activation(
            W_out_sb[:, k, :], buf[:, :], AF.Identity, scale=nwb_col[:, k:k + 1]),
            reads=(key, "nw_col", "W_out_sb"), writes=("W_out_sb",))
    r.op("scalar", lambda e: e.activation(battn[:, :], psB2[:, :], AF.Identity),
         reads=("psB2",), writes=("battn",))
    rnd_i = [0]
    def bounce(wsrc, wdst, wname, r0, rows, wkey):
        i = rnd_i[0] % 2
        rnd_i[0] += 1
        m = rows // 128
        r.dma("scalar", wi32b[i][:, 0:m, :],
              wsrc[r0:r0 + rows, :].rearrange("(m p) c -> p m c", p=128),
              stream="wi32b0", reads=("wi32b0",), writes=("wi32b0",))
        r.op("scalar", lambda e, i=i, m=m: e.activation(
            wibfb[i][:, 0:m, :], wi32b[i][:, 0:m, :], AF.Identity),
            reads=("wi32b0", "wibfb0"), writes=("wibfb0",))
        r.dma("scalar", wdst[r0:r0 + rows, :].rearrange("(m p) c -> p m c", p=128),
              wibfb[i][:, 0:m, :], stream="wibfb0", reads=("wibfb0",),
              writes=(wkey,))
    for wsrc, wdst, wname in ((wg, SP_WGN, "SP_WGN"), (wu, SP_WUN, "SP_WUN")):
        for r0 in range(0, HID, 256):
            bounce(wsrc, wdst, wname, r0, 256, f"{wname}:{r0 // 1024 * 1024}")
    def bounce_wd(r0, co):
        i = rnd_i[0] % 2
        rnd_i[0] += 1
        fv32 = wi32b[i][:, :, :].rearrange("p a b -> p (a b)")
        fvbf = wibfb[i][:, :, :].rearrange("p a b -> p (a b)")
        r.dma("scalar", fv32[:, 0:1536], wd[r0:r0 + 128, co:co + 1536],
              stream="wi32b0", reads=("wi32b0",), writes=("wi32b0",))
        r.op("scalar", lambda e, i=i, fv32=fv32, fvbf=fvbf: e.activation(
            fvbf[:, 0:1536], fv32[:, 0:1536], AF.Identity),
            reads=("wi32b0", "wibfb0"), writes=("wibfb0",))
        r.dma("scalar", SP_WDN[r0:r0 + 128, co:co + 1536], fvbf[:, 0:1536],
              stream="wibfb0", reads=("wibfb0",),
              writes=(f"SP_WDN:{r0 // 256 * 256}",))
    for r0 in range(0, D, 128):
        for co in (0, 1536):
            bounce_wd(r0, co)
    r.dma("sync", D_rep[:, :, :],
          bass.AP(SP_SB.tensor, 128 * NH, [[0, 128], [NH, NCK], [1, NH]]),
          reads=("SP_SB2",), writes=("D_rep",))
    first = [True]
    for quarter in range(4):
        r.xpose(xcT[:, :, :], SP_XC[:, 1024 * quarter:1024 * (quarter + 1)],
                reads=(f"SP_XC:{2 * quarter}", f"SP_XC:{2 * quarter + 1}",
                       f"SP_XCB:{2 * quarter}", f"SP_XCB:{2 * quarter + 1}", "xcT"),
                writes=("xcT",))
        for m in range(8):
            pp = 8 * quarter + m
            r.op("vector", lambda e, m=m, pp=pp: e.tensor_tensor(
                xs_b[:, :].rearrange("p (h q) -> p h q", q=HP),
                xcT[:, m, 0:DI].rearrange("p (h q) -> p h q", q=HP),
                bc(dt_TM[:, pp, :], [128, NH, HP], 2), op=ALU.mult),
                reads=("xcT", "dt_TM", "xs_b"), writes=("xs_b",))
            r.op("gpsimd", lambda e, pp=pp: e.tensor_tensor(
                xd_b[:, :].rearrange("p (h q) -> p h q", q=HP),
                xs_b[:, :].rearrange("p (h q) -> p h q", q=HP),
                bc(dsT[:, pp, :], [128, NH, HP], 2), op=ALU.mult),
                reads=("xs_b", "dsT", "xd_b"), writes=("xd_b",))
            r.op("gpsimd", lambda e, pp=pp: e.tensor_tensor(
                xd2_b[:, :].rearrange("p (h q) -> p h q", q=HP),
                xs_b[:, :].rearrange("p (h q) -> p h q", q=HP),
                bc(ds2T[:, pp, :], [128, NH, HP], 2), op=ALU.mult),
                reads=("xs_b", "ds2T", "xd2_b"), writes=("xd2_b",))
            r.op("vector", lambda e, m=m, pp=pp: e.tensor_copy(B_TM[:, pp, :],
                                                               xcT[:, m, 2048:2176]),
                 reads=("xcT", "B_TM"), writes=("B_TM",))
            r.dma("scalar", SP_XS[128 * pp:128 * (pp + 1), :], xs_b[:, :],
                  stream="xs_b", reads=("xs_b",), writes=(f"SP_XS:{pp // 4}",))
            r.dma("scalar", SP_XD[128 * pp:128 * (pp + 1), :], xd_b[:, :],
                  stream="xd_b", reads=("xd_b",), writes=(f"SP_XD:{pp // 4}",))
            def f_st2(e, m=m, first=first, pp=pp):
                inst = None
                for cc2 in range(2):
                    sl = slice(64 * cc2, 64 * (cc2 + 1))
                    for h in range(NH):
                        inst = e.matmul(psCB[:, HP * h:HP * (h + 1)], xcT[sl, m, 2048:2176],
                                        xd2_b[sl, HP * h:HP * (h + 1)], start=(first[0] and h % 8 == 0),
                                        stop=(pp == NPAIR - 1 and cc2 == 1 and h == NH - 1))
                        if h == NH - 1:
                            first[0] = False
                return inst
            r.op("tensor", f_st2, reads=("xcT", "xd2_b", "psCB"), writes=("psCB",))
    r.op("vector", lambda e: e.tensor_copy(cb_f[:, :], psCB[:, :]),
         reads=("psCB",), writes=("cb_f",))
    r.dma("sync", CBT[:, 0:DI], cb_f[:, :], reads=("cb_f",), writes=("CBT",))
    r.dma("sync", CBT[0:NH, DI:DI + 1], aseg_col[:, :], reads=("aseg_col",), writes=("CBT2",), allow_slow_non_contiguous=True)
    r.build()
    nc.all_core_barrier()
    r = rec()
    r.op("gpsimd", lambda e: e.collective_compute(
        "AllGather", ALU.bypass, replica_groups=[list(range(8))], ins=[CBT], outs=[GATH])
        .then_inc(cc_sem, 1) and None,
        reads=("CBT", "CBT2"), writes=(), inc=0)
    r.op("gpsimd", lambda e: e.wait_ge(cc_sem, 1) or e.nop(), reads=(), writes=("GATH",))
    r.build()
    nc.all_core_barrier()
    r = rec()
    r.dma("sync", a8[:, NH:NH + 8], SEL2, writes=("a8sel",))
    r.dma("sync", a8[0:8, 0:NH], GATH[:, DI].rearrange("(j p) -> j p", p=128)[:, 0:NH],
          reads=("GATH",), writes=("a8",), allow_slow_non_contiguous=True)
    r.dma("sync", gm_col[:, :], GMASK, writes=("gm_col",))
    r.op("tensor", lambda e: e.matmul(psE[:, :], a8[:, NH:NH + 8], a8[0:8, 0:NH],
                                      start=True, stop=True),
         reads=("a8sel", "a8", "psE"), writes=("psE",))
    r.op("scalar", lambda e: e.activation(smB[:, :], psE[:, :], AF.Exp),
         reads=("psE",), writes=("smB",))
    r.op("vector", lambda e: e.tensor_scalar(smC[:, :], smB[:, :], gm_col[:, 0:1], None,
                                             op0=ALU.mult, op1=ALU.bypass),
         reads=("smB", "gm_col"), writes=("smC",))
    r.dma("sync", SP_W8, smC[:, :], reads=("smC",), writes=("SP_W8",))
    r.dma("sync", wrep[:, :, :], bass.AP(SP_W8.tensor, 0, [[0, 128], [NH, 8], [1, NH]]),
          reads=("SP_W8",), writes=("wrep",))
    for j in range(8):
        r.dma("sync", gth[:, :], GATH[128 * j:128 * (j + 1), 0:DI],
              reads=("GATH", "gth"), writes=("gth",))
        if j == 0:
            r.op("vector", lambda e: e.tensor_tensor(
                acc_f[:, :].rearrange("p (h q) -> p h q", q=HP),
                gth[:, :].rearrange("p (h q) -> p h q", q=HP),
                bc(wrep[:, 0, :], [128, NH, HP], 2), op=ALU.mult),
                reads=("gth", "wrep", "acc_f"), writes=("acc_f",))
        else:
            r.op("vector", lambda e, j=j: e.tensor_tensor(
                tmp_f[:, :].rearrange("p (h q) -> p h q", q=HP),
                gth[:, :].rearrange("p (h q) -> p h q", q=HP),
                bc(wrep[:, j, :], [128, NH, HP], 2), op=ALU.mult),
                reads=("gth", "wrep", "tmp_f"), writes=("tmp_f",))
            r.op("vector", lambda e: e.tensor_tensor(acc_f[:, :], acc_f[:, :], tmp_f[:, :],
                                                     op=ALU.add),
                 reads=("tmp_f", "acc_f"), writes=("acc_f",))
    r.op("vector", lambda e: e.tensor_copy(S_sb[:, :], acc_f[:, :]),
         reads=("acc_f", "S_sb"), writes=("S_sb",))
    if "SP_DBGS2" in dbg:
        r.dma("sync", SP_DBGS2[:, 0:DI], cb_f[:, :], stream="cb_f", reads=("cb_f",), writes=("dbgs9",))
        r.dma("sync", SP_DBGS2[0:128, DI:DI + NH].rearrange("p c -> p c"), wrep[:, 0, :],
              reads=("wrep",), writes=("dbgs10",))
    if "SP_DBGS" in dbg:
        r.dma("sync", SP_DBGS[:, 0:DI], acc_f[:, :], stream="acc_f", reads=("acc_f",), writes=("dbgs",))
        r.dma("sync", SP_DBGS[0:8, DI:DI + NH], smC[:, :], reads=("smC",), writes=("dbgs2",))
        r.dma("sync", SP_DBGS[8:16, DI:DI + NH], a8[:, 0:NH], reads=("a8",), writes=("dbgs3",))
        r.dma("sync", SP_DBGS[16:24, DI:DI + NH], smB[:, :], reads=("smB",), writes=("dbgs4",))
    r.build()
    esC.close()

    esD = ExitStack()
    xs_p = [mk(esD, f"xs_p{i}", [128, DI], BF16) for i in range(2)]
    xd_p = [mk(esD, f"xd_p{i}", [128, DI], BF16) for i in range(2)]
    z_p = [mk(esD, f"z_p{i}", [128, DI], BF16) for i in range(2)]
    AM_sb = mk(esD, "AM_sb", [128, DI], BF16)
    M0_sb = mk(esD, "M0_sb", [128, DI], BF16)
    M_sb = mk(esD, "M_sb", [128, DI], BF16)
    Gt_sb = mk(esD, "Gt_sb", [128, 64], BF16)
    y_sb = mk(esD, "y_sb", [128, DI], F32)
    zs_sb = mk(esD, "zs_sb", [128, DI], BF16)
    yn_sb = mk(esD, "yn_sb", [128, DI], BF16)
    st2 = mk(esD, "st2", [128, 24], F32)
    mv2 = mk(esD, "mv2", [128, 2], F32)
    rs2 = mk(esD, "rs2", [128, 2], F32)
    ynF = mk(esD, "ynF", [128, 16, TT], BF16)
    at_f = mk(esD, "at_f", [128, D], F32)
    u2_t = [mk(esD, f"u2_t{i}", [128, D], F32) for i in range(2)]
    x1_t = mk(esD, "x1_t", [128, D], F32)
    x1n_t = mk(esD, "x1n_t", [128, D], BF16)
    psT = esD.enter_context(nc.psum_tensor("psT", [128, DI], F32))
    psO = esD.enter_context(nc.psum_tensor("psO", [128, DI], F32))

    r = rec()
    for pp in range(NPAIR):
        bi = pp % 2
        xsb, xdb, zb = xs_p[bi], xd_p[bi], z_p[bi]
        xsk, xdk, zk = f"xs_p{bi}", f"xd_p{bi}", f"z_p{bi}"
        r.dma("sync", xsb[:, :], SP_XS[128 * pp:128 * (pp + 1), :],
              reads=(f"SP_XS:{pp // 4}", xsk), writes=(xsk,))
        r.dma("sync", xdb[:, :], SP_XD[128 * pp:128 * (pp + 1), :],
              reads=(f"SP_XD:{pp // 4}", xdk), writes=(xdk,))
        r.dma("sync", zb[:, :], SP_Z[128 * pp:128 * (pp + 1), :],
              reads=(f"SP_Z:{pp // 4}", zk), writes=(zk,))
        for cc in range(2):
            c = 2 * pp + cc
            sl = slice(64 * cc, 64 * (cc + 1))
            amk = f"AM{cc}"
            r.op("gpsimd", lambda e, pp=pp, sl=sl: e.tensor_tensor(
                AM_sb[sl, :].rearrange("p (l h) -> p l h", h=NH),
                bc(a_TM[sl, pp, :], [64, 64, NH], 1),
                bc(trillt[sl, :], [64, 64, NH], 2), op=ALU.mult),
                reads=("a_TM", "trillt", amk), writes=(amk,))
            def f_tmm(e, pp=pp, sl=sl):
                inst = None
                for nb4 in range(4):
                    ls = slice(16 * nb4, 16 * (nb4 + 1))
                    ps = psT[sl, 512 * nb4:512 * (nb4 + 1)].rearrange("p (l h) -> p l h", h=NH)
                    e.matmul(ps, trilneg[sl, :],
                             bc(a_TMb[sl, pp, :], [64, 64, NH], 1)[:, ls, :],
                             start=True, stop=False)
                    e.matmul(ps, ones64[sl, :],
                             AM_sb[sl, :].rearrange("p (l h) -> p l h", h=NH)[:, ls, :],
                             start=False, stop=False)
                    inst = e.matmul(ps, trillt[sl, :],
                                    maskrx[sl, :].rearrange("p (l h) -> p l h", h=NH)[:, ls, :],
                                    start=False, stop=True)
                return inst
            r.op("tensor", f_tmm, reads=("a_TMb", "trilneg", "ones64", amk, "maskrx", f"psT{cc}"),
                 writes=(f"psT{cc}",))
            r.op("scalar", lambda e, sl=sl: e.activation(M0_sb[sl, :], psT[sl, :], AF.Exp),
                 reads=(f"psT{cc}", f"M0{cc}"), writes=(f"M0{cc}", f"psTr{cc}"))
            r.op("tensor", lambda e, c=c, sl=sl: e.matmul(
                psT[sl, 0:64], B_FM[:, 64 * c:64 * (c + 1)], C_FM[:, 64 * c:64 * (c + 1)],
                start=True, stop=True),
                reads=("B_FM", "C_FM", f"psTr{cc}"), writes=(f"psT{cc}",))
            r.op("vector", lambda e, sl=sl, cc=cc: e.tensor_tensor(
                Gt_sb[sl, :], psT[sl, 0:64], trilsl[sl, :], op=ALU.mult),
                reads=(f"psT{cc}", "trilsl", f"Gt{cc}"), writes=(f"Gt{cc}", f"psTr{cc}"))
            r.op("gpsimd", lambda e, sl=sl, cc=cc: e.tensor_tensor(
                M_sb[sl, :].rearrange("p (l h) -> p l h", h=NH),
                M0_sb[sl, :].rearrange("p (l h) -> p l h", h=NH),
                bc(Gt_sb[sl, :], [64, 64, NH], 2), op=ALU.mult),
                reads=(f"M0{cc}", f"Gt{cc}", f"M{cc}"), writes=(f"M{cc}",))
            def f_yoff(e, c=c, sl=sl):
                inst = None
                for h in range(NH):
                    inst = e.matmul(psO[sl, HP * h:HP * (h + 1)], C_FM[:, 64 * c:64 * (c + 1)],
                                    S_sb[:, HP * h:HP * (h + 1)], start=True, stop=True)
                return inst
            r.op("tensor", f_yoff, reads=("S_sb", "C_FM", f"psO{cc}"), writes=(f"psO{cc}",))
            r.op("vector", lambda e, sl=sl, pp=pp: e.tensor_tensor(
                y_sb[sl, :].rearrange("p (h q) -> p h q", q=HP),
                psO[sl, :].rearrange("p (h q) -> p h q", q=HP),
                bc(sdT[sl, pp, :], [64, NH, HP], 2), op=ALU.mult),
                reads=(f"psO{cc}", "sdT", f"y{cc}", "psOF"), writes=(f"y{cc}", f"psO{cc}"))
            def f_st(e, pp=pp, sl=sl, bi=bi):
                inst = None
                for h in range(NH):
                    inst = e.matmul(psO[:, HP * h:HP * (h + 1)], B_TM[sl, pp, :],
                                    xd_p[bi][sl, HP * h:HP * (h + 1)], start=True, stop=True)
                return inst
            r.op("tensor", f_st, reads=("B_TM", xdk, "psOF"), writes=("psOF", "psO0", "psO1"))
            r.op("gpsimd", lambda e, c=c: e.tensor_tensor(
                S_sb[:, :].rearrange("p (h q) -> p h q", q=HP),
                S_sb[:, :].rearrange("p (h q) -> p h q", q=HP),
                bc(D_rep[:, c, :], [128, NH, HP], 2), op=ALU.mult),
                reads=("S_sb", "D_rep"), writes=("S_sb",))
            r.op("vector", lambda e: e.tensor_tensor(S_sb[:, :], S_sb[:, :], psO[:, :],
                                                     op=ALU.add),
                 reads=("S_sb", "psOF"), writes=("S_sb", "psOF"))
            def f_yd(e, sl=sl, bi=bi):
                inst = None
                for h in range(NH):
                    inst = e.matmul(psT[sl, HP * h:HP * (h + 1)],
                                    M_sb[sl, :].rearrange("p (l h) -> p l h", h=NH)[:, :, h],
                                    xs_p[bi][sl, HP * h:HP * (h + 1)], start=True, stop=True)
                return inst
            r.op("tensor", f_yd, reads=(f"M{cc}", xsk, f"psTr{cc}"), writes=(f"psT{cc}",))
            r.op("vector", lambda e, sl=sl, cc=cc: e.tensor_tensor(
                y_sb[sl, :], y_sb[sl, :], psT[sl, :], op=ALU.add),
                reads=(f"psT{cc}", f"y{cc}"), writes=(f"y{cc}", f"psTr{cc}"))
        r.op("scalar", lambda e, zb=zb: e.activation(zs_sb[:, :], zb[:, :], AF.Silu),
             reads=(zk, "zs"), writes=("zs",))
        r.op("gpsimd", lambda e: e.tensor_tensor(y_sb[:, :], y_sb[:, :], zs_sb[:, :],
                                                 op=ALU.mult),
             reads=("y0", "y1", "zs"), writes=("y0", "y1"))
        ln_apply(r, "ln", ("y0", "y1"), [y_sb[:, 512 * i:512 * (i + 1)] for i in range(4)],
                 st2, mv2, rs2)
        r.op("scalar", lambda e: e.activation(yn_sb[:, :], y_sb[:, :], AF.Identity,
                                              scale=rs2[:, 0:1], bias=rs2[:, 1:2]),
             reads=("y0", "y1", "lnrs", "yn"), writes=("yn",))
        r.dma("scalar", SP_YN[128 * pp:128 * (pp + 1), :], yn_sb[:, :],
              stream="yn", reads=("yn",), writes=(f"SP_YN:{pp // 4}",))
        if pp % 4 == 3:
            t = pp // 4
            r.xpose(ynF[:, :, :], SP_YN[TT * t:TT * (t + 1), :],
                    reads=(f"SP_YN:{t}", "ynF"), writes=("ynF",))
            for g in range(4):
                ub, ubk = u2_t[g % 2], f"u2_t{g % 2}"
                r.dma("sync", ub[:, :],
                      u_seg[TT * t + 128 * g:TT * t + 128 * (g + 1), :],
                      reads=(ubk,), writes=(ubk,))
                def f_op(e, g=g):
                    inst = None
                    for n2 in range(2):
                        for k in range(16):
                            e.matmul(psO[:, 512 * n2:512 * (n2 + 1)],
                                     ynF[:, k, 128 * g:128 * (g + 1)],
                                     W_out_sb[:, k, 512 * n2:512 * (n2 + 1)],
                                     start=(k == 0), stop=False)
                        inst = e.matmul(psO[:, 512 * n2:512 * (n2 + 1)], onesr[:, :],
                                        battn[:, 512 * n2:512 * (n2 + 1)],
                                        start=False, stop=True)
                    return inst
                r.op("tensor", f_op,
                     reads=("ynF", "battn", "onesr", "psOF", "W_out_sb"),
                     writes=("psOF", "psO0", "psO1"))
                ln_apply(r, "ln", ("psOF",), [psO[:, 0:512], psO[:, 512:1024]], st2, mv2, rs2)
                def f_atn(e):
                    e.activation(at_f[:, 0:512], psO[:, 0:512], AF.Identity,
                                 scale=rs2[:, 0:1], bias=rs2[:, 1:2])
                    return e.activation(at_f[:, 512:1024], psO[:, 512:1024], AF.Identity,
                                        scale=rs2[:, 0:1], bias=rs2[:, 1:2])
                r.op("scalar", f_atn, reads=("psOF", "lnrs", "at_f"), writes=("at_f", "psOF"))
                r.op("vector", lambda e, g=g, ub=ub: e.tensor_tensor(
                    x1_t[:, :], at_f[:, :], ub[:, :], op=ALU.add),
                    reads=("at_f", ubk, "x1_t"), writes=("x1_t",))
                ln_apply(r, "ln", ("x1_t",), [x1_t[:, 0:512], x1_t[:, 512:1024]],
                         st2, mv2, rs2)
                r.op("scalar", lambda e, g=g: e.activation(
                    x1n_t[:, :], x1_t[:, :], AF.Identity,
                    scale=rs2[:, 0:1], bias=rs2[:, 1:2]),
                    reads=("x1_t", "lnrs", "x1n_t"), writes=("x1n_t",))
                r.dma("scalar", SP_X1[TT * t + 128 * g:TT * t + 128 * (g + 1), :],
                      x1_t[:, :], stream="x1_t", reads=("x1_t",), writes=(f"SP_X1:{t}",))
                r.dma("scalar", SP_X1N[TT * t + 128 * g:TT * t + 128 * (g + 1), :],
                      x1n_t[:, :], stream="x1n_t", reads=("x1n_t",), writes=(f"SP_X1N:{t}",))
    r.build()
    esD.close()
    es_b.close()
    es_mid.close()

    esE = ExitStack()
    wgT = mk(esE, "wgT", [128, 8, HID], BF16)
    wuT = mk(esE, "wuT", [128, 8, HID], BF16)
    wdT = mk(esE, "wdT", [128, 24, D], BF16)
    x1F = [mk(esE, f"x1F{i}", [128, 8, TT], BF16) for i in range(2)]
    gs_sb = mk(esE, "gs_sb", [128, 512], BF16)
    h_sb = mk(esE, "h_sb", [128, 24, TT], BF16)
    x1_2 = mk(esE, "x1_2", [128, 2, D], F32)
    o_sb = mk(esE, "o_sb", [128, D], F32)
    stf = mk(esE, "stf", [128, 12], F32)
    mvf = mk(esE, "mvf", [128, 2], F32)
    rsf = mk(esE, "rsf", [128, 2], F32)
    psG2 = esE.enter_context(nc.psum_tensor("psG2", [128, 2, 512], F32))
    psU2 = esE.enter_context(nc.psum_tensor("psU2", [128, 2, 512], F32))
    psD2 = esE.enter_context(nc.psum_tensor("psD2", [128, 4, 512], F32))

    r = rec()
    r.xpose(wgT[:, :, :], SP_WGN,
            reads=tuple(f"SP_WGN:{r0}" for r0 in range(0, HID, 1024)), writes=("wgT",))
    r.xpose(wuT[:, :, :], SP_WUN,
            reads=tuple(f"SP_WUN:{r0}" for r0 in range(0, HID, 1024)), writes=("wuT",))
    r.xpose(wdT[:, :, :], SP_WDN,
            reads=tuple(f"SP_WDN:{r0}" for r0 in range(0, D, 256)), writes=("wdT",))
    for t in range(NT):
        xb, xk = x1F[t % 2], f"x1F{t % 2}"
        r.xpose(xb[:, :, :], SP_X1N[TT * t:TT * (t + 1), :],
                reads=(f"SP_X1N:{t}", xk), writes=(xk,))
        for m in range(24):
            pb = m % 2
            def f_gmm(e, m=m, pb=pb, xb=xb):
                inst = None
                for k in range(8):
                    inst = e.matmul(psG2[:, pb, :], wgT[:, k, 128 * m:128 * (m + 1)],
                                    xb[:, k, :], start=(k == 0), stop=(k == 7))
                return inst
            r.op("tensor", f_gmm, reads=(xk, "wgT", f"psG2{pb}"), writes=(f"psG2{pb}",))
            def f_umm(e, m=m, pb=pb, xb=xb):
                inst = None
                for k in range(8):
                    inst = e.matmul(psU2[:, pb, :], wuT[:, k, 128 * m:128 * (m + 1)],
                                    xb[:, k, :], start=(k == 0), stop=(k == 7))
                return inst
            r.op("tensor", f_umm, reads=(xk, "wuT", f"psU2{pb}"), writes=(f"psU2{pb}",))
            r.op("scalar", lambda e, pb=pb: e.activation(gs_sb[:, :], psG2[:, pb, :],
                                                         AF.Silu, scale=1.0 / 127.0),
                 reads=(f"psG2{pb}", "gs"), writes=("gs", f"psG2{pb}"))
            r.op("vector", lambda e, m=m, pb=pb: e.tensor_tensor(
                h_sb[:, m, :], gs_sb[:, :], psU2[:, pb, :], op=ALU.mult),
                reads=("gs", f"psU2{pb}", "h_sb"), writes=("h_sb", f"psU2{pb}"))
        for g in range(4):
            gp = g % 2
            if g % 2 == 0:
                r.dma("sync", x1_2[:, :, :],
                      SP_X1[TT * t + 128 * g:TT * t + 128 * (g + 2), :]
                      .rearrange("(g2 p) d -> p g2 d", p=128),
                      stream="x1_2", reads=("x1_2",), writes=("x1_2",))
            def f_dmm(e, g=g, gp=gp):
                inst = None
                for n2 in range(2):
                    for k in range(24):
                        inst = e.matmul(psD2[:, 2 * gp + n2, :],
                                        h_sb[:, k, 128 * g:128 * (g + 1)],
                                        wdT[:, k, 512 * n2:512 * (n2 + 1)],
                                        start=(k == 0), stop=(k == 23))
                return inst
            r.op("tensor", f_dmm, reads=("h_sb", "wdT", f"psD2{gp}"), writes=(f"psD2{gp}",))
            def f_stats(e, gp=gp):
                e.bn_stats(stf[:, 0:6], psD2[:, 2 * gp, :])
                return e.bn_stats(stf[:, 6:12], psD2[:, 2 * gp + 1, :])
            r.op("vector", lambda e, gp=gp: e.bn_stats(stf[:, 0:6], psD2[:, 2 * gp, :]),
                 reads=(f"psD2{gp}", "stf"), writes=("stf",))
            r.op("vector", lambda e, gp=gp: e.bn_stats(stf[:, 6:12], psD2[:, 2 * gp + 1, :]),
                 reads=(f"psD2{gp}", "stf"), writes=("stf",))
            r.op("vector", lambda e: e.bn_aggr(mvf[:, :], stf[:, 0:12]),
                 reads=("stf", "mvf"), writes=("mvf",))
            r.op("scalar", lambda e: e.activation(rsf[:, 0:1], mvf[:, 1:2], AF.Ln,
                                                  bias=cst[:, 0:1]),
                 reads=("mvf", "rsf", "cst"), writes=("rsf",))
            r.op("scalar", lambda e: e.activation(rsf[:, 0:1], rsf[:, 0:1], AF.Exp,
                                                  scale=-0.5),
                 reads=("rsf",), writes=("rsf",))
            r.op("vector", lambda e: e.scalar_tensor_tensor(
                rsf[:, 1:2], mvf[:, 0:1], -1.0, rsf[:, 0:1], op0=ALU.mult, op1=ALU.mult),
                reads=("mvf", "rsf"), writes=("rsf",))
            def f_lnf(e, gp=gp):
                e.activation(o_sb[:, 0:512], psD2[:, 2 * gp, :], AF.Identity,
                             scale=rsf[:, 0:1], bias=rsf[:, 1:2])
                return e.activation(o_sb[:, 512:1024], psD2[:, 2 * gp + 1, :], AF.Identity,
                                    scale=rsf[:, 0:1], bias=rsf[:, 1:2])
            r.op("scalar", f_lnf, reads=(f"psD2{gp}", "rsf", "o_sb"),
                 writes=("o_sb", f"psD2{gp}"))
            r.op("vector", lambda e, g=g: e.tensor_tensor(
                o_sb[:, :], o_sb[:, :], x1_2[:, g % 2, :], op=ALU.add),
                reads=("x1_2", "o_sb"), writes=("o_sb",))
            r.dma("scalar",
                  out_seg[TT * t + 128 * g:TT * t + 128 * (g + 1), :], o_sb[:, :],
                  stream="o_sb", reads=("o_sb",), writes=(f"OUT:{t}",))
    r.op("gpsimd", lambda e: None, reads=tuple(f"OUT:{t}" for t in range(NT)), writes=())
    r.build()
    esE.close()
    es_g.close()
    nc._sem_ctr = dict(sem_ctr)
    return nc


def kernel(L0_attn_u, L0_attn_W_in, L0_attn_conv_w, L0_attn_conv_b,
           L0_attn_dt_bias, L0_attn_A_log, L0_attn_norm_w, L0_attn_norm_b,
           L0_attn_W_out, L0_ffn_w_gate, L0_ffn_w_up, L0_ffn_w_down):
    u = np.ascontiguousarray(np.asarray(L0_attn_u, np.float32))
    consts = _np_consts()
    nc = build_nc(dbg=DBG)
    in_maps = []
    for core in range(8):
        b, q = core // 4, core % 4
        seg = np.ascontiguousarray(u[b, T * q:T * (q + 1)])
        halo = np.zeros((16, D), np.float32)
        if q > 0:
            halo[13:16] = u[b, T * q - 3:T * q]
        sel2 = np.zeros((8, 8), np.float32)
        gmask = np.zeros((8, 1), np.float32)
        for j in range(8):
            if j // 4 == b and j % 4 < q:
                gmask[j, 0] = 1.0
                for i in range(8):
                    if i // 4 == b and j % 4 < i % 4 < q:
                        sel2[i, j] = 1.0
        m = dict(u_seg=seg, u_halo=halo,
                 W_in=np.ascontiguousarray(np.asarray(L0_attn_W_in, np.float32)),
                 conv_w=np.ascontiguousarray(np.asarray(L0_attn_conv_w, np.float32)),
                 conv_b=np.ascontiguousarray(np.asarray(L0_attn_conv_b, np.float32)),
                 dt_bias=np.ascontiguousarray(np.asarray(L0_attn_dt_bias, np.float32)),
                 A_log=np.ascontiguousarray(np.asarray(L0_attn_A_log, np.float32)),
                 norm_w=np.ascontiguousarray(np.asarray(L0_attn_norm_w, np.float32)),
                 norm_b=np.ascontiguousarray(np.asarray(L0_attn_norm_b, np.float32)),
                 W_out=np.ascontiguousarray(np.asarray(L0_attn_W_out, np.float32)),
                 wg=np.ascontiguousarray(np.asarray(L0_ffn_w_gate, np.int32)),
                 wu=np.ascontiguousarray(np.asarray(L0_ffn_w_up, np.int32)),
                 wd=np.ascontiguousarray(np.asarray(L0_ffn_w_down, np.int32)),
                 SEL2=sel2, GMASK=gmask, **consts)
        in_maps.append(m)
    res = run_bass_kernel_spmd(nc, in_maps, core_ids=list(range(8)))
    out = np.empty((2, 16384, D), np.float32)
    for core in range(8):
        b, q = core // 4, core % 4
        out[b, T * q:T * (q + 1)] = res.results[core]["out_seg"]
    return out


# version 13
# speedup vs baseline: 1.0643x; 1.0018x over previous
import numpy as np
import ml_dtypes
from contextlib import ExitStack

import concourse.bass as bass
import concourse.mybir as mybir
from concourse.bass_utils import run_bass_kernel_spmd

F32 = mybir.dt.float32
BF16 = mybir.dt.bfloat16
I32 = mybir.dt.int32
AF = mybir.ActivationFunctionType
ALU = mybir.AluOpType
AX = mybir.AxisListType

T = 4096
D = 1024
DI = 2048
DS = 128
NH = 32
HP = 64
CS = 64
NCK = T // CS
NPAIR = NCK // 2
DPROJ = 4384
HID = 3072
EPS = 1e-5
BIG = 1e30
NT = 8
TT = 512

ENGINES = ("sync", "tensor", "vector", "scalar", "gpsimd")
DMA_ENGINES = ("sync", "scalar", "gpsimd")

DBG = []


class Rec:

    def __init__(self, nc, getsem, ctr, shared):
        self.nc = nc
        self.getsem = getsem
        self.ctr = ctr
        self.last_w, self.readers = shared
        self.ops = []

    def _rec(self, engine, fn, own, inc, reads, writes, forced=()):
        deps = {}
        for k in reads:
            h = self.last_w.get(k)
            if h is not None:
                deps[h] = True
        for k in writes:
            h = self.last_w.get(k)
            if h is not None:
                deps[h] = True
            for rh in self.readers.get(k, ()):
                deps[rh] = True
        for h in forced:
            if h is not None:
                deps[h] = True
        self.getsem(own)
        self.ctr[own] = self.ctr.get(own, 0) + inc
        handle = (own, self.ctr[own])
        emit_deps = [h for h in deps
                     if not (h[0] == own and h[1] >= handle[1])]
        self.ops.append((engine, fn, sorted(emit_deps), own, inc))
        for k in writes:
            self.last_w[k] = handle
            self.readers[k] = []
        for k in reads:
            self.readers.setdefault(k, []).append(handle)
        return handle

    def op(self, engine, fn, reads=(), writes=(), inc=1):
        return self._rec(engine, fn, "eng:" + engine, inc, reads, writes)

    def dma(self, engine, out, in_, stream=None, reads=(), writes=(), **kw):
        if stream is None:
            stream = (tuple(writes) + tuple(reads) + ("misc",))[0]
        xp = self.last_w.get("XPOSE")
        return self._rec(engine, lambda e: e.dma_start(out, in_, **kw),
                         "dma:" + stream, 16, reads, writes,
                         forced=[xp] if xp else ())

    def xpose(self, out, in_, reads=(), writes=()):
        forced = [(s, v) for s, v in self.ctr.items()
                  if s.startswith("dma:") and v > 0]
        return self._rec("sync", lambda e: e.dma_start_transpose(out, in_),
                         "dma:xpose", 16, reads, tuple(writes) + ("XPOSE",),
                         forced=forced)

    def build(self):
        getsem, ops = self.getsem, self.ops
        with self.nc.Block() as blk:
            for eng_name in ENGINES:
                mine = [(f, d, own, inc) for (e, f, d, own, inc) in ops
                        if e == eng_name]
                if not mine:
                    continue

                def emit(eng, mine=mine):
                    waited = {}
                    for fn, deps, own, inc in mine:
                        for (sname, pval) in deps:
                            if waited.get(sname, -1) < pval:
                                eng.wait_ge(getsem(sname), pval)
                                waited[sname] = pval
                        inst = fn(eng)
                        if inst is None:
                            inst = eng.nop()
                        if inc:
                            inst.then_inc(getsem(own), inc)
                getattr(blk, eng_name)(emit)


def _np_consts():
    i = np.arange(64)
    tril_is = (i[:, None] <= i[None, :]).astype(np.float32)
    dup = lambda a: np.concatenate([a, a], 0)
    bf = ml_dtypes.bfloat16
    return dict(
        TRILNEG=dup(-tril_is).astype(bf),
        TRILLT=dup(tril_is).astype(bf),
        MASKRX=dup(np.repeat(-BIG * (i[None, :] == i[:, None] - 1).astype(np.float32)[:, :, None], 32, axis=2).reshape(64, 2048)).astype(bf),
        ONES64=np.ones((128, 64), np.float32).astype(bf),
        TRILSL=dup((i[None, :] >= i[:, None]).astype(np.float32)),
        IDENT=np.eye(128, dtype=np.float32),
        ONESR=np.ones((1, 128), np.float32).astype(bf),
        IDENTB=np.eye(128, dtype=np.float32).astype(bf),
        CONSTS=np.array([[1e-5, 1.0, 0.0, 0.0]], np.float32),
    )


def bc(ap, shape, axis):
    return ap.unsqueeze(axis).broadcast_to(shape)


def build_nc(dbg=()):
    dbg = set(dbg)
    nc = bass.Bass("TRN2", target_bir_lowering=False, num_devices=8)

    u_seg = nc.dram_tensor("u_seg", [T, D], F32, kind="ExternalInput").ap()
    u_halo = nc.dram_tensor("u_halo", [16, D], F32, kind="ExternalInput").ap()
    W_in = nc.dram_tensor("W_in", [D, DPROJ], F32, kind="ExternalInput").ap()
    conv_w = nc.dram_tensor("conv_w", [2304, 4], F32, kind="ExternalInput").ap()
    conv_b = nc.dram_tensor("conv_b", [2304], F32, kind="ExternalInput").ap()
    dt_bias = nc.dram_tensor("dt_bias", [NH], F32, kind="ExternalInput").ap()
    A_log = nc.dram_tensor("A_log", [NH], F32, kind="ExternalInput").ap()
    norm_w = nc.dram_tensor("norm_w", [DI], F32, kind="ExternalInput").ap()
    norm_b = nc.dram_tensor("norm_b", [DI], F32, kind="ExternalInput").ap()
    W_out = nc.dram_tensor("W_out", [DI, D], F32, kind="ExternalInput").ap()
    wg = nc.dram_tensor("wg", [HID, D], I32, kind="ExternalInput").ap()
    wu = nc.dram_tensor("wu", [HID, D], I32, kind="ExternalInput").ap()
    wd = nc.dram_tensor("wd", [D, HID], I32, kind="ExternalInput").ap()
    TRILNEG = nc.dram_tensor("TRILNEG", [128, 64], BF16, kind="ExternalInput").ap()
    TRILLT = nc.dram_tensor("TRILLT", [128, 64], BF16, kind="ExternalInput").ap()
    MASKRX = nc.dram_tensor("MASKRX", [128, 2048], BF16, kind="ExternalInput").ap()
    ONES64 = nc.dram_tensor("ONES64", [128, 64], BF16, kind="ExternalInput").ap()
    TRILSL = nc.dram_tensor("TRILSL", [128, 64], F32, kind="ExternalInput").ap()
    IDENT = nc.dram_tensor("IDENT", [128, 128], F32, kind="ExternalInput").ap()
    ONESR = nc.dram_tensor("ONESR", [1, 128], BF16, kind="ExternalInput").ap()
    IDENTB = nc.dram_tensor("IDENTB", [128, 128], BF16, kind="ExternalInput").ap()
    CONSTS = nc.dram_tensor("CONSTS", [1, 4], F32, kind="ExternalInput").ap()
    SEL2 = nc.dram_tensor("SEL2", [8, 8], F32, kind="ExternalInput").ap()
    GMASK = nc.dram_tensor("GMASK", [8, 1], F32, kind="ExternalInput").ap()
    out_seg = nc.dram_tensor("out_seg", [T, D], F32, kind="ExternalOutput").ap()

    def scratch(name, shape, dtype):
        kind = "ExternalOutput" if name in dbg else "Internal"
        return nc.dram_tensor(name, shape, dtype, kind=kind).ap()

    SP_X0N = scratch("SP_X0N", [T, D], BF16)
    SP_X0NH = scratch("SP_X0NH", [16, D], BF16)
    SP_Z = scratch("SP_Z", [T, DI], BF16)
    SP_XC = scratch("SP_XC", [2176, T], BF16)
    SP_XS = scratch("SP_XS", [T, DI], BF16)
    SP_XD = scratch("SP_XD", [T, DI], BF16)
    SP_YN = scratch("SP_YN", [T, DI], BF16)
    SP_X1 = scratch("SP_X1", [T, D], F32)
    SP_X1N = scratch("SP_X1N", [T, D], BF16)
    SP_H = scratch("SP_H", [HID, T], BF16)
    SP_WGN = scratch("SP_WGN", [HID, D], BF16)
    SP_WUN = scratch("SP_WUN", [HID, D], BF16)
    SP_WDN = scratch("SP_WDN", [D, HID], BF16)
    SP_SB = scratch("SP_SB", [192, NH], F32)
    SP_NEGA = scratch("SP_NEGA", [NH, 1], F32)
    SP_W8 = scratch("SP_W8", [8, NH], F32)
    CBT = scratch("CBT", [128, 2049], F32)
    SP_DBGS = scratch("SP_DBGS", [128, 2049 + 64], F32)
    SP_DBGS2 = scratch("SP_DBGS2", [128, 2049 + 64], F32)
    GATH = nc.dram_tensor("GATH", [1024, 2049], F32, kind="Internal", addr_space="Shared").ap()

    es_g = ExitStack()
    es_mid = ExitStack()
    es_b = ExitStack()

    def mk(stack, name, shape, dtype=F32):
        return stack.enter_context(nc.sbuf_tensor(name, shape, dtype))

    trilneg = mk(es_g, "trilneg", [128, 64], BF16)
    trillt = mk(es_g, "trillt", [128, 64], BF16)
    maskrx = mk(es_g, "maskrx", [128, 2048], BF16)
    ones64 = mk(es_g, "ones64", [128, 64], BF16)
    trilsl = mk(es_g, "trilsl", [128, 64], F32)
    ident = mk(es_g, "ident", [128, 128], F32)
    onesr = mk(es_g, "onesr", [1, 128], BF16)
    identb = mk(es_g, "identb", [128, 128], BF16)
    cst = mk(es_g, "cst", [128, 4], F32)
    negA_col = mk(es_g, "negA_col", [NH, 1], F32)
    negA_rep = mk(es_g, "negA_rep", [128, NH], F32)
    dtb_rep = mk(es_g, "dtb_rep", [128, NH], F32)
    convw_sb = mk(es_g, "convw_sb", [128, 18, 4], F32)
    convb_sb = mk(es_g, "convb_sb", [128, 18], F32)

    B_FM = mk(es_mid, "B_FM", [128, T], BF16)
    C_FM = mk(es_mid, "C_FM", [128, T], BF16)
    dt_TM = mk(es_mid, "dt_TM", [128, 32, NH], F32)
    a_TM = mk(es_mid, "a_TM", [128, 32, NH], F32)
    a_TMb = mk(es_mid, "a_TMb", [128, 32, NH], BF16)
    atot_HM = mk(es_mid, "atot_HM", [NH, NCK], F32)
    cpi_HM = mk(es_mid, "cpi_HM", [NH, NCK], F32)
    rc_HM = mk(es_mid, "rc_HM", [NH, NCK], F32)
    b2_HM = mk(es_mid, "b2_HM", [NH, NCK], F32)
    dexp_HM = mk(es_mid, "dexp_HM", [NH, NCK], F32)
    aseg_col = mk(es_mid, "aseg_col", [NH, 1], F32)
    cs_TM = mk(es_mid, "cs_TM", [128, 32, NH], F32)
    dsT = mk(es_mid, "dsT", [128, 32, NH], F32)
    ds2T = mk(es_mid, "ds2T", [128, 32, NH], F32)
    sdT = mk(es_mid, "sdT", [128, 32, NH], F32)

    sem_map = {}
    sem_ctr = {}

    def getsem(name):
        if name not in sem_map:
            sem_map[name] = es_g.enter_context(
                nc.semaphore("s_" + name.replace(":", "_")))
        return sem_map[name]

    cc_sem = es_g.enter_context(nc.semaphore("sem_cc"))
    shared = ({}, {})

    def rec():
        return Rec(nc, getsem, sem_ctr, shared)

    def ln_apply(r, pre, stat_reads, stat_aps, st, mv, rs, npart=128):
        for i, ap in enumerate(stat_aps):
            r.op("vector", lambda e, i=i, ap=ap: e.bn_stats(st[:, 6 * i:6 * (i + 1)], ap),
                 reads=tuple(stat_reads) + (pre + "st",), writes=(pre + "st",))
        r.op("vector", lambda e: e.bn_aggr(mv[:, :], st[:, 0:6 * len(stat_aps)]),
             reads=(pre + "st", pre + "mv"), writes=(pre + "mv",))
        r.op("scalar", lambda e: e.activation(rs[:, 0:1], mv[:, 1:2], AF.Ln,
                                              bias=cst[0:npart, 0:1]),
             reads=(pre + "mv", pre + "rs", "cst"), writes=(pre + "rs",))
        r.op("scalar", lambda e: e.activation(rs[:, 0:1], rs[:, 0:1], AF.Exp, scale=-0.5),
             reads=(pre + "rs",), writes=(pre + "rs",))
        r.op("vector", lambda e: e.scalar_tensor_tensor(
            rs[:, 1:2], mv[:, 0:1], -1.0, rs[:, 0:1], op0=ALU.mult, op1=ALU.mult),
            reads=(pre + "mv", pre + "rs"), writes=(pre + "rs",))

    es_win = ExitStack()
    W_in_sb = es_win.enter_context(nc.sbuf_tensor("W_in_sb", [128, 8, DPROJ], BF16))
    es0 = ExitStack()
    w32 = [mk(es0, f"w32_{i}", [128, DPROJ], F32) for i in range(2)]
    w1 = mk(es0, "w1", [NH, 2], F32)

    r = rec()
    for dst, src in ((trilneg, TRILNEG), (trillt, TRILLT), (maskrx, MASKRX),
                     (ones64, ONES64), (trilsl, TRILSL), (ident, IDENT),
                     (onesr, ONESR), (identb, IDENTB)):
        r.dma("sync", dst[:, :], src, writes=(dst.name,))
    r.dma("sync", cst[:, :], bc(CONSTS.rearrange("one c -> (one c)"), [128, 4], 0), writes=("cst",))
    r.dma("sync", dtb_rep[:, :], bc(dt_bias, [128, NH], 0), writes=("dtb_rep",))
    r.dma("sync", convw_sb[:, :, :], conv_w.rearrange("(m p) k -> p m k", p=128),
          writes=("convw",))
    r.dma("sync", convb_sb[:, :], conv_b.rearrange("(m p) -> p m", p=128),
          writes=("convb",), allow_slow_non_contiguous=True)
    r.dma("sync", w1[:, 0:1], A_log.unsqueeze(1), writes=("w1",))
    r.op("scalar", lambda e: e.activation(w1[:, 1:2], w1[:, 0:1], AF.Exp),
         reads=("w1",), writes=("w1e",))
    r.op("scalar", lambda e: e.mul(negA_col[:, :], w1[:, 1:2], -1.0),
         reads=("w1e",), writes=("negA_col",))
    r.dma("sync", SP_NEGA, negA_col[:, :], reads=("negA_col",), writes=("SP_NEGA",))
    r.dma("sync", negA_rep[:, :],
          bc(SP_NEGA.rearrange("h one -> (h one)"), [128, NH], 0),
          reads=("SP_NEGA",), writes=("negA_rep",))
    for k in range(8):
        buf, key = w32[k % 2], f"w32_{k % 2}"
        r.dma("sync", buf[:, :], W_in[128 * k:128 * (k + 1), :], writes=(key,))
        r.op("scalar",
             lambda e, k=k, buf=buf: e.activation(W_in_sb[:, k, :], buf[:, :], AF.Identity),
             reads=(key,), writes=("W_in_sb",))
    esA = ExitStack()
    u_t = [mk(esA, f"u_t{i}", [128, 4, D], F32) for i in range(2)]
    x0_t = [mk(esA, f"x0_t{i}", [128, 4, D], BF16) for i in range(2)]
    st_t = mk(esA, "st_t", [128, 12], F32)
    mv_t = mk(esA, "mv_t", [128, 2], F32)
    rs_t = mk(esA, "rs_t", [128, 2], F32)

    r.dma("sync", u_t[0][0:16, 0, :], u_halo, writes=("u_t0",))
    ln_apply(r, "a", ("u_t0",), [u_t[0][0:16, 0, 0:512], u_t[0][0:16, 0, 512:1024]],
             st_t[0:16, :], mv_t[0:16, :], rs_t[0:16, :], npart=16)
    r.op("scalar", lambda e: e.activation(x0_t[0][0:16, 0, :], u_t[0][0:16, 0, :],
                                          AF.Identity, scale=rs_t[0:16, 0:1],
                                          bias=rs_t[0:16, 1:2]),
         reads=("u_t0", "ars", "x0_t0"), writes=("x0_t0",))
    r.dma("scalar", SP_X0NH, x0_t[0][0:16, 0, :], stream="x0_t0", reads=("x0_t0",), writes=("SP_X0NH",))
    for t in range(NT):
        ub, xb = u_t[t % 2], x0_t[t % 2]
        uk, xk = f"u_t{t % 2}", f"x0_t{t % 2}"
        r.dma("sync", ub[:, :, :],
              u_seg[TT * t:TT * (t + 1), :].rearrange("(g p) d -> p g d", p=128),
              reads=(uk,), writes=(uk,))
        for g in range(4):
            ln_apply(r, "a", (uk,), [ub[:, g, 0:512], ub[:, g, 512:1024]], st_t, mv_t, rs_t)
            r.op("scalar", lambda e, g=g, ub=ub, xb=xb: e.activation(
                xb[:, g, :], ub[:, g, :], AF.Identity,
                scale=rs_t[:, 0:1], bias=rs_t[:, 1:2]),
                reads=(uk, "ars", xk), writes=(xk,))
        r.dma("scalar", SP_X0N[TT * t:TT * (t + 1), :].rearrange("(g p) d -> p g d", p=128),
              xb[:, :, :], stream=xk, reads=(xk,), writes=(f"SP_X0N:{t}",))
    r.build()
    esA.close()
    es0.close()

    esB = ExitStack()
    x0F = [mk(esB, f"x0F{i}", [128, 8, TT], BF16) for i in range(2)]
    x0hF = mk(esB, "x0hF", [128, 8, 16], BF16)
    z_t = [mk(esB, f"z_t{i}", [128, 1024], BF16) for i in range(2)]
    xc = mk(esB, "xc", [128, 18, 515], BF16)
    tail_b = mk(esB, "tail_b", [128, 18, 3], BF16)
    cacc = mk(esB, "cacc", [128, 2, TT], F32)
    hm_buf = mk(esB, "hm_buf", [NH, T], F32)
    cs_HM = mk(esB, "cs_HM", [NH, T], F32)
    sp1 = mk(esB, "sp1", [128, 32, NH], F32)
    sp2 = mk(esB, "sp2", [128, 32, NH], F32)
    psZ = esB.enter_context(nc.psum_tensor("psZ", [128, 2, 1024], F32))
    psX = esB.enter_context(nc.psum_tensor("psX", [128, 2, 512], F32))
    psM = esB.enter_context(nc.psum_tensor("psM", [128, 128], F32))

    WINK = ("W_in_sb",)
    r = rec()
    r.xpose(x0hF[:, :, :], SP_X0NH, reads=("SP_X0NH",), writes=("x0hF",))
    for m in range(18):
        def f_hmm(e, m=m):
            inst = None
            for k in range(8):
                inst = e.matmul(psM[:, 0:16], W_in_sb[:, k, DI + 128 * m:DI + 128 * (m + 1)],
                                x0hF[:, k, :], start=(k == 0), stop=(k == 7))
            return inst
        r.op("tensor", f_hmm, reads=("x0hF", "psM") + WINK, writes=("psM",))
        r.op("scalar", lambda e, m=m: e.activation(tail_b[:, m, :], psM[:, 13:16], AF.Identity),
             reads=("psM", "tail"), writes=("tail",))

    for t in range(NT):
        x0b, x0k = x0F[t % 2], f"x0F{t % 2}"
        r.xpose(x0b[:, :, :], SP_X0N[TT * t:TT * (t + 1), :],
                reads=(f"SP_X0N:{t}", x0k), writes=(x0k,))
        xcb, xck = xc, "xc"
        for g in range(4):
            for n2 in range(2):
                zb, zbk = z_t[n2], f"z_t{n2}"
                def f_zmm(e, g=g, n2=n2, x0b=x0b):
                    inst = None
                    for half in range(2):
                        for k in range(8):
                            inst = e.matmul(psZ[:, n2, 512 * half:512 * (half + 1)],
                                            x0b[:, k, 128 * g:128 * (g + 1)],
                                            W_in_sb[:, k, 1024 * n2 + 512 * half:1024 * n2 + 512 * (half + 1)],
                                            start=(k == 0), stop=(k == 7))
                    return inst
                r.op("tensor", f_zmm, reads=(x0k, f"psZ{n2}") + WINK, writes=(f"psZ{n2}",))
                r.op("scalar", lambda e, n2=n2, zb=zb: e.activation(
                    zb[:, :], psZ[:, n2, :], AF.Identity),
                    reads=(f"psZ{n2}", zbk), writes=(zbk, f"psZ{n2}"))
                r.dma("scalar", SP_Z[TT * t + 128 * g:TT * t + 128 * (g + 1),
                                     1024 * n2:1024 * (n2 + 1)], zb[:, :],
                      stream=zbk, reads=(zbk,), writes=(f"SP_Z:{t // 4}",))
        for g in range(4):
            def f_dtmm(e, g=g, x0b=x0b):
                inst = None
                for k in range(8):
                    inst = e.matmul(psM[:, 0:NH], x0b[:, k, 128 * g:128 * (g + 1)],
                                    W_in_sb[:, k, DPROJ - NH:DPROJ],
                                    start=(k == 0), stop=(k == 7))
                return inst
            r.op("tensor", f_dtmm, reads=(x0k, "psM") + WINK, writes=("psM",))
            r.op("vector", lambda e, t=t, g=g: e.tensor_copy(dt_TM[:, 4 * t + g, :],
                                                             psM[:, 0:NH]),
                 reads=("psM", "dt_TM"), writes=("dt_TM",))
        for m in range(18):
            pb = m % 2
            def f_xmm(e, m=m, pb=pb, x0b=x0b):
                inst = None
                for k in range(8):
                    inst = e.matmul(psX[:, pb, :],
                                    W_in_sb[:, k, DI + 128 * m:DI + 128 * (m + 1)],
                                    x0b[:, k, :], start=(k == 0), stop=(k == 7))
                return inst
            r.op("tensor", f_xmm, reads=(x0k, f"psX{pb}") + WINK, writes=(f"psX{pb}",))
            r.op("scalar", lambda e, m=m, pb=pb, xcb=xcb: e.activation(
                xcb[:, m, 3:515], psX[:, pb, :], AF.Identity),
                reads=(f"psX{pb}", f"{xck}:{m}"), writes=(f"{xck}:{m}", f"psX{pb}"))
        r.op("vector", lambda e, xcb=xcb: e.tensor_copy(xcb[:, :, 0:3], tail_b[:, :, :]),
             reads=("tail",) + tuple(f"{xck}:{m}" for m in range(18)),
             writes=tuple(f"{xck}:{m}t" for m in range(18)))
        for m in range(18):
            ca, cak = cacc[:, m % 2, :], f"cacc{m % 2}"
            xmk = (f"{xck}:{m}", f"{xck}:{m}t")
            r.op("vector", lambda e, m=m, ca=ca, xcb=xcb: e.tensor_scalar(
                ca, xcb[:, m, 0:512], convw_sb[:, m, 0:1], convb_sb[:, m:m + 1],
                op0=ALU.mult, op1=ALU.add),
                reads=xmk + ("convw", "convb", cak), writes=(cak,))
            for i in range(1, 4):
                r.op("vector", lambda e, m=m, i=i, ca=ca, xcb=xcb: e.scalar_tensor_tensor(
                    ca, xcb[:, m, i:i + 512], convw_sb[:, m, i:i + 1], ca,
                    op0=ALU.mult, op1=ALU.add),
                    reads=xmk + ("convw", cak), writes=(cak,))
            r.op("vector", lambda e, m=m, xcb=xcb: e.tensor_copy(tail_b[:, m, :],
                                                                 xcb[:, m, 512:515]),
                 reads=xmk + ("tail",), writes=("tail",))
            r.op("scalar", lambda e, m=m, ca=ca, xcb=xcb: e.activation(
                xcb[:, m, 3:515], ca, AF.Silu),
                reads=(cak,) + xmk, writes=(f"{xck}:{m}",))
        r.op("vector", lambda e, t=t, xcb=xcb: e.tensor_copy(B_FM[:, TT * t:TT * (t + 1)],
                                                             xcb[:, 16, 3:515]),
             reads=(f"{xck}:16",), writes=("B_FM",))
        r.op("vector", lambda e, t=t, xcb=xcb: e.tensor_copy(C_FM[:, TT * t:TT * (t + 1)],
                                                             xcb[:, 17, 3:515]),
             reads=(f"{xck}:17",), writes=("C_FM",))
        r.dma("scalar",
              SP_XC[0:2048, TT * t:TT * (t + 1)].rearrange("(m p) c -> p m c", p=128),
              xcb[:, 0:16, 3:515], stream=xck,
              reads=tuple(f"{xck}:{m}" for m in range(16)), writes=(f"SP_XC:{t}",))
        r.dma("scalar", SP_XC[2048:2176, TT * t:TT * (t + 1)], xcb[:, 16, 3:515],
              stream="xcB", reads=(f"{xck}:16",), writes=(f"SP_XCB:{t}",))

    dtv = dt_TM[:, :, :].rearrange("p a b -> p (a b)")
    sp1v = sp1[:, :, :].rearrange("p a b -> p (a b)")
    sp2v = sp2[:, :, :].rearrange("p a b -> p (a b)")
    r.op("vector", lambda e: e.tensor_tensor(
        dt_TM[:, :, :], dt_TM[:, :, :], bc(dtb_rep[:, :], [128, 32, NH], 1),
        op=ALU.add), reads=("dt_TM", "dtb_rep"), writes=("dt_TM",))
    r.op("scalar", lambda e: e.activation(sp1v, dtv, AF.Abs),
         reads=("dt_TM", "sp1"), writes=("sp1",))
    r.op("scalar", lambda e: e.activation(sp1v, sp1v, AF.Exp, scale=-1.0),
         reads=("sp1",), writes=("sp1",))
    r.op("scalar", lambda e: e.activation(sp1v, sp1v, AF.Ln, bias=cst[:, 1:2]),
         reads=("sp1", "cst"), writes=("sp1",))
    r.op("scalar", lambda e: e.activation(sp2v, dtv, AF.Relu),
         reads=("dt_TM", "sp2"), writes=("sp2",))
    r.op("vector", lambda e: e.tensor_tensor(dtv, sp1v, sp2v, op=ALU.add),
         reads=("sp1", "sp2", "dt_TM"), writes=("dt_TM",))
    r.op("vector", lambda e: e.tensor_tensor(
        a_TM[:, :, :], dt_TM[:, :, :],
        bc(negA_rep[:, :], [128, 32, NH], 1), op=ALU.mult),
        reads=("dt_TM", "negA_rep"), writes=("a_TM",))
    r.op("vector", lambda e: e.tensor_copy(a_TMb[:, :, :], a_TM[:, :, :]),
         reads=("a_TM",), writes=("a_TMb",))
    for p in range(32):
        r.op("tensor", lambda e, p=p: e.transpose(psM[0:32, 0:128], dt_TM[:, p, :], ident[:, :]),
             reads=("dt_TM", "ident", "psM"), writes=("psM",))
        r.op("vector", lambda e, p=p: e.tensor_copy(hm_buf[:, 128 * p:128 * (p + 1)],
                                                    psM[0:32, 0:128]),
             reads=("psM", "hm_buf"), writes=("hm_buf",))
    r.op("vector", lambda e: e.tensor_scalar(hm_buf[:, :], hm_buf[:, :], negA_col[:, 0:1],
                                             None, op0=ALU.mult, op1=ALU.bypass),
         reads=("hm_buf", "negA_col"), writes=("hm_buf",))
    def f_scans(e):
        inst = None
        for c in range(NCK):
            inst = e.tensor_tensor_scan(cs_HM[:, CS * c:CS * (c + 1)], ones64[0:NH, :],
                                        hm_buf[:, CS * c:CS * (c + 1)], initial=0.0,
                                        op0=ALU.mult, op1=ALU.add)
        return inst
    r.op("vector", f_scans, reads=("hm_buf", "ones64", "cs_HM"), writes=("cs_HM",))
    r.op("vector", lambda e: e.tensor_copy(
        atot_HM[:, :], cs_HM[:, :].rearrange("h (c s) -> h c s", s=CS)[:, :, CS - 1]),
        reads=("cs_HM",), writes=("atot_HM",))
    r.op("vector", lambda e: e.tensor_tensor_scan(cpi_HM[:, :], ones64[0:NH, :], atot_HM[:, :],
                                                  initial=0.0, op0=ALU.mult, op1=ALU.add),
         reads=("atot_HM", "ones64"), writes=("cpi_HM",))
    r.op("vector", lambda e: e.reduce_sum(aseg_col[:, :], atot_HM[:, :], axis=AX.X),
         reads=("atot_HM",), writes=("aseg_col",))
    r.op("vector", lambda e: e.tensor_scalar(rc_HM[:, :], cpi_HM[:, :], -1.0, aseg_col[:, 0:1],
                                             op0=ALU.mult, op1=ALU.add),
         reads=("cpi_HM", "aseg_col"), writes=("rc_HM",))
    r.op("vector", lambda e: e.tensor_tensor(b2_HM[:, :], atot_HM[:, :], rc_HM[:, :], op=ALU.add),
         reads=("atot_HM", "rc_HM"), writes=("b2_HM",))
    r.op("scalar", lambda e: e.activation(dexp_HM[:, :], atot_HM[:, :], AF.Exp),
         reads=("atot_HM",), writes=("dexp_HM",))
    r.dma("sync", SP_SB[0:64, :].rearrange("c h -> h c"), atot_HM[:, :],
          reads=("atot_HM",), writes=("SP_SB0",), allow_slow_non_contiguous=True)
    r.dma("sync", SP_SB[64:128, :].rearrange("c h -> h c"), b2_HM[:, :],
          reads=("b2_HM",), writes=("SP_SB1",), allow_slow_non_contiguous=True)
    r.dma("sync", SP_SB[128:192, :].rearrange("c h -> h c"), dexp_HM[:, :],
          reads=("dexp_HM",), writes=("SP_SB2",), allow_slow_non_contiguous=True)
    for p in range(32):
        r.op("tensor", lambda e, p=p: e.transpose(psM[:, 0:32], cs_HM[:, 128 * p:128 * (p + 1)],
                                                  ident[0:32, 0:32]),
             reads=("cs_HM", "ident", "psM"), writes=("psM",))
        r.op("vector", lambda e, p=p: e.tensor_copy(cs_TM[:, p, :], psM[:, 0:32]),
             reads=("psM", "cs_TM"), writes=("cs_TM",))
    csv = cs_TM[:, :, :].rearrange("p a b -> p (a b)")
    for cpar in range(2):
        r.dma("sync", sp1[64 * cpar:64 * (cpar + 1), :, :],
              bass.AP(SP_SB.tensor, NH * cpar, [[0, 64], [2 * NH, 32], [1, NH]]),
              reads=("SP_SB0", "sp1"), writes=("sp1",))
        r.dma("sync", sp2[64 * cpar:64 * (cpar + 1), :, :],
              bass.AP(SP_SB.tensor, 64 * NH + NH * cpar, [[0, 64], [2 * NH, 32], [1, NH]]),
              reads=("SP_SB1", "sp2"), writes=("sp2",))
    r.op("vector", lambda e: e.tensor_tensor(sp1v, sp1v, csv, op=ALU.subtract),
         reads=("sp1", "cs_TM"), writes=("sp1",))
    r.op("vector", lambda e: e.tensor_tensor(sp2v, sp2v, csv, op=ALU.subtract),
         reads=("sp2", "cs_TM"), writes=("sp2",))
    def f_exps(e):
        e.activation(dsT[:, :, :].rearrange("p a b -> p (a b)"), sp1v, AF.Exp)
        e.activation(ds2T[:, :, :].rearrange("p a b -> p (a b)"), sp2v, AF.Exp)
        return e.activation(sdT[:, :, :].rearrange("p a b -> p (a b)"), csv, AF.Exp)
    r.op("scalar", f_exps, reads=("sp1", "sp2", "cs_TM"), writes=("dsT", "ds2T", "sdT"))
    r.build()
    esB.close()
    es_win.close()

    B_TM = mk(es_b, "B_TM", [128, 32, 128], BF16)
    D_rep = mk(es_b, "D_rep", [128, NCK, NH], F32)
    S_sb = mk(es_b, "S_sb", [128, DI], BF16)
    battn = mk(es_b, "battn", [1, D], BF16)
    W_out_sb = mk(es_b, "W_out_sb", [128, 16, D], BF16)

    esC = ExitStack()
    xcT = mk(esC, "xcT", [128, 8, 2176], BF16)
    xs_b = mk(esC, "xs_b", [128, DI], BF16)
    xd_b = mk(esC, "xd_b", [128, DI], BF16)
    xd2_b = mk(esC, "xd2_b", [128, DI], BF16)
    cb_f = mk(esC, "cb_f", [128, DI], F32)
    gth = mk(esC, "gth", [128, DI], F32)
    acc_f = mk(esC, "acc_f", [128, DI], F32)
    tmp_f = mk(esC, "tmp_f", [128, DI], F32)
    a8 = mk(esC, "a8", [8, NH + 8], F32)
    gm_col = mk(esC, "gm_col", [8, 1], F32)
    wrep = mk(esC, "wrep", [128, 8, NH], F32)
    smB = mk(esC, "smB", [8, NH], F32)
    smC = mk(esC, "smC", [8, NH], F32)
    wo32 = [mk(esC, "wo32_0", [128, D], F32)] * 2
    wo16 = mk(esC, "wo16", [128, D], BF16)
    nwb_col = mk(esC, "nwb_col", [128, 32], F32)
    nb_colb = mk(esC, "nb_colb", [128, 16], BF16)
    wi32b = [mk(esC, "wi32b0", [128, 2, D], I32)] * 2
    wibfb = [mk(esC, "wibfb0", [128, 2, D], BF16)] * 2
    psCB = esC.enter_context(nc.psum_tensor("psCB", [128, DI], F32))
    psE = esC.enter_context(nc.psum_tensor("psE", [8, NH], F32))
    psB2 = esC.enter_context(nc.psum_tensor("psB2", [1, D], F32))

    r = rec()
    r.dma("sync", nwb_col[:, 0:16], norm_w.rearrange("(k p) -> p k", p=128),
          writes=("nw_col",), allow_slow_non_contiguous=True)
    r.dma("sync", nwb_col[:, 16:32], norm_b.rearrange("(k p) -> p k", p=128),
          writes=("nb_col32",), allow_slow_non_contiguous=True)
    r.op("scalar", lambda e: e.activation(nb_colb[:, :], nwb_col[:, 16:32], AF.Identity),
         reads=("nb_col32",), writes=("nb_colb",))
    for k in range(16):
        buf, key = wo32[0], "wo32_0"
        r.dma("sync", buf[:, :], W_out[128 * k:128 * (k + 1), :], reads=(key,), writes=(key,))
        r.op("scalar", lambda e, buf=buf: e.activation(wo16[:, :], buf[:, :], AF.Identity),
             reads=(key, "wo16"), writes=("wo16",))
        def f_bmm(e, k=k):
            e.matmul(psB2[:, 0:512], nb_colb[:, k:k + 1], wo16[:, 0:512],
                     start=(k == 0), stop=(k == 15))
            return e.matmul(psB2[:, 512:1024], nb_colb[:, k:k + 1], wo16[:, 512:1024],
                            start=(k == 0), stop=(k == 15))
        r.op("tensor", f_bmm, reads=("wo16", "nb_colb", "psB2"), writes=("psB2",))
        r.op("scalar", lambda e, k=k, buf=buf: e.activation(
            W_out_sb[:, k, :], buf[:, :], AF.Identity, scale=nwb_col[:, k:k + 1]),
            reads=(key, "nw_col", "W_out_sb"), writes=("W_out_sb",))
    r.op("scalar", lambda e: e.activation(battn[:, :], psB2[:, :], AF.Identity),
         reads=("psB2",), writes=("battn",))
    rnd_i = [0]
    def bounce(wsrc, wdst, wname, r0, rows, wkey):
        i = rnd_i[0] % 2
        rnd_i[0] += 1
        m = rows // 128
        r.dma("scalar", wi32b[i][:, 0:m, :],
              wsrc[r0:r0 + rows, :].rearrange("(m p) c -> p m c", p=128),
              stream="wi32b0", reads=("wi32b0",), writes=("wi32b0",))
        r.op("scalar", lambda e, i=i, m=m: e.activation(
            wibfb[i][:, 0:m, :], wi32b[i][:, 0:m, :], AF.Identity),
            reads=("wi32b0", "wibfb0"), writes=("wibfb0",))
        r.dma("scalar", wdst[r0:r0 + rows, :].rearrange("(m p) c -> p m c", p=128),
              wibfb[i][:, 0:m, :], stream="wibfb0", reads=("wibfb0",),
              writes=(wkey,))
    for wsrc, wdst, wname in ((wg, SP_WGN, "SP_WGN"), (wu, SP_WUN, "SP_WUN")):
        for r0 in range(0, HID, 256):
            bounce(wsrc, wdst, wname, r0, 256, f"{wname}:{r0 // 1024 * 1024}")
    def bounce_wd(r0, co):
        i = rnd_i[0] % 2
        rnd_i[0] += 1
        fv32 = wi32b[i][:, :, :].rearrange("p a b -> p (a b)")
        fvbf = wibfb[i][:, :, :].rearrange("p a b -> p (a b)")
        r.dma("scalar", fv32[:, 0:1536], wd[r0:r0 + 128, co:co + 1536],
              stream="wi32b0", reads=("wi32b0",), writes=("wi32b0",))
        r.op("scalar", lambda e, i=i, fv32=fv32, fvbf=fvbf: e.activation(
            fvbf[:, 0:1536], fv32[:, 0:1536], AF.Identity),
            reads=("wi32b0", "wibfb0"), writes=("wibfb0",))
        r.dma("scalar", SP_WDN[r0:r0 + 128, co:co + 1536], fvbf[:, 0:1536],
              stream="wibfb0", reads=("wibfb0",),
              writes=(f"SP_WDN:{r0 // 256 * 256}",))
    for r0 in range(0, D, 128):
        for co in (0, 1536):
            bounce_wd(r0, co)
    r.dma("sync", D_rep[:, :, :],
          bass.AP(SP_SB.tensor, 128 * NH, [[0, 128], [NH, NCK], [1, NH]]),
          reads=("SP_SB2",), writes=("D_rep",))
    first = [True]
    for quarter in range(4):
        r.xpose(xcT[:, :, :], SP_XC[:, 1024 * quarter:1024 * (quarter + 1)],
                reads=(f"SP_XC:{2 * quarter}", f"SP_XC:{2 * quarter + 1}",
                       f"SP_XCB:{2 * quarter}", f"SP_XCB:{2 * quarter + 1}", "xcT"),
                writes=("xcT",))
        for m in range(8):
            pp = 8 * quarter + m
            r.op("vector", lambda e, m=m, pp=pp: e.tensor_tensor(
                xs_b[:, :].rearrange("p (h q) -> p h q", q=HP),
                xcT[:, m, 0:DI].rearrange("p (h q) -> p h q", q=HP),
                bc(dt_TM[:, pp, :], [128, NH, HP], 2), op=ALU.mult),
                reads=("xcT", "dt_TM", "xs_b"), writes=("xs_b",))
            r.op("gpsimd", lambda e, pp=pp: e.tensor_tensor(
                xd_b[:, :].rearrange("p (h q) -> p h q", q=HP),
                xs_b[:, :].rearrange("p (h q) -> p h q", q=HP),
                bc(dsT[:, pp, :], [128, NH, HP], 2), op=ALU.mult),
                reads=("xs_b", "dsT", "xd_b"), writes=("xd_b",))
            r.op("gpsimd", lambda e, pp=pp: e.tensor_tensor(
                xd2_b[:, :].rearrange("p (h q) -> p h q", q=HP),
                xs_b[:, :].rearrange("p (h q) -> p h q", q=HP),
                bc(ds2T[:, pp, :], [128, NH, HP], 2), op=ALU.mult),
                reads=("xs_b", "ds2T", "xd2_b"), writes=("xd2_b",))
            r.op("vector", lambda e, m=m, pp=pp: e.tensor_copy(B_TM[:, pp, :],
                                                               xcT[:, m, 2048:2176]),
                 reads=("xcT", "B_TM"), writes=("B_TM",))
            r.dma("scalar", SP_XS[128 * pp:128 * (pp + 1), :], xs_b[:, :],
                  stream="xs_b", reads=("xs_b",), writes=(f"SP_XS:{pp // 4}",))
            r.dma("scalar", SP_XD[128 * pp:128 * (pp + 1), :], xd_b[:, :],
                  stream="xd_b", reads=("xd_b",), writes=(f"SP_XD:{pp // 4}",))
            def f_st2(e, m=m, first=first, pp=pp):
                inst = None
                for cc2 in range(2):
                    sl = slice(64 * cc2, 64 * (cc2 + 1))
                    for h in range(NH):
                        inst = e.matmul(psCB[:, HP * h:HP * (h + 1)], xcT[sl, m, 2048:2176],
                                        xd2_b[sl, HP * h:HP * (h + 1)], start=(first[0] and h % 8 == 0),
                                        stop=(pp == NPAIR - 1 and cc2 == 1 and h == NH - 1))
                        if h == NH - 1:
                            first[0] = False
                return inst
            r.op("tensor", f_st2, reads=("xcT", "xd2_b", "psCB"), writes=("psCB",))
    r.op("vector", lambda e: e.tensor_copy(cb_f[:, :], psCB[:, :]),
         reads=("psCB",), writes=("cb_f",))
    r.dma("sync", CBT[:, 0:DI], cb_f[:, :], reads=("cb_f",), writes=("CBT",))
    r.dma("sync", CBT[0:NH, DI:DI + 1], aseg_col[:, :], reads=("aseg_col",), writes=("CBT2",), allow_slow_non_contiguous=True)
    r.build()
    nc.all_core_barrier()
    r = rec()
    r.op("gpsimd", lambda e: e.collective_compute(
        "AllGather", ALU.bypass, replica_groups=[list(range(8))], ins=[CBT], outs=[GATH])
        .then_inc(cc_sem, 1) and None,
        reads=("CBT", "CBT2"), writes=(), inc=0)
    r.op("gpsimd", lambda e: e.wait_ge(cc_sem, 1) or e.nop(), reads=(), writes=("GATH",))
    r.build()
    nc.all_core_barrier()
    r = rec()
    r.dma("sync", a8[:, NH:NH + 8], SEL2, writes=("a8sel",))
    r.dma("sync", a8[0:8, 0:NH], GATH[:, DI].rearrange("(j p) -> j p", p=128)[:, 0:NH],
          reads=("GATH",), writes=("a8",), allow_slow_non_contiguous=True)
    r.dma("sync", gm_col[:, :], GMASK, writes=("gm_col",))
    r.op("tensor", lambda e: e.matmul(psE[:, :], a8[:, NH:NH + 8], a8[0:8, 0:NH],
                                      start=True, stop=True),
         reads=("a8sel", "a8", "psE"), writes=("psE",))
    r.op("scalar", lambda e: e.activation(smB[:, :], psE[:, :], AF.Exp),
         reads=("psE",), writes=("smB",))
    r.op("vector", lambda e: e.tensor_scalar(smC[:, :], smB[:, :], gm_col[:, 0:1], None,
                                             op0=ALU.mult, op1=ALU.bypass),
         reads=("smB", "gm_col"), writes=("smC",))
    r.dma("sync", SP_W8, smC[:, :], reads=("smC",), writes=("SP_W8",))
    r.dma("sync", wrep[:, :, :], bass.AP(SP_W8.tensor, 0, [[0, 128], [NH, 8], [1, NH]]),
          reads=("SP_W8",), writes=("wrep",))
    for j in range(8):
        r.dma("sync", gth[:, :], GATH[128 * j:128 * (j + 1), 0:DI],
              reads=("GATH", "gth"), writes=("gth",))
        if j == 0:
            r.op("vector", lambda e: e.tensor_tensor(
                acc_f[:, :].rearrange("p (h q) -> p h q", q=HP),
                gth[:, :].rearrange("p (h q) -> p h q", q=HP),
                bc(wrep[:, 0, :], [128, NH, HP], 2), op=ALU.mult),
                reads=("gth", "wrep", "acc_f"), writes=("acc_f",))
        else:
            r.op("vector", lambda e, j=j: e.tensor_tensor(
                tmp_f[:, :].rearrange("p (h q) -> p h q", q=HP),
                gth[:, :].rearrange("p (h q) -> p h q", q=HP),
                bc(wrep[:, j, :], [128, NH, HP], 2), op=ALU.mult),
                reads=("gth", "wrep", "tmp_f"), writes=("tmp_f",))
            r.op("vector", lambda e: e.tensor_tensor(acc_f[:, :], acc_f[:, :], tmp_f[:, :],
                                                     op=ALU.add),
                 reads=("tmp_f", "acc_f"), writes=("acc_f",))
    r.op("vector", lambda e: e.tensor_copy(S_sb[:, :], acc_f[:, :]),
         reads=("acc_f", "S_sb"), writes=("S_sb",))
    if "SP_DBGS2" in dbg:
        r.dma("sync", SP_DBGS2[:, 0:DI], cb_f[:, :], stream="cb_f", reads=("cb_f",), writes=("dbgs9",))
        r.dma("sync", SP_DBGS2[0:128, DI:DI + NH].rearrange("p c -> p c"), wrep[:, 0, :],
              reads=("wrep",), writes=("dbgs10",))
    if "SP_DBGS" in dbg:
        r.dma("sync", SP_DBGS[:, 0:DI], acc_f[:, :], stream="acc_f", reads=("acc_f",), writes=("dbgs",))
        r.dma("sync", SP_DBGS[0:8, DI:DI + NH], smC[:, :], reads=("smC",), writes=("dbgs2",))
        r.dma("sync", SP_DBGS[8:16, DI:DI + NH], a8[:, 0:NH], reads=("a8",), writes=("dbgs3",))
        r.dma("sync", SP_DBGS[16:24, DI:DI + NH], smB[:, :], reads=("smB",), writes=("dbgs4",))
    r.build()
    esC.close()

    esD = ExitStack()
    xs_p = [mk(esD, f"xs_p{i}", [128, DI], BF16) for i in range(2)]
    xd_p = [mk(esD, f"xd_p{i}", [128, DI], BF16) for i in range(2)]
    z_p = [mk(esD, f"z_p{i}", [128, DI], BF16) for i in range(2)]
    AM_sb = mk(esD, "AM_sb", [128, DI], BF16)
    M0_sb = mk(esD, "M0_sb", [128, DI], BF16)
    M_sb = mk(esD, "M_sb", [128, DI], BF16)
    Gt_sb = mk(esD, "Gt_sb", [128, 64], BF16)
    y_sb = mk(esD, "y_sb", [128, DI], F32)
    zs_sb = mk(esD, "zs_sb", [128, DI], BF16)
    yn_sb = mk(esD, "yn_sb", [128, DI], BF16)
    st2 = mk(esD, "st2", [128, 24], F32)
    mv2 = mk(esD, "mv2", [128, 2], F32)
    rs2 = mk(esD, "rs2", [128, 2], F32)
    ynF = mk(esD, "ynF", [128, 16, TT], BF16)
    at_f = mk(esD, "at_f", [128, D], F32)
    u2_t = [mk(esD, f"u2_t{i}", [128, D], F32) for i in range(2)]
    x1_t = mk(esD, "x1_t", [128, D], F32)
    x1n_t = mk(esD, "x1n_t", [128, D], BF16)
    psT = esD.enter_context(nc.psum_tensor("psT", [128, DI], F32))
    psO = esD.enter_context(nc.psum_tensor("psO", [128, DI], F32))

    r = rec()
    for pp in range(NPAIR):
        bi = pp % 2
        xsb, xdb, zb = xs_p[bi], xd_p[bi], z_p[bi]
        xsk, xdk, zk = f"xs_p{bi}", f"xd_p{bi}", f"z_p{bi}"
        r.dma("sync", xsb[:, :], SP_XS[128 * pp:128 * (pp + 1), :],
              reads=(f"SP_XS:{pp // 4}", xsk), writes=(xsk,))
        r.dma("sync", xdb[:, :], SP_XD[128 * pp:128 * (pp + 1), :],
              reads=(f"SP_XD:{pp // 4}", xdk), writes=(xdk,))
        r.dma("sync", zb[:, :], SP_Z[128 * pp:128 * (pp + 1), :],
              reads=(f"SP_Z:{pp // 4}", zk), writes=(zk,))
        for cc in range(2):
            c = 2 * pp + cc
            sl = slice(64 * cc, 64 * (cc + 1))
            amk = f"AM{cc}"
            r.op("gpsimd", lambda e, pp=pp, sl=sl: e.tensor_tensor(
                AM_sb[sl, :].rearrange("p (l h) -> p l h", h=NH),
                bc(a_TM[sl, pp, :], [64, 64, NH], 1),
                bc(trillt[sl, :], [64, 64, NH], 2), op=ALU.mult),
                reads=("a_TM", "trillt", amk), writes=(amk,))
            def f_tmm(e, pp=pp, sl=sl):
                inst = None
                for nb4 in range(4):
                    ls = slice(16 * nb4, 16 * (nb4 + 1))
                    ps = psT[sl, 512 * nb4:512 * (nb4 + 1)].rearrange("p (l h) -> p l h", h=NH)
                    e.matmul(ps, trilneg[sl, :],
                             bc(a_TMb[sl, pp, :], [64, 64, NH], 1)[:, ls, :],
                             start=True, stop=False)
                    e.matmul(ps, ones64[sl, :],
                             AM_sb[sl, :].rearrange("p (l h) -> p l h", h=NH)[:, ls, :],
                             start=False, stop=False)
                    inst = e.matmul(ps, trillt[sl, :],
                                    maskrx[sl, :].rearrange("p (l h) -> p l h", h=NH)[:, ls, :],
                                    start=False, stop=True)
                return inst
            r.op("tensor", f_tmm, reads=("a_TMb", "trilneg", "ones64", amk, "maskrx", f"psT{cc}"),
                 writes=(f"psT{cc}",))
            r.op("scalar", lambda e, sl=sl: e.activation(M0_sb[sl, :], psT[sl, :], AF.Exp),
                 reads=(f"psT{cc}", f"M0{cc}"), writes=(f"M0{cc}", f"psTr{cc}"))
            r.op("tensor", lambda e, c=c, sl=sl: e.matmul(
                psT[sl, 0:64], B_FM[:, 64 * c:64 * (c + 1)], C_FM[:, 64 * c:64 * (c + 1)],
                start=True, stop=True),
                reads=("B_FM", "C_FM", f"psTr{cc}"), writes=(f"psT{cc}",))
            r.op("vector", lambda e, sl=sl, cc=cc: e.tensor_tensor(
                Gt_sb[sl, :], psT[sl, 0:64], trilsl[sl, :], op=ALU.mult),
                reads=(f"psT{cc}", "trilsl", f"Gt{cc}"), writes=(f"Gt{cc}", f"psTr{cc}"))
            r.op("gpsimd", lambda e, sl=sl, cc=cc: e.tensor_tensor(
                M_sb[sl, :].rearrange("p (l h) -> p l h", h=NH),
                M0_sb[sl, :].rearrange("p (l h) -> p l h", h=NH),
                bc(Gt_sb[sl, :], [64, 64, NH], 2), op=ALU.mult),
                reads=(f"M0{cc}", f"Gt{cc}", f"M{cc}"), writes=(f"M{cc}",))
            def f_yoff(e, c=c, sl=sl):
                inst = None
                for h in range(NH):
                    inst = e.matmul(psO[sl, HP * h:HP * (h + 1)], C_FM[:, 64 * c:64 * (c + 1)],
                                    S_sb[:, HP * h:HP * (h + 1)], start=True, stop=True)
                return inst
            r.op("tensor", f_yoff, reads=("S_sb", "C_FM", f"psO{cc}"), writes=(f"psO{cc}",))
            r.op("vector", lambda e, sl=sl, pp=pp: e.tensor_tensor(
                y_sb[sl, :].rearrange("p (h q) -> p h q", q=HP),
                psO[sl, :].rearrange("p (h q) -> p h q", q=HP),
                bc(sdT[sl, pp, :], [64, NH, HP], 2), op=ALU.mult),
                reads=(f"psO{cc}", "sdT", f"y{cc}", "psOF"), writes=(f"y{cc}", f"psO{cc}"))
            def f_st(e, pp=pp, sl=sl, bi=bi):
                inst = None
                for h in range(NH):
                    inst = e.matmul(psO[:, HP * h:HP * (h + 1)], B_TM[sl, pp, :],
                                    xd_p[bi][sl, HP * h:HP * (h + 1)], start=True, stop=True)
                return inst
            r.op("tensor", f_st, reads=("B_TM", xdk, "psOF"), writes=("psOF", "psO0", "psO1"))
            r.op("gpsimd", lambda e, c=c: e.tensor_tensor(
                S_sb[:, :].rearrange("p (h q) -> p h q", q=HP),
                S_sb[:, :].rearrange("p (h q) -> p h q", q=HP),
                bc(D_rep[:, c, :], [128, NH, HP], 2), op=ALU.mult),
                reads=("S_sb", "D_rep"), writes=("S_sb",))
            r.op("vector", lambda e: e.tensor_tensor(S_sb[:, :], S_sb[:, :], psO[:, :],
                                                     op=ALU.add),
                 reads=("S_sb", "psOF"), writes=("S_sb", "psOF"))
            def f_yd(e, sl=sl, bi=bi):
                inst = None
                for h in range(NH):
                    inst = e.matmul(psT[sl, HP * h:HP * (h + 1)],
                                    M_sb[sl, :].rearrange("p (l h) -> p l h", h=NH)[:, :, h],
                                    xs_p[bi][sl, HP * h:HP * (h + 1)], start=True, stop=True)
                return inst
            r.op("tensor", f_yd, reads=(f"M{cc}", xsk, f"psTr{cc}"), writes=(f"psT{cc}",))
            r.op("vector", lambda e, sl=sl, cc=cc: e.tensor_tensor(
                y_sb[sl, :], y_sb[sl, :], psT[sl, :], op=ALU.add),
                reads=(f"psT{cc}", f"y{cc}"), writes=(f"y{cc}", f"psTr{cc}"))
        r.op("scalar", lambda e, zb=zb: e.activation(zs_sb[:, :], zb[:, :], AF.Silu),
             reads=(zk, "zs"), writes=("zs",))
        r.op("gpsimd", lambda e: e.tensor_tensor(y_sb[:, :], y_sb[:, :], zs_sb[:, :],
                                                 op=ALU.mult),
             reads=("y0", "y1", "zs"), writes=("y0", "y1"))
        ln_apply(r, "ln", ("y0", "y1"), [y_sb[:, 512 * i:512 * (i + 1)] for i in range(4)],
                 st2, mv2, rs2)
        r.op("scalar", lambda e: e.activation(yn_sb[:, :], y_sb[:, :], AF.Identity,
                                              scale=rs2[:, 0:1], bias=rs2[:, 1:2]),
             reads=("y0", "y1", "lnrs", "yn"), writes=("yn",))
        r.dma("scalar", SP_YN[128 * pp:128 * (pp + 1), :], yn_sb[:, :],
              stream="yn", reads=("yn",), writes=(f"SP_YN:{pp // 4}",))
        if pp % 4 == 3:
            t = pp // 4
            r.xpose(ynF[:, :, :], SP_YN[TT * t:TT * (t + 1), :],
                    reads=(f"SP_YN:{t}", "ynF"), writes=("ynF",))
            for g in range(4):
                ub, ubk = u2_t[g % 2], f"u2_t{g % 2}"
                r.dma("sync", ub[:, :],
                      u_seg[TT * t + 128 * g:TT * t + 128 * (g + 1), :],
                      reads=(ubk,), writes=(ubk,))
                def f_op(e, g=g):
                    inst = None
                    for n2 in range(2):
                        for k in range(16):
                            e.matmul(psO[:, 512 * n2:512 * (n2 + 1)],
                                     ynF[:, k, 128 * g:128 * (g + 1)],
                                     W_out_sb[:, k, 512 * n2:512 * (n2 + 1)],
                                     start=(k == 0), stop=False)
                        inst = e.matmul(psO[:, 512 * n2:512 * (n2 + 1)], onesr[:, :],
                                        battn[:, 512 * n2:512 * (n2 + 1)],
                                        start=False, stop=True)
                    return inst
                r.op("tensor", f_op,
                     reads=("ynF", "battn", "onesr", "psOF", "W_out_sb"),
                     writes=("psOF", "psO0", "psO1"))
                ln_apply(r, "ln", ("psOF",), [psO[:, 0:512], psO[:, 512:1024]], st2, mv2, rs2)
                def f_atn(e):
                    e.activation(at_f[:, 0:512], psO[:, 0:512], AF.Identity,
                                 scale=rs2[:, 0:1], bias=rs2[:, 1:2])
                    return e.activation(at_f[:, 512:1024], psO[:, 512:1024], AF.Identity,
                                        scale=rs2[:, 0:1], bias=rs2[:, 1:2])
                r.op("scalar", f_atn, reads=("psOF", "lnrs", "at_f"), writes=("at_f", "psOF"))
                r.op("vector", lambda e, g=g, ub=ub: e.tensor_tensor(
                    x1_t[:, :], at_f[:, :], ub[:, :], op=ALU.add),
                    reads=("at_f", ubk, "x1_t"), writes=("x1_t",))
                ln_apply(r, "ln", ("x1_t",), [x1_t[:, 0:512], x1_t[:, 512:1024]],
                         st2, mv2, rs2)
                r.op("scalar", lambda e, g=g: e.activation(
                    x1n_t[:, :], x1_t[:, :], AF.Identity,
                    scale=rs2[:, 0:1], bias=rs2[:, 1:2]),
                    reads=("x1_t", "lnrs", "x1n_t"), writes=("x1n_t",))
                r.dma("scalar", SP_X1[TT * t + 128 * g:TT * t + 128 * (g + 1), :],
                      x1_t[:, :], stream="x1_t", reads=("x1_t",), writes=(f"SP_X1:{t}",))
                r.dma("scalar", SP_X1N[TT * t + 128 * g:TT * t + 128 * (g + 1), :],
                      x1n_t[:, :], stream="x1n_t", reads=("x1n_t",), writes=(f"SP_X1N:{t}",))
    r.build()
    esD.close()
    es_b.close()
    es_mid.close()

    esE = ExitStack()
    wgT = mk(esE, "wgT", [128, 8, HID], BF16)
    wuT = mk(esE, "wuT", [128, 8, HID], BF16)
    wdT = mk(esE, "wdT", [128, 24, D], BF16)
    x1F = [mk(esE, f"x1F{i}", [128, 8, TT], BF16) for i in range(2)]
    gs_sb = mk(esE, "gs_sb", [128, 512], BF16)
    h_sb = mk(esE, "h_sb", [128, 24, TT], BF16)
    x1_2 = mk(esE, "x1_2", [128, 2, D], F32)
    o_sb = mk(esE, "o_sb", [128, D], F32)
    stf = mk(esE, "stf", [128, 12], F32)
    mvf = mk(esE, "mvf", [128, 2], F32)
    rsf = mk(esE, "rsf", [128, 2], F32)
    psG2 = esE.enter_context(nc.psum_tensor("psG2", [128, 2, 512], F32))
    psU2 = esE.enter_context(nc.psum_tensor("psU2", [128, 2, 512], F32))
    psD2 = esE.enter_context(nc.psum_tensor("psD2", [128, 4, 512], F32))

    r = rec()
    r.xpose(wgT[:, :, :], SP_WGN,
            reads=tuple(f"SP_WGN:{r0}" for r0 in range(0, HID, 1024)), writes=("wgT",))
    r.xpose(wuT[:, :, :], SP_WUN,
            reads=tuple(f"SP_WUN:{r0}" for r0 in range(0, HID, 1024)), writes=("wuT",))
    r.xpose(wdT[:, :, :], SP_WDN,
            reads=tuple(f"SP_WDN:{r0}" for r0 in range(0, D, 256)), writes=("wdT",))
    for t in range(NT):
        xb, xk = x1F[t % 2], f"x1F{t % 2}"
        r.xpose(xb[:, :, :], SP_X1N[TT * t:TT * (t + 1), :],
                reads=(f"SP_X1N:{t}", xk), writes=(xk,))
        for m in range(24):
            pb = m % 2
            def f_gmm(e, m=m, pb=pb, xb=xb):
                inst = None
                for k in range(8):
                    inst = e.matmul(psG2[:, pb, :], wgT[:, k, 128 * m:128 * (m + 1)],
                                    xb[:, k, :], start=(k == 0), stop=(k == 7))
                return inst
            r.op("tensor", f_gmm, reads=(xk, "wgT", f"psG2{pb}"), writes=(f"psG2{pb}",))
            def f_umm(e, m=m, pb=pb, xb=xb):
                inst = None
                for k in range(8):
                    inst = e.matmul(psU2[:, pb, :], wuT[:, k, 128 * m:128 * (m + 1)],
                                    xb[:, k, :], start=(k == 0), stop=(k == 7))
                return inst
            r.op("tensor", f_umm, reads=(xk, "wuT", f"psU2{pb}"), writes=(f"psU2{pb}",))
            r.op("scalar", lambda e, pb=pb: e.activation(gs_sb[:, :], psG2[:, pb, :],
                                                         AF.Silu, scale=1.0 / 127.0),
                 reads=(f"psG2{pb}", "gs"), writes=("gs", f"psG2{pb}"))
            r.op("vector", lambda e, m=m, pb=pb: e.tensor_tensor(
                h_sb[:, m, :], gs_sb[:, :], psU2[:, pb, :], op=ALU.mult),
                reads=("gs", f"psU2{pb}", "h_sb"), writes=("h_sb", f"psU2{pb}"))
        for g in range(4):
            gp = g % 2
            if g % 2 == 0:
                r.dma("sync", x1_2[:, :, :],
                      SP_X1[TT * t + 128 * g:TT * t + 128 * (g + 2), :]
                      .rearrange("(g2 p) d -> p g2 d", p=128),
                      stream="x1_2", reads=("x1_2",), writes=("x1_2",))
            def f_dmm(e, g=g, gp=gp):
                inst = None
                for n2 in range(2):
                    for k in range(24):
                        inst = e.matmul(psD2[:, 2 * gp + n2, :],
                                        h_sb[:, k, 128 * g:128 * (g + 1)],
                                        wdT[:, k, 512 * n2:512 * (n2 + 1)],
                                        start=(k == 0), stop=(k == 23))
                return inst
            r.op("tensor", f_dmm, reads=("h_sb", "wdT", f"psD2{gp}"), writes=(f"psD2{gp}",))
            def f_stats(e, gp=gp):
                e.bn_stats(stf[:, 0:6], psD2[:, 2 * gp, :])
                return e.bn_stats(stf[:, 6:12], psD2[:, 2 * gp + 1, :])
            r.op("vector", lambda e, gp=gp: e.bn_stats(stf[:, 0:6], psD2[:, 2 * gp, :]),
                 reads=(f"psD2{gp}", "stf"), writes=("stf",))
            r.op("vector", lambda e, gp=gp: e.bn_stats(stf[:, 6:12], psD2[:, 2 * gp + 1, :]),
                 reads=(f"psD2{gp}", "stf"), writes=("stf",))
            r.op("vector", lambda e: e.bn_aggr(mvf[:, :], stf[:, 0:12]),
                 reads=("stf", "mvf"), writes=("mvf",))
            r.op("scalar", lambda e: e.activation(rsf[:, 0:1], mvf[:, 1:2], AF.Ln,
                                                  bias=cst[:, 0:1]),
                 reads=("mvf", "rsf", "cst"), writes=("rsf",))
            r.op("scalar", lambda e: e.activation(rsf[:, 0:1], rsf[:, 0:1], AF.Exp,
                                                  scale=-0.5),
                 reads=("rsf",), writes=("rsf",))
            r.op("vector", lambda e: e.scalar_tensor_tensor(
                rsf[:, 1:2], mvf[:, 0:1], -1.0, rsf[:, 0:1], op0=ALU.mult, op1=ALU.mult),
                reads=("mvf", "rsf"), writes=("rsf",))
            def f_lnf(e, gp=gp):
                e.activation(o_sb[:, 0:512], psD2[:, 2 * gp, :], AF.Identity,
                             scale=rsf[:, 0:1], bias=rsf[:, 1:2])
                return e.activation(o_sb[:, 512:1024], psD2[:, 2 * gp + 1, :], AF.Identity,
                                    scale=rsf[:, 0:1], bias=rsf[:, 1:2])
            r.op("scalar", f_lnf, reads=(f"psD2{gp}", "rsf", "o_sb"),
                 writes=("o_sb", f"psD2{gp}"))
            r.op("vector", lambda e, g=g: e.tensor_tensor(
                o_sb[:, :], o_sb[:, :], x1_2[:, g % 2, :], op=ALU.add),
                reads=("x1_2", "o_sb"), writes=("o_sb",))
            r.dma("scalar",
                  out_seg[TT * t + 128 * g:TT * t + 128 * (g + 1), :], o_sb[:, :],
                  stream="o_sb", reads=("o_sb",), writes=(f"OUT:{t}",))
    r.op("gpsimd", lambda e: None, reads=tuple(f"OUT:{t}" for t in range(NT)), writes=())
    r.build()
    esE.close()
    es_g.close()
    nc._sem_ctr = dict(sem_ctr)
    return nc


def kernel(L0_attn_u, L0_attn_W_in, L0_attn_conv_w, L0_attn_conv_b,
           L0_attn_dt_bias, L0_attn_A_log, L0_attn_norm_w, L0_attn_norm_b,
           L0_attn_W_out, L0_ffn_w_gate, L0_ffn_w_up, L0_ffn_w_down):
    u = np.ascontiguousarray(np.asarray(L0_attn_u, np.float32))
    consts = _np_consts()
    nc = build_nc(dbg=DBG)
    in_maps = []
    for core in range(8):
        b, q = core // 4, core % 4
        seg = np.ascontiguousarray(u[b, T * q:T * (q + 1)])
        halo = np.zeros((16, D), np.float32)
        if q > 0:
            halo[13:16] = u[b, T * q - 3:T * q]
        sel2 = np.zeros((8, 8), np.float32)
        gmask = np.zeros((8, 1), np.float32)
        for j in range(8):
            if j // 4 == b and j % 4 < q:
                gmask[j, 0] = 1.0
                for i in range(8):
                    if i // 4 == b and j % 4 < i % 4 < q:
                        sel2[i, j] = 1.0
        m = dict(u_seg=seg, u_halo=halo,
                 W_in=np.ascontiguousarray(np.asarray(L0_attn_W_in, np.float32)),
                 conv_w=np.ascontiguousarray(np.asarray(L0_attn_conv_w, np.float32)),
                 conv_b=np.ascontiguousarray(np.asarray(L0_attn_conv_b, np.float32)),
                 dt_bias=np.ascontiguousarray(np.asarray(L0_attn_dt_bias, np.float32)),
                 A_log=np.ascontiguousarray(np.asarray(L0_attn_A_log, np.float32)),
                 norm_w=np.ascontiguousarray(np.asarray(L0_attn_norm_w, np.float32)),
                 norm_b=np.ascontiguousarray(np.asarray(L0_attn_norm_b, np.float32)),
                 W_out=np.ascontiguousarray(np.asarray(L0_attn_W_out, np.float32)),
                 wg=np.ascontiguousarray(np.asarray(L0_ffn_w_gate, np.int32)),
                 wu=np.ascontiguousarray(np.asarray(L0_ffn_w_up, np.int32)),
                 wd=np.ascontiguousarray(np.asarray(L0_ffn_w_down, np.int32)),
                 SEL2=sel2, GMASK=gmask, **consts)
        in_maps.append(m)
    res = run_bass_kernel_spmd(nc, in_maps, core_ids=list(range(8)))
    out = np.empty((2, 16384, D), np.float32)
    for core in range(8):
        b, q = core // 4, core % 4
        out[b, T * q:T * (q + 1)] = res.results[core]["out_seg"]
    return out
